# version 06
# baseline (speedup 1.0000x reference)
import dataclasses
import numpy as np
import ml_dtypes
import concourse.bass as bass
import concourse.mybir as mybir
from concourse.bass_utils import run_bass_kernel_spmd

F32 = mybir.dt.float32
BF16 = mybir.dt.bfloat16
AF = mybir.ActivationFunctionType
ALU = mybir.AluOpType
AX = mybir.AxisListType

B, S, D = 2, 8192, 1024
NH, HD = 16, 64
LH = 4
NE, FFN = 8, 4096
NEG = -1.0e9
TOK = 2048
NT = 16
HALO = 128
XW = HALO + TOK
NTX = XW // 128

_DEBUG = False
_TRUNC = None
_SUB = 9


class Prog:
    ENGINES = ("sync", "gpsimd", "vector", "scalar", "tensor")

    def __init__(self, nc):
        self.nc = nc
        self.q = {e: [] for e in self.ENGINES}
        self.sem_names = (list(self.ENGINES) + ["sdma", "gdma", "cc1", "cc2", "cc3", "cst", "xt", "wb0", "wb1", "wc", "gmy", "gam", "c3", "w10", "w11", "w20", "w21"])
        self.cnt = {s: 0 for s in self.sem_names}
        self.hwm = {}
        self.wtok = {}
        self.rtok = {}
        self.sems = {}
        self.used = {s: set() for s in self.sem_names}
        self.incs = {s: {} for s in self.sem_names}
        self.remap = {s: {} for s in self.sem_names}
        self.final_inc = None
        self.btoks = []
        self.dma_owner = {}
        for sname in self.sem_names:
            if sname in ("sdma",):
                self.dma_owner[sname] = "sync"
            elif sname not in self.ENGINES and not sname.startswith("cc"):
                self.dma_owner[sname] = "gpsimd"
        self.owned = {}
        for sname, eng in self.dma_owner.items():
            self.owned.setdefault(eng, []).append(sname)
        self.boundaries = {sname: [0] for sname in self.dma_owner}

    def barrier(self):
        self.btoks = [(s, self.cnt[s]) for s in self.sem_names if self.cnt[s] > 0]

    def snap(self, sem, val):
        if sem not in self.dma_owner:
            return val
        for b in self.boundaries[sem]:
            if b >= val:
                return b
        owner = self.dma_owner[sem]
        total = self.cnt[sem]
        self.boundaries[sem].append(total)
        self._wait_raw(owner, sem, total)
        return total

    def _wait_raw(self, eng, sem, val):
        if val <= 0 or self.hwm.get((eng, sem), 0) >= val:
            return
        self.hwm[(eng, sem)] = val
        self.used[sem].add(val)
        P = self

        def w(e, s=sem, v=val):
            e.wait_ge(P.sems[s], P.remap[s][v])
        self.q[eng].append(w)

    def _wait(self, eng, sem, val):
        self._wait_raw(eng, sem, self.snap(sem, val))

    def op(self, eng, fn, reads=(), writes=(), sem=None, inc=1, deps=()):
        dep = list(deps) + list(self.btoks)
        for b in reads:
            t = self.wtok.get(b)
            if t:
                dep.append(t)
        for b in writes:
            t = self.wtok.get(b)
            if t:
                dep.append(t)
            dep.extend(self.rtok.get(b, ()))
        sem = sem or eng
        best = {}
        for (s, v) in dep:
            if eng == "tensor" and s == "tensor":
                continue
            if v > best.get(s, 0):
                best[s] = v
        for s, v in sorted(best.items()):
            assert v <= self.cnt[s], (eng, s, v, self.cnt[s])
            self._wait(eng, s, v)
        self.cnt[sem] += inc
        raw = self.cnt[sem]
        self.incs[sem][raw] = inc
        P = self

        def run(e, f=fn, s=sem, r=raw):
            ins = f(e)
            amt = P.final_inc[s].get(r)
            if amt:
                ins.then_inc(P.sems[s], amt)
        self.q[eng].append(run)
        tok = (sem, raw)
        for b in writes:
            self.wtok[b] = tok
            self.rtok[b] = []
        for b in reads:
            self.rtok.setdefault(b, []).append(tok)
        return tok

    def finalize(self):
        self.final_inc = {}
        for s in self.sem_names:
            keep_all = s in self.dma_owner or s.startswith("cc")
            mx = max(self.used[s]) if self.used[s] else 0
            fi = {}
            run = 0
            for raw in sorted(self.incs[s]):
                inc = self.incs[s][raw]
                if keep_all or raw <= mx:
                    fi[raw] = inc
                    run += inc
                self.remap[s][raw] = run
            self.final_inc[s] = fi

    def replay(self, block):
        for ename in self.ENGINES:
            if not self.q[ename]:
                continue

            def body(e, lst=tuple(self.q[ename])):
                for f in lst:
                    f(e)
            getattr(block, ename)(body)


def mkap(base_ap, offset, dims):
    return dataclasses.replace(base_ap, offset=offset, ap=dims)


def build_nc():
    nc = bass.Bass("TRN2", target_bir_lowering=False)
    P = Prog(nc)
    fin_state = {}

    def trunc_finish():
        out_toks = []
        for t2 in range(NT):
            out_toks.append(fin_state["sdma"](
                fin_state["out"][t2 * 128:(t2 + 1) * 128, :],
                fin_state["f32a"][:, :]))
        for tok in out_toks:
            P._wait("sync", tok[0], tok[1])
        P.finalize()
        import contextlib
        with contextlib.ExitStack() as st:
            for s2 in P.sem_names:
                P.sems[s2] = st.enter_context(nc.semaphore(s2))
            block = st.enter_context(nc.Block())
            P.replay(block)
        return nc

    def din(name, shape):
        return nc.dram_tensor(name, shape, F32, kind="ExternalInput").ap()

    xT_in = din("xT", [D, XW])
    xtok_in = din("xtok", [TOK, D])
    w_qkv = din("w_qkv", [D, 3 * D])
    bcol_qkv_in = din("bcol_qkv", [128, 24])
    w_rq = din("w_rq", [D, 256])
    bcol_rq_in = din("bcol_rq", [128, 2])
    w_rkv = din("w_rkv", [D, 512])
    bcol_rkv_in = din("bcol_rkv", [128, 4])
    w_rout = din("w_rout", [256, D])
    bcol_rout_in = din("bcol_rout", [128, 8])
    w_out = din("w_out", [D, D])
    outb_in = din("outb_bc", [128, D])
    gl_in = din("gl", [128, 1])
    gr_in = din("gr", [128, 1])
    w_Wqkv = din("w_Wqkv", [D, 3 * D])
    bcol_Wqkv_in = din("bcol_Wqkv", [128, 24])
    w_Wo = din("w_Wo", [D, D])
    bo_in = din("bo_bc", [128, D])
    ln1w_in = din("ln1w_bc", [128, D])
    ln1b_in = din("ln1b_bc", [128, D])
    ln2w_in = din("ln2w_bc", [128, D])
    ln2b_in = din("ln2b_bc", [128, D])
    w_gate = din("w_gate", [D, NE])
    w1_in = din("w1", [NE, D, FFN])
    bcol_fc1_in = din("bcol_fc1", [NE, 128, 32])
    w2_in = din("w2", [NE, FFN, D])
    fc2b_in = din("fc2b", [NE, D])
    mlo0_in = din("mlo0", [128, 128])
    mlo_in = din("mlo", [128, 128])
    mhi_in = din("mhi", [128, 128])
    mcau_in = din("mcau", [4, 128, 512])
    mlat_in = din("mlat", [64, NT])
    ident_in = din("ident", [128, 128])

    out_dram = nc.dram_tensor("out", [TOK, D], F32, kind="ExternalOutput").ap()
    dbg = {}
    if _DEBUG:
        for nm in ("xmid", "attn", "nx1"):
            dbg[nm] = nc.dram_tensor("dbg_" + nm, [TOK, D], F32,
                                     kind="ExternalOutput").ap()

    cc1_in = nc.dram_tensor("cc1_in", [D, 16], BF16).ap()
    cc1_out = nc.dram_tensor("cc1_out", [4 * D, 16], BF16).ap()
    cc2_in = nc.dram_tensor("cc2_in", [3 * D, TOK], BF16).ap()
    cc2_out = nc.dram_tensor("cc2_out", [12 * D, TOK], BF16).ap()
    cc3_in = nc.dram_tensor("cc3_in", [4 * HD, S], BF16).ap()
    cc3_out = nc.dram_tensor("cc3_out", [D, S], BF16).ap()
    qm_d = nc.dram_tensor("qm_d", [4 * HD, S], BF16).ap()
    km_d = nc.dram_tensor("km_d", [4 * HD, S], BF16).ap()
    vm_d = nc.dram_tensor("vm_d", [4 * HD, S], BF16).ap()
    am_d = nc.dram_tensor("am_d", [D, TOK], BF16).ap()
    w1bf = nc.dram_tensor("w1bf", [NE, D, FFN], BF16).ap()
    w2bf = nc.dram_tensor("w2bf", [NE, FFN, D], BF16).ap()

    SB_LIMIT = 212800
    SB_BASE = 16512
    sb = {}

    def sal(name, shape, dt, at):
        per = int(np.prod(shape[1:])) * mybir.dt.size(dt)
        assert at % 32 == 0 and at + per <= SB_LIMIT, (name, at, per, at + per)
        sb[name] = nc.alloc_sbuf_tensor_at(name, list(shape), dt,
                                           offset=at + SB_BASE)
        return sb[name]

    ident_bf = sal("ident_bf", [128, 128], BF16, 0)
    sal("mlo0", [128, 128], BF16, 256)
    sal("mlo", [128, 128], BF16, 512)
    sal("mhi", [128, 128], BF16, 768)
    mlat = sal("mlat", [64, NT], F32, 1024)
    sal("bcol_qkv", [128, 24], F32, 1088)
    sal("bcol_rq", [128, 2], F32, 1184)
    sal("bcol_rkv", [128, 4], F32, 1216)
    sal("bcol_rout", [128, 8], F32, 1248)
    sal("bcol_Wqkv", [128, 24], F32, 1280)
    gl_sb = sal("gl", [128, 1], BF16, 1376)
    gr_sb = sal("gr", [128, 1], BF16, 1408)
    ones_b = sal("ones_b", [128, 1], BF16, 1440)
    ones_f = sal("ones_f", [1, 128], F32, 1472)
    bcA = sal("bcA", [128, D], BF16, 1984)
    bcB = sal("bcB", [128, D], BF16, 4032)
    bcC = sal("bcC", [128, D], BF16, 6080)
    cols = sal("cols", [128, 8], F32, 8128)
    mk1 = sal("mk1", [1, 16], F32, 8160)
    gateW = sal("gateW_bf", [128, NE * 8], BF16, 8224)
    PB = 8352

    SLOT_A = PB
    SLOT_K = PB + 16384
    SLOT_Q = PB + 51200
    SLOT_V = PB + 83968
    SLOT_R = PB + 118784
    SLOT_M = PB + 126976
    SCR = PB + 159744
    assert SCR == 168096

    xTq = sal("xTq", [128, 8 * 512], BF16, SLOT_A)
    woutT = sal("woutT", [128, 8 * D], BF16, SLOT_A)
    kT = sal("kT", [128, 8 * XW], BF16, SLOT_K)
    qT1 = sal("qT1", [128, 8 * TOK], BF16, SLOT_Q)
    vtok = sal("vtok", [128, NTX * D], BF16, SLOT_V)
    rqT = sal("rqT", [128, 2 * TOK], BF16, SLOT_R)
    xmid = sal("xmid", [128, NT * D], BF16, SLOT_M)

    _o = [SCR]

    def bump(n):
        a = _o[0]
        _o[0] = (a + n + 31) // 32 * 32
        return a

    wbuf = [sal(f"wbuf{i}", [128, 8 * 128], BF16, bump(2048)) for i in range(2)]
    pool_f = sal("pool_f", [128, 8 * 16], F32, bump(512))
    pooledT = sal("pooledT", [128, 8 * 64], BF16, bump(1024))
    rkT = sal("rkT", [128, 2 * 64], BF16, bump(256))
    rvT = sal("rvT", [128, 2 * 64], BF16, bump(256))
    rkhat = sal("rkhat", [65, LH * 64], BF16, bump(512))
    rvhat = sal("rvhat", [64, LH * 65], BF16, bump(544))
    rv_tok = sal("rv_tok", [64, 256], BF16, bump(512))
    cneg_lat = sal("cneg_lat", [LH, TOK], BF16, bump(4096))
    cneg_loc = sal("cneg_loc", [NH, TOK], BF16, bump(4096))
    sq64 = sal("sq64", [64, 512], BF16, bump(1024))
    n2row = sal("n2row", [1, 512], F32, bump(2048))
    khatb = [sal(f"khatb{i}", [65, 256], BF16, bump(512)) for i in range(2)]
    vhatb = [sal(f"vhatb{i}", [128, 130], BF16, bump(288)) for i in range(2)]
    qhatb = [sal(f"qhatb{i}", [65, 128], BF16, bump(256)) for i in range(2)]
    latqb = [sal(f"latqb{i}", [65, 128], BF16, bump(256)) for i in range(2)]
    ptile = [sal(f"ptile{i}", [128, 512], BF16, bump(1024)) for i in range(2)]
    tloc = [sal(f"tloc{i}", [65, 128], BF16, bump(256)) for i in range(2)]
    lr = [sal(f"lr{i}", [1, 128], F32, bump(512)) for i in range(2)]
    grow = [sal(f"grow{i}", [1, 384], F32, bump(1536)) for i in range(2)]
    bcsb = [sal(f"bcs{i}", [128, 256], BF16, bump(512)) for i in range(2)]
    remh = [sal(f"remh{i}", [64, 128], BF16, bump(256)) for i in range(2)]
    latnT = sal("latnT", [128, 2 * 128], BF16, bump(512))
    remT = sal("remT", [128, 8 * 128], BF16, bump(2048))
    mixT = sal("mixT", [128, 8 * 128], BF16, bump(2048))
    f32a = sal("f32a", [128, D], F32, bump(4096))
    f32b = sal("f32b", [128, 512], F32, bump(2048))
    rout_sb = sal("rout_sb", [128, 2 * D], BF16, bump(4096))
    assert _o[0] <= SB_LIMIT, _o[0]

    khat2 = [sal(f"khat2_{i}", [65, S], BF16, SLOT_K + i * 16384) for i in range(2)]
    vth = [sal(f"vth{i}", [64, TOK], BF16, SLOT_Q + i * 4096) for i in range(2)]
    vhat2 = [sal(f"vhat2_{i}", [128, 64 * 65], BF16, SLOT_Q + 8192 + i * 8320)
             for i in range(2)]
    attnT2 = sal("attnT2", [128, 2 * S], BF16, SLOT_V)
    nx1T = sal("nx1T", [128, 8 * TOK], BF16, SLOT_V)
    qhat2 = [sal(f"qhat2_{i}", [65, 512], BF16, SCR + i * 1024) for i in range(2)]
    ksq2 = sal("ksq2", [64, 512], BF16, SCR + 2048)
    l2 = [sal(f"l2_{i}", [1, 512], F32, SCR + 3072 + i * 2048) for i in range(2)]
    bcs2 = [sal(f"bcs2_{i}", [64, 512], BF16, SCR + 7168 + i * 1024) for i in range(2)]
    mk2 = sal("mk2", [1, 8], F32, SCR + 9216)
    mcau = sal("mcau", [128, 4 * 512], BF16, SCR + 9248)

    nx2T = sal("nx2T", [128, 8 * TOK], BF16, SLOT_K)
    moe_acc = sal("moe_acc", [128, NT * D], F32, SLOT_Q)
    hT = sal("hT", [128, 32 * 512], BF16, SLOT_M)
    w1buf = [sal(f"w1buf{i}", [128, 8 * 512], BF16, SCR + i * 8192) for i in range(2)]
    w2buf = [sal(f"w2buf{i}", [128, 2 * D], BF16, SCR + 16384 + i * 4096)
             for i in range(2)]
    _o3 = [SCR + 24576]

    def bump3(n):
        a = _o3[0]
        _o3[0] = (a + n + 31) // 32 * 32
        return a

    attnTo_t = [sal(f"attnTo{i}", [128, 8 * 128], BF16, bump3(2048))
                for i in range(2)]
    f3a = sal("f3a", [128, D], F32, bump3(4096))
    f3b = sal("f3b", [128, 512], F32, bump3(2048))
    rT3 = sal("rT3", [128, 8 * 128], BF16, bump3(2048))
    gates = sal("gates", [128, NT * 8], F32, bump3(512))
    g8T = sal("g8T", [8, 128], F32, bump3(512))
    ident_f = sal("ident_f", [128, 128], F32, bump3(512))
    lg8 = sal("lg8", [128, 8], F32, bump3(32))
    is1 = sal("is1", [128, 8], F32, bump3(32))
    is2 = sal("is2", [128, 8], F32, bump3(32))
    msk8 = sal("msk8", [128, 8], F32, bump3(32))
    w12 = sal("w12", [128, 4], F32, bump3(32))
    fc2b = sal("fc2b", [NE, D], F32, bump3(4096))
    fc1b = sal("fc1b", [128, NE * 32], F32, bump3(1024))
    assert _o3[0] <= SB_LIMIT, _o3[0]
    xres_t = sal("xres_t", [128, D], F32, SLOT_R)
    nx2f = sal("nx2f", [128, D], F32, SLOT_R + 4096)

    ps = {i: nc.alloc_psum_tensor(f"ps{i}", [128, 512], F32) for i in range(7)}
    psT = nc.alloc_psum_tensor("psT", [128, 1024], BF16)

    def sdma(dst, src, reads=(), writes=(), deps=()):
        return P.op("sync", lambda e: e.dma_start(out=dst, in_=src),
                    reads=reads, writes=writes, sem="sdma", inc=16, deps=deps)

    def gdma(dst, src, reads=(), writes=(), deps=(), dsem="gdma"):
        return P.op("gpsimd", lambda e: e.dma_start(out=dst, in_=src),
                    reads=reads, writes=writes, sem=dsem, inc=16, deps=deps)

    def mm(out, lhsT, rhs, start, stop, reads=(), writes=(), deps=()):
        return P.op("tensor",
                    lambda e: e.matmul(out, lhsT, rhs, start=start, stop=stop),
                    reads=reads, writes=writes, deps=deps)

    def tpose(out, in_, ident, reads=(), writes=(), deps=()):
        return P.op("tensor", lambda e: e.transpose(out, in_, ident),
                    reads=reads, writes=writes, deps=deps)

    def vo(fn, reads=(), writes=(), deps=()):
        return P.op("vector", fn, reads=reads, writes=writes, deps=deps)

    def so(fn, reads=(), writes=(), deps=()):
        return P.op("scalar", fn, reads=reads, writes=writes, deps=deps)

    def go(fn, reads=(), writes=(), deps=()):
        return P.op("gpsimd", fn, reads=reads, writes=writes, deps=deps)

    def plain_ln(src_ap, src_name, fin, sq_scr, sq_name):
        vo(lambda e: e.reduce_sum(cols[:, 0:1], src_ap, axis=AX.X),
           reads=(src_name,), writes=("c0",))
        so(lambda e: e.activation(sq_scr[:, 0:512], src_ap[:, 0:512], AF.Square,
                                  accum_out=cols[:, 5:6]),
           reads=(src_name,), writes=(sq_name, "c5"))
        so(lambda e: e.activation(sq_scr[:, 0:512], src_ap[:, 512:1024], AF.Square,
                                  accum_out=cols[:, 6:7]),
           reads=(src_name,), writes=(sq_name, "c6"))
        vo(lambda e: e.tensor_add(cols[:, 5:6], cols[:, 5:6], cols[:, 6:7]),
           reads=("c5", "c6"), writes=("c5",))
        vo(lambda e: e.tensor_scalar_mul(cols[:, 0:1], cols[:, 0:1], 1.0 / D),
           reads=("c0",), writes=("c0",))
        vo(lambda e: e.tensor_mul(cols[:, 1:2], cols[:, 0:1], cols[:, 0:1]),
           reads=("c0",), writes=("c1",))
        vo(lambda e: e.tensor_scalar(cols[:, 2:3], cols[:, 5:6], 1.0 / D, None,
                                     ALU.mult),
           reads=("c5",), writes=("c2",))
        vo(lambda e: e.tensor_sub(cols[:, 2:3], cols[:, 2:3], cols[:, 1:2]),
           reads=("c2", "c1"), writes=("c2",))
        so(lambda e: e.activation(cols[:, 3:4], cols[:, 2:3], AF.Ln,
                                  bias=np.float32(1e-5)),
           reads=("c2",), writes=("c3",))
        so(lambda e: e.activation(cols[:, 4:5], cols[:, 3:4], AF.Exp,
                                  scale=np.float32(-0.5)),
           reads=("c3",), writes=("c4",))
        fin(cols[:, 0:1], cols[:, 4:5])

    fin_state["sdma"] = sdma
    fin_state["out"] = out_dram
    fin_state["f32a"] = f32a

    gdma(ident_bf[:, :], ident_in[:, :], writes=("ident_bf",), dsem="cst")
    gdma(sb["mlo0"][:, :], mlo0_in[:, :], writes=("mlo0",), dsem="cst")
    gdma(sb["mlo"][:, :], mlo_in[:, :], writes=("mlo",), dsem="cst")
    gdma(sb["mhi"][:, :], mhi_in[:, :], writes=("mhi",), dsem="cst")
    sdma(mlat[:, :], mlat_in[:, :], writes=("mlat",))
    for nm, src in [("bcol_qkv", bcol_qkv_in), ("bcol_rq", bcol_rq_in),
                    ("bcol_rkv", bcol_rkv_in), ("bcol_rout", bcol_rout_in),
                    ("bcol_Wqkv", bcol_Wqkv_in)]:
        sdma(sb[nm][:, :], src[:, :], writes=(nm,))
    gdma(bcA[:, :], outb_in[:, :], writes=("bcA",), dsem="cst")
    gdma(bcB[:, :], ln1w_in[:, :], writes=("bcB",), dsem="cst")
    gdma(bcC[:, :], ln1b_in[:, :], writes=("bcC",), dsem="cst")
    gdma(gl_sb[:, :], gl_in[:, :], writes=("gl",), dsem="cst")
    gdma(gr_sb[:, :], gr_in[:, :], writes=("gr",), dsem="cst")
    cst_last = gdma(gateW[:, :].rearrange("p (r e) -> p r e", r=8),
                    mkap(w_gate, 0, [[NE, 128], [NE * 128, 8], [1, NE]]),
                    writes=("gateW",), dsem="cst")
    for nm in ("ident_bf", "mlo0", "mlo", "mhi", "bcA", "bcB", "bcC",
               "gl", "gr", "gateW"):
        P.wtok[nm] = cst_last
    vo(lambda e: e.memset(ones_b[:, :], 1.0), writes=("ones_b",))
    vo(lambda e: e.memset(ones_f[:, :], 1.0), writes=("ones_f",))
    for i in range(2):
        vo(lambda e, i=i: e.memset(khatb[i][64:65, :], 1.0), writes=(f"khatb{i}_ones",))
        vo(lambda e, i=i: e.memset(vhatb[i][:, 64:65], 1.0), writes=(f"vhatb{i}_o1",))
        vo(lambda e, i=i: e.memset(vhatb[i][:, 129:130], 1.0), writes=(f"vhatb{i}_o2",))
    vo(lambda e: e.memset(rkhat[64:65, :], 1.0), writes=("rkhat_ones",))
    for h in range(LH):
        vo(lambda e, h=h: e.memset(rvhat[:, h * 65 + 64:h * 65 + 65], 1.0),
           writes=(f"rvhat_o{h}",))

    psrot = [0]

    def nps():
        r = psrot[0] % 4
        psrot[0] += 1
        return r

    wrot = [0]

    def load_w(w_ap, wtot, ci, ncols=128):
        sl = wrot[0] % 2
        wrot[0] += 1
        t = gdma(wbuf[sl][:, 0:8 * ncols].rearrange("p (r c) -> p r c", r=8),
                 mkap(w_ap, ci * ncols,
                      [[wtot, 128], [128 * wtot, 8], [1, ncols]]),
                 writes=(f"wbuf{sl}",), dsem=f"wb{sl}")
        return sl, t

    XCH = [(0, 512), (512, 512), (1024, 512), (1536, 512), (2048, 128)]
    for (x0, xw) in XCH:
        txq = gdma(xTq[:, 0:8 * xw].rearrange("p (r w) -> p r w", r=8),
                   mkap(xT_in, x0, [[XW, 128], [128 * XW, 8], [1, xw]]),
                   writes=("xTq",), dsem="xt")
        for li0 in range(xw // 128):
            gtok = x0 + li0 * 128
            if gtok < HALO:
                continue
            li = (gtok - HALO) // 128
            for r in range(8):
                vo(lambda e, r=r, li0=li0, li=li, xw=xw: e.reduce_sum(
                    pool_f[:, r * 16 + li:r * 16 + li + 1],
                    xTq[:, r * xw + li0 * 128: r * xw + (li0 + 1) * 128],
                    axis=AX.X),
                    reads=("xTq",), writes=(f"pool_{r}_{li}",), deps=[txq])
        for ci in range(26):
            if ci < 24:
                w_ap, wtot, wci = w_qkv, 3 * D, ci
            else:
                w_ap, wtot, wci = w_rq, 256, ci - 24
            sl, tw = load_w(w_ap, wtot, wci)
            pn = nps()
            for r in range(8):
                mm(ps[pn][:, 0:xw],
                   wbuf[sl][:, r * 128:(r + 1) * 128],
                   xTq[:, r * xw:(r + 1) * xw],
                   start=(r == 0), stop=(r == 7),
                   reads=(f"wbuf{sl}", "xTq"), writes=(f"ps{pn}",), deps=[tw, txq])
            if ci < 8:
                lo = max(x0, HALO)
                if lo < x0 + xw:
                    off, wd = lo - x0, x0 + xw - lo
                    vo(lambda e, pn=pn, ci=ci, lo=lo, off=off, wd=wd: e.tensor_scalar(
                        qT1[:, ci * TOK + lo - HALO: ci * TOK + lo - HALO + wd],
                        ps[pn][:, off:off + wd], sb["bcol_qkv"][:, ci:ci + 1],
                        0.125, ALU.add, ALU.mult),
                        reads=(f"ps{pn}", "bcol_qkv"), writes=(f"qT1_{ci}",))
            elif ci < 16:
                kc = ci - 8
                vo(lambda e, pn=pn, ci=ci, kc=kc, x0=x0, xw=xw: e.tensor_single_scalar(
                    kT[:, kc * XW + x0: kc * XW + x0 + xw],
                    ps[pn][:, 0:xw], sb["bcol_qkv"][:, ci:ci + 1], ALU.add),
                    reads=(f"ps{pn}", "bcol_qkv"), writes=(f"kT_{kc}",))
            elif ci < 24:
                vc = ci - 16
                tv = vo(lambda e, pn=pn, ci=ci, xw=xw: e.tensor_single_scalar(
                    ptile[1][:, 0:xw], ps[pn][:, 0:xw],
                    sb["bcol_qkv"][:, ci:ci + 1], ALU.add),
                    reads=(f"ps{pn}", "bcol_qkv"), writes=("ptile1",))
                nt_ = xw // 128
                for u in range(nt_):
                    tpose(psT[:, u * 128:(u + 1) * 128],
                          ptile[1][:, u * 128:(u + 1) * 128], ident_bf[:, :],
                          reads=("ptile1", "ident_bf"), writes=("psT",), deps=[tv])
                tt0 = x0 // 128
                for u in range(nt_):
                    vo(lambda e, u=u, tt0=tt0, vc=vc: e.tensor_copy(
                        vtok[:, (tt0 + u) * D + vc * 128:(tt0 + u) * D + (vc + 1) * 128],
                        psT[:, u * 128:(u + 1) * 128]),
                        reads=("psT",), writes=(f"vtok_{tt0 + u}",))
            else:
                rc = ci - 24
                lo = max(x0, HALO)
                if lo < x0 + xw:
                    off, wd = lo - x0, x0 + xw - lo
                    vo(lambda e, pn=pn, rc=rc, lo=lo, off=off, wd=wd: e.tensor_scalar(
                        rqT[:, rc * TOK + lo - HALO: rc * TOK + lo - HALO + wd],
                        ps[pn][:, off:off + wd], sb["bcol_rq"][:, rc:rc + 1],
                        0.125, ALU.add, ALU.mult),
                        reads=(f"ps{pn}", "bcol_rq"), writes=(f"rqT_{rc}",))

    if _TRUNC == "P1a":
        return trunc_finish()
    tpool = vo(lambda e: e.tensor_scalar_mul(pooledT[:, 0:128], pool_f[:, :],
                                             1.0 / 128),
               reads=tuple(f"pool_{r}_{li}" for r in range(8) for li in range(16)),
               writes=("pool_own",))
    tc1 = sdma(mkap(cc1_in, 0, [[16, 128], [2048, 8], [1, 16]]),
               pooledT[:, 0:128].rearrange("p (r l) -> p r l", r=8),
               reads=("pool_own",), writes=("cc1_in",))
    tcc1 = P.op("gpsimd", lambda e: e.collective_compute(
        "AllGather", ALU.bypass, replica_groups=[[0, 1, 2, 3], [4, 5, 6, 7]],
        ins=[cc1_in.opt()], outs=[cc1_out.opt()]),
        reads=("cc1_in",), writes=("cc1_out",), sem="cc1", inc=1, deps=[tc1])
    for rr in range(4):
        tpoolg = sdma(
            pooledT[:, :].rearrange("p (d l) -> p d l", d=8)[:, :, rr * 16:(rr + 1) * 16],
            mkap(cc1_out, rr * 16384, [[16, 128], [2048, 8], [1, 16]]),
            reads=("cc1_out",), writes=("pooledT",), deps=[tcc1])

    towt = gdma(woutT[:, :].rearrange("p (r c) -> p r c", r=8),
                mkap(w_out, 0, [[D, 128], [128 * D, 8], [1, D]]),
                writes=("woutT",), dsem="cst",
                deps=[P.wtok["xTq"]] + list(P.rtok.get("xTq", [])))

    if _TRUNC == "P1b0":
        return trunc_finish()
    for ci in range(4):
        sl, tw = load_w(w_rkv, 512, ci)
        pn = nps()
        for r in range(8):
            mm(ps[pn][:, 0:64],
               wbuf[sl][:, r * 128:(r + 1) * 128],
               pooledT[:, r * 64:(r + 1) * 64],
               start=(r == 0), stop=(r == 7),
               reads=(f"wbuf{sl}", "pooledT"), writes=(f"ps{pn}",),
               deps=[tw, tpoolg])
        dst, dnm = (rkT, "rkT") if ci < 2 else (rvT, "rvT")
        cc = ci % 2
        vo(lambda e, pn=pn, ci=ci, dst=dst, cc=cc: e.tensor_single_scalar(
            dst[:, cc * 64:(cc + 1) * 64], ps[pn][:, 0:64],
            sb["bcol_rkv"][:, ci:ci + 1], ALU.add),
            reads=(f"ps{pn}", "bcol_rkv"), writes=(f"{dnm}_{cc}",))
    for h in range(LH):
        go(lambda e, h=h: e.tensor_copy(
            rkhat[0:64, h * 64:(h + 1) * 64],
            rkT[(h % 2) * 64:(h % 2) * 64 + 64, (h // 2) * 64:(h // 2) * 64 + 64]),
            reads=(f"rkT_{h // 2}",), writes=(f"rkhat_{h}",))
    for cchunk in range(2):
        tpose(psT[0:64, 0:128], rvT[:, cchunk * 64:(cchunk + 1) * 64], ident_bf[:, :],
              reads=(f"rvT_{cchunk}", "ident_bf"), writes=("psT",))
        vo(lambda e, cchunk=cchunk: e.tensor_copy(
            rv_tok[0:64, cchunk * 128:(cchunk + 1) * 128], psT[0:64, 0:128]),
            reads=("psT",), writes=(f"rv_tok_{cchunk}",))
    for h in range(LH):
        go(lambda e, h=h: e.tensor_copy(
            rvhat[0:64, h * 65:h * 65 + 64], rv_tok[0:64, h * 64:(h + 1) * 64]),
            reads=(f"rv_tok_{h // 2}",), writes=(f"rvhat_{h}",))

    def shift_rows(n_heads, kbuf, kname_fn, kw, qbuf, qname_fn, dst, dname_fn, mko):
        for h in range(n_heads):
            hp, hc = (h % 2) * 64, h // 2
            first = True
            for g0 in range(0, kw, 512):
                gw = min(512, kw - g0)
                t1 = so(lambda e, hp=hp, hc=hc, g0=g0, gw=gw: e.activation(
                    sq64[0:64, 0:gw],
                    kbuf[hp:hp + 64, hc * kw + g0: hc * kw + g0 + gw], AF.Square),
                    reads=(kname_fn(hc),), writes=("sq64",))
                t2 = mm(ps[6][0:1, 0:gw], ones_b[0:64, 0:1], sq64[0:64, 0:gw],
                        start=True, stop=True, reads=("sq64", "ones_b"),
                        writes=("ps6",), deps=[t1])
                t3 = vo(lambda e, gw=gw: e.reduce_max(cols[0:1, 7:8],
                                                      ps[6][0:1, 0:gw], axis=AX.X),
                        reads=("ps6",), writes=("c7",))
                if first:
                    vo(lambda e, h=h, mko=mko: e.tensor_copy(
                        mk1[:, mko + h:mko + h + 1], cols[0:1, 7:8]),
                        reads=("c7",), writes=(f"mk_{mko + h}",))
                    first = False
                else:
                    vo(lambda e, h=h, mko=mko: e.tensor_max(
                        mk1[:, mko + h:mko + h + 1], mk1[:, mko + h:mko + h + 1],
                        cols[0:1, 7:8]),
                        reads=("c7", f"mk_{mko + h}"), writes=(f"mk_{mko + h}",))
            for g0 in range(0, TOK, 512):
                t4 = so(lambda e, hp=hp, hc=hc, g0=g0: e.activation(
                    sq64[0:64, 0:512],
                    qbuf[hp:hp + 64, hc * TOK + g0: hc * TOK + g0 + 512], AF.Square),
                    reads=(qname_fn(hc),), writes=("sq64",))
                t5 = mm(ps[6][0:1, 0:512], ones_b[0:64, 0:1], sq64[0:64, 0:512],
                        start=True, stop=True, reads=("sq64", "ones_b"),
                        writes=("ps6",), deps=[t4])
                vo(lambda e: e.tensor_copy(n2row[:, 0:512], ps[6][0:1, 0:512]),
                   reads=("ps6",), writes=("n2row",))
                vo(lambda e, h=h, mko=mko: e.tensor_single_scalar(
                    n2row[:, 0:512], n2row[:, 0:512], mk1[:, mko + h:mko + h + 1],
                    ALU.mult),
                    reads=("n2row", f"mk_{mko + h}"), writes=("n2row",))
                so(lambda e: e.activation(n2row[:, 0:512], n2row[:, 0:512], AF.Ln,
                                          bias=np.float32(1e-30)),
                   reads=("n2row",), writes=("n2row",))
                so(lambda e: e.activation(n2row[:, 0:512], n2row[:, 0:512], AF.Exp,
                                          scale=np.float32(-0.5) if False else np.float32(0.5)),
                   reads=("n2row",), writes=("n2row",))
                vo(lambda e: e.tensor_scalar_mul(ptile[1][0:1, 0:512],
                                                 n2row[:, 0:512], -1.0),
                   reads=("n2row",), writes=("ptile1",))
                sdma(dst[h:h + 1, g0:g0 + 512], ptile[1][0:1, 0:512],
                     reads=("ptile1",), writes=(dname_fn(h),))

    shift_rows(LH, rkT, lambda hc: f"rkT_{hc}", 64,
               rqT, lambda hc: f"rqT_{hc}", cneg_lat, lambda h: f"cneg_lat{h}", 0)
    shift_rows(NH, kT, lambda hc: f"kT_{hc}", XW,
               qT1, lambda hc: f"qT1_{hc}", cneg_loc, lambda h: f"cneg_loc{h}", 0)

    trout = gdma(rout_sb[:, :].rearrange("p (r c) -> p r c", r=2),
                 mkap(w_rout, 0, [[D, 128], [128 * D, 2], [1, D]]),
                 writes=("rout_sb",), dsem="cst", deps=[towt])
    P.wtok["woutT"] = trout

    if _TRUNC == "P1b":
        return trunc_finish()
    for t in range(NT):
        for h in range(LH if _SUB >= 1 else 0):
            par = (t * LH + h) % 2
            lq = latqb[par]
            lqn = f"latqb{par}"
            hp, hc = (h % 2) * 64, h // 2
            ta = go(lambda e, hp=hp, hc=hc, t=t, lq=lq: e.tensor_copy(
                lq[0:64, :], rqT[hp:hp + 64, hc * TOK + t * 128: hc * TOK + (t + 1) * 128]),
                reads=(f"rqT_{hc}",), writes=(lqn,))
            tb = sdma(lq[64:65, :], cneg_lat[h:h + 1, t * 128:(t + 1) * 128],
                      reads=(f"cneg_lat{h}",), writes=(lqn,))
            pA = par
            mm(ps[pA][0:64, 0:128], rkhat[:, h * 64:(h + 1) * 64], lq[:, :],
               start=True, stop=True,
               reads=(lqn, f"rkhat_{h}", "rkhat_ones"), writes=(f"ps{pA}",),
               deps=[ta, tb])
            so(lambda e, t=t, pA=pA, par=par: e.activation(
                ptile[par][0:64, 0:128], ps[pA][0:64, 0:128], AF.Exp,
                bias=mlat[:, t:t + 1]),
                reads=(f"ps{pA}", "mlat"), writes=(f"ptile{par}",))
            pE = 2 + par
            mm(ps[pE][0:65, 0:128], rvhat[0:64, h * 65:(h + 1) * 65],
               ptile[par][0:64, 0:128], start=True, stop=True,
               reads=(f"ptile{par}", f"rvhat_{h}", f"rvhat_o{h}"),
               writes=(f"ps{pE}",))
            vo(lambda e, pE=pE, par=par: e.tensor_scalar_max(
                lr[par][:, 0:128], ps[pE][64:65, 0:128], 1e-30),
                reads=(f"ps{pE}",), writes=(f"lr{par}",))
            vo(lambda e, par=par: e.reciprocal(lr[par][:, 0:128], lr[par][:, 0:128]),
               reads=(f"lr{par}",), writes=(f"lr{par}",))
            pG = 4 + par
            mm(ps[pG][:, 0:128], ones_f[0:1, 0:128], lr[par][:, 0:128],
               start=True, stop=True, reads=(f"lr{par}", "ones_f"),
               writes=(f"ps{pG}",))
            vo(lambda e, pG=pG, par=par: e.tensor_copy(
                bcsb[par][:, 0:128], ps[pG][:, 0:128]),
                reads=(f"ps{pG}",), writes=(f"bcs{par}",))
            vo(lambda e, hp=hp, hc=hc, pE=pE, par=par: e.tensor_tensor(
                latnT[hp:hp + 64, hc * 128: hc * 128 + 128],
                ps[pE][0:64, 0:128], bcsb[par][0:64, 0:128], ALU.mult),
                reads=(f"ps{pE}", f"bcs{par}"), writes=(f"latnT_{h}",))
        for ci in range(8 if _SUB >= 2 else 0):
            pn = nps()
            for r in range(2):
                mm(ps[pn][:, 0:128],
                   rout_sb[:, r * D + ci * 128: r * D + (ci + 1) * 128],
                   latnT[:, r * 128:(r + 1) * 128],
                   start=(r == 0), stop=(r == 1),
                   reads=("rout_sb",) + tuple(f"latnT_{hh}" for hh in range(LH)),
                   writes=(f"ps{pn}",), deps=[trout])
            vo(lambda e, pn=pn, ci=ci: e.tensor_single_scalar(
                remT[:, ci * 128:(ci + 1) * 128], ps[pn][:, 0:128],
                sb["bcol_rout"][:, ci:ci + 1], ALU.add),
                reads=(f"ps{pn}", "bcol_rout"), writes=(f"remT_{ci}",))
        for h in range(NH if _SUB >= 3 else 0):
            par = (t * NH + h) % 2
            kb, vb, qb = khatb[par], vhatb[par], qhatb[par]
            kbn, vbn, qbn = f"khatb{par}", f"vhatb{par}", f"qhatb{par}"
            hp, hc = (h % 2) * 64, h // 2
            tk = go(lambda e, hp=hp, hc=hc, t=t, kb=kb: e.tensor_copy(
                kb[0:64, :], kT[hp:hp + 64, hc * XW + t * 128: hc * XW + t * 128 + 256]),
                reads=(f"kT_{hc}",), writes=(kbn,))
            tv1 = go(lambda e, h=h, t=t, vb=vb: e.tensor_copy(
                vb[:, 0:64], vtok[:, t * D + h * 64: t * D + h * 64 + 64]),
                reads=(f"vtok_{t}",), writes=(vbn,))
            tv2 = go(lambda e, h=h, t=t, vb=vb: e.tensor_copy(
                vb[:, 65:129], vtok[:, (t + 1) * D + h * 64:(t + 1) * D + h * 64 + 64]),
                reads=(f"vtok_{t + 1}",), writes=(vbn,))
            trh = go(lambda e, hp=hp, hc=hc, par=par: e.tensor_copy(
                remh[par][0:64, :], remT[hp:hp + 64, hc * 128: hc * 128 + 128]),
                reads=(f"remT_{hc}",), writes=(f"remh{par}",))
            tq1 = go(lambda e, hp=hp, hc=hc, t=t, qb=qb: e.tensor_copy(
                qb[0:64, :], qT1[hp:hp + 64, hc * TOK + t * 128: hc * TOK + (t + 1) * 128]),
                reads=(f"qT1_{hc}",), writes=(qbn,))
            tq2 = sdma(qb[64:65, :], cneg_loc[h:h + 1, t * 128:(t + 1) * 128],
                       reads=(f"cneg_loc{h}",), writes=(qbn,))
            pA = par
            mm(ps[pA][:, 0:128], kb[:, 0:128], qb[:, :], start=True, stop=True,
               reads=(kbn, qbn, f"khatb{par}_ones"), writes=(f"ps{pA}",),
               deps=[tk, tq1, tq2])
            mm(ps[pA][:, 128:256], kb[:, 128:256], qb[:, :], start=True, stop=True,
               reads=(kbn, qbn), writes=(f"ps{pA}",))
            mlon = "mlo0" if t == 0 else "mlo"
            vo(lambda e, pA=pA, mlon=mlon: e.tensor_tensor(
                ps[pA][:, 0:128], ps[pA][:, 0:128], sb[mlon][:, :], ALU.add),
                reads=(f"ps{pA}", mlon), writes=(f"ps{pA}",))
            vo(lambda e, pA=pA: e.tensor_tensor(
                ps[pA][:, 128:256], ps[pA][:, 128:256], sb["mhi"][:, :], ALU.add),
                reads=(f"ps{pA}", "mhi"), writes=(f"ps{pA}",))
            so(lambda e, pA=pA, par=par: e.activation(
                ptile[par][:, 0:256], ps[pA][:, 0:256], AF.Exp),
                reads=(f"ps{pA}",), writes=(f"ptile{par}",))
            if _SUB < 4:
                continue
            pE = 2 + par
            mm(ps[pE][0:65, 0:128], vb[:, 0:65], ptile[par][:, 0:128],
               start=True, stop=False,
               reads=(vbn, f"ptile{par}", f"vhatb{par}_o1"), writes=(f"ps{pE}",),
               deps=[tv1])
            mm(ps[pE][0:65, 0:128], vb[:, 65:130], ptile[par][:, 128:256],
               start=False, stop=True,
               reads=(vbn, f"ptile{par}", f"vhatb{par}_o2"), writes=(f"ps{pE}",),
               deps=[tv2])
            tl1 = vo(lambda e, pE=pE, par=par: e.tensor_copy(
                tloc[par][0:64, :], ps[pE][0:64, 0:128]),
                reads=(f"ps{pE}",), writes=(f"tloc{par}",))
            vo(lambda e, pE=pE, par=par: e.tensor_scalar_max(
                lr[par][:, 0:128], ps[pE][64:65, 0:128], 1e-30),
                reads=(f"ps{pE}",), writes=(f"lr{par}",))
            if _SUB < 5:
                continue
            pG = 4 + par
            mm(ps[pG][0:1, 0:128], gl_sb[0:64, :], tloc[par][0:64, :],
               start=True, stop=True, reads=(f"tloc{par}", "gl"),
               writes=(f"ps{pG}",), deps=[tl1])
            mm(ps[pG][0:1, 128:256], gr_sb[0:64, :], remh[par][0:64, :],
               start=True, stop=True,
               reads=(f"remh{par}", "gr"), writes=(f"ps{pG}",))
            vo(lambda e, par=par: e.reciprocal(lr[par][:, 0:128], lr[par][:, 0:128]),
               reads=(f"lr{par}",), writes=(f"lr{par}",))
            vo(lambda e, pG=pG, par=par: e.tensor_tensor(
                grow[par][:, 0:128], ps[pG][0:1, 0:128], lr[par][:, 0:128], ALU.mult),
                reads=(f"ps{pG}", f"lr{par}"), writes=(f"grow{par}",))
            vo(lambda e, pG=pG, par=par: e.tensor_tensor(
                grow[par][:, 0:128], grow[par][:, 0:128], ps[pG][0:1, 128:256],
                ALU.add),
                reads=(f"grow{par}", f"ps{pG}"), writes=(f"grow{par}",))
            so(lambda e, par=par: e.activation(
                grow[par][:, 128:256], grow[par][:, 0:128], AF.Exp,
                scale=np.float32(-1.0)),
                reads=(f"grow{par}",), writes=(f"grow{par}",))
            vo(lambda e, par=par: e.tensor_scalar_add(
                grow[par][:, 128:256], grow[par][:, 128:256], 1.0),
                reads=(f"grow{par}",), writes=(f"grow{par}",))
            vo(lambda e, par=par: e.reciprocal(
                grow[par][:, 128:256], grow[par][:, 128:256]),
                reads=(f"grow{par}",), writes=(f"grow{par}",))
            vo(lambda e, par=par: e.tensor_tensor(
                grow[par][:, 0:128], grow[par][:, 128:256], lr[par][:, 0:128],
                ALU.mult),
                reads=(f"grow{par}", f"lr{par}"), writes=(f"grow{par}",))
            vo(lambda e, par=par: e.tensor_scalar(
                grow[par][:, 256:384], grow[par][:, 128:256], -1.0, 1.0,
                ALU.mult, ALU.add),
                reads=(f"grow{par}",), writes=(f"grow{par}",))
            if _SUB < 6:
                continue
            mm(ps[pG][:, 256:384], ones_f[0:1, 0:128], grow[par][:, 0:128],
               start=True, stop=True, reads=(f"grow{par}", "ones_f"),
               writes=(f"ps{pG}",))
            mm(ps[pG][:, 384:512], ones_f[0:1, 0:128], grow[par][:, 256:384],
               start=True, stop=True, reads=(f"grow{par}", "ones_f"),
               writes=(f"ps{pG}",))
            vo(lambda e, pG=pG, par=par: e.tensor_copy(
                bcsb[par][:, 0:256], ps[pG][:, 256:512]),
                reads=(f"ps{pG}",), writes=(f"bcs{par}",))
            vo(lambda e, par=par: e.tensor_tensor(
                ptile[par][0:64, 0:128], tloc[par][0:64, :], bcsb[par][0:64, 0:128],
                ALU.mult),
                reads=(f"tloc{par}", f"bcs{par}"), writes=(f"ptile{par}",))
            vo(lambda e, par=par: e.tensor_tensor(
                ptile[par][0:64, 128:256],
                remh[par][0:64, :],
                bcsb[par][0:64, 128:256], ALU.mult),
                reads=(f"remh{par}", f"bcs{par}"), writes=(f"ptile{par}",))
            vo(lambda e, hp=hp, hc=hc, par=par: e.tensor_tensor(
                mixT[hp:hp + 64, hc * 128: hc * 128 + 128],
                ptile[par][0:64, 0:128], ptile[par][0:64, 128:256], ALU.add),
                reads=(f"ptile{par}",), writes=(f"mixT_{hc}",))
        if _SUB < 7:
            continue
        tx = sdma(f32a[:, :], xtok_in[t * 128:(t + 1) * 128, :], writes=("f32a",))
        for half in range(2):
            pn = half
            for r in range(8):
                mm(ps[pn][:, 0:512],
                   mixT[:, r * 128:(r + 1) * 128],
                   woutT[:, r * D + half * 512: r * D + half * 512 + 512],
                   start=(r == 0), stop=(r == 7),
                   reads=("woutT",) + tuple(f"mixT_{rr}" for rr in range(8)),
                   writes=(f"ps{pn}",), deps=[towt])
            vo(lambda e, pn=pn, half=half: e.tensor_tensor(
                f32a[:, half * 512:(half + 1) * 512], ps[pn][:, 0:512],
                f32a[:, half * 512:(half + 1) * 512], ALU.add),
                reads=(f"ps{pn}", "f32a"), writes=("f32a",), deps=[tx])
            vo(lambda e, half=half: e.tensor_tensor(
                f32a[:, half * 512:(half + 1) * 512],
                f32a[:, half * 512:(half + 1) * 512],
                bcA[:, half * 512:(half + 1) * 512], ALU.add),
                reads=("f32a", "bcA"), writes=("f32a",))

        def fin_xmid(mean_ap, rstd_ap, t=t):
            vo(lambda e: e.tensor_scalar(
                xmid[:, t * D:(t + 1) * D], f32a[:, :], mean_ap, rstd_ap,
                ALU.subtract, ALU.mult),
                reads=("f32a", "c0", "c4"), writes=(f"xmid_{t}",))
            if _DEBUG:
                gdma(dbg["xmid"][t * 128:(t + 1) * 128, :],
                     xmid[:, t * D:(t + 1) * D], reads=(f"xmid_{t}",))
        plain_ln(f32a[:, :], "f32a", fin_xmid, f32b, "f32b")

    if _TRUNC == "P1e":
        return trunc_finish()
    P.barrier()
    for t in range(NT):
        def fin_nx1(mean_ap, rstd_ap, t=t):
            vo(lambda e: e.tensor_scalar(
                f32a[:, :], xmid[:, t * D:(t + 1) * D], mean_ap, rstd_ap,
                ALU.subtract, ALU.mult),
                reads=(f"xmid_{t}", "c0", "c4"), writes=("f32a",))
        plain_ln(xmid[:, t * D:(t + 1) * D], f"xmid_{t}", fin_nx1, f32b, "f32b")
        vo(lambda e: e.tensor_tensor(f32a[:, :], f32a[:, :], bcB[:, :], ALU.mult),
           reads=("f32a", "bcB"), writes=("f32a",))
        vo(lambda e: e.tensor_tensor(f32a[:, :], f32a[:, :], bcC[:, :], ALU.add),
           reads=("f32a", "bcC"), writes=("f32a",))
        if _DEBUG:
            sdma(dbg["nx1"][t * 128:(t + 1) * 128, :], f32a[:, :], reads=("f32a",))
        for half in range(2):
            tch = vo(lambda e, half=half: e.tensor_copy(
                ptile[0][:, 0:512], f32a[:, half * 512:(half + 1) * 512]),
                reads=("f32a",), writes=("ptile0",))
            for u in range(4):
                tpose(psT[:, (half * 4 + u) * 128:(half * 4 + u + 1) * 128],
                      ptile[0][:, u * 128:(u + 1) * 128], ident_bf[:, :],
                      reads=("ptile0", "ident_bf"), writes=("psT",), deps=[tch])
        for r in range(8):
            vo(lambda e, t=t, r=r: e.tensor_copy(
                nx1T[:, r * TOK + t * 128: r * TOK + (t + 1) * 128],
                psT[:, r * 128:(r + 1) * 128]),
                reads=("psT",), writes=(f"nx1T_{r}",))

    cc2_toks = []
    for ci in range(24):
        sl, tw = load_w(w_Wqkv, 3 * D, ci)
        for gg in range(4):
            g0 = gg * 512
            pn = nps()
            for r in range(8):
                mm(ps[pn][:, 0:512],
                   wbuf[sl][:, r * 128:(r + 1) * 128],
                   nx1T[:, r * TOK + g0: r * TOK + g0 + 512],
                   start=(r == 0), stop=(r == 7),
                   reads=(f"wbuf{sl}", f"nx1T_{r}"), writes=(f"ps{pn}",), deps=[tw])
            if ci < 8:
                vo(lambda e, pn=pn, ci=ci: e.tensor_scalar(
                    ptile[1][:, 0:512], ps[pn][:, 0:512], sb["bcol_Wqkv"][:, ci:ci + 1],
                    0.125, ALU.add, ALU.mult),
                    reads=(f"ps{pn}", "bcol_Wqkv"), writes=("ptile1",))
            else:
                vo(lambda e, pn=pn, ci=ci: e.tensor_single_scalar(
                    ptile[1][:, 0:512], ps[pn][:, 0:512], sb["bcol_Wqkv"][:, ci:ci + 1],
                    ALU.add),
                    reads=(f"ps{pn}", "bcol_Wqkv"), writes=("ptile1",))
            cc2_toks.append(sdma(
                mkap(cc2_in, ci * 128 * TOK + g0, [[TOK, 128], [1, 512]]),
                ptile[1][:, 0:512], reads=("ptile1",), writes=("cc2_in",)))

    tcc2 = P.op("gpsimd", lambda e: e.collective_compute(
        "AllGather", ALU.bypass, replica_groups=[[0, 1, 2, 3], [4, 5, 6, 7]],
        ins=[cc2_in.opt()], outs=[cc2_out.opt()]),
        reads=("cc2_in",), writes=("cc2_out",), sem="cc2", inc=1, deps=cc2_toks)

    def my_rows(dst, dname, sec):
        def f(e):
            pid4 = nc.partition_id() % 4
            src = mkap(cc2_out, sec * D * TOK + pid4 * (4 * HD * TOK),
                       [[3 * D * TOK, 4], [TOK, 4 * HD], [1, TOK]])
            dd = mkap(dst, 0, [[TOK, 4], [S, 4 * HD], [1, TOK]])
            return e.dma_start(out=dd, in_=src)
        return P.op("gpsimd", f, reads=("cc2_out",), writes=(dname,),
                    sem="gmy", inc=16, deps=[tcc2])

    tqm = my_rows(qm_d, "qm_d", 0)
    tkm = my_rows(km_d, "km_d", 1)
    tvm = my_rows(vm_d, "vm_d", 2)
    for nm in ("qm_d", "km_d", "vm_d"):
        P.wtok[nm] = tvm
    tqm = tkm = tvm

    for ei in range(NE):
        gdma(mkap(w1bf, ei * D * FFN, [[FFN, D], [1, FFN]]),
             mkap(w1_in, ei * D * FFN, [[FFN, D], [1, FFN]]),
             writes=(f"w1bf_{ei}",), deps=[tvm], dsem="wc")
        wc_last = gdma(mkap(w2bf, ei * FFN * D, [[D, FFN], [1, D]]),
                       mkap(w2_in, ei * FFN * D, [[D, FFN], [1, D]]),
                       writes=(f"w2bf_{ei}",), deps=[tvm], dsem="wc")
    for ei in range(NE):
        P.wtok[f"w1bf_{ei}"] = wc_last
        P.wtok[f"w2bf_{ei}"] = wc_last

    if _TRUNC == "P1f":
        return trunc_finish()
    P.barrier()
    tmcau = gdma(mcau[:, :].rearrange("p (m w) -> p m w", m=4),
                 mkap(mcau_in, 0, [[512, 128], [128 * 512, 4], [1, 512]]),
                 writes=("mcau",), dsem="c3")
    for hh in range(4):
        par = hh % 2
        kb = khat2[par]
        kbn = f"khat2_{par}"
        tkl = sdma(kb[0:64, :], km_d[hh * 64:(hh + 1) * 64, :],
                   reads=("km_d",), writes=(kbn,), deps=[tkm])
        if hh < 2:
            vo(lambda e, kb=kb: e.memset(kb[64:65, :], 1.0), writes=(f"{kbn}_ones",))
        vb = vhat2[par]
        vbn = f"vhat2_{par}"
        if hh < 2:
            vo(lambda e, vb=vb: e.memset(
                vb[:, :].rearrange("p (t j) -> p t j", j=65)[:, :, 64:65], 1.0),
               writes=(f"{vbn}_ones",))
        for s4 in range(4):
            vs = vth[s4 % 2]
            vsn = f"vth{s4 % 2}"
            tvl = sdma(vs[0:64, :], vm_d[hh * 64:(hh + 1) * 64,
                                         s4 * TOK:(s4 + 1) * TOK],
                       reads=("vm_d",), writes=(vsn,), deps=[tvm])
            for u in range(16):
                tpose(psT[:, u * 64:(u + 1) * 64],
                      vs[0:64, u * 128:(u + 1) * 128], ident_bf[0:64, 0:64],
                      reads=(vsn, "ident_bf"), writes=("psT",), deps=[tvl])
            vo(lambda e, vb=vb, s4=s4: e.tensor_copy(
                vb[:, :].rearrange("p (t j) -> p t j", j=65)
                [:, s4 * 16:(s4 + 1) * 16, 0:64],
                psT[:, 0:1024].rearrange("p (t j) -> p t j", j=64)),
                reads=("psT",), writes=(vbn,))
        for g0 in range(0, S, 512):
            t1 = so(lambda e, kb=kb, g0=g0: e.activation(
                ksq2[0:64, 0:512], kb[0:64, g0:g0 + 512], AF.Square),
                reads=(kbn,), writes=("ksq2",), deps=[tkl])
            t2 = mm(ps[6][0:1, 0:512], ones_b[0:64, 0:1], ksq2[0:64, 0:512],
                    start=True, stop=True, reads=("ksq2", "ones_b"),
                    writes=("ps6",), deps=[t1])
            vo(lambda e: e.reduce_max(cols[0:1, 7:8], ps[6][0:1, 0:512], axis=AX.X),
               reads=("ps6",), writes=("c7",))
            if g0 == 0:
                vo(lambda e, hh=hh: e.tensor_copy(mk2[:, hh:hh + 1], cols[0:1, 7:8]),
                   reads=("c7",), writes=(f"mk2_{hh}",))
            else:
                vo(lambda e, hh=hh: e.tensor_max(mk2[:, hh:hh + 1], mk2[:, hh:hh + 1],
                                                 cols[0:1, 7:8]),
                   reads=("c7", f"mk2_{hh}"), writes=(f"mk2_{hh}",))
        for tau in range(16):
            qpar = tau % 2
            qb = qhat2[qpar]
            qbn = f"qhat2_{qpar}"
            tql = sdma(qb[0:64, :], qm_d[hh * 64:(hh + 1) * 64,
                                         tau * 512:(tau + 1) * 512],
                       reads=("qm_d",), writes=(qbn,), deps=[tqm])
            t1 = so(lambda e, qb=qb: e.activation(ksq2[0:64, 0:512], qb[0:64, :],
                                                  AF.Square),
                    reads=(qbn,), writes=("ksq2",), deps=[tql])
            t2 = mm(ps[6][0:1, 0:512], ones_b[0:64, 0:1], ksq2[0:64, 0:512],
                    start=True, stop=True, reads=("ksq2", "ones_b"),
                    writes=("ps6",), deps=[t1])
            vo(lambda e, qpar=qpar: e.tensor_copy(l2[qpar][:, 0:512],
                                                  ps[6][0:1, 0:512]),
               reads=("ps6",), writes=(f"l2_{qpar}",))
            vo(lambda e, hh=hh, qpar=qpar: e.tensor_single_scalar(
                l2[qpar][:, 0:512], l2[qpar][:, 0:512], mk2[:, hh:hh + 1], ALU.mult),
                reads=(f"l2_{qpar}", f"mk2_{hh}"), writes=(f"l2_{qpar}",))
            so(lambda e, qpar=qpar: e.activation(
                l2[qpar][:, 0:512], l2[qpar][:, 0:512], AF.Ln,
                bias=np.float32(1e-30)),
                reads=(f"l2_{qpar}",), writes=(f"l2_{qpar}",))
            so(lambda e, qpar=qpar: e.activation(
                l2[qpar][:, 0:512], l2[qpar][:, 0:512], AF.Exp,
                scale=np.float32(0.5)),
                reads=(f"l2_{qpar}",), writes=(f"l2_{qpar}",))
            t7 = vo(lambda e, qb=qb, qpar=qpar: e.tensor_scalar_mul(
                qb[64:65, :], l2[qpar][:, 0:512], -1.0),
                reads=(f"l2_{qpar}",), writes=(qbn,))
            nch = 4 * tau + 4
            pE = 2 + qpar
            for kc in range(nch):
                pA = kc % 2
                mm(ps[pA][:, 0:512], kb[:, kc * 128:(kc + 1) * 128], qb[:, :],
                   start=True, stop=True,
                   reads=(kbn, qbn, f"khat2_{par}_ones"), writes=(f"ps{pA}",),
                   deps=[t7])
                dk = kc - 4 * tau
                if dk >= 0:
                    vo(lambda e, pA=pA, dk=dk: e.tensor_tensor(
                        ps[pA][:, 0:512], ps[pA][:, 0:512],
                        mcau[:, dk * 512:(dk + 1) * 512], ALU.add),
                        reads=(f"ps{pA}", "mcau"), writes=(f"ps{pA}",),
                        deps=[tmcau])
                so(lambda e, pA=pA, qpar=qpar: e.activation(
                    ptile[qpar][:, 0:512], ps[pA][:, 0:512], AF.Exp),
                    reads=(f"ps{pA}",), writes=(f"ptile{qpar}",))
                mm(ps[pE][0:65, 0:512],
                   vb[:, kc * 65:(kc + 1) * 65], ptile[qpar][:, 0:512],
                   start=(kc == 0), stop=(kc == nch - 1),
                   reads=(f"ptile{qpar}", vbn, f"vhat2_{par}_ones"),
                   writes=(f"ps{pE}",))
            vo(lambda e, pE=pE, qpar=qpar: e.tensor_scalar_max(
                l2[qpar][:, 0:512], ps[pE][64:65, 0:512], 1e-30),
                reads=(f"ps{pE}",), writes=(f"l2_{qpar}",))
            vo(lambda e, qpar=qpar: e.reciprocal(l2[qpar][:, 0:512],
                                                 l2[qpar][:, 0:512]),
               reads=(f"l2_{qpar}",), writes=(f"l2_{qpar}",))
            mm(ps[6][0:64, 0:512], ones_f[0:1, 0:64], l2[qpar][:, 0:512],
               start=True, stop=True, reads=(f"l2_{qpar}", "ones_f"),
               writes=("ps6",))
            vo(lambda e, qpar=qpar: e.tensor_copy(bcs2[qpar][:, 0:512],
                                                  ps[6][0:64, 0:512]),
               reads=("ps6",), writes=(f"bcs2_{qpar}",))
            vo(lambda e, hh=hh, tau=tau, pE=pE, qpar=qpar: e.tensor_tensor(
                attnT2[(hh % 2) * 64:(hh % 2) * 64 + 64,
                       (hh // 2) * S + tau * 512:(hh // 2) * S + (tau + 1) * 512],
                ps[pE][0:64, 0:512], bcs2[qpar][:, 0:512], ALU.mult),
                reads=(f"ps{pE}", f"bcs2_{qpar}"), writes=(f"attnT2_{hh}",))

    tct1 = sdma(cc3_in[0:128, :], attnT2[:, 0:S],
                reads=("attnT2_0", "attnT2_1"), writes=("cc3_in",))
    tct2 = sdma(cc3_in[128:256, :], attnT2[:, S:2 * S],
                reads=("attnT2_2", "attnT2_3"), writes=("cc3_in",))
    tcc3 = P.op("gpsimd", lambda e: e.collective_compute(
        "AllGather", ALU.bypass, replica_groups=[[0, 1, 2, 3], [4, 5, 6, 7]],
        ins=[cc3_in.opt()], outs=[cc3_out.opt()]),
        reads=("cc3_in",), writes=("cc3_out",), sem="cc3", inc=1,
        deps=[tct1, tct2])

    def my_cols(e):
        pid4 = nc.partition_id() % 4
        src = mkap(cc3_out, pid4 * TOK, [[S, D], [1, TOK]])
        dd = mkap(am_d, 0, [[TOK, D], [1, TOK]])
        return e.dma_start(out=dd, in_=src)
    tam = P.op("gpsimd", my_cols, reads=("cc3_out",), writes=("am_d",),
               sem="gam", inc=16, deps=[tcc3])

    if _TRUNC == "P2":
        return trunc_finish()
    P.barrier()
    sdma(fc2b[:, :], fc2b_in[:, :], writes=("fc2b",))
    sdma(ident_f[:, :], ident_in[:, :], writes=("ident_f",))
    sdma(fc1b[:, :].rearrange("p (ee c) -> p ee c", ee=8),
         mkap(bcol_fc1_in, 0, [[32, 128], [128 * 32, 8], [1, 32]]),
         writes=("fc1b",))
    gdma(bcA[:, :], bo_in[:, :], writes=("bcA",), dsem="c3")
    gdma(bcB[:, :], ln2w_in[:, :], writes=("bcB",), dsem="c3")
    c3_last = gdma(bcC[:, :], ln2b_in[:, :], writes=("bcC",), dsem="c3")
    for nm in ("bcA", "bcB", "bcC"):
        P.wtok[nm] = c3_last
    twL = gdma(woutT[:, :].rearrange("p (r c) -> p r c", r=8),
               mkap(w_Wo, 0, [[D, 128], [128 * D, 8], [1, D]]),
               writes=("woutT",))

    for t in range(NT):
        p3 = t % 2
        tat = sdma(attnTo_t[p3][:, :].rearrange("p (r c) -> p r c", r=8),
                   mkap(am_d, t * 128, [[TOK, 128], [128 * TOK, 8], [1, 128]]),
                   reads=("am_d",), writes=(f"attnTo{p3}",), deps=[tam])
        for half in range(2):
            pn = half
            for r in range(8):
                mm(ps[pn][:, 0:512],
                   attnTo_t[p3][:, r * 128:(r + 1) * 128],
                   woutT[:, r * D + half * 512: r * D + half * 512 + 512],
                   start=(r == 0), stop=(r == 7),
                   reads=("woutT", f"attnTo{p3}"), writes=(f"ps{pn}",),
                   deps=[twL, tat])
            vo(lambda e, pn=pn, half=half: e.tensor_tensor(
                f3a[:, half * 512:(half + 1) * 512], ps[pn][:, 0:512],
                bcA[:, half * 512:(half + 1) * 512], ALU.add),
                reads=(f"ps{pn}", "bcA"), writes=("f3a",))
        if _DEBUG:
            sdma(dbg["attn"][t * 128:(t + 1) * 128, :], f3a[:, :], reads=("f3a",))
        vo(lambda e, t=t: e.tensor_tensor(xres_t[:, :], f3a[:, :],
                                          xmid[:, t * D:(t + 1) * D], ALU.add),
           reads=("f3a", f"xmid_{t}"), writes=("xres_t",))
        vo(lambda e, t=t: e.tensor_tensor(moe_acc[:, t * D:(t + 1) * D],
                                          xres_t[:, :],
                                          xmid[:, t * D:(t + 1) * D], ALU.add),
           reads=("xres_t", f"xmid_{t}"), writes=(f"moe_{t}",))

        def fin_nx2(mean_ap, rstd_ap):
            vo(lambda e: e.tensor_scalar(f3a[:, :], xres_t[:, :], mean_ap, rstd_ap,
                                         ALU.subtract, ALU.mult),
               reads=("xres_t", "c0", "c4"), writes=("f3a",))
        plain_ln(xres_t[:, :], "xres_t", fin_nx2, f3b, "f3b")
        vo(lambda e: e.tensor_tensor(f3a[:, :], f3a[:, :], bcB[:, :], ALU.mult),
           reads=("f3a", "bcB"), writes=("f3a",))
        vo(lambda e: e.tensor_tensor(nx2f[:, :], f3a[:, :], bcC[:, :], ALU.add),
           reads=("f3a", "bcC"), writes=("nx2f",))
        for half in range(2):
            tch = vo(lambda e, half=half: e.tensor_copy(
                rT3[:, half * 512:(half + 1) * 512],
                nx2f[:, half * 512:(half + 1) * 512]),
                reads=("nx2f",), writes=("rT3",))
            for u in range(4):
                tpose(psT[:, (half * 4 + u) * 128:(half * 4 + u + 1) * 128],
                      rT3[:, half * 512 + u * 128: half * 512 + (u + 1) * 128],
                      ident_bf[:, :],
                      reads=("rT3", "ident_bf"), writes=("psT",), deps=[tch])
        for r in range(8):
            vo(lambda e, t=t, r=r: e.tensor_copy(
                nx2T[:, r * TOK + t * 128: r * TOK + (t + 1) * 128],
                psT[:, r * 128:(r + 1) * 128]),
                reads=("psT",), writes=(f"nx2T_{r}",))
        for half in range(2):
            tsub = vo(lambda e, half=half: e.tensor_tensor(
                f3b[:, 0:512], nx2f[:, half * 512:(half + 1) * 512],
                rT3[:, half * 512:(half + 1) * 512], ALU.subtract),
                reads=("nx2f", "rT3"), writes=("f3b",))
            tch = vo(lambda e, half=half: e.tensor_copy(
                rT3[:, half * 512:(half + 1) * 512], f3b[:, 0:512]),
                reads=("f3b",), writes=("rT3x",))
            for u in range(4):
                tpose(psT[:, (half * 4 + u) * 128:(half * 4 + u + 1) * 128],
                      rT3[:, half * 512 + u * 128: half * 512 + (u + 1) * 128],
                      ident_bf[:, :],
                      reads=("rT3x", "ident_bf"), writes=("psT",), deps=[tch])
        vo(lambda e: e.tensor_copy(rT3[:, :], psT[:, 0:1024]),
           reads=("psT", "rT3x"), writes=("rT3",))
        for r in range(8):
            mm(ps[2][:, 0:8],
               nx2T[:, r * TOK + t * 128: r * TOK + (t + 1) * 128],
               gateW[:, r * 8:(r + 1) * 8],
               start=(r == 0), stop=False,
               reads=(f"nx2T_{r}", "gateW"), writes=("ps2",))
        for r in range(8):
            mm(ps[2][:, 0:8],
               rT3[:, r * 128:(r + 1) * 128],
               gateW[:, r * 8:(r + 1) * 8],
               start=False, stop=(r == 7),
               reads=("rT3", "gateW"), writes=("ps2",))
        vo(lambda e: e.tensor_copy(lg8[:, :], ps[2][:, 0:8]),
           reads=("ps2",), writes=("lg8",))
        vo(lambda e: e.reduce_max(w12[:, 0:1], lg8[:, :], axis=AX.X),
           reads=("lg8",), writes=("w0",))
        vo(lambda e: e.tensor_single_scalar(is1[:, :], lg8[:, :], w12[:, 0:1],
                                            ALU.is_ge),
           reads=("lg8", "w0"), writes=("is1",))
        vo(lambda e: e.scalar_tensor_tensor(msk8[:, :], is1[:, :], NEG, lg8[:, :],
                                            ALU.mult, ALU.add),
           reads=("is1", "lg8"), writes=("msk8",))
        vo(lambda e: e.reduce_max(w12[:, 1:2], msk8[:, :], axis=AX.X),
           reads=("msk8",), writes=("w1c",))
        vo(lambda e: e.tensor_single_scalar(is2[:, :], msk8[:, :], w12[:, 1:2],
                                            ALU.is_ge),
           reads=("msk8", "w1c"), writes=("is2",))
        vo(lambda e: e.tensor_sub(w12[:, 2:3], w12[:, 1:2], w12[:, 0:1]),
           reads=("w0", "w1c"), writes=("w2c",))
        so(lambda e: e.activation(w12[:, 2:3], w12[:, 2:3], AF.Exp),
           reads=("w2c",), writes=("w2c",))
        vo(lambda e: e.tensor_scalar_add(w12[:, 3:4], w12[:, 2:3], 1.0),
           reads=("w2c",), writes=("w3c",))
        vo(lambda e: e.reciprocal(w12[:, 3:4], w12[:, 3:4]),
           reads=("w3c",), writes=("w3c",))
        vo(lambda e: e.tensor_mul(w12[:, 2:3], w12[:, 2:3], w12[:, 3:4]),
           reads=("w2c", "w3c"), writes=("w2c",))
        vo(lambda e: e.tensor_single_scalar(is2[:, :], is2[:, :], w12[:, 2:3],
                                            ALU.mult),
           reads=("is2", "w2c"), writes=("is2",))
        vo(lambda e, t=t: e.scalar_tensor_tensor(
            gates[:, t * 8:(t + 1) * 8], is1[:, :], w12[:, 3:4], is2[:, :],
            ALU.mult, ALU.add),
            reads=("is1", "is2", "w3c"), writes=(f"gates_{t}",))
        tgt = tpose(ps[3][0:8, 0:128], gates[:, t * 8:(t + 1) * 8], ident_f[:, :],
                    reads=(f"gates_{t}", "ident_f"), writes=("ps3",))
        tge = vo(lambda e: e.tensor_copy(g8T[:, :], ps[3][0:8, 0:128]),
                 reads=("ps3",), writes=("g8T",))
        for half in range(2):
            pn = half
            mm(ps[pn][:, 0:512], g8T[:, :], fc2b[:, half * 512:(half + 1) * 512],
               start=True, stop=True, reads=("g8T", "fc2b"), writes=(f"ps{pn}",),
               deps=[tge])
            vo(lambda e, pn=pn, t=t, half=half: e.tensor_tensor(
                moe_acc[:, t * D + half * 512: t * D + (half + 1) * 512],
                moe_acc[:, t * D + half * 512: t * D + (half + 1) * 512],
                ps[pn][:, 0:512], ALU.add),
                reads=(f"ps{pn}", f"moe_{t}"), writes=(f"moe_{t}",))

    if _TRUNC == "P3wo":
        return trunc_finish()
    P.barrier()
    w1rot = [0]
    w2rot = [0]
    for ei in range(NE):
        for qg in range(4):
            for fh in range(8):
                sl = w1rot[0] % 2
                w1rot[0] += 1
                t_w1 = gdma(w1buf[sl][:, :].rearrange("p (r c) -> p r c", r=8),
                            mkap(w1bf, ei * D * FFN + fh * 512,
                                 [[FFN, 128], [128 * FFN, 8], [1, 512]]),
                            writes=(f"w1buf{sl}",), deps=[P.wtok[f"w1bf_{ei}"]],
                            dsem=f"w1{sl}")
                for fs in range(4):
                    pn = nps()
                    for r in range(8):
                        mm(ps[pn][:, 0:512],
                           w1buf[sl][:, r * 512 + fs * 128: r * 512 + (fs + 1) * 128],
                           nx2T[:, r * TOK + qg * 512: r * TOK + (qg + 1) * 512],
                           start=(r == 0), stop=(r == 7),
                           reads=(f"w1buf{sl}", f"nx2T_{r}"), writes=(f"ps{pn}",),
                           deps=[t_w1])
                    fc = fh * 4 + fs
                    so(lambda e, pn=pn, ei=ei, fc=fc: e.activation(
                        hT[:, fc * 512:(fc + 1) * 512], ps[pn][:, 0:512], AF.Gelu,
                        bias=fc1b[:, ei * 32 + fc: ei * 32 + fc + 1]),
                        reads=(f"ps{pn}", "fc1b"), writes=(f"hT_{fc}",))
            for f2h in range(16):
                sl = w2rot[0] % 2
                w2rot[0] += 1
                t_w2 = gdma(w2buf[sl][:, :].rearrange("p (r c) -> p r c", r=2),
                            mkap(w2bf, ei * FFN * D + f2h * 256 * D,
                                 [[D, 128], [128 * D, 2], [1, D]]),
                            writes=(f"w2buf{sl}",), deps=[P.wtok[f"w2bf_{ei}"]],
                            dsem=f"w2{sl}")
                for qt in range(4):
                    for nh in range(2):
                        pn = qt * 2 + nh
                        pname = f"ps{pn}" if pn < 7 else "psT"
                        out_ap = (ps[pn][:, 0:512] if pn < 7
                                  else psT[:, 0:1024].bitcast(F32))
                        for fs in range(2):
                            fc = f2h * 2 + fs
                            mm(out_ap,
                               hT[:, fc * 512 + qt * 128: fc * 512 + (qt + 1) * 128],
                               w2buf[sl][:, fs * D + nh * 512: fs * D + nh * 512 + 512],
                               start=(f2h == 0 and fs == 0),
                               stop=(f2h == 15 and fs == 1),
                               reads=(f"w2buf{sl}", f"hT_{fc}"),
                               writes=(pname,), deps=[t_w2])
            for qt in range(4):
                for nh in range(2):
                    pn = qt * 2 + nh
                    pname = f"ps{pn}" if pn < 7 else "psT"
                    src = (ps[pn][:, 0:512] if pn < 7
                           else psT[:, 0:1024].bitcast(F32))
                    tt = qg * 4 + qt
                    vo(lambda e, src=src, tt=tt, nh=nh, ei=ei: e.scalar_tensor_tensor(
                        moe_acc[:, tt * D + nh * 512: tt * D + (nh + 1) * 512],
                        src, gates[:, tt * 8 + ei: tt * 8 + ei + 1],
                        moe_acc[:, tt * D + nh * 512: tt * D + (nh + 1) * 512],
                        ALU.mult, ALU.add),
                        reads=(pname, f"gates_{tt}", f"moe_{tt}"),
                        writes=(f"moe_{tt}",))

    P.barrier()
    out_toks = []
    for t in range(NT):
        def fin_out(mean_ap, rstd_ap, t=t):
            vo(lambda e: e.tensor_scalar(
                f3a[:, :], moe_acc[:, t * D:(t + 1) * D], mean_ap, rstd_ap,
                ALU.subtract, ALU.mult),
                reads=(f"moe_{t}", "c0", "c4"), writes=("f3a",))
        plain_ln(moe_acc[:, t * D:(t + 1) * D], f"moe_{t}", fin_out, f3b, "f3b")
        out_toks.append(sdma(out_dram[t * 128:(t + 1) * 128, :], f3a[:, :],
                             reads=("f3a",)))

    for tok in out_toks:
        P._wait("sync", tok[0], tok[1])

    P.finalize()
    import contextlib
    with contextlib.ExitStack() as st:
        for s in P.sem_names:
            P.sems[s] = st.enter_context(nc.semaphore(s))
        block = st.enter_context(nc.Block())
        P.replay(block)
    return nc


_NC_CACHE = {}


def kernel(**inputs):
    f32 = np.float32
    g = {k: np.asarray(v, dtype=f32) for k, v in inputs.items()}
    x = g["L0_attn_x"]

    def bc(v):
        return np.ascontiguousarray(np.broadcast_to(v.reshape(1, D), (128, D)))

    shared = {
        "w_qkv": g["L0_attn_qkv_w"],
        "bcol_qkv": np.ascontiguousarray(g["L0_attn_qkv_b"].reshape(24, 128).T),
        "w_rq": g["L0_attn_rq_w"],
        "bcol_rq": np.ascontiguousarray(g["L0_attn_rq_b"].reshape(2, 128).T),
        "w_rkv": g["L0_attn_rkv_w"],
        "bcol_rkv": np.ascontiguousarray(g["L0_attn_rkv_b"].reshape(4, 128).T),
        "w_rout": g["L0_attn_rout_w"],
        "bcol_rout": np.ascontiguousarray(g["L0_attn_rout_b"].reshape(8, 128).T),
        "w_out": g["L0_attn_out_w"],
        "outb_bc": bc(g["L0_attn_out_b"]),
        "gl": np.ascontiguousarray(np.concatenate([g["L0_attn_gl_w"]] * 2, 0)),
        "gr": np.ascontiguousarray(np.concatenate([g["L0_attn_gr_w"]] * 2, 0)),
        "w_Wqkv": g["L0_ffn_Wqkv"],
        "bcol_Wqkv": np.ascontiguousarray(g["L0_ffn_bqkv"].reshape(24, 128).T),
        "w_Wo": g["L0_ffn_Wo"],
        "bo_bc": bc(g["L0_ffn_bo"]),
        "ln1w_bc": bc(g["L0_ffn_ln1_w"]),
        "ln1b_bc": bc(g["L0_ffn_ln1_b"]),
        "ln2w_bc": bc(g["L0_ffn_ln2_w"]),
        "ln2b_bc": bc(g["L0_ffn_ln2_b"]),
        "w_gate": g["L0_ffn_gate_W"],
        "w1": np.ascontiguousarray(g["L0_ffn_fc1_w"].transpose(0, 2, 1)),
        "bcol_fc1": np.ascontiguousarray(
            g["L0_ffn_fc1_b"].reshape(NE, 32, 128).transpose(0, 2, 1)),
        "w2": np.ascontiguousarray(g["L0_ffn_fc2_w"].transpose(0, 2, 1)),
        "fc2b": g["L0_ffn_fc2_b"],
        "ident": np.eye(128, dtype=f32),
    }
    jj, ii = np.meshgrid(np.arange(128), np.arange(128), indexing="ij")
    shared["mlo"] = np.where(ii <= jj, 0.0, NEG).astype(f32)
    shared["mhi"] = np.where(jj < ii, 0.0, NEG).astype(f32)
    mcau = np.empty((4, 128, 512), f32)
    uu = np.arange(512)
    for m in range(4):
        mcau[m] = np.where(uu[None, :] >= (128 * m + np.arange(128))[:, None],
                           0.0, NEG)
    shared["mcau"] = mcau

    in_maps = []
    for c in range(8):
        b, qd = c // 4, c % 4
        t0 = qd * TOK
        xT = np.zeros((D, XW), f32)
        xb = np.ascontiguousarray(x[b].T)
        xT[:, HALO:] = xb[:, t0:t0 + TOK]
        if qd > 0:
            xT[:, :HALO] = xb[:, t0 - HALO:t0]
        lat = np.arange(64)[:, None]
        tt = np.arange(NT)[None, :]
        m = dict(shared)
        m["xT"] = xT
        m["xtok"] = np.ascontiguousarray(x[b, t0:t0 + TOK])
        m["mlo0"] = (np.full((128, 128), NEG, f32) if qd == 0 else shared["mlo"])
        m["mlat"] = np.where(lat < (qd * 16 + tt), 0.0, NEG).astype(f32)
        in_maps.append(m)

    if "nc" not in _NC_CACHE:
        _NC_CACHE["nc"] = build_nc()
    nc = _NC_CACHE["nc"]
    res = run_bass_kernel_spmd(nc, in_maps, core_ids=list(range(8)))
    out = np.empty((B, S, D), f32)
    for c in range(8):
        b, qd = c // 4, c % 4
        out[b, qd * TOK:(qd + 1) * TOK] = np.asarray(res.results[c]["out"])
    return out


# version 07
# speedup vs baseline: 1.0279x; 1.0279x over previous
import dataclasses
import numpy as np
import ml_dtypes
import concourse.bass as bass
import concourse.mybir as mybir
from concourse.bass_utils import run_bass_kernel_spmd

F32 = mybir.dt.float32
BF16 = mybir.dt.bfloat16
AF = mybir.ActivationFunctionType
ALU = mybir.AluOpType
AX = mybir.AxisListType

B, S, D = 2, 8192, 1024
NH, HD = 16, 64
LH = 4
NE, FFN = 8, 4096
NEG = -1.0e9
TOK = 2048
NT = 16
HALO = 128
XW = HALO + TOK
NTX = XW // 128

_DEBUG = False
_TRUNC = None
_SUB = 9


class Prog:
    ENGINES = ("sync", "gpsimd", "vector", "scalar", "tensor")

    def __init__(self, nc):
        self.nc = nc
        self.q = {e: [] for e in self.ENGINES}
        self.sem_names = (list(self.ENGINES) + ["sdma", "gdma", "cc1", "cc2", "cc3", "cst", "xt", "wb0", "wb1", "wc", "gmy", "gam", "c3", "w10", "w11", "w20", "w21"])
        self.cnt = {s: 0 for s in self.sem_names}
        self.hwm = {}
        self.wtok = {}
        self.rtok = {}
        self.sems = {}
        self.used = {s: set() for s in self.sem_names}
        self.incs = {s: {} for s in self.sem_names}
        self.remap = {s: {} for s in self.sem_names}
        self.final_inc = None
        self.btoks = []
        self.dma_owner = {}
        for sname in self.sem_names:
            if sname in ("sdma",):
                self.dma_owner[sname] = "sync"
            elif sname not in self.ENGINES and not sname.startswith("cc"):
                self.dma_owner[sname] = "gpsimd"
        self.owned = {}
        for sname, eng in self.dma_owner.items():
            self.owned.setdefault(eng, []).append(sname)
        self.boundaries = {sname: [0] for sname in self.dma_owner}

    def barrier(self):
        self.btoks = [(s, self.cnt[s]) for s in self.sem_names if self.cnt[s] > 0]

    def snap(self, sem, val):
        if sem not in self.dma_owner:
            return val
        for b in self.boundaries[sem]:
            if b >= val:
                return b
        owner = self.dma_owner[sem]
        total = self.cnt[sem]
        self.boundaries[sem].append(total)
        self._wait_raw(owner, sem, total)
        return total

    def _wait_raw(self, eng, sem, val):
        if val <= 0 or self.hwm.get((eng, sem), 0) >= val:
            return
        self.hwm[(eng, sem)] = val
        self.used[sem].add(val)
        P = self

        def w(e, s=sem, v=val):
            e.wait_ge(P.sems[s], P.remap[s][v])
        self.q[eng].append(w)

    def _wait(self, eng, sem, val):
        self._wait_raw(eng, sem, self.snap(sem, val))

    def op(self, eng, fn, reads=(), writes=(), sem=None, inc=1, deps=()):
        dep = list(deps) + list(self.btoks)
        for b in reads:
            t = self.wtok.get(b)
            if t:
                dep.append(t)
        for b in writes:
            t = self.wtok.get(b)
            if t:
                dep.append(t)
            dep.extend(self.rtok.get(b, ()))
        sem = sem or eng
        best = {}
        for (s, v) in dep:
            if eng == "tensor" and s == "tensor":
                continue
            if v > best.get(s, 0):
                best[s] = v
        for s, v in sorted(best.items()):
            assert v <= self.cnt[s], (eng, s, v, self.cnt[s])
            self._wait(eng, s, v)
        self.cnt[sem] += inc
        raw = self.cnt[sem]
        self.incs[sem][raw] = inc
        P = self

        def run(e, f=fn, s=sem, r=raw):
            ins = f(e)
            amt = P.final_inc[s].get(r)
            if amt:
                ins.then_inc(P.sems[s], amt)
        self.q[eng].append(run)
        tok = (sem, raw)
        for b in writes:
            self.wtok[b] = tok
            self.rtok[b] = []
        for b in reads:
            self.rtok.setdefault(b, []).append(tok)
        return tok

    def finalize(self):
        self.final_inc = {}
        for s in self.sem_names:
            keep_all = s in self.dma_owner or s.startswith("cc")
            mx = max(self.used[s]) if self.used[s] else 0
            fi = {}
            run = 0
            for raw in sorted(self.incs[s]):
                inc = self.incs[s][raw]
                if keep_all or raw <= mx:
                    fi[raw] = inc
                    run += inc
                self.remap[s][raw] = run
            self.final_inc[s] = fi

    def replay(self, block):
        for ename in self.ENGINES:
            if not self.q[ename]:
                continue

            def body(e, lst=tuple(self.q[ename])):
                for f in lst:
                    f(e)
            getattr(block, ename)(body)


def mkap(base_ap, offset, dims):
    return dataclasses.replace(base_ap, offset=offset, ap=dims)


def build_nc():
    nc = bass.Bass("TRN2", target_bir_lowering=False)
    P = Prog(nc)
    fin_state = {}

    def trunc_finish():
        out_toks = []
        for t2 in range(NT):
            out_toks.append(fin_state["sdma"](
                fin_state["out"][t2 * 128:(t2 + 1) * 128, :],
                fin_state["f32a"][:, :]))
        for tok in out_toks:
            P._wait("sync", tok[0], tok[1])
        P.finalize()
        import contextlib
        with contextlib.ExitStack() as st:
            for s2 in P.sem_names:
                P.sems[s2] = st.enter_context(nc.semaphore(s2))
            block = st.enter_context(nc.Block())
            P.replay(block)
        return nc

    def din(name, shape):
        return nc.dram_tensor(name, shape, F32, kind="ExternalInput").ap()

    xT_in = din("xT", [D, XW])
    xtok_in = din("xtok", [TOK, D])
    w_qkv = din("w_qkv", [D, 3 * D])
    bcol_qkv_in = din("bcol_qkv", [128, 24])
    w_rq = din("w_rq", [D, 256])
    bcol_rq_in = din("bcol_rq", [128, 2])
    w_rkv = din("w_rkv", [D, 512])
    bcol_rkv_in = din("bcol_rkv", [128, 4])
    w_rout = din("w_rout", [256, D])
    bcol_rout_in = din("bcol_rout", [128, 8])
    w_out = din("w_out", [D, D])
    outb_in = din("outb_bc", [128, D])
    gl_in = din("gl", [128, 1])
    gr_in = din("gr", [128, 1])
    w_Wqkv = din("w_Wqkv", [D, 3 * D])
    bcol_Wqkv_in = din("bcol_Wqkv", [128, 24])
    w_Wo = din("w_Wo", [D, D])
    bo_in = din("bo_bc", [128, D])
    ln1w_in = din("ln1w_bc", [128, D])
    ln1b_in = din("ln1b_bc", [128, D])
    ln2w_in = din("ln2w_bc", [128, D])
    ln2b_in = din("ln2b_bc", [128, D])
    w_gate = din("w_gate", [D, NE])
    w1_in = din("w1", [NE, D, FFN])
    bcol_fc1_in = din("bcol_fc1", [NE, 128, 32])
    w2_in = din("w2", [NE, FFN, D])
    fc2b_in = din("fc2b", [NE, D])
    mlo0_in = din("mlo0", [128, 128])
    mlo_in = din("mlo", [128, 128])
    mhi_in = din("mhi", [128, 128])
    mcau_in = din("mcau", [4, 128, 512])
    mlat_in = din("mlat", [64, NT])
    ident_in = din("ident", [128, 128])

    out_dram = nc.dram_tensor("out", [TOK, D], F32, kind="ExternalOutput").ap()
    dbg = {}
    if _DEBUG:
        for nm in ("xmid", "attn", "nx1"):
            dbg[nm] = nc.dram_tensor("dbg_" + nm, [TOK, D], F32,
                                     kind="ExternalOutput").ap()

    cc1_in = nc.dram_tensor("cc1_in", [D, 16], BF16).ap()
    cc1_out = nc.dram_tensor("cc1_out", [4 * D, 16], BF16).ap()
    cc2_in = nc.dram_tensor("cc2_in", [3 * D, TOK], BF16).ap()
    cc2_out = nc.dram_tensor("cc2_out", [12 * D, TOK], BF16).ap()
    cc3_in = nc.dram_tensor("cc3_in", [4 * HD, S], BF16).ap()
    cc3_out = nc.dram_tensor("cc3_out", [D, S], BF16).ap()
    qm_d = nc.dram_tensor("qm_d", [4 * HD, S], BF16).ap()
    km_d = nc.dram_tensor("km_d", [4 * HD, S], BF16).ap()
    vm_d = nc.dram_tensor("vm_d", [4 * HD, S], BF16).ap()
    am_d = nc.dram_tensor("am_d", [D, TOK], BF16).ap()
    w1bf = nc.dram_tensor("w1bf", [NE, D, FFN], BF16).ap()
    w2bf = nc.dram_tensor("w2bf", [NE, FFN, D], BF16).ap()

    SB_LIMIT = 212800
    SB_BASE = 16512
    sb = {}

    def sal(name, shape, dt, at):
        per = int(np.prod(shape[1:])) * mybir.dt.size(dt)
        assert at % 32 == 0 and at + per <= SB_LIMIT, (name, at, per, at + per)
        sb[name] = nc.alloc_sbuf_tensor_at(name, list(shape), dt,
                                           offset=at + SB_BASE)
        return sb[name]

    ident_bf = sal("ident_bf", [128, 128], BF16, 0)
    sal("mlo0", [128, 128], BF16, 256)
    sal("mlo", [128, 128], BF16, 512)
    sal("mhi", [128, 128], BF16, 768)
    mlat = sal("mlat", [64, NT], F32, 1024)
    sal("bcol_qkv", [128, 24], F32, 1088)
    sal("bcol_rq", [128, 2], F32, 1184)
    sal("bcol_rkv", [128, 4], F32, 1216)
    sal("bcol_rout", [128, 8], F32, 1248)
    sal("bcol_Wqkv", [128, 24], F32, 1280)
    gl_sb = sal("gl", [128, 1], BF16, 1376)
    gr_sb = sal("gr", [128, 1], BF16, 1408)
    ones_b = sal("ones_b", [128, 1], BF16, 1440)
    ones_f = sal("ones_f", [1, 128], F32, 1472)
    bcA = sal("bcA", [128, D], BF16, 1984)
    bcB = sal("bcB", [128, D], BF16, 4032)
    bcC = sal("bcC", [128, D], BF16, 6080)
    cols = sal("cols", [128, 8], F32, 8128)
    mk1 = sal("mk1", [1, 16], F32, 8160)
    gateW = sal("gateW_bf", [128, NE * 8], BF16, 8224)
    PB = 8352

    SLOT_A = PB
    SLOT_K = PB + 16384
    SLOT_Q = PB + 51200
    SLOT_V = PB + 83968
    SLOT_R = PB + 118784
    SLOT_M = PB + 126976
    SCR = PB + 159744
    assert SCR == 168096

    xTq = sal("xTq", [128, 8 * 512], BF16, SLOT_A)
    woutT = sal("woutT", [128, 8 * D], BF16, SLOT_A)
    kT = sal("kT", [128, 8 * XW], BF16, SLOT_K)
    qT1 = sal("qT1", [128, 8 * TOK], BF16, SLOT_Q)
    vtok = sal("vtok", [128, NTX * D], BF16, SLOT_V)
    rqT = sal("rqT", [128, 2 * TOK], BF16, SLOT_R)
    xmid = sal("xmid", [128, NT * D], BF16, SLOT_M)

    _o = [SCR]

    def bump(n):
        a = _o[0]
        _o[0] = (a + n + 31) // 32 * 32
        return a

    wbuf = [sal(f"wbuf{i}", [128, 8 * 128], BF16, bump(2048)) for i in range(2)]
    pool_f = sal("pool_f", [128, 8 * 16], F32, bump(512))
    pooledT = sal("pooledT", [128, 8 * 64], BF16, bump(1024))
    rkT = sal("rkT", [128, 2 * 64], BF16, bump(256))
    rvT = sal("rvT", [128, 2 * 64], BF16, bump(256))
    rkhat = sal("rkhat", [65, LH * 64], BF16, bump(512))
    rvhat = sal("rvhat", [64, LH * 65], BF16, bump(544))
    rv_tok = sal("rv_tok", [64, 256], BF16, bump(512))
    cneg_lat = sal("cneg_lat", [LH, TOK], BF16, bump(4096))
    cneg_loc = sal("cneg_loc", [NH, TOK], BF16, bump(4096))
    sq64 = sal("sq64", [64, 512], BF16, bump(1024))
    n2row = sal("n2row", [1, 512], F32, bump(2048))
    khatb = [sal(f"khatb{i}", [65, 256], BF16, bump(512)) for i in range(2)]
    vhatb = [sal(f"vhatb{i}", [128, 130], BF16, bump(288)) for i in range(2)]
    qhatb = [sal(f"qhatb{i}", [65, 128], BF16, bump(256)) for i in range(2)]
    latqb = [sal(f"latqb{i}", [65, 128], BF16, bump(256)) for i in range(2)]
    ptile = [sal(f"ptile{i}", [128, 512], BF16, bump(1024)) for i in range(2)]
    tloc = [sal(f"tloc{i}", [65, 128], BF16, bump(256)) for i in range(2)]
    lr = [sal(f"lr{i}", [1, 128], F32, bump(512)) for i in range(2)]
    grow = [sal(f"grow{i}", [1, 384], F32, bump(1536)) for i in range(2)]
    bcsb = [sal(f"bcs{i}", [128, 256], BF16, bump(512)) for i in range(2)]
    remh = [sal(f"remh{i}", [64, 128], BF16, bump(256)) for i in range(2)]
    latnT = sal("latnT", [128, 2 * 128], BF16, bump(512))
    remT = sal("remT", [128, 8 * 128], BF16, bump(2048))
    mixT = sal("mixT", [128, 8 * 128], BF16, bump(2048))
    f32a = sal("f32a", [128, D], F32, bump(4096))
    f32b = sal("f32b", [128, 512], F32, bump(2048))
    rout_sb = sal("rout_sb", [128, 2 * D], BF16, bump(4096))
    assert _o[0] <= SB_LIMIT, _o[0]

    khat2 = [sal(f"khat2_{i}", [65, S], BF16, SLOT_K + i * 16384) for i in range(2)]
    vth = [sal(f"vth{i}", [64, TOK], BF16, SLOT_Q + i * 4096) for i in range(2)]
    vhat2 = [sal(f"vhat2_{i}", [128, 64 * 65], BF16, SLOT_Q + 8192 + i * 8320)
             for i in range(2)]
    attnT2 = sal("attnT2", [128, 2 * S], BF16, SLOT_V)
    nx1T = sal("nx1T", [128, 8 * TOK], BF16, SLOT_V)
    qhat2 = [sal(f"qhat2_{i}", [65, 512], BF16, SCR + i * 1024) for i in range(2)]
    pt_extra = [sal(f"ptx{i}", [128, 512], BF16, SCR + 13344 + i * 1024)
                for i in range(6)]
    ksq2 = sal("ksq2", [64, 512], BF16, SCR + 2048)
    l2 = [sal(f"l2_{i}", [1, 512], F32, SCR + 3072 + i * 2048) for i in range(2)]
    bcs2 = [sal(f"bcs2_{i}", [64, 512], BF16, SCR + 7168 + i * 1024) for i in range(2)]
    mk2 = sal("mk2", [1, 8], F32, SCR + 9216)
    mcau = sal("mcau", [128, 4 * 512], BF16, SCR + 9248)

    nx2T = sal("nx2T", [128, 8 * TOK], BF16, SLOT_K)
    moe_acc = sal("moe_acc", [128, NT * D], F32, SLOT_Q)
    hT = sal("hT", [128, 32 * 512], BF16, SLOT_M)
    w1buf = [sal(f"w1buf{i}", [128, 8 * 512], BF16, SCR + i * 8192) for i in range(2)]
    w2buf = [sal(f"w2buf{i}", [128, 2 * D], BF16, SCR + 16384 + i * 4096)
             for i in range(2)]
    _o3 = [SCR + 24576]

    def bump3(n):
        a = _o3[0]
        _o3[0] = (a + n + 31) // 32 * 32
        return a

    attnTo_t = [sal(f"attnTo{i}", [128, 8 * 128], BF16, bump3(2048))
                for i in range(2)]
    f3a = sal("f3a", [128, D], F32, bump3(4096))
    f3b = sal("f3b", [128, 512], F32, bump3(2048))
    rT3 = sal("rT3", [128, 8 * 128], BF16, bump3(2048))
    gates = sal("gates", [128, NT * 8], F32, bump3(512))
    g8T = sal("g8T", [8, 128], F32, bump3(512))
    ident_f = sal("ident_f", [128, 128], F32, bump3(512))
    lg8 = sal("lg8", [128, 8], F32, bump3(32))
    is1 = sal("is1", [128, 8], F32, bump3(32))
    is2 = sal("is2", [128, 8], F32, bump3(32))
    msk8 = sal("msk8", [128, 8], F32, bump3(32))
    w12 = sal("w12", [128, 4], F32, bump3(32))
    fc2b = sal("fc2b", [NE, D], F32, bump3(4096))
    fc1b = sal("fc1b", [128, NE * 32], F32, bump3(1024))
    assert _o3[0] <= SB_LIMIT, _o3[0]
    xres_t = sal("xres_t", [128, D], F32, SLOT_R)
    nx2f = sal("nx2f", [128, D], F32, SLOT_R + 4096)

    ps = {i: nc.alloc_psum_tensor(f"ps{i}", [128, 512], F32) for i in range(7)}
    psT = nc.alloc_psum_tensor("psT", [128, 1024], BF16)

    def sdma(dst, src, reads=(), writes=(), deps=()):
        return P.op("sync", lambda e: e.dma_start(out=dst, in_=src),
                    reads=reads, writes=writes, sem="sdma", inc=16, deps=deps)

    def gdma(dst, src, reads=(), writes=(), deps=(), dsem="gdma"):
        return P.op("gpsimd", lambda e: e.dma_start(out=dst, in_=src),
                    reads=reads, writes=writes, sem=dsem, inc=16, deps=deps)

    def mm(out, lhsT, rhs, start, stop, reads=(), writes=(), deps=()):
        return P.op("tensor",
                    lambda e: e.matmul(out, lhsT, rhs, start=start, stop=stop),
                    reads=reads, writes=writes, deps=deps)

    def tpose(out, in_, ident, reads=(), writes=(), deps=()):
        return P.op("tensor", lambda e: e.transpose(out, in_, ident),
                    reads=reads, writes=writes, deps=deps)

    def vo(fn, reads=(), writes=(), deps=()):
        return P.op("vector", fn, reads=reads, writes=writes, deps=deps)

    def so(fn, reads=(), writes=(), deps=()):
        return P.op("scalar", fn, reads=reads, writes=writes, deps=deps)

    def go(fn, reads=(), writes=(), deps=()):
        return P.op("gpsimd", fn, reads=reads, writes=writes, deps=deps)

    def plain_ln(src_ap, src_name, fin, sq_scr, sq_name):
        vo(lambda e: e.reduce_sum(cols[:, 0:1], src_ap, axis=AX.X),
           reads=(src_name,), writes=("c0",))
        so(lambda e: e.activation(sq_scr[:, 0:512], src_ap[:, 0:512], AF.Square,
                                  accum_out=cols[:, 5:6]),
           reads=(src_name,), writes=(sq_name, "c5"))
        so(lambda e: e.activation(sq_scr[:, 0:512], src_ap[:, 512:1024], AF.Square,
                                  accum_out=cols[:, 6:7]),
           reads=(src_name,), writes=(sq_name, "c6"))
        vo(lambda e: e.tensor_add(cols[:, 5:6], cols[:, 5:6], cols[:, 6:7]),
           reads=("c5", "c6"), writes=("c5",))
        vo(lambda e: e.tensor_scalar_mul(cols[:, 0:1], cols[:, 0:1], 1.0 / D),
           reads=("c0",), writes=("c0",))
        vo(lambda e: e.tensor_mul(cols[:, 1:2], cols[:, 0:1], cols[:, 0:1]),
           reads=("c0",), writes=("c1",))
        vo(lambda e: e.tensor_scalar(cols[:, 2:3], cols[:, 5:6], 1.0 / D, None,
                                     ALU.mult),
           reads=("c5",), writes=("c2",))
        vo(lambda e: e.tensor_sub(cols[:, 2:3], cols[:, 2:3], cols[:, 1:2]),
           reads=("c2", "c1"), writes=("c2",))
        so(lambda e: e.activation(cols[:, 3:4], cols[:, 2:3], AF.Ln,
                                  bias=np.float32(1e-5)),
           reads=("c2",), writes=("c3",))
        so(lambda e: e.activation(cols[:, 4:5], cols[:, 3:4], AF.Exp,
                                  scale=np.float32(-0.5)),
           reads=("c3",), writes=("c4",))
        fin(cols[:, 0:1], cols[:, 4:5])

    fin_state["sdma"] = sdma
    fin_state["out"] = out_dram
    fin_state["f32a"] = f32a

    gdma(ident_bf[:, :], ident_in[:, :], writes=("ident_bf",), dsem="cst")
    gdma(sb["mlo0"][:, :], mlo0_in[:, :], writes=("mlo0",), dsem="cst")
    gdma(sb["mlo"][:, :], mlo_in[:, :], writes=("mlo",), dsem="cst")
    gdma(sb["mhi"][:, :], mhi_in[:, :], writes=("mhi",), dsem="cst")
    sdma(mlat[:, :], mlat_in[:, :], writes=("mlat",))
    for nm, src in [("bcol_qkv", bcol_qkv_in), ("bcol_rq", bcol_rq_in),
                    ("bcol_rkv", bcol_rkv_in), ("bcol_rout", bcol_rout_in),
                    ("bcol_Wqkv", bcol_Wqkv_in)]:
        sdma(sb[nm][:, :], src[:, :], writes=(nm,))
    gdma(bcA[:, :], outb_in[:, :], writes=("bcA",), dsem="cst")
    gdma(bcB[:, :], ln1w_in[:, :], writes=("bcB",), dsem="cst")
    gdma(bcC[:, :], ln1b_in[:, :], writes=("bcC",), dsem="cst")
    gdma(gl_sb[:, :], gl_in[:, :], writes=("gl",), dsem="cst")
    gdma(gr_sb[:, :], gr_in[:, :], writes=("gr",), dsem="cst")
    cst_last = gdma(gateW[:, :].rearrange("p (r e) -> p r e", r=8),
                    mkap(w_gate, 0, [[NE, 128], [NE * 128, 8], [1, NE]]),
                    writes=("gateW",), dsem="cst")
    for nm in ("ident_bf", "mlo0", "mlo", "mhi", "bcA", "bcB", "bcC",
               "gl", "gr", "gateW"):
        P.wtok[nm] = cst_last
    vo(lambda e: e.memset(ones_b[:, :], 1.0), writes=("ones_b",))
    vo(lambda e: e.memset(ones_f[:, :], 1.0), writes=("ones_f",))
    for i in range(2):
        vo(lambda e, i=i: e.memset(khatb[i][64:65, :], 1.0), writes=(f"khatb{i}_ones",))
        vo(lambda e, i=i: e.memset(vhatb[i][:, 64:65], 1.0), writes=(f"vhatb{i}_o1",))
        vo(lambda e, i=i: e.memset(vhatb[i][:, 129:130], 1.0), writes=(f"vhatb{i}_o2",))
    vo(lambda e: e.memset(rkhat[64:65, :], 1.0), writes=("rkhat_ones",))
    for h in range(LH):
        vo(lambda e, h=h: e.memset(rvhat[:, h * 65 + 64:h * 65 + 65], 1.0),
           writes=(f"rvhat_o{h}",))

    psrot = [0]

    def nps():
        r = psrot[0] % 4
        psrot[0] += 1
        return r

    wrot = [0]

    def load_w(w_ap, wtot, ci, ncols=128):
        sl = wrot[0] % 2
        wrot[0] += 1
        t = gdma(wbuf[sl][:, 0:8 * ncols].rearrange("p (r c) -> p r c", r=8),
                 mkap(w_ap, ci * ncols,
                      [[wtot, 128], [128 * wtot, 8], [1, ncols]]),
                 writes=(f"wbuf{sl}",), dsem=f"wb{sl}")
        return sl, t

    XCH = [(0, 512), (512, 512), (1024, 512), (1536, 512), (2048, 128)]
    for (x0, xw) in XCH:
        txq = gdma(xTq[:, 0:8 * xw].rearrange("p (r w) -> p r w", r=8),
                   mkap(xT_in, x0, [[XW, 128], [128 * XW, 8], [1, xw]]),
                   writes=("xTq",), dsem="xt")
        for li0 in range(xw // 128):
            gtok = x0 + li0 * 128
            if gtok < HALO:
                continue
            li = (gtok - HALO) // 128
            for r in range(8):
                vo(lambda e, r=r, li0=li0, li=li, xw=xw: e.reduce_sum(
                    pool_f[:, r * 16 + li:r * 16 + li + 1],
                    xTq[:, r * xw + li0 * 128: r * xw + (li0 + 1) * 128],
                    axis=AX.X),
                    reads=("xTq",), writes=(f"pool_{r}_{li}",), deps=[txq])
        for ci in range(26):
            if ci < 24:
                w_ap, wtot, wci = w_qkv, 3 * D, ci
            else:
                w_ap, wtot, wci = w_rq, 256, ci - 24
            sl, tw = load_w(w_ap, wtot, wci)
            pn = nps()
            for r in range(8):
                mm(ps[pn][:, 0:xw],
                   wbuf[sl][:, r * 128:(r + 1) * 128],
                   xTq[:, r * xw:(r + 1) * xw],
                   start=(r == 0), stop=(r == 7),
                   reads=(f"wbuf{sl}", "xTq"), writes=(f"ps{pn}",), deps=[tw, txq])
            if ci < 8:
                lo = max(x0, HALO)
                if lo < x0 + xw:
                    off, wd = lo - x0, x0 + xw - lo
                    vo(lambda e, pn=pn, ci=ci, lo=lo, off=off, wd=wd: e.tensor_scalar(
                        qT1[:, ci * TOK + lo - HALO: ci * TOK + lo - HALO + wd],
                        ps[pn][:, off:off + wd], sb["bcol_qkv"][:, ci:ci + 1],
                        0.125, ALU.add, ALU.mult),
                        reads=(f"ps{pn}", "bcol_qkv"), writes=(f"qT1_{ci}",))
            elif ci < 16:
                kc = ci - 8
                vo(lambda e, pn=pn, ci=ci, kc=kc, x0=x0, xw=xw: e.tensor_single_scalar(
                    kT[:, kc * XW + x0: kc * XW + x0 + xw],
                    ps[pn][:, 0:xw], sb["bcol_qkv"][:, ci:ci + 1], ALU.add),
                    reads=(f"ps{pn}", "bcol_qkv"), writes=(f"kT_{kc}",))
            elif ci < 24:
                vc = ci - 16
                tv = vo(lambda e, pn=pn, ci=ci, xw=xw: e.tensor_single_scalar(
                    ptile[1][:, 0:xw], ps[pn][:, 0:xw],
                    sb["bcol_qkv"][:, ci:ci + 1], ALU.add),
                    reads=(f"ps{pn}", "bcol_qkv"), writes=("ptile1",))
                nt_ = xw // 128
                for u in range(nt_):
                    tpose(psT[:, u * 128:(u + 1) * 128],
                          ptile[1][:, u * 128:(u + 1) * 128], ident_bf[:, :],
                          reads=("ptile1", "ident_bf"), writes=("psT",), deps=[tv])
                tt0 = x0 // 128
                for u in range(nt_):
                    vo(lambda e, u=u, tt0=tt0, vc=vc: e.tensor_copy(
                        vtok[:, (tt0 + u) * D + vc * 128:(tt0 + u) * D + (vc + 1) * 128],
                        psT[:, u * 128:(u + 1) * 128]),
                        reads=("psT",), writes=(f"vtok_{tt0 + u}",))
            else:
                rc = ci - 24
                lo = max(x0, HALO)
                if lo < x0 + xw:
                    off, wd = lo - x0, x0 + xw - lo
                    vo(lambda e, pn=pn, rc=rc, lo=lo, off=off, wd=wd: e.tensor_scalar(
                        rqT[:, rc * TOK + lo - HALO: rc * TOK + lo - HALO + wd],
                        ps[pn][:, off:off + wd], sb["bcol_rq"][:, rc:rc + 1],
                        0.125, ALU.add, ALU.mult),
                        reads=(f"ps{pn}", "bcol_rq"), writes=(f"rqT_{rc}",))

    if _TRUNC == "P1a":
        return trunc_finish()
    tpool = vo(lambda e: e.tensor_scalar_mul(pooledT[:, 0:128], pool_f[:, :],
                                             1.0 / 128),
               reads=tuple(f"pool_{r}_{li}" for r in range(8) for li in range(16)),
               writes=("pool_own",))
    tc1 = sdma(mkap(cc1_in, 0, [[16, 128], [2048, 8], [1, 16]]),
               pooledT[:, 0:128].rearrange("p (r l) -> p r l", r=8),
               reads=("pool_own",), writes=("cc1_in",))
    tcc1 = P.op("gpsimd", lambda e: e.collective_compute(
        "AllGather", ALU.bypass, replica_groups=[[0, 1, 2, 3], [4, 5, 6, 7]],
        ins=[cc1_in.opt()], outs=[cc1_out.opt()]),
        reads=("cc1_in",), writes=("cc1_out",), sem="cc1", inc=1, deps=[tc1])
    for rr in range(4):
        tpoolg = sdma(
            pooledT[:, :].rearrange("p (d l) -> p d l", d=8)[:, :, rr * 16:(rr + 1) * 16],
            mkap(cc1_out, rr * 16384, [[16, 128], [2048, 8], [1, 16]]),
            reads=("cc1_out",), writes=("pooledT",), deps=[tcc1])

    towt = gdma(woutT[:, :].rearrange("p (r c) -> p r c", r=8),
                mkap(w_out, 0, [[D, 128], [128 * D, 8], [1, D]]),
                writes=("woutT",), dsem="cst",
                deps=[P.wtok["xTq"]] + list(P.rtok.get("xTq", [])))

    if _TRUNC == "P1b0":
        return trunc_finish()
    for ci in range(4):
        sl, tw = load_w(w_rkv, 512, ci)
        pn = nps()
        for r in range(8):
            mm(ps[pn][:, 0:64],
               wbuf[sl][:, r * 128:(r + 1) * 128],
               pooledT[:, r * 64:(r + 1) * 64],
               start=(r == 0), stop=(r == 7),
               reads=(f"wbuf{sl}", "pooledT"), writes=(f"ps{pn}",),
               deps=[tw, tpoolg])
        dst, dnm = (rkT, "rkT") if ci < 2 else (rvT, "rvT")
        cc = ci % 2
        vo(lambda e, pn=pn, ci=ci, dst=dst, cc=cc: e.tensor_single_scalar(
            dst[:, cc * 64:(cc + 1) * 64], ps[pn][:, 0:64],
            sb["bcol_rkv"][:, ci:ci + 1], ALU.add),
            reads=(f"ps{pn}", "bcol_rkv"), writes=(f"{dnm}_{cc}",))
    for h in range(LH):
        go(lambda e, h=h: e.tensor_copy(
            rkhat[0:64, h * 64:(h + 1) * 64],
            rkT[(h % 2) * 64:(h % 2) * 64 + 64, (h // 2) * 64:(h // 2) * 64 + 64]),
            reads=(f"rkT_{h // 2}",), writes=(f"rkhat_{h}",))
    for cchunk in range(2):
        tpose(psT[0:64, 0:128], rvT[:, cchunk * 64:(cchunk + 1) * 64], ident_bf[:, :],
              reads=(f"rvT_{cchunk}", "ident_bf"), writes=("psT",))
        vo(lambda e, cchunk=cchunk: e.tensor_copy(
            rv_tok[0:64, cchunk * 128:(cchunk + 1) * 128], psT[0:64, 0:128]),
            reads=("psT",), writes=(f"rv_tok_{cchunk}",))
    for h in range(LH):
        go(lambda e, h=h: e.tensor_copy(
            rvhat[0:64, h * 65:h * 65 + 64], rv_tok[0:64, h * 64:(h + 1) * 64]),
            reads=(f"rv_tok_{h // 2}",), writes=(f"rvhat_{h}",))

    def shift_rows(n_heads, kbuf, kname_fn, kw, qbuf, qname_fn, dst, dname_fn, mko):
        for h in range(n_heads):
            hp, hc = (h % 2) * 64, h // 2
            first = True
            for g0 in range(0, kw, 512):
                gw = min(512, kw - g0)
                t1 = so(lambda e, hp=hp, hc=hc, g0=g0, gw=gw: e.activation(
                    sq64[0:64, 0:gw],
                    kbuf[hp:hp + 64, hc * kw + g0: hc * kw + g0 + gw], AF.Square),
                    reads=(kname_fn(hc),), writes=("sq64",))
                t2 = mm(ps[6][0:1, 0:gw], ones_b[0:64, 0:1], sq64[0:64, 0:gw],
                        start=True, stop=True, reads=("sq64", "ones_b"),
                        writes=("ps6",), deps=[t1])
                t3 = vo(lambda e, gw=gw: e.reduce_max(cols[0:1, 7:8],
                                                      ps[6][0:1, 0:gw], axis=AX.X),
                        reads=("ps6",), writes=("c7",))
                if first:
                    vo(lambda e, h=h, mko=mko: e.tensor_copy(
                        mk1[:, mko + h:mko + h + 1], cols[0:1, 7:8]),
                        reads=("c7",), writes=(f"mk_{mko + h}",))
                    first = False
                else:
                    vo(lambda e, h=h, mko=mko: e.tensor_max(
                        mk1[:, mko + h:mko + h + 1], mk1[:, mko + h:mko + h + 1],
                        cols[0:1, 7:8]),
                        reads=("c7", f"mk_{mko + h}"), writes=(f"mk_{mko + h}",))
            for g0 in range(0, TOK, 512):
                t4 = so(lambda e, hp=hp, hc=hc, g0=g0: e.activation(
                    sq64[0:64, 0:512],
                    qbuf[hp:hp + 64, hc * TOK + g0: hc * TOK + g0 + 512], AF.Square),
                    reads=(qname_fn(hc),), writes=("sq64",))
                t5 = mm(ps[6][0:1, 0:512], ones_b[0:64, 0:1], sq64[0:64, 0:512],
                        start=True, stop=True, reads=("sq64", "ones_b"),
                        writes=("ps6",), deps=[t4])
                vo(lambda e: e.tensor_copy(n2row[:, 0:512], ps[6][0:1, 0:512]),
                   reads=("ps6",), writes=("n2row",))
                vo(lambda e, h=h, mko=mko: e.tensor_single_scalar(
                    n2row[:, 0:512], n2row[:, 0:512], mk1[:, mko + h:mko + h + 1],
                    ALU.mult),
                    reads=("n2row", f"mk_{mko + h}"), writes=("n2row",))
                so(lambda e: e.activation(n2row[:, 0:512], n2row[:, 0:512], AF.Ln,
                                          bias=np.float32(1e-30)),
                   reads=("n2row",), writes=("n2row",))
                so(lambda e: e.activation(n2row[:, 0:512], n2row[:, 0:512], AF.Exp,
                                          scale=np.float32(-0.5) if False else np.float32(0.5)),
                   reads=("n2row",), writes=("n2row",))
                vo(lambda e: e.tensor_scalar_mul(ptile[1][0:1, 0:512],
                                                 n2row[:, 0:512], -1.0),
                   reads=("n2row",), writes=("ptile1",))
                sdma(dst[h:h + 1, g0:g0 + 512], ptile[1][0:1, 0:512],
                     reads=("ptile1",), writes=(dname_fn(h),))

    shift_rows(LH, rkT, lambda hc: f"rkT_{hc}", 64,
               rqT, lambda hc: f"rqT_{hc}", cneg_lat, lambda h: f"cneg_lat{h}", 0)
    shift_rows(NH, kT, lambda hc: f"kT_{hc}", XW,
               qT1, lambda hc: f"qT1_{hc}", cneg_loc, lambda h: f"cneg_loc{h}", 0)

    trout = gdma(rout_sb[:, :].rearrange("p (r c) -> p r c", r=2),
                 mkap(w_rout, 0, [[D, 128], [128 * D, 2], [1, D]]),
                 writes=("rout_sb",), dsem="cst", deps=[towt])
    P.wtok["woutT"] = trout

    if _TRUNC == "P1b":
        return trunc_finish()
    for t in range(NT):
        for h in range(LH if _SUB >= 1 else 0):
            par = (t * LH + h) % 2
            lq = latqb[par]
            lqn = f"latqb{par}"
            hp, hc = (h % 2) * 64, h // 2
            ta = go(lambda e, hp=hp, hc=hc, t=t, lq=lq: e.tensor_copy(
                lq[0:64, :], rqT[hp:hp + 64, hc * TOK + t * 128: hc * TOK + (t + 1) * 128]),
                reads=(f"rqT_{hc}",), writes=(lqn,))
            tb = sdma(lq[64:65, :], cneg_lat[h:h + 1, t * 128:(t + 1) * 128],
                      reads=(f"cneg_lat{h}",), writes=(lqn,))
            pA = par
            mm(ps[pA][0:64, 0:128], rkhat[:, h * 64:(h + 1) * 64], lq[:, :],
               start=True, stop=True,
               reads=(lqn, f"rkhat_{h}", "rkhat_ones"), writes=(f"ps{pA}",),
               deps=[ta, tb])
            so(lambda e, t=t, pA=pA, par=par: e.activation(
                ptile[par][0:64, 0:128], ps[pA][0:64, 0:128], AF.Exp,
                bias=mlat[:, t:t + 1]),
                reads=(f"ps{pA}", "mlat"), writes=(f"ptile{par}",))
            pE = 2 + par
            mm(ps[pE][0:65, 0:128], rvhat[0:64, h * 65:(h + 1) * 65],
               ptile[par][0:64, 0:128], start=True, stop=True,
               reads=(f"ptile{par}", f"rvhat_{h}", f"rvhat_o{h}"),
               writes=(f"ps{pE}",))
            vo(lambda e, pE=pE, par=par: e.tensor_scalar_max(
                lr[par][:, 0:128], ps[pE][64:65, 0:128], 1e-30),
                reads=(f"ps{pE}",), writes=(f"lr{par}",))
            vo(lambda e, par=par: e.reciprocal(lr[par][:, 0:128], lr[par][:, 0:128]),
               reads=(f"lr{par}",), writes=(f"lr{par}",))
            pG = 4 + par
            mm(ps[pG][:, 0:128], ones_f[0:1, 0:128], lr[par][:, 0:128],
               start=True, stop=True, reads=(f"lr{par}", "ones_f"),
               writes=(f"ps{pG}",))
            vo(lambda e, pG=pG, par=par: e.tensor_copy(
                bcsb[par][:, 0:128], ps[pG][:, 0:128]),
                reads=(f"ps{pG}",), writes=(f"bcs{par}",))
            vo(lambda e, hp=hp, hc=hc, pE=pE, par=par: e.tensor_tensor(
                latnT[hp:hp + 64, hc * 128: hc * 128 + 128],
                ps[pE][0:64, 0:128], bcsb[par][0:64, 0:128], ALU.mult),
                reads=(f"ps{pE}", f"bcs{par}"), writes=(f"latnT_{h}",))
        for ci in range(8 if _SUB >= 2 else 0):
            pn = nps()
            for r in range(2):
                mm(ps[pn][:, 0:128],
                   rout_sb[:, r * D + ci * 128: r * D + (ci + 1) * 128],
                   latnT[:, r * 128:(r + 1) * 128],
                   start=(r == 0), stop=(r == 1),
                   reads=("rout_sb",) + tuple(f"latnT_{hh}" for hh in range(LH)),
                   writes=(f"ps{pn}",), deps=[trout])
            vo(lambda e, pn=pn, ci=ci: e.tensor_single_scalar(
                remT[:, ci * 128:(ci + 1) * 128], ps[pn][:, 0:128],
                sb["bcol_rout"][:, ci:ci + 1], ALU.add),
                reads=(f"ps{pn}", "bcol_rout"), writes=(f"remT_{ci}",))
        for h in range(NH if _SUB >= 3 else 0):
            par = (t * NH + h) % 2
            kb, vb, qb = khatb[par], vhatb[par], qhatb[par]
            kbn, vbn, qbn = f"khatb{par}", f"vhatb{par}", f"qhatb{par}"
            hp, hc = (h % 2) * 64, h // 2
            tk = go(lambda e, hp=hp, hc=hc, t=t, kb=kb: e.tensor_copy(
                kb[0:64, :], kT[hp:hp + 64, hc * XW + t * 128: hc * XW + t * 128 + 256]),
                reads=(f"kT_{hc}",), writes=(kbn,))
            tv1 = go(lambda e, h=h, t=t, vb=vb: e.tensor_copy(
                vb[:, 0:64], vtok[:, t * D + h * 64: t * D + h * 64 + 64]),
                reads=(f"vtok_{t}",), writes=(vbn,))
            tv2 = go(lambda e, h=h, t=t, vb=vb: e.tensor_copy(
                vb[:, 65:129], vtok[:, (t + 1) * D + h * 64:(t + 1) * D + h * 64 + 64]),
                reads=(f"vtok_{t + 1}",), writes=(vbn,))
            trh = go(lambda e, hp=hp, hc=hc, par=par: e.tensor_copy(
                remh[par][0:64, :], remT[hp:hp + 64, hc * 128: hc * 128 + 128]),
                reads=(f"remT_{hc}",), writes=(f"remh{par}",))
            tq1 = go(lambda e, hp=hp, hc=hc, t=t, qb=qb: e.tensor_copy(
                qb[0:64, :], qT1[hp:hp + 64, hc * TOK + t * 128: hc * TOK + (t + 1) * 128]),
                reads=(f"qT1_{hc}",), writes=(qbn,))
            tq2 = sdma(qb[64:65, :], cneg_loc[h:h + 1, t * 128:(t + 1) * 128],
                       reads=(f"cneg_loc{h}",), writes=(qbn,))
            pA = par
            mm(ps[pA][:, 0:128], kb[:, 0:128], qb[:, :], start=True, stop=True,
               reads=(kbn, qbn, f"khatb{par}_ones"), writes=(f"ps{pA}",),
               deps=[tk, tq1, tq2])
            mm(ps[pA][:, 128:256], kb[:, 128:256], qb[:, :], start=True, stop=True,
               reads=(kbn, qbn), writes=(f"ps{pA}",))
            mlon = "mlo0" if t == 0 else "mlo"
            vo(lambda e, pA=pA, mlon=mlon: e.tensor_tensor(
                ps[pA][:, 0:128], ps[pA][:, 0:128], sb[mlon][:, :], ALU.add),
                reads=(f"ps{pA}", mlon), writes=(f"ps{pA}",))
            vo(lambda e, pA=pA: e.tensor_tensor(
                ps[pA][:, 128:256], ps[pA][:, 128:256], sb["mhi"][:, :], ALU.add),
                reads=(f"ps{pA}", "mhi"), writes=(f"ps{pA}",))
            so(lambda e, pA=pA, par=par: e.activation(
                ptile[par][:, 0:256], ps[pA][:, 0:256], AF.Exp),
                reads=(f"ps{pA}",), writes=(f"ptile{par}",))
            if _SUB < 4:
                continue
            pE = 2 + par
            mm(ps[pE][0:65, 0:128], vb[:, 0:65], ptile[par][:, 0:128],
               start=True, stop=False,
               reads=(vbn, f"ptile{par}", f"vhatb{par}_o1"), writes=(f"ps{pE}",),
               deps=[tv1])
            mm(ps[pE][0:65, 0:128], vb[:, 65:130], ptile[par][:, 128:256],
               start=False, stop=True,
               reads=(vbn, f"ptile{par}", f"vhatb{par}_o2"), writes=(f"ps{pE}",),
               deps=[tv2])
            tl1 = vo(lambda e, pE=pE, par=par: e.tensor_copy(
                tloc[par][0:64, :], ps[pE][0:64, 0:128]),
                reads=(f"ps{pE}",), writes=(f"tloc{par}",))
            vo(lambda e, pE=pE, par=par: e.tensor_scalar_max(
                lr[par][:, 0:128], ps[pE][64:65, 0:128], 1e-30),
                reads=(f"ps{pE}",), writes=(f"lr{par}",))
            if _SUB < 5:
                continue
            pG = 4 + par
            mm(ps[pG][0:1, 0:128], gl_sb[0:64, :], tloc[par][0:64, :],
               start=True, stop=True, reads=(f"tloc{par}", "gl"),
               writes=(f"ps{pG}",), deps=[tl1])
            mm(ps[pG][0:1, 128:256], gr_sb[0:64, :], remh[par][0:64, :],
               start=True, stop=True,
               reads=(f"remh{par}", "gr"), writes=(f"ps{pG}",))
            vo(lambda e, par=par: e.reciprocal(lr[par][:, 0:128], lr[par][:, 0:128]),
               reads=(f"lr{par}",), writes=(f"lr{par}",))
            vo(lambda e, pG=pG, par=par: e.tensor_tensor(
                grow[par][:, 0:128], ps[pG][0:1, 0:128], lr[par][:, 0:128], ALU.mult),
                reads=(f"ps{pG}", f"lr{par}"), writes=(f"grow{par}",))
            vo(lambda e, pG=pG, par=par: e.tensor_tensor(
                grow[par][:, 0:128], grow[par][:, 0:128], ps[pG][0:1, 128:256],
                ALU.add),
                reads=(f"grow{par}", f"ps{pG}"), writes=(f"grow{par}",))
            so(lambda e, par=par: e.activation(
                grow[par][:, 128:256], grow[par][:, 0:128], AF.Exp,
                scale=np.float32(-1.0)),
                reads=(f"grow{par}",), writes=(f"grow{par}",))
            vo(lambda e, par=par: e.tensor_scalar_add(
                grow[par][:, 128:256], grow[par][:, 128:256], 1.0),
                reads=(f"grow{par}",), writes=(f"grow{par}",))
            vo(lambda e, par=par: e.reciprocal(
                grow[par][:, 128:256], grow[par][:, 128:256]),
                reads=(f"grow{par}",), writes=(f"grow{par}",))
            vo(lambda e, par=par: e.tensor_tensor(
                grow[par][:, 0:128], grow[par][:, 128:256], lr[par][:, 0:128],
                ALU.mult),
                reads=(f"grow{par}", f"lr{par}"), writes=(f"grow{par}",))
            vo(lambda e, par=par: e.tensor_scalar(
                grow[par][:, 256:384], grow[par][:, 128:256], -1.0, 1.0,
                ALU.mult, ALU.add),
                reads=(f"grow{par}",), writes=(f"grow{par}",))
            if _SUB < 6:
                continue
            mm(ps[pG][:, 256:384], ones_f[0:1, 0:128], grow[par][:, 0:128],
               start=True, stop=True, reads=(f"grow{par}", "ones_f"),
               writes=(f"ps{pG}",))
            mm(ps[pG][:, 384:512], ones_f[0:1, 0:128], grow[par][:, 256:384],
               start=True, stop=True, reads=(f"grow{par}", "ones_f"),
               writes=(f"ps{pG}",))
            vo(lambda e, pG=pG, par=par: e.tensor_copy(
                bcsb[par][:, 0:256], ps[pG][:, 256:512]),
                reads=(f"ps{pG}",), writes=(f"bcs{par}",))
            vo(lambda e, par=par: e.tensor_tensor(
                ptile[par][0:64, 0:128], tloc[par][0:64, :], bcsb[par][0:64, 0:128],
                ALU.mult),
                reads=(f"tloc{par}", f"bcs{par}"), writes=(f"ptile{par}",))
            vo(lambda e, par=par: e.tensor_tensor(
                ptile[par][0:64, 128:256],
                remh[par][0:64, :],
                bcsb[par][0:64, 128:256], ALU.mult),
                reads=(f"remh{par}", f"bcs{par}"), writes=(f"ptile{par}",))
            vo(lambda e, hp=hp, hc=hc, par=par: e.tensor_tensor(
                mixT[hp:hp + 64, hc * 128: hc * 128 + 128],
                ptile[par][0:64, 0:128], ptile[par][0:64, 128:256], ALU.add),
                reads=(f"ptile{par}",), writes=(f"mixT_{hc}",))
        if _SUB < 7:
            continue
        tx = sdma(f32a[:, :], xtok_in[t * 128:(t + 1) * 128, :], writes=("f32a",))
        for half in range(2):
            pn = half
            for r in range(8):
                mm(ps[pn][:, 0:512],
                   mixT[:, r * 128:(r + 1) * 128],
                   woutT[:, r * D + half * 512: r * D + half * 512 + 512],
                   start=(r == 0), stop=(r == 7),
                   reads=("woutT",) + tuple(f"mixT_{rr}" for rr in range(8)),
                   writes=(f"ps{pn}",), deps=[towt])
            vo(lambda e, pn=pn, half=half: e.tensor_tensor(
                f32a[:, half * 512:(half + 1) * 512], ps[pn][:, 0:512],
                f32a[:, half * 512:(half + 1) * 512], ALU.add),
                reads=(f"ps{pn}", "f32a"), writes=("f32a",), deps=[tx])
            vo(lambda e, half=half: e.tensor_tensor(
                f32a[:, half * 512:(half + 1) * 512],
                f32a[:, half * 512:(half + 1) * 512],
                bcA[:, half * 512:(half + 1) * 512], ALU.add),
                reads=("f32a", "bcA"), writes=("f32a",))

        def fin_xmid(mean_ap, rstd_ap, t=t):
            vo(lambda e: e.tensor_scalar(
                xmid[:, t * D:(t + 1) * D], f32a[:, :], mean_ap, rstd_ap,
                ALU.subtract, ALU.mult),
                reads=("f32a", "c0", "c4"), writes=(f"xmid_{t}",))
            if _DEBUG:
                gdma(dbg["xmid"][t * 128:(t + 1) * 128, :],
                     xmid[:, t * D:(t + 1) * D], reads=(f"xmid_{t}",))
        plain_ln(f32a[:, :], "f32a", fin_xmid, f32b, "f32b")

    if _TRUNC == "P1e":
        return trunc_finish()
    P.barrier()
    for t in range(NT):
        def fin_nx1(mean_ap, rstd_ap, t=t):
            vo(lambda e: e.tensor_scalar(
                f32a[:, :], xmid[:, t * D:(t + 1) * D], mean_ap, rstd_ap,
                ALU.subtract, ALU.mult),
                reads=(f"xmid_{t}", "c0", "c4"), writes=("f32a",))
        plain_ln(xmid[:, t * D:(t + 1) * D], f"xmid_{t}", fin_nx1, f32b, "f32b")
        vo(lambda e: e.tensor_tensor(f32a[:, :], f32a[:, :], bcB[:, :], ALU.mult),
           reads=("f32a", "bcB"), writes=("f32a",))
        vo(lambda e: e.tensor_tensor(f32a[:, :], f32a[:, :], bcC[:, :], ALU.add),
           reads=("f32a", "bcC"), writes=("f32a",))
        if _DEBUG:
            sdma(dbg["nx1"][t * 128:(t + 1) * 128, :], f32a[:, :], reads=("f32a",))
        for half in range(2):
            tch = vo(lambda e, half=half: e.tensor_copy(
                ptile[0][:, 0:512], f32a[:, half * 512:(half + 1) * 512]),
                reads=("f32a",), writes=("ptile0",))
            for u in range(4):
                tpose(psT[:, (half * 4 + u) * 128:(half * 4 + u + 1) * 128],
                      ptile[0][:, u * 128:(u + 1) * 128], ident_bf[:, :],
                      reads=("ptile0", "ident_bf"), writes=("psT",), deps=[tch])
        for r in range(8):
            vo(lambda e, t=t, r=r: e.tensor_copy(
                nx1T[:, r * TOK + t * 128: r * TOK + (t + 1) * 128],
                psT[:, r * 128:(r + 1) * 128]),
                reads=("psT",), writes=(f"nx1T_{r}",))

    cc2_toks = []
    for ci in range(24):
        sl, tw = load_w(w_Wqkv, 3 * D, ci)
        for gg in range(4):
            g0 = gg * 512
            pn = nps()
            for r in range(8):
                mm(ps[pn][:, 0:512],
                   wbuf[sl][:, r * 128:(r + 1) * 128],
                   nx1T[:, r * TOK + g0: r * TOK + g0 + 512],
                   start=(r == 0), stop=(r == 7),
                   reads=(f"wbuf{sl}", f"nx1T_{r}"), writes=(f"ps{pn}",), deps=[tw])
            qsl = (ci * 4 + gg) % 2
            if ci < 8:
                vo(lambda e, pn=pn, ci=ci, qsl=qsl: e.tensor_scalar(
                    ptile[qsl][:, 0:512], ps[pn][:, 0:512],
                    sb["bcol_Wqkv"][:, ci:ci + 1],
                    0.125, ALU.add, ALU.mult),
                    reads=(f"ps{pn}", "bcol_Wqkv"), writes=(f"ptile{qsl}",))
            else:
                vo(lambda e, pn=pn, ci=ci, qsl=qsl: e.tensor_single_scalar(
                    ptile[qsl][:, 0:512], ps[pn][:, 0:512],
                    sb["bcol_Wqkv"][:, ci:ci + 1], ALU.add),
                    reads=(f"ps{pn}", "bcol_Wqkv"), writes=(f"ptile{qsl}",))
            cc2_toks.append(sdma(
                mkap(cc2_in, ci * 128 * TOK + g0, [[TOK, 128], [1, 512]]),
                ptile[qsl][:, 0:512], reads=(f"ptile{qsl}",), writes=("cc2_in",)))

    tcc2 = P.op("gpsimd", lambda e: e.collective_compute(
        "AllGather", ALU.bypass, replica_groups=[[0, 1, 2, 3], [4, 5, 6, 7]],
        ins=[cc2_in.opt()], outs=[cc2_out.opt()]),
        reads=("cc2_in",), writes=("cc2_out",), sem="cc2", inc=1, deps=cc2_toks)

    def my_rows(dst, dname, sec):
        def f(e):
            pid4 = nc.partition_id() % 4
            src = mkap(cc2_out, sec * D * TOK + pid4 * (4 * HD * TOK),
                       [[3 * D * TOK, 4], [TOK, 4 * HD], [1, TOK]])
            dd = mkap(dst, 0, [[TOK, 4], [S, 4 * HD], [1, TOK]])
            return e.dma_start(out=dd, in_=src)
        return P.op("gpsimd", f, reads=("cc2_out",), writes=(dname,),
                    sem="gmy", inc=16, deps=[tcc2])

    tqm = my_rows(qm_d, "qm_d", 0)
    tkm = my_rows(km_d, "km_d", 1)
    tvm = my_rows(vm_d, "vm_d", 2)
    for nm in ("qm_d", "km_d", "vm_d"):
        P.wtok[nm] = tvm
    tqm = tkm = tvm

    for ei in range(NE):
        gdma(mkap(w1bf, ei * D * FFN, [[FFN, D], [1, FFN]]),
             mkap(w1_in, ei * D * FFN, [[FFN, D], [1, FFN]]),
             writes=(f"w1bf_{ei}",), deps=[tvm], dsem="wc")
        wc_last = gdma(mkap(w2bf, ei * FFN * D, [[D, FFN], [1, D]]),
                       mkap(w2_in, ei * FFN * D, [[D, FFN], [1, D]]),
                       writes=(f"w2bf_{ei}",), deps=[tvm], dsem="wc")
    for ei in range(NE):
        P.wtok[f"w1bf_{ei}"] = wc_last
        P.wtok[f"w2bf_{ei}"] = wc_last

    if _TRUNC == "P1f":
        return trunc_finish()
    P.barrier()
    ptile4 = {0: [ptile[0], pt_extra[0], pt_extra[1], pt_extra[2]],
              1: [ptile[1], pt_extra[3], pt_extra[4], pt_extra[5]]}
    tmcau = gdma(mcau[:, :].rearrange("p (m w) -> p m w", m=4),
                 mkap(mcau_in, 0, [[512, 128], [128 * 512, 4], [1, 512]]),
                 writes=("mcau",), dsem="c3")
    for hh in range(4):
        par = hh % 2
        kb = khat2[par]
        kbn = f"khat2_{par}"
        tkl = sdma(kb[0:64, :], km_d[hh * 64:(hh + 1) * 64, :],
                   reads=("km_d",), writes=(kbn,), deps=[tkm])
        if hh < 2:
            vo(lambda e, kb=kb: e.memset(kb[64:65, :], 1.0), writes=(f"{kbn}_ones",))
        vb = vhat2[par]
        vbn = f"vhat2_{par}"
        if hh < 2:
            vo(lambda e, vb=vb: e.memset(
                vb[:, :].rearrange("p (t j) -> p t j", j=65)[:, :, 64:65], 1.0),
               writes=(f"{vbn}_ones",))
        for s4 in range(4):
            vs = vth[s4 % 2]
            vsn = f"vth{s4 % 2}"
            tvl = sdma(vs[0:64, :], vm_d[hh * 64:(hh + 1) * 64,
                                         s4 * TOK:(s4 + 1) * TOK],
                       reads=("vm_d",), writes=(vsn,), deps=[tvm])
            for u in range(16):
                tpose(psT[:, u * 64:(u + 1) * 64],
                      vs[0:64, u * 128:(u + 1) * 128], ident_bf[0:64, 0:64],
                      reads=(vsn, "ident_bf"), writes=("psT",), deps=[tvl])
            vo(lambda e, vb=vb, s4=s4: e.tensor_copy(
                vb[:, :].rearrange("p (t j) -> p t j", j=65)
                [:, s4 * 16:(s4 + 1) * 16, 0:64],
                psT[:, 0:1024].rearrange("p (t j) -> p t j", j=64)),
                reads=("psT",), writes=(vbn,))
        for g0 in range(0, S, 512):
            t1 = so(lambda e, kb=kb, g0=g0: e.activation(
                ksq2[0:64, 0:512], kb[0:64, g0:g0 + 512], AF.Square),
                reads=(kbn,), writes=("ksq2",), deps=[tkl])
            t2 = mm(ps[6][0:1, 0:512], ones_b[0:64, 0:1], ksq2[0:64, 0:512],
                    start=True, stop=True, reads=("ksq2", "ones_b"),
                    writes=("ps6",), deps=[t1])
            vo(lambda e: e.reduce_max(cols[0:1, 7:8], ps[6][0:1, 0:512], axis=AX.X),
               reads=("ps6",), writes=("c7",))
            if g0 == 0:
                vo(lambda e, hh=hh: e.tensor_copy(mk2[:, hh:hh + 1], cols[0:1, 7:8]),
                   reads=("c7",), writes=(f"mk2_{hh}",))
            else:
                vo(lambda e, hh=hh: e.tensor_max(mk2[:, hh:hh + 1], mk2[:, hh:hh + 1],
                                                 cols[0:1, 7:8]),
                   reads=("c7", f"mk2_{hh}"), writes=(f"mk2_{hh}",))
        for tau in range(16):
            qpar = tau % 2
            qb = qhat2[qpar]
            qbn = f"qhat2_{qpar}"
            tql = sdma(qb[0:64, :], qm_d[hh * 64:(hh + 1) * 64,
                                         tau * 512:(tau + 1) * 512],
                       reads=("qm_d",), writes=(qbn,), deps=[tqm])
            t1 = so(lambda e, qb=qb: e.activation(ksq2[0:64, 0:512], qb[0:64, :],
                                                  AF.Square),
                    reads=(qbn,), writes=("ksq2",), deps=[tql])
            t2 = mm(ps[6][0:1, 0:512], ones_b[0:64, 0:1], ksq2[0:64, 0:512],
                    start=True, stop=True, reads=("ksq2", "ones_b"),
                    writes=("ps6",), deps=[t1])
            vo(lambda e, qpar=qpar: e.tensor_copy(l2[qpar][:, 0:512],
                                                  ps[6][0:1, 0:512]),
               reads=("ps6",), writes=(f"l2_{qpar}",))
            vo(lambda e, hh=hh, qpar=qpar: e.tensor_single_scalar(
                l2[qpar][:, 0:512], l2[qpar][:, 0:512], mk2[:, hh:hh + 1], ALU.mult),
                reads=(f"l2_{qpar}", f"mk2_{hh}"), writes=(f"l2_{qpar}",))
            so(lambda e, qpar=qpar: e.activation(
                l2[qpar][:, 0:512], l2[qpar][:, 0:512], AF.Ln,
                bias=np.float32(1e-30)),
                reads=(f"l2_{qpar}",), writes=(f"l2_{qpar}",))
            so(lambda e, qpar=qpar: e.activation(
                l2[qpar][:, 0:512], l2[qpar][:, 0:512], AF.Exp,
                scale=np.float32(0.5)),
                reads=(f"l2_{qpar}",), writes=(f"l2_{qpar}",))
            t7 = vo(lambda e, qb=qb, qpar=qpar: e.tensor_scalar_mul(
                qb[64:65, :], l2[qpar][:, 0:512], -1.0),
                reads=(f"l2_{qpar}",), writes=(qbn,))
            nch = 4 * tau + 4
            pE = 2 + qpar
            QKB = (0, 1, 6)
            exp_done = {}

            def emit_qk(kc):
                pA = QKB[kc % 3]
                mm(ps[pA][:, 0:512], kb[:, kc * 128:(kc + 1) * 128], qb[:, :],
                   start=True, stop=True,
                   reads=(kbn, qbn, f"khat2_{par}_ones"), writes=(f"ps{pA}",),
                   deps=[t7])
                dk = kc - 4 * tau
                if dk >= 0:
                    vo(lambda e, pA=pA, dk=dk: e.tensor_tensor(
                        ps[pA][:, 0:512], ps[pA][:, 0:512],
                        mcau[:, dk * 512:(dk + 1) * 512], ALU.add),
                        reads=(f"ps{pA}", "mcau"), writes=(f"ps{pA}",),
                        deps=[tmcau])
                pp = kc % 4
                so(lambda e, pA=pA, pp=pp, qpar=qpar: e.activation(
                    ptile4[qpar][pp][:, 0:512], ps[pA][:, 0:512], AF.Exp),
                    reads=(f"ps{pA}",), writes=(f"pt{qpar}_{pp}",))

            def emit_av(kc):
                pp = kc % 4
                mm(ps[pE][0:65, 0:512],
                   vb[:, kc * 65:(kc + 1) * 65], ptile4[qpar][pp][:, 0:512],
                   start=(kc == 0), stop=(kc == nch - 1),
                   reads=(f"pt{qpar}_{pp}", vbn, f"vhat2_{par}_ones"),
                   writes=(f"ps{pE}",))

            emit_qk(0)
            for kc in range(1, nch):
                emit_qk(kc)
                emit_av(kc - 1)
            emit_av(nch - 1)
            vo(lambda e, pE=pE, qpar=qpar: e.tensor_scalar_max(
                l2[qpar][:, 0:512], ps[pE][64:65, 0:512], 1e-30),
                reads=(f"ps{pE}",), writes=(f"l2_{qpar}",))
            vo(lambda e, qpar=qpar: e.reciprocal(l2[qpar][:, 0:512],
                                                 l2[qpar][:, 0:512]),
               reads=(f"l2_{qpar}",), writes=(f"l2_{qpar}",))
            mm(ps[6][0:64, 0:512], ones_f[0:1, 0:64], l2[qpar][:, 0:512],
               start=True, stop=True, reads=(f"l2_{qpar}", "ones_f"),
               writes=("ps6",))
            vo(lambda e, qpar=qpar: e.tensor_copy(bcs2[qpar][:, 0:512],
                                                  ps[6][0:64, 0:512]),
               reads=("ps6",), writes=(f"bcs2_{qpar}",))
            vo(lambda e, hh=hh, tau=tau, pE=pE, qpar=qpar: e.tensor_tensor(
                attnT2[(hh % 2) * 64:(hh % 2) * 64 + 64,
                       (hh // 2) * S + tau * 512:(hh // 2) * S + (tau + 1) * 512],
                ps[pE][0:64, 0:512], bcs2[qpar][:, 0:512], ALU.mult),
                reads=(f"ps{pE}", f"bcs2_{qpar}"), writes=(f"attnT2_{hh}",))

    tct1 = sdma(cc3_in[0:128, :], attnT2[:, 0:S],
                reads=("attnT2_0", "attnT2_1"), writes=("cc3_in",))
    tct2 = sdma(cc3_in[128:256, :], attnT2[:, S:2 * S],
                reads=("attnT2_2", "attnT2_3"), writes=("cc3_in",))
    tcc3 = P.op("gpsimd", lambda e: e.collective_compute(
        "AllGather", ALU.bypass, replica_groups=[[0, 1, 2, 3], [4, 5, 6, 7]],
        ins=[cc3_in.opt()], outs=[cc3_out.opt()]),
        reads=("cc3_in",), writes=("cc3_out",), sem="cc3", inc=1,
        deps=[tct1, tct2])

    def my_cols(e):
        pid4 = nc.partition_id() % 4
        src = mkap(cc3_out, pid4 * TOK, [[S, D], [1, TOK]])
        dd = mkap(am_d, 0, [[TOK, D], [1, TOK]])
        return e.dma_start(out=dd, in_=src)
    tam = P.op("gpsimd", my_cols, reads=("cc3_out",), writes=("am_d",),
               sem="gam", inc=16, deps=[tcc3])

    if _TRUNC == "P2":
        return trunc_finish()
    P.barrier()
    sdma(fc2b[:, :], fc2b_in[:, :], writes=("fc2b",))
    sdma(ident_f[:, :], ident_in[:, :], writes=("ident_f",))
    sdma(fc1b[:, :].rearrange("p (ee c) -> p ee c", ee=8),
         mkap(bcol_fc1_in, 0, [[32, 128], [128 * 32, 8], [1, 32]]),
         writes=("fc1b",))
    gdma(bcA[:, :], bo_in[:, :], writes=("bcA",), dsem="c3")
    gdma(bcB[:, :], ln2w_in[:, :], writes=("bcB",), dsem="c3")
    c3_last = gdma(bcC[:, :], ln2b_in[:, :], writes=("bcC",), dsem="c3")
    for nm in ("bcA", "bcB", "bcC"):
        P.wtok[nm] = c3_last
    twL = gdma(woutT[:, :].rearrange("p (r c) -> p r c", r=8),
               mkap(w_Wo, 0, [[D, 128], [128 * D, 8], [1, D]]),
               writes=("woutT",))

    for t in range(NT):
        p3 = t % 2
        tat = sdma(attnTo_t[p3][:, :].rearrange("p (r c) -> p r c", r=8),
                   mkap(am_d, t * 128, [[TOK, 128], [128 * TOK, 8], [1, 128]]),
                   reads=("am_d",), writes=(f"attnTo{p3}",), deps=[tam])
        for half in range(2):
            pn = half
            for r in range(8):
                mm(ps[pn][:, 0:512],
                   attnTo_t[p3][:, r * 128:(r + 1) * 128],
                   woutT[:, r * D + half * 512: r * D + half * 512 + 512],
                   start=(r == 0), stop=(r == 7),
                   reads=("woutT", f"attnTo{p3}"), writes=(f"ps{pn}",),
                   deps=[twL, tat])
            vo(lambda e, pn=pn, half=half: e.tensor_tensor(
                f3a[:, half * 512:(half + 1) * 512], ps[pn][:, 0:512],
                bcA[:, half * 512:(half + 1) * 512], ALU.add),
                reads=(f"ps{pn}", "bcA"), writes=("f3a",))
        if _DEBUG:
            sdma(dbg["attn"][t * 128:(t + 1) * 128, :], f3a[:, :], reads=("f3a",))
        vo(lambda e, t=t: e.tensor_tensor(xres_t[:, :], f3a[:, :],
                                          xmid[:, t * D:(t + 1) * D], ALU.add),
           reads=("f3a", f"xmid_{t}"), writes=("xres_t",))
        vo(lambda e, t=t: e.tensor_tensor(moe_acc[:, t * D:(t + 1) * D],
                                          xres_t[:, :],
                                          xmid[:, t * D:(t + 1) * D], ALU.add),
           reads=("xres_t", f"xmid_{t}"), writes=(f"moe_{t}",))

        def fin_nx2(mean_ap, rstd_ap):
            vo(lambda e: e.tensor_scalar(f3a[:, :], xres_t[:, :], mean_ap, rstd_ap,
                                         ALU.subtract, ALU.mult),
               reads=("xres_t", "c0", "c4"), writes=("f3a",))
        plain_ln(xres_t[:, :], "xres_t", fin_nx2, f3b, "f3b")
        vo(lambda e: e.tensor_tensor(f3a[:, :], f3a[:, :], bcB[:, :], ALU.mult),
           reads=("f3a", "bcB"), writes=("f3a",))
        vo(lambda e: e.tensor_tensor(nx2f[:, :], f3a[:, :], bcC[:, :], ALU.add),
           reads=("f3a", "bcC"), writes=("nx2f",))
        for half in range(2):
            tch = vo(lambda e, half=half: e.tensor_copy(
                rT3[:, half * 512:(half + 1) * 512],
                nx2f[:, half * 512:(half + 1) * 512]),
                reads=("nx2f",), writes=("rT3",))
            for u in range(4):
                tpose(psT[:, (half * 4 + u) * 128:(half * 4 + u + 1) * 128],
                      rT3[:, half * 512 + u * 128: half * 512 + (u + 1) * 128],
                      ident_bf[:, :],
                      reads=("rT3", "ident_bf"), writes=("psT",), deps=[tch])
        for r in range(8):
            vo(lambda e, t=t, r=r: e.tensor_copy(
                nx2T[:, r * TOK + t * 128: r * TOK + (t + 1) * 128],
                psT[:, r * 128:(r + 1) * 128]),
                reads=("psT",), writes=(f"nx2T_{r}",))
        for half in range(2):
            tsub = vo(lambda e, half=half: e.tensor_tensor(
                f3b[:, 0:512], nx2f[:, half * 512:(half + 1) * 512],
                rT3[:, half * 512:(half + 1) * 512], ALU.subtract),
                reads=("nx2f", "rT3"), writes=("f3b",))
            tch = vo(lambda e, half=half: e.tensor_copy(
                rT3[:, half * 512:(half + 1) * 512], f3b[:, 0:512]),
                reads=("f3b",), writes=("rT3x",))
            for u in range(4):
                tpose(psT[:, (half * 4 + u) * 128:(half * 4 + u + 1) * 128],
                      rT3[:, half * 512 + u * 128: half * 512 + (u + 1) * 128],
                      ident_bf[:, :],
                      reads=("rT3x", "ident_bf"), writes=("psT",), deps=[tch])
        vo(lambda e: e.tensor_copy(rT3[:, :], psT[:, 0:1024]),
           reads=("psT", "rT3x"), writes=("rT3",))
        for r in range(8):
            mm(ps[2][:, 0:8],
               nx2T[:, r * TOK + t * 128: r * TOK + (t + 1) * 128],
               gateW[:, r * 8:(r + 1) * 8],
               start=(r == 0), stop=False,
               reads=(f"nx2T_{r}", "gateW"), writes=("ps2",))
        for r in range(8):
            mm(ps[2][:, 0:8],
               rT3[:, r * 128:(r + 1) * 128],
               gateW[:, r * 8:(r + 1) * 8],
               start=False, stop=(r == 7),
               reads=("rT3", "gateW"), writes=("ps2",))
        vo(lambda e: e.tensor_copy(lg8[:, :], ps[2][:, 0:8]),
           reads=("ps2",), writes=("lg8",))
        vo(lambda e: e.reduce_max(w12[:, 0:1], lg8[:, :], axis=AX.X),
           reads=("lg8",), writes=("w0",))
        vo(lambda e: e.tensor_single_scalar(is1[:, :], lg8[:, :], w12[:, 0:1],
                                            ALU.is_ge),
           reads=("lg8", "w0"), writes=("is1",))
        vo(lambda e: e.scalar_tensor_tensor(msk8[:, :], is1[:, :], NEG, lg8[:, :],
                                            ALU.mult, ALU.add),
           reads=("is1", "lg8"), writes=("msk8",))
        vo(lambda e: e.reduce_max(w12[:, 1:2], msk8[:, :], axis=AX.X),
           reads=("msk8",), writes=("w1c",))
        vo(lambda e: e.tensor_single_scalar(is2[:, :], msk8[:, :], w12[:, 1:2],
                                            ALU.is_ge),
           reads=("msk8", "w1c"), writes=("is2",))
        vo(lambda e: e.tensor_sub(w12[:, 2:3], w12[:, 1:2], w12[:, 0:1]),
           reads=("w0", "w1c"), writes=("w2c",))
        so(lambda e: e.activation(w12[:, 2:3], w12[:, 2:3], AF.Exp),
           reads=("w2c",), writes=("w2c",))
        vo(lambda e: e.tensor_scalar_add(w12[:, 3:4], w12[:, 2:3], 1.0),
           reads=("w2c",), writes=("w3c",))
        vo(lambda e: e.reciprocal(w12[:, 3:4], w12[:, 3:4]),
           reads=("w3c",), writes=("w3c",))
        vo(lambda e: e.tensor_mul(w12[:, 2:3], w12[:, 2:3], w12[:, 3:4]),
           reads=("w2c", "w3c"), writes=("w2c",))
        vo(lambda e: e.tensor_single_scalar(is2[:, :], is2[:, :], w12[:, 2:3],
                                            ALU.mult),
           reads=("is2", "w2c"), writes=("is2",))
        vo(lambda e, t=t: e.scalar_tensor_tensor(
            gates[:, t * 8:(t + 1) * 8], is1[:, :], w12[:, 3:4], is2[:, :],
            ALU.mult, ALU.add),
            reads=("is1", "is2", "w3c"), writes=(f"gates_{t}",))
        tgt = tpose(ps[3][0:8, 0:128], gates[:, t * 8:(t + 1) * 8], ident_f[:, :],
                    reads=(f"gates_{t}", "ident_f"), writes=("ps3",))
        tge = vo(lambda e: e.tensor_copy(g8T[:, :], ps[3][0:8, 0:128]),
                 reads=("ps3",), writes=("g8T",))
        for half in range(2):
            pn = half
            mm(ps[pn][:, 0:512], g8T[:, :], fc2b[:, half * 512:(half + 1) * 512],
               start=True, stop=True, reads=("g8T", "fc2b"), writes=(f"ps{pn}",),
               deps=[tge])
            vo(lambda e, pn=pn, t=t, half=half: e.tensor_tensor(
                moe_acc[:, t * D + half * 512: t * D + (half + 1) * 512],
                moe_acc[:, t * D + half * 512: t * D + (half + 1) * 512],
                ps[pn][:, 0:512], ALU.add),
                reads=(f"ps{pn}", f"moe_{t}"), writes=(f"moe_{t}",))

    if _TRUNC == "P3wo":
        return trunc_finish()
    P.barrier()
    w1rot = [0]
    w2rot = [0]
    for ei in range(NE):
        for qg in range(4):
            for fh in range(8):
                sl = w1rot[0] % 2
                w1rot[0] += 1
                t_w1 = gdma(w1buf[sl][:, :].rearrange("p (r c) -> p r c", r=8),
                            mkap(w1bf, ei * D * FFN + fh * 512,
                                 [[FFN, 128], [128 * FFN, 8], [1, 512]]),
                            writes=(f"w1buf{sl}",), deps=[P.wtok[f"w1bf_{ei}"]],
                            dsem=f"w1{sl}")
                for fs in range(4):
                    pn = nps()
                    for r in range(8):
                        mm(ps[pn][:, 0:512],
                           w1buf[sl][:, r * 512 + fs * 128: r * 512 + (fs + 1) * 128],
                           nx2T[:, r * TOK + qg * 512: r * TOK + (qg + 1) * 512],
                           start=(r == 0), stop=(r == 7),
                           reads=(f"w1buf{sl}", f"nx2T_{r}"), writes=(f"ps{pn}",),
                           deps=[t_w1])
                    fc = fh * 4 + fs
                    so(lambda e, pn=pn, ei=ei, fc=fc: e.activation(
                        hT[:, fc * 512:(fc + 1) * 512], ps[pn][:, 0:512], AF.Gelu,
                        bias=fc1b[:, ei * 32 + fc: ei * 32 + fc + 1]),
                        reads=(f"ps{pn}", "fc1b"), writes=(f"hT_{fc}",))
            for f2h in range(16):
                sl = w2rot[0] % 2
                w2rot[0] += 1
                t_w2 = gdma(w2buf[sl][:, :].rearrange("p (r c) -> p r c", r=2),
                            mkap(w2bf, ei * FFN * D + f2h * 256 * D,
                                 [[D, 128], [128 * D, 2], [1, D]]),
                            writes=(f"w2buf{sl}",), deps=[P.wtok[f"w2bf_{ei}"]],
                            dsem=f"w2{sl}")
                for qt in range(4):
                    for nh in range(2):
                        pn = qt * 2 + nh
                        pname = f"ps{pn}" if pn < 7 else "psT"
                        out_ap = (ps[pn][:, 0:512] if pn < 7
                                  else psT[:, 0:1024].bitcast(F32))
                        for fs in range(2):
                            fc = f2h * 2 + fs
                            mm(out_ap,
                               hT[:, fc * 512 + qt * 128: fc * 512 + (qt + 1) * 128],
                               w2buf[sl][:, fs * D + nh * 512: fs * D + nh * 512 + 512],
                               start=(f2h == 0 and fs == 0),
                               stop=(f2h == 15 and fs == 1),
                               reads=(f"w2buf{sl}", f"hT_{fc}"),
                               writes=(pname,), deps=[t_w2])
            for qt in range(4):
                for nh in range(2):
                    pn = qt * 2 + nh
                    pname = f"ps{pn}" if pn < 7 else "psT"
                    src = (ps[pn][:, 0:512] if pn < 7
                           else psT[:, 0:1024].bitcast(F32))
                    tt = qg * 4 + qt
                    vo(lambda e, src=src, tt=tt, nh=nh, ei=ei: e.scalar_tensor_tensor(
                        moe_acc[:, tt * D + nh * 512: tt * D + (nh + 1) * 512],
                        src, gates[:, tt * 8 + ei: tt * 8 + ei + 1],
                        moe_acc[:, tt * D + nh * 512: tt * D + (nh + 1) * 512],
                        ALU.mult, ALU.add),
                        reads=(pname, f"gates_{tt}", f"moe_{tt}"),
                        writes=(f"moe_{tt}",))

    P.barrier()
    out_toks = []
    for t in range(NT):
        def fin_out(mean_ap, rstd_ap, t=t):
            vo(lambda e: e.tensor_scalar(
                f3a[:, :], moe_acc[:, t * D:(t + 1) * D], mean_ap, rstd_ap,
                ALU.subtract, ALU.mult),
                reads=(f"moe_{t}", "c0", "c4"), writes=("f3a",))
        plain_ln(moe_acc[:, t * D:(t + 1) * D], f"moe_{t}", fin_out, f3b, "f3b")
        out_toks.append(sdma(out_dram[t * 128:(t + 1) * 128, :], f3a[:, :],
                             reads=("f3a",)))

    for tok in out_toks:
        P._wait("sync", tok[0], tok[1])

    P.finalize()
    import contextlib
    with contextlib.ExitStack() as st:
        for s in P.sem_names:
            P.sems[s] = st.enter_context(nc.semaphore(s))
        block = st.enter_context(nc.Block())
        P.replay(block)
    return nc


_NC_CACHE = {}


def kernel(**inputs):
    f32 = np.float32
    g = {k: np.asarray(v, dtype=f32) for k, v in inputs.items()}
    x = g["L0_attn_x"]

    def bc(v):
        return np.ascontiguousarray(np.broadcast_to(v.reshape(1, D), (128, D)))

    shared = {
        "w_qkv": g["L0_attn_qkv_w"],
        "bcol_qkv": np.ascontiguousarray(g["L0_attn_qkv_b"].reshape(24, 128).T),
        "w_rq": g["L0_attn_rq_w"],
        "bcol_rq": np.ascontiguousarray(g["L0_attn_rq_b"].reshape(2, 128).T),
        "w_rkv": g["L0_attn_rkv_w"],
        "bcol_rkv": np.ascontiguousarray(g["L0_attn_rkv_b"].reshape(4, 128).T),
        "w_rout": g["L0_attn_rout_w"],
        "bcol_rout": np.ascontiguousarray(g["L0_attn_rout_b"].reshape(8, 128).T),
        "w_out": g["L0_attn_out_w"],
        "outb_bc": bc(g["L0_attn_out_b"]),
        "gl": np.ascontiguousarray(np.concatenate([g["L0_attn_gl_w"]] * 2, 0)),
        "gr": np.ascontiguousarray(np.concatenate([g["L0_attn_gr_w"]] * 2, 0)),
        "w_Wqkv": g["L0_ffn_Wqkv"],
        "bcol_Wqkv": np.ascontiguousarray(g["L0_ffn_bqkv"].reshape(24, 128).T),
        "w_Wo": g["L0_ffn_Wo"],
        "bo_bc": bc(g["L0_ffn_bo"]),
        "ln1w_bc": bc(g["L0_ffn_ln1_w"]),
        "ln1b_bc": bc(g["L0_ffn_ln1_b"]),
        "ln2w_bc": bc(g["L0_ffn_ln2_w"]),
        "ln2b_bc": bc(g["L0_ffn_ln2_b"]),
        "w_gate": g["L0_ffn_gate_W"],
        "w1": np.ascontiguousarray(g["L0_ffn_fc1_w"].transpose(0, 2, 1)),
        "bcol_fc1": np.ascontiguousarray(
            g["L0_ffn_fc1_b"].reshape(NE, 32, 128).transpose(0, 2, 1)),
        "w2": np.ascontiguousarray(g["L0_ffn_fc2_w"].transpose(0, 2, 1)),
        "fc2b": g["L0_ffn_fc2_b"],
        "ident": np.eye(128, dtype=f32),
    }
    jj, ii = np.meshgrid(np.arange(128), np.arange(128), indexing="ij")
    shared["mlo"] = np.where(ii <= jj, 0.0, NEG).astype(f32)
    shared["mhi"] = np.where(jj < ii, 0.0, NEG).astype(f32)
    mcau = np.empty((4, 128, 512), f32)
    uu = np.arange(512)
    for m in range(4):
        mcau[m] = np.where(uu[None, :] >= (128 * m + np.arange(128))[:, None],
                           0.0, NEG)
    shared["mcau"] = mcau

    in_maps = []
    for c in range(8):
        b, qd = c // 4, c % 4
        t0 = qd * TOK
        xT = np.zeros((D, XW), f32)
        xb = np.ascontiguousarray(x[b].T)
        xT[:, HALO:] = xb[:, t0:t0 + TOK]
        if qd > 0:
            xT[:, :HALO] = xb[:, t0 - HALO:t0]
        lat = np.arange(64)[:, None]
        tt = np.arange(NT)[None, :]
        m = dict(shared)
        m["xT"] = xT
        m["xtok"] = np.ascontiguousarray(x[b, t0:t0 + TOK])
        m["mlo0"] = (np.full((128, 128), NEG, f32) if qd == 0 else shared["mlo"])
        m["mlat"] = np.where(lat < (qd * 16 + tt), 0.0, NEG).astype(f32)
        in_maps.append(m)

    if "nc" not in _NC_CACHE:
        _NC_CACHE["nc"] = build_nc()
    nc = _NC_CACHE["nc"]
    res = run_bass_kernel_spmd(nc, in_maps, core_ids=list(range(8)))
    out = np.empty((B, S, D), f32)
    for c in range(8):
        b, qd = c // 4, c % 4
        out[b, qd * TOK:(qd + 1) * TOK] = np.asarray(res.results[c]["out"])
    return out


# version 09
# speedup vs baseline: 1.2773x; 1.2426x over previous
import dataclasses
import numpy as np
import ml_dtypes
import concourse.bass as bass
import concourse.mybir as mybir
from concourse.bass_utils import run_bass_kernel_spmd

F32 = mybir.dt.float32
BF16 = mybir.dt.bfloat16
AF = mybir.ActivationFunctionType
ALU = mybir.AluOpType
AX = mybir.AxisListType

B, S, D = 2, 8192, 1024
NH, HD = 16, 64
LH = 4
NE, FFN = 8, 4096
NEG = -1.0e9
TOK = 2048
NT = 16
HALO = 128
XW = HALO + TOK
NTX = XW // 128

_DEBUG = False
_TRUNC = None
_SUB = 9


class Prog:
    ENGINES = ("sync", "gpsimd", "vector", "scalar", "tensor")

    def __init__(self, nc):
        self.nc = nc
        self.q = {e: [] for e in self.ENGINES}
        self.sem_names = (list(self.ENGINES) + ["sdma", "gdma", "cc1", "cc2", "cc3", "cst", "xt", "wb0", "wb1", "wc", "gmy", "gam", "c3", "w10", "w11", "w20", "w21", "kv0", "kv1", "qq0", "qq1"])
        self.cnt = {s: 0 for s in self.sem_names}
        self.hwm = {}
        self.wtok = {}
        self.rtok = {}
        self.sems = {}
        self.used = {s: set() for s in self.sem_names}
        self.incs = {s: {} for s in self.sem_names}
        self.remap = {s: {} for s in self.sem_names}
        self.final_inc = None
        self.btoks = []
        self.dma_owner = {}
        for sname in self.sem_names:
            if sname in ("sdma",):
                self.dma_owner[sname] = "sync"
            elif sname not in self.ENGINES and not sname.startswith("cc"):
                self.dma_owner[sname] = "gpsimd"
        self.owned = {}
        for sname, eng in self.dma_owner.items():
            self.owned.setdefault(eng, []).append(sname)
        self.boundaries = {sname: [0] for sname in self.dma_owner}

    def barrier(self, exclude=()):
        self.btoks = [(s, self.cnt[s]) for s in self.sem_names
                      if self.cnt[s] > 0 and s not in exclude]

    def snap(self, sem, val):
        if sem not in self.dma_owner:
            return val
        for b in self.boundaries[sem]:
            if b >= val:
                return b
        owner = self.dma_owner[sem]
        total = self.cnt[sem]
        self.boundaries[sem].append(total)
        self._wait_raw(owner, sem, total)
        return total

    def _wait_raw(self, eng, sem, val):
        if val <= 0 or self.hwm.get((eng, sem), 0) >= val:
            return
        self.hwm[(eng, sem)] = val
        self.used[sem].add(val)
        P = self

        def w(e, s=sem, v=val):
            e.wait_ge(P.sems[s], P.remap[s][v])
        self.q[eng].append(w)

    def _wait(self, eng, sem, val):
        self._wait_raw(eng, sem, self.snap(sem, val))

    def op(self, eng, fn, reads=(), writes=(), sem=None, inc=1, deps=()):
        dep = list(deps) + list(self.btoks)
        for b in reads:
            t = self.wtok.get(b)
            if t:
                dep.append(t)
        for b in writes:
            t = self.wtok.get(b)
            if t:
                dep.append(t)
            dep.extend(self.rtok.get(b, ()))
        sem = sem or eng
        best = {}
        for (s, v) in dep:
            if eng == "tensor" and s == "tensor":
                continue
            if v > best.get(s, 0):
                best[s] = v
        for s, v in sorted(best.items()):
            assert v <= self.cnt[s], (eng, s, v, self.cnt[s])
            self._wait(eng, s, v)
        self.cnt[sem] += inc
        raw = self.cnt[sem]
        self.incs[sem][raw] = inc
        P = self

        def run(e, f=fn, s=sem, r=raw):
            ins = f(e)
            amt = P.final_inc[s].get(r)
            if amt:
                ins.then_inc(P.sems[s], amt)
        self.q[eng].append(run)
        tok = (sem, raw)
        for b in writes:
            self.wtok[b] = tok
            self.rtok[b] = []
        for b in reads:
            self.rtok.setdefault(b, []).append(tok)
        return tok

    def finalize(self):
        self.final_inc = {}
        for s in self.sem_names:
            keep_all = s in self.dma_owner or s.startswith("cc")
            mx = max(self.used[s]) if self.used[s] else 0
            fi = {}
            run = 0
            for raw in sorted(self.incs[s]):
                inc = self.incs[s][raw]
                if keep_all or raw <= mx:
                    fi[raw] = inc
                    run += inc
                self.remap[s][raw] = run
            self.final_inc[s] = fi

    def replay(self, block):
        for ename in self.ENGINES:
            if not self.q[ename]:
                continue

            def body(e, lst=tuple(self.q[ename])):
                for f in lst:
                    f(e)
            getattr(block, ename)(body)


def mkap(base_ap, offset, dims):
    return dataclasses.replace(base_ap, offset=offset, ap=dims)


def build_nc():
    nc = bass.Bass("TRN2", target_bir_lowering=False)
    P = Prog(nc)
    fin_state = {}
    _dyn = {}

    def dynbase(key, mult):
        if key not in _dyn:
            _dyn[key] = (nc.partition_id() % 4) * mult
        return _dyn[key]

    def trunc_finish():
        out_toks = []
        for t2 in range(NT):
            out_toks.append(fin_state["sdma"](
                fin_state["out"][t2 * 128:(t2 + 1) * 128, :],
                fin_state["f32a"][:, :]))
        for tok in out_toks:
            P._wait("sync", tok[0], tok[1])
        P.finalize()
        import contextlib
        with contextlib.ExitStack() as st:
            for s2 in P.sem_names:
                P.sems[s2] = st.enter_context(nc.semaphore(s2))
            block = st.enter_context(nc.Block())
            P.replay(block)
        return nc

    def din(name, shape):
        return nc.dram_tensor(name, shape, F32, kind="ExternalInput").ap()

    xT_in = din("xT", [D, XW])
    xtok_in = din("xtok", [TOK, D])
    w_qkv = din("w_qkv", [D, 3 * D])
    bcol_qkv_in = din("bcol_qkv", [128, 24])
    w_rq = din("w_rq", [D, 256])
    bcol_rq_in = din("bcol_rq", [128, 2])
    w_rkv = din("w_rkv", [D, 512])
    bcol_rkv_in = din("bcol_rkv", [128, 4])
    w_rout = din("w_rout", [256, D])
    bcol_rout_in = din("bcol_rout", [128, 8])
    w_out = din("w_out", [D, D])
    outb_in = din("outb_bc", [128, D])
    gl_in = din("gl", [128, 1])
    gr_in = din("gr", [128, 1])
    w_Wqkv = din("w_Wqkv", [D, 3 * D])
    bcol_Wqkv_in = din("bcol_Wqkv", [128, 24])
    w_Wo = din("w_Wo", [D, D])
    bo_in = din("bo_bc", [128, D])
    ln1w_in = din("ln1w_bc", [128, D])
    ln1b_in = din("ln1b_bc", [128, D])
    ln2w_in = din("ln2w_bc", [128, D])
    ln2b_in = din("ln2b_bc", [128, D])
    w_gate = din("w_gate", [D, NE])
    w1_in = din("w1", [NE, D, FFN])
    bcol_fc1_in = din("bcol_fc1", [NE, 128, 32])
    w2_in = din("w2", [NE, FFN, D])
    fc2b_in = din("fc2b", [NE, D])
    mlo0_in = din("mlo0", [128, 128])
    mlo_in = din("mlo", [128, 128])
    mhi_in = din("mhi", [128, 128])
    mcau_in = din("mcau", [4, 128, 512])
    mlat_in = din("mlat", [64, NT])
    ident_in = din("ident", [128, 128])

    out_dram = nc.dram_tensor("out", [TOK, D], F32, kind="ExternalOutput").ap()
    dbg = {}
    if _DEBUG:
        for nm in ("xmid", "attn", "nx1"):
            dbg[nm] = nc.dram_tensor("dbg_" + nm, [TOK, D], F32,
                                     kind="ExternalOutput").ap()

    cc1_in = nc.dram_tensor("cc1_in", [D, 16], BF16).ap()
    cc1_out = nc.dram_tensor("cc1_out", [4 * D, 16], BF16).ap()
    cc2kv_in = nc.dram_tensor("cc2kv_in", [2 * D, TOK], BF16).ap()
    cc2kv_out = nc.dram_tensor("cc2kv_out", [8 * D, TOK], BF16).ap()
    cc2q_in = nc.dram_tensor("cc2q_in", [D, TOK], BF16).ap()
    cc2q_out = nc.dram_tensor("cc2q_out", [4 * D, TOK], BF16).ap()
    cc3_in = nc.dram_tensor("cc3_in", [4 * HD, S], BF16).ap()
    cc3_out = nc.dram_tensor("cc3_out", [D, S], BF16).ap()
    qm_d = nc.dram_tensor("qm_d", [4 * HD, S], BF16).ap()
    km_d = nc.dram_tensor("km_d", [4 * HD, S], BF16).ap()
    vm_d = nc.dram_tensor("vm_d", [4 * HD, S], BF16).ap()
    am_d = nc.dram_tensor("am_d", [D, TOK], BF16).ap()
    w1bf = nc.dram_tensor("w1bf", [NE, D, FFN], BF16).ap()
    w2bf = nc.dram_tensor("w2bf", [NE, FFN, D], BF16).ap()

    SB_LIMIT = 212800
    SB_BASE = 16512
    sb = {}

    def sal(name, shape, dt, at):
        per = int(np.prod(shape[1:])) * mybir.dt.size(dt)
        assert at % 32 == 0 and at + per <= SB_LIMIT, (name, at, per, at + per)
        sb[name] = nc.alloc_sbuf_tensor_at(name, list(shape), dt,
                                           offset=at + SB_BASE)
        return sb[name]

    ident_bf = sal("ident_bf", [128, 128], BF16, 0)
    sal("mlo0", [128, 128], BF16, 256)
    sal("mlo", [128, 128], BF16, 512)
    sal("mhi", [128, 128], BF16, 768)
    mlat = sal("mlat", [64, NT], F32, 1024)
    sal("bcol_qkv", [128, 24], F32, 1088)
    sal("bcol_rq", [128, 2], F32, 1184)
    sal("bcol_rkv", [128, 4], F32, 1216)
    sal("bcol_rout", [128, 8], F32, 1248)
    sal("bcol_Wqkv", [128, 24], F32, 1280)
    gl_sb = sal("gl", [128, 1], BF16, 1376)
    gr_sb = sal("gr", [128, 1], BF16, 1408)
    ones_b = sal("ones_b", [128, 1], BF16, 1440)
    ones_f = sal("ones_f", [1, 128], F32, 1472)
    bcA = sal("bcA", [128, D], BF16, 1984)
    bcB = sal("bcB", [128, D], BF16, 4032)
    bcC = sal("bcC", [128, D], BF16, 6080)
    cols = sal("cols", [128, 8], F32, 8128)
    mk1 = sal("mk1", [1, 16], F32, 8160)
    gateW = sal("gateW_bf", [128, NE * 8], BF16, 8224)
    PB = 8352

    SLOT_A = PB
    SLOT_K = PB + 16384
    SLOT_Q = PB + 51200
    SLOT_V = PB + 83968
    SLOT_R = PB + 118784
    SLOT_M = PB + 126976
    SCR = PB + 159744
    assert SCR == 168096

    xTq = sal("xTq", [128, 8 * 512], BF16, SLOT_A)
    woutT = sal("woutT", [128, 8 * D], BF16, SLOT_A)
    kT = sal("kT", [128, 8 * XW], BF16, SLOT_K)
    qT1 = sal("qT1", [128, 8 * TOK], BF16, SLOT_Q)
    vtok = sal("vtok", [128, NTX * D], BF16, SLOT_V)
    rqT = sal("rqT", [128, 2 * TOK], BF16, SLOT_R)
    xmid = sal("xmid", [128, NT * D], BF16, SLOT_M)

    _o = [SCR]

    def bump(n):
        a = _o[0]
        _o[0] = (a + n + 31) // 32 * 32
        return a

    wbuf = [sal(f"wbuf{i}", [128, 8 * 128], BF16, bump(2048)) for i in range(2)]
    pool_f = sal("pool_f", [128, 8 * 16], F32, bump(512))
    pooledT = sal("pooledT", [128, 8 * 64], BF16, bump(1024))
    rkT = sal("rkT", [128, 2 * 64], BF16, bump(256))
    rvT = sal("rvT", [128, 2 * 64], BF16, bump(256))
    rkhat = sal("rkhat", [65, LH * 64], BF16, bump(512))
    rvhat = sal("rvhat", [64, LH * 65], BF16, bump(544))
    rv_tok = sal("rv_tok", [64, 256], BF16, bump(512))
    cneg_lat = sal("cneg_lat", [LH, TOK], BF16, bump(4096))
    cneg_loc = sal("cneg_loc", [NH, TOK], BF16, bump(4096))
    sq64 = sal("sq64", [64, 512], BF16, bump(1024))
    n2row = sal("n2row", [1, 512], F32, bump(2048))
    khatb = [sal(f"khatb{i}", [65, 256], BF16, bump(512)) for i in range(2)]
    vhatb = [sal(f"vhatb{i}", [128, 130], BF16, bump(288)) for i in range(2)]
    qhatb = [sal(f"qhatb{i}", [65, 128], BF16, bump(256)) for i in range(2)]
    latqb = [sal(f"latqb{i}", [65, 128], BF16, bump(256)) for i in range(2)]
    ptile = [sal(f"ptile{i}", [128, 512], BF16, bump(1024)) for i in range(2)]
    tloc = [sal(f"tloc{i}", [65, 128], BF16, bump(256)) for i in range(2)]
    lr = [sal(f"lr{i}", [1, 128], F32, bump(512)) for i in range(2)]
    grow = [sal(f"grow{i}", [1, 384], F32, bump(1536)) for i in range(2)]
    bcsb = [sal(f"bcs{i}", [128, 256], BF16, bump(512)) for i in range(2)]
    remh = [sal(f"remh{i}", [64, 128], BF16, bump(256)) for i in range(2)]
    latnT = sal("latnT", [128, 2 * 128], BF16, bump(512))
    remT = sal("remT", [128, 8 * 128], BF16, bump(2048))
    mixT = sal("mixT", [128, 8 * 128], BF16, bump(2048))
    f32a = sal("f32a", [128, D], F32, bump(4096))
    f32b = sal("f32b", [128, 512], F32, bump(2048))
    rout_sb = sal("rout_sb", [128, 2 * D], BF16, bump(4096))
    assert _o[0] <= SB_LIMIT, _o[0]

    khat2 = [sal(f"khat2_{i}", [65, S], BF16, SLOT_K + i * 16384) for i in range(2)]
    vth = [sal(f"vth{i}", [64, TOK], BF16, SLOT_Q + i * 4096) for i in range(2)]
    vhat2 = [sal(f"vhat2_{i}", [128, 64 * 65], BF16, SLOT_Q + 8192 + i * 8320)
             for i in range(2)]
    attnT2 = sal("attnT2", [128, 2 * S], BF16, SLOT_V)
    nx1T = sal("nx1T", [128, 8 * TOK], BF16, SLOT_V)
    qhat2 = [sal(f"qhat2_{i}", [65, 512], BF16, SCR + i * 1024) for i in range(2)]
    pt_extra = [sal(f"ptx{i}", [128, 512], BF16, SCR + 13344 + i * 1024)
                for i in range(6)]
    ksq2 = sal("ksq2", [64, 512], BF16, SCR + 2048)
    l2 = [sal(f"l2_{i}", [1, 512], F32, SCR + 3072 + i * 2048) for i in range(2)]
    bcs2 = [sal(f"bcs2_{i}", [64, 512], BF16, SCR + 7168 + i * 1024) for i in range(2)]
    mk2 = sal("mk2", [1, 8], F32, SCR + 9216)
    mcau = sal("mcau", [128, 4 * 512], BF16, SCR + 9248)

    nx2T = sal("nx2T", [128, 8 * TOK], BF16, SLOT_K)
    moe_acc = sal("moe_acc", [128, NT * D], F32, SLOT_Q)
    hT = sal("hT", [128, 32 * 512], BF16, SLOT_M)
    w1buf = [sal(f"w1buf{i}", [128, 8 * 512], BF16, SCR + i * 8192) for i in range(2)]
    w2buf = [sal(f"w2buf{i}", [128, 2 * D], BF16, SCR + 16384 + i * 4096)
             for i in range(2)]
    _o3 = [SCR + 24576]

    def bump3(n):
        a = _o3[0]
        _o3[0] = (a + n + 31) // 32 * 32
        return a

    attnTo_t = [sal(f"attnTo{i}", [128, 8 * 128], BF16, bump3(2048))
                for i in range(2)]
    f3a = sal("f3a", [128, D], F32, bump3(4096))
    f3b = sal("f3b", [128, 512], F32, bump3(2048))
    rT3 = sal("rT3", [128, 8 * 128], BF16, bump3(2048))
    gates = sal("gates", [128, NT * 8], F32, bump3(512))
    g8T = sal("g8T", [8, 128], F32, bump3(512))
    ident_f = sal("ident_f", [128, 128], F32, bump3(512))
    lg8 = sal("lg8", [128, 8], F32, bump3(32))
    is1 = sal("is1", [128, 8], F32, bump3(32))
    is2 = sal("is2", [128, 8], F32, bump3(32))
    msk8 = sal("msk8", [128, 8], F32, bump3(32))
    w12 = sal("w12", [128, 4], F32, bump3(32))
    fc2b = sal("fc2b", [NE, D], F32, bump3(4096))
    fc1b = sal("fc1b", [128, NE * 32], F32, bump3(1024))
    assert _o3[0] <= SB_LIMIT, _o3[0]
    xres_t = sal("xres_t", [128, D], F32, SLOT_R)
    nx2f = sal("nx2f", [128, D], F32, SLOT_R + 4096)

    ps = {i: nc.alloc_psum_tensor(f"ps{i}", [128, 512], F32) for i in range(7)}
    psT = nc.alloc_psum_tensor("psT", [128, 1024], BF16)

    def sdma(dst, src, reads=(), writes=(), deps=()):
        return P.op("sync", lambda e: e.dma_start(out=dst, in_=src),
                    reads=reads, writes=writes, sem="sdma", inc=16, deps=deps)

    def gdma(dst, src, reads=(), writes=(), deps=(), dsem="gdma"):
        return P.op("gpsimd", lambda e: e.dma_start(out=dst, in_=src),
                    reads=reads, writes=writes, sem=dsem, inc=16, deps=deps)

    def mm(out, lhsT, rhs, start, stop, reads=(), writes=(), deps=()):
        return P.op("tensor",
                    lambda e: e.matmul(out, lhsT, rhs, start=start, stop=stop),
                    reads=reads, writes=writes, deps=deps)

    def tpose(out, in_, ident, reads=(), writes=(), deps=()):
        return P.op("tensor", lambda e: e.transpose(out, in_, ident),
                    reads=reads, writes=writes, deps=deps)

    def vo(fn, reads=(), writes=(), deps=()):
        return P.op("vector", fn, reads=reads, writes=writes, deps=deps)

    def so(fn, reads=(), writes=(), deps=()):
        return P.op("scalar", fn, reads=reads, writes=writes, deps=deps)

    def go(fn, reads=(), writes=(), deps=()):
        return P.op("gpsimd", fn, reads=reads, writes=writes, deps=deps)

    def plain_ln(src_ap, src_name, fin, sq_scr, sq_name):
        vo(lambda e: e.reduce_sum(cols[:, 0:1], src_ap, axis=AX.X),
           reads=(src_name,), writes=("c0",))
        so(lambda e: e.activation(sq_scr[:, 0:512], src_ap[:, 0:512], AF.Square,
                                  accum_out=cols[:, 5:6]),
           reads=(src_name,), writes=(sq_name, "c5"))
        so(lambda e: e.activation(sq_scr[:, 0:512], src_ap[:, 512:1024], AF.Square,
                                  accum_out=cols[:, 6:7]),
           reads=(src_name,), writes=(sq_name, "c6"))
        vo(lambda e: e.tensor_add(cols[:, 5:6], cols[:, 5:6], cols[:, 6:7]),
           reads=("c5", "c6"), writes=("c5",))
        vo(lambda e: e.tensor_scalar_mul(cols[:, 0:1], cols[:, 0:1], 1.0 / D),
           reads=("c0",), writes=("c0",))
        vo(lambda e: e.tensor_mul(cols[:, 1:2], cols[:, 0:1], cols[:, 0:1]),
           reads=("c0",), writes=("c1",))
        vo(lambda e: e.tensor_scalar(cols[:, 2:3], cols[:, 5:6], 1.0 / D, None,
                                     ALU.mult),
           reads=("c5",), writes=("c2",))
        vo(lambda e: e.tensor_sub(cols[:, 2:3], cols[:, 2:3], cols[:, 1:2]),
           reads=("c2", "c1"), writes=("c2",))
        so(lambda e: e.activation(cols[:, 3:4], cols[:, 2:3], AF.Ln,
                                  bias=np.float32(1e-5)),
           reads=("c2",), writes=("c3",))
        so(lambda e: e.activation(cols[:, 4:5], cols[:, 3:4], AF.Exp,
                                  scale=np.float32(-0.5)),
           reads=("c3",), writes=("c4",))
        fin(cols[:, 0:1], cols[:, 4:5])

    fin_state["sdma"] = sdma
    fin_state["out"] = out_dram
    fin_state["f32a"] = f32a

    gdma(ident_bf[:, :], ident_in[:, :], writes=("ident_bf",), dsem="cst")
    gdma(sb["mlo0"][:, :], mlo0_in[:, :], writes=("mlo0",), dsem="cst")
    gdma(sb["mlo"][:, :], mlo_in[:, :], writes=("mlo",), dsem="cst")
    gdma(sb["mhi"][:, :], mhi_in[:, :], writes=("mhi",), dsem="cst")
    sdma(mlat[:, :], mlat_in[:, :], writes=("mlat",))
    for nm, src in [("bcol_qkv", bcol_qkv_in), ("bcol_rq", bcol_rq_in),
                    ("bcol_rkv", bcol_rkv_in), ("bcol_rout", bcol_rout_in),
                    ("bcol_Wqkv", bcol_Wqkv_in)]:
        sdma(sb[nm][:, :], src[:, :], writes=(nm,))
    gdma(bcA[:, :], outb_in[:, :], writes=("bcA",), dsem="cst")
    gdma(bcB[:, :], ln1w_in[:, :], writes=("bcB",), dsem="cst")
    gdma(bcC[:, :], ln1b_in[:, :], writes=("bcC",), dsem="cst")
    gdma(gl_sb[:, :], gl_in[:, :], writes=("gl",), dsem="cst")
    gdma(gr_sb[:, :], gr_in[:, :], writes=("gr",), dsem="cst")
    cst_last = gdma(gateW[:, :].rearrange("p (r e) -> p r e", r=8),
                    mkap(w_gate, 0, [[NE, 128], [NE * 128, 8], [1, NE]]),
                    writes=("gateW",), dsem="cst")
    for nm in ("ident_bf", "mlo0", "mlo", "mhi", "bcA", "bcB", "bcC",
               "gl", "gr", "gateW"):
        P.wtok[nm] = cst_last
    vo(lambda e: e.memset(ones_b[:, :], 1.0), writes=("ones_b",))
    vo(lambda e: e.memset(ones_f[:, :], 1.0), writes=("ones_f",))
    for i in range(2):
        vo(lambda e, i=i: e.memset(khatb[i][64:65, :], 1.0), writes=(f"khatb{i}_ones",))
        vo(lambda e, i=i: e.memset(vhatb[i][:, 64:65], 1.0), writes=(f"vhatb{i}_o1",))
        vo(lambda e, i=i: e.memset(vhatb[i][:, 129:130], 1.0), writes=(f"vhatb{i}_o2",))
    vo(lambda e: e.memset(rkhat[64:65, :], 1.0), writes=("rkhat_ones",))
    for h in range(LH):
        vo(lambda e, h=h: e.memset(rvhat[:, h * 65 + 64:h * 65 + 65], 1.0),
           writes=(f"rvhat_o{h}",))

    psrot = [0]

    def nps():
        r = psrot[0] % 4
        psrot[0] += 1
        return r

    wrot = [0]

    def load_w(w_ap, wtot, ci, ncols=128):
        sl = wrot[0] % 2
        wrot[0] += 1
        t = gdma(wbuf[sl][:, 0:8 * ncols].rearrange("p (r c) -> p r c", r=8),
                 mkap(w_ap, ci * ncols,
                      [[wtot, 128], [128 * wtot, 8], [1, ncols]]),
                 writes=(f"wbuf{sl}",), dsem=f"wb{sl}")
        return sl, t

    XCH = [(0, 512), (512, 512), (1024, 512), (1536, 512), (2048, 128)]
    for (x0, xw) in XCH:
        txq = gdma(xTq[:, 0:8 * xw].rearrange("p (r w) -> p r w", r=8),
                   mkap(xT_in, x0, [[XW, 128], [128 * XW, 8], [1, xw]]),
                   writes=("xTq",), dsem="xt")
        for li0 in range(xw // 128):
            gtok = x0 + li0 * 128
            if gtok < HALO:
                continue
            li = (gtok - HALO) // 128
            for r in range(8):
                vo(lambda e, r=r, li0=li0, li=li, xw=xw: e.reduce_sum(
                    pool_f[:, r * 16 + li:r * 16 + li + 1],
                    xTq[:, r * xw + li0 * 128: r * xw + (li0 + 1) * 128],
                    axis=AX.X),
                    reads=("xTq",), writes=(f"pool_{r}_{li}",), deps=[txq])
        for ci in range(26):
            if ci < 24:
                w_ap, wtot, wci = w_qkv, 3 * D, ci
            else:
                w_ap, wtot, wci = w_rq, 256, ci - 24
            sl, tw = load_w(w_ap, wtot, wci)
            pn = nps()
            for r in range(8):
                mm(ps[pn][:, 0:xw],
                   wbuf[sl][:, r * 128:(r + 1) * 128],
                   xTq[:, r * xw:(r + 1) * xw],
                   start=(r == 0), stop=(r == 7),
                   reads=(f"wbuf{sl}", "xTq"), writes=(f"ps{pn}",), deps=[tw, txq])
            if ci < 8:
                lo = max(x0, HALO)
                if lo < x0 + xw:
                    off, wd = lo - x0, x0 + xw - lo
                    vo(lambda e, pn=pn, ci=ci, lo=lo, off=off, wd=wd: e.tensor_scalar(
                        qT1[:, ci * TOK + lo - HALO: ci * TOK + lo - HALO + wd],
                        ps[pn][:, off:off + wd], sb["bcol_qkv"][:, ci:ci + 1],
                        0.125, ALU.add, ALU.mult),
                        reads=(f"ps{pn}", "bcol_qkv"), writes=(f"qT1_{ci}",))
            elif ci < 16:
                kc = ci - 8
                vo(lambda e, pn=pn, ci=ci, kc=kc, x0=x0, xw=xw: e.tensor_single_scalar(
                    kT[:, kc * XW + x0: kc * XW + x0 + xw],
                    ps[pn][:, 0:xw], sb["bcol_qkv"][:, ci:ci + 1], ALU.add),
                    reads=(f"ps{pn}", "bcol_qkv"), writes=(f"kT_{kc}",))
            elif ci < 24:
                vc = ci - 16
                tv = vo(lambda e, pn=pn, ci=ci, xw=xw: e.tensor_single_scalar(
                    ptile[1][:, 0:xw], ps[pn][:, 0:xw],
                    sb["bcol_qkv"][:, ci:ci + 1], ALU.add),
                    reads=(f"ps{pn}", "bcol_qkv"), writes=("ptile1",))
                nt_ = xw // 128
                for u in range(nt_):
                    tpose(psT[:, u * 128:(u + 1) * 128],
                          ptile[1][:, u * 128:(u + 1) * 128], ident_bf[:, :],
                          reads=("ptile1", "ident_bf"), writes=("psT",), deps=[tv])
                tt0 = x0 // 128
                for u in range(nt_):
                    vo(lambda e, u=u, tt0=tt0, vc=vc: e.tensor_copy(
                        vtok[:, (tt0 + u) * D + vc * 128:(tt0 + u) * D + (vc + 1) * 128],
                        psT[:, u * 128:(u + 1) * 128]),
                        reads=("psT",), writes=(f"vtok_{tt0 + u}",))
            else:
                rc = ci - 24
                lo = max(x0, HALO)
                if lo < x0 + xw:
                    off, wd = lo - x0, x0 + xw - lo
                    vo(lambda e, pn=pn, rc=rc, lo=lo, off=off, wd=wd: e.tensor_scalar(
                        rqT[:, rc * TOK + lo - HALO: rc * TOK + lo - HALO + wd],
                        ps[pn][:, off:off + wd], sb["bcol_rq"][:, rc:rc + 1],
                        0.125, ALU.add, ALU.mult),
                        reads=(f"ps{pn}", "bcol_rq"), writes=(f"rqT_{rc}",))

    if _TRUNC == "P1a":
        return trunc_finish()
    tpool = vo(lambda e: e.tensor_scalar_mul(pooledT[:, 0:128], pool_f[:, :],
                                             1.0 / 128),
               reads=tuple(f"pool_{r}_{li}" for r in range(8) for li in range(16)),
               writes=("pool_own",))
    tc1 = sdma(mkap(cc1_in, 0, [[16, 128], [2048, 8], [1, 16]]),
               pooledT[:, 0:128].rearrange("p (r l) -> p r l", r=8),
               reads=("pool_own",), writes=("cc1_in",))
    tcc1 = P.op("gpsimd", lambda e: e.collective_compute(
        "AllGather", ALU.bypass, replica_groups=[[0, 1, 2, 3], [4, 5, 6, 7]],
        ins=[cc1_in.opt()], outs=[cc1_out.opt()]),
        reads=("cc1_in",), writes=("cc1_out",), sem="cc1", inc=1, deps=[tc1])
    for rr in range(4):
        tpoolg = sdma(
            pooledT[:, :].rearrange("p (d l) -> p d l", d=8)[:, :, rr * 16:(rr + 1) * 16],
            mkap(cc1_out, rr * 16384, [[16, 128], [2048, 8], [1, 16]]),
            reads=("cc1_out",), writes=("pooledT",), deps=[tcc1])

    towt = gdma(woutT[:, :].rearrange("p (r c) -> p r c", r=8),
                mkap(w_out, 0, [[D, 128], [128 * D, 8], [1, D]]),
                writes=("woutT",), dsem="cst",
                deps=[P.wtok["xTq"]] + list(P.rtok.get("xTq", [])))

    if _TRUNC == "P1b0":
        return trunc_finish()
    for ci in range(4):
        sl, tw = load_w(w_rkv, 512, ci)
        pn = nps()
        for r in range(8):
            mm(ps[pn][:, 0:64],
               wbuf[sl][:, r * 128:(r + 1) * 128],
               pooledT[:, r * 64:(r + 1) * 64],
               start=(r == 0), stop=(r == 7),
               reads=(f"wbuf{sl}", "pooledT"), writes=(f"ps{pn}",),
               deps=[tw, tpoolg])
        dst, dnm = (rkT, "rkT") if ci < 2 else (rvT, "rvT")
        cc = ci % 2
        vo(lambda e, pn=pn, ci=ci, dst=dst, cc=cc: e.tensor_single_scalar(
            dst[:, cc * 64:(cc + 1) * 64], ps[pn][:, 0:64],
            sb["bcol_rkv"][:, ci:ci + 1], ALU.add),
            reads=(f"ps{pn}", "bcol_rkv"), writes=(f"{dnm}_{cc}",))
    for h in range(LH):
        go(lambda e, h=h: e.tensor_copy(
            rkhat[0:64, h * 64:(h + 1) * 64],
            rkT[(h % 2) * 64:(h % 2) * 64 + 64, (h // 2) * 64:(h // 2) * 64 + 64]),
            reads=(f"rkT_{h // 2}",), writes=(f"rkhat_{h}",))
    for cchunk in range(2):
        tpose(psT[0:64, 0:128], rvT[:, cchunk * 64:(cchunk + 1) * 64], ident_bf[:, :],
              reads=(f"rvT_{cchunk}", "ident_bf"), writes=("psT",))
        vo(lambda e, cchunk=cchunk: e.tensor_copy(
            rv_tok[0:64, cchunk * 128:(cchunk + 1) * 128], psT[0:64, 0:128]),
            reads=("psT",), writes=(f"rv_tok_{cchunk}",))
    for h in range(LH):
        go(lambda e, h=h: e.tensor_copy(
            rvhat[0:64, h * 65:h * 65 + 64], rv_tok[0:64, h * 64:(h + 1) * 64]),
            reads=(f"rv_tok_{h // 2}",), writes=(f"rvhat_{h}",))

    def shift_rows(n_heads, kbuf, kname_fn, kw, qbuf, qname_fn, dst, dname_fn, mko):
        for h in range(n_heads):
            hp, hc = (h % 2) * 64, h // 2
            first = True
            for g0 in range(0, kw, 512):
                gw = min(512, kw - g0)
                t1 = so(lambda e, hp=hp, hc=hc, g0=g0, gw=gw: e.activation(
                    sq64[0:64, 0:gw],
                    kbuf[hp:hp + 64, hc * kw + g0: hc * kw + g0 + gw], AF.Square),
                    reads=(kname_fn(hc),), writes=("sq64",))
                t2 = mm(ps[6][0:1, 0:gw], ones_b[0:64, 0:1], sq64[0:64, 0:gw],
                        start=True, stop=True, reads=("sq64", "ones_b"),
                        writes=("ps6",), deps=[t1])
                t3 = vo(lambda e, gw=gw: e.reduce_max(cols[0:1, 7:8],
                                                      ps[6][0:1, 0:gw], axis=AX.X),
                        reads=("ps6",), writes=("c7",))
                if first:
                    vo(lambda e, h=h, mko=mko: e.tensor_copy(
                        mk1[:, mko + h:mko + h + 1], cols[0:1, 7:8]),
                        reads=("c7",), writes=(f"mk_{mko + h}",))
                    first = False
                else:
                    vo(lambda e, h=h, mko=mko: e.tensor_max(
                        mk1[:, mko + h:mko + h + 1], mk1[:, mko + h:mko + h + 1],
                        cols[0:1, 7:8]),
                        reads=("c7", f"mk_{mko + h}"), writes=(f"mk_{mko + h}",))
            for g0 in range(0, TOK, 512):
                t4 = so(lambda e, hp=hp, hc=hc, g0=g0: e.activation(
                    sq64[0:64, 0:512],
                    qbuf[hp:hp + 64, hc * TOK + g0: hc * TOK + g0 + 512], AF.Square),
                    reads=(qname_fn(hc),), writes=("sq64",))
                t5 = mm(ps[6][0:1, 0:512], ones_b[0:64, 0:1], sq64[0:64, 0:512],
                        start=True, stop=True, reads=("sq64", "ones_b"),
                        writes=("ps6",), deps=[t4])
                vo(lambda e: e.tensor_copy(n2row[:, 0:512], ps[6][0:1, 0:512]),
                   reads=("ps6",), writes=("n2row",))
                vo(lambda e, h=h, mko=mko: e.tensor_single_scalar(
                    n2row[:, 0:512], n2row[:, 0:512], mk1[:, mko + h:mko + h + 1],
                    ALU.mult),
                    reads=("n2row", f"mk_{mko + h}"), writes=("n2row",))
                so(lambda e: e.activation(n2row[:, 0:512], n2row[:, 0:512], AF.Ln,
                                          bias=np.float32(1e-30)),
                   reads=("n2row",), writes=("n2row",))
                so(lambda e: e.activation(n2row[:, 0:512], n2row[:, 0:512], AF.Exp,
                                          scale=np.float32(-0.5) if False else np.float32(0.5)),
                   reads=("n2row",), writes=("n2row",))
                vo(lambda e: e.tensor_scalar_mul(ptile[1][0:1, 0:512],
                                                 n2row[:, 0:512], -1.0),
                   reads=("n2row",), writes=("ptile1",))
                sdma(dst[h:h + 1, g0:g0 + 512], ptile[1][0:1, 0:512],
                     reads=("ptile1",), writes=(dname_fn(h),))

    shift_rows(LH, rkT, lambda hc: f"rkT_{hc}", 64,
               rqT, lambda hc: f"rqT_{hc}", cneg_lat, lambda h: f"cneg_lat{h}", 0)
    shift_rows(NH, kT, lambda hc: f"kT_{hc}", XW,
               qT1, lambda hc: f"qT1_{hc}", cneg_loc, lambda h: f"cneg_loc{h}", 0)

    trout = gdma(rout_sb[:, :].rearrange("p (r c) -> p r c", r=2),
                 mkap(w_rout, 0, [[D, 128], [128 * D, 2], [1, D]]),
                 writes=("rout_sb",), dsem="cst", deps=[towt])
    P.wtok["woutT"] = trout

    if _TRUNC == "P1b":
        return trunc_finish()
    for t in range(NT):
        for h in range(LH if _SUB >= 1 else 0):
            par = (t * LH + h) % 2
            lq = latqb[par]
            lqn = f"latqb{par}"
            hp, hc = (h % 2) * 64, h // 2
            ta = go(lambda e, hp=hp, hc=hc, t=t, lq=lq: e.tensor_copy(
                lq[0:64, :], rqT[hp:hp + 64, hc * TOK + t * 128: hc * TOK + (t + 1) * 128]),
                reads=(f"rqT_{hc}",), writes=(lqn,))
            tb = sdma(lq[64:65, :], cneg_lat[h:h + 1, t * 128:(t + 1) * 128],
                      reads=(f"cneg_lat{h}",), writes=(lqn,))
            pA = par
            mm(ps[pA][0:64, 0:128], rkhat[:, h * 64:(h + 1) * 64], lq[:, :],
               start=True, stop=True,
               reads=(lqn, f"rkhat_{h}", "rkhat_ones"), writes=(f"ps{pA}",),
               deps=[ta, tb])
            so(lambda e, t=t, pA=pA, par=par: e.activation(
                ptile[par][0:64, 0:128], ps[pA][0:64, 0:128], AF.Exp,
                bias=mlat[:, t:t + 1]),
                reads=(f"ps{pA}", "mlat"), writes=(f"ptile{par}",))
            pE = 2 + par
            mm(ps[pE][0:65, 0:128], rvhat[0:64, h * 65:(h + 1) * 65],
               ptile[par][0:64, 0:128], start=True, stop=True,
               reads=(f"ptile{par}", f"rvhat_{h}", f"rvhat_o{h}"),
               writes=(f"ps{pE}",))
            vo(lambda e, pE=pE, par=par: e.tensor_scalar_max(
                lr[par][:, 0:128], ps[pE][64:65, 0:128], 1e-30),
                reads=(f"ps{pE}",), writes=(f"lr{par}",))
            vo(lambda e, par=par: e.reciprocal(lr[par][:, 0:128], lr[par][:, 0:128]),
               reads=(f"lr{par}",), writes=(f"lr{par}",))
            pG = 4 + par
            mm(ps[pG][:, 0:128], ones_f[0:1, 0:128], lr[par][:, 0:128],
               start=True, stop=True, reads=(f"lr{par}", "ones_f"),
               writes=(f"ps{pG}",))
            vo(lambda e, pG=pG, par=par: e.tensor_copy(
                bcsb[par][:, 0:128], ps[pG][:, 0:128]),
                reads=(f"ps{pG}",), writes=(f"bcs{par}",))
            vo(lambda e, hp=hp, hc=hc, pE=pE, par=par: e.tensor_tensor(
                latnT[hp:hp + 64, hc * 128: hc * 128 + 128],
                ps[pE][0:64, 0:128], bcsb[par][0:64, 0:128], ALU.mult),
                reads=(f"ps{pE}", f"bcs{par}"), writes=(f"latnT_{h}",))
        for ci in range(8 if _SUB >= 2 else 0):
            pn = nps()
            for r in range(2):
                mm(ps[pn][:, 0:128],
                   rout_sb[:, r * D + ci * 128: r * D + (ci + 1) * 128],
                   latnT[:, r * 128:(r + 1) * 128],
                   start=(r == 0), stop=(r == 1),
                   reads=("rout_sb",) + tuple(f"latnT_{hh}" for hh in range(LH)),
                   writes=(f"ps{pn}",), deps=[trout])
            vo(lambda e, pn=pn, ci=ci: e.tensor_single_scalar(
                remT[:, ci * 128:(ci + 1) * 128], ps[pn][:, 0:128],
                sb["bcol_rout"][:, ci:ci + 1], ALU.add),
                reads=(f"ps{pn}", "bcol_rout"), writes=(f"remT_{ci}",))
        for h in range(NH if _SUB >= 3 else 0):
            par = (t * NH + h) % 2
            kb, vb, qb = khatb[par], vhatb[par], qhatb[par]
            kbn, vbn, qbn = f"khatb{par}", f"vhatb{par}", f"qhatb{par}"
            hp, hc = (h % 2) * 64, h // 2
            tk = go(lambda e, hp=hp, hc=hc, t=t, kb=kb: e.tensor_copy(
                kb[0:64, :], kT[hp:hp + 64, hc * XW + t * 128: hc * XW + t * 128 + 256]),
                reads=(f"kT_{hc}",), writes=(kbn,))
            tv1 = go(lambda e, h=h, t=t, vb=vb: e.tensor_copy(
                vb[:, 0:64], vtok[:, t * D + h * 64: t * D + h * 64 + 64]),
                reads=(f"vtok_{t}",), writes=(vbn,))
            tv2 = go(lambda e, h=h, t=t, vb=vb: e.tensor_copy(
                vb[:, 65:129], vtok[:, (t + 1) * D + h * 64:(t + 1) * D + h * 64 + 64]),
                reads=(f"vtok_{t + 1}",), writes=(vbn,))
            trh = go(lambda e, hp=hp, hc=hc, par=par: e.tensor_copy(
                remh[par][0:64, :], remT[hp:hp + 64, hc * 128: hc * 128 + 128]),
                reads=(f"remT_{hc}",), writes=(f"remh{par}",))
            tq1 = go(lambda e, hp=hp, hc=hc, t=t, qb=qb: e.tensor_copy(
                qb[0:64, :], qT1[hp:hp + 64, hc * TOK + t * 128: hc * TOK + (t + 1) * 128]),
                reads=(f"qT1_{hc}",), writes=(qbn,))
            tq2 = sdma(qb[64:65, :], cneg_loc[h:h + 1, t * 128:(t + 1) * 128],
                       reads=(f"cneg_loc{h}",), writes=(qbn,))
            pA = par
            mm(ps[pA][:, 0:128], kb[:, 0:128], qb[:, :], start=True, stop=True,
               reads=(kbn, qbn, f"khatb{par}_ones"), writes=(f"ps{pA}",),
               deps=[tk, tq1, tq2])
            mm(ps[pA][:, 128:256], kb[:, 128:256], qb[:, :], start=True, stop=True,
               reads=(kbn, qbn), writes=(f"ps{pA}",))
            mlon = "mlo0" if t == 0 else "mlo"
            vo(lambda e, pA=pA, mlon=mlon: e.tensor_tensor(
                ps[pA][:, 0:128], ps[pA][:, 0:128], sb[mlon][:, :], ALU.add),
                reads=(f"ps{pA}", mlon), writes=(f"ps{pA}",))
            vo(lambda e, pA=pA: e.tensor_tensor(
                ps[pA][:, 128:256], ps[pA][:, 128:256], sb["mhi"][:, :], ALU.add),
                reads=(f"ps{pA}", "mhi"), writes=(f"ps{pA}",))
            so(lambda e, pA=pA, par=par: e.activation(
                ptile[par][:, 0:256], ps[pA][:, 0:256], AF.Exp),
                reads=(f"ps{pA}",), writes=(f"ptile{par}",))
            if _SUB < 4:
                continue
            pE = 2 + par
            mm(ps[pE][0:65, 0:128], vb[:, 0:65], ptile[par][:, 0:128],
               start=True, stop=False,
               reads=(vbn, f"ptile{par}", f"vhatb{par}_o1"), writes=(f"ps{pE}",),
               deps=[tv1])
            mm(ps[pE][0:65, 0:128], vb[:, 65:130], ptile[par][:, 128:256],
               start=False, stop=True,
               reads=(vbn, f"ptile{par}", f"vhatb{par}_o2"), writes=(f"ps{pE}",),
               deps=[tv2])
            tl1 = vo(lambda e, pE=pE, par=par: e.tensor_copy(
                tloc[par][0:64, :], ps[pE][0:64, 0:128]),
                reads=(f"ps{pE}",), writes=(f"tloc{par}",))
            vo(lambda e, pE=pE, par=par: e.tensor_scalar_max(
                lr[par][:, 0:128], ps[pE][64:65, 0:128], 1e-30),
                reads=(f"ps{pE}",), writes=(f"lr{par}",))
            if _SUB < 5:
                continue
            pG = 4 + par
            mm(ps[pG][0:1, 0:128], gl_sb[0:64, :], tloc[par][0:64, :],
               start=True, stop=True, reads=(f"tloc{par}", "gl"),
               writes=(f"ps{pG}",), deps=[tl1])
            mm(ps[pG][0:1, 128:256], gr_sb[0:64, :], remh[par][0:64, :],
               start=True, stop=True,
               reads=(f"remh{par}", "gr"), writes=(f"ps{pG}",))
            vo(lambda e, par=par: e.reciprocal(lr[par][:, 0:128], lr[par][:, 0:128]),
               reads=(f"lr{par}",), writes=(f"lr{par}",))
            vo(lambda e, pG=pG, par=par: e.tensor_tensor(
                grow[par][:, 0:128], ps[pG][0:1, 0:128], lr[par][:, 0:128], ALU.mult),
                reads=(f"ps{pG}", f"lr{par}"), writes=(f"grow{par}",))
            vo(lambda e, pG=pG, par=par: e.tensor_tensor(
                grow[par][:, 0:128], grow[par][:, 0:128], ps[pG][0:1, 128:256],
                ALU.add),
                reads=(f"grow{par}", f"ps{pG}"), writes=(f"grow{par}",))
            so(lambda e, par=par: e.activation(
                grow[par][:, 128:256], grow[par][:, 0:128], AF.Exp,
                scale=np.float32(-1.0)),
                reads=(f"grow{par}",), writes=(f"grow{par}",))
            vo(lambda e, par=par: e.tensor_scalar_add(
                grow[par][:, 128:256], grow[par][:, 128:256], 1.0),
                reads=(f"grow{par}",), writes=(f"grow{par}",))
            vo(lambda e, par=par: e.reciprocal(
                grow[par][:, 128:256], grow[par][:, 128:256]),
                reads=(f"grow{par}",), writes=(f"grow{par}",))
            vo(lambda e, par=par: e.tensor_tensor(
                grow[par][:, 0:128], grow[par][:, 128:256], lr[par][:, 0:128],
                ALU.mult),
                reads=(f"grow{par}", f"lr{par}"), writes=(f"grow{par}",))
            vo(lambda e, par=par: e.tensor_scalar(
                grow[par][:, 256:384], grow[par][:, 128:256], -1.0, 1.0,
                ALU.mult, ALU.add),
                reads=(f"grow{par}",), writes=(f"grow{par}",))
            if _SUB < 6:
                continue
            mm(ps[pG][:, 256:384], ones_f[0:1, 0:128], grow[par][:, 0:128],
               start=True, stop=True, reads=(f"grow{par}", "ones_f"),
               writes=(f"ps{pG}",))
            mm(ps[pG][:, 384:512], ones_f[0:1, 0:128], grow[par][:, 256:384],
               start=True, stop=True, reads=(f"grow{par}", "ones_f"),
               writes=(f"ps{pG}",))
            vo(lambda e, pG=pG, par=par: e.tensor_copy(
                bcsb[par][:, 0:256], ps[pG][:, 256:512]),
                reads=(f"ps{pG}",), writes=(f"bcs{par}",))
            vo(lambda e, par=par: e.tensor_tensor(
                ptile[par][0:64, 0:128], tloc[par][0:64, :], bcsb[par][0:64, 0:128],
                ALU.mult),
                reads=(f"tloc{par}", f"bcs{par}"), writes=(f"ptile{par}",))
            vo(lambda e, par=par: e.tensor_tensor(
                ptile[par][0:64, 128:256],
                remh[par][0:64, :],
                bcsb[par][0:64, 128:256], ALU.mult),
                reads=(f"remh{par}", f"bcs{par}"), writes=(f"ptile{par}",))
            vo(lambda e, hp=hp, hc=hc, par=par: e.tensor_tensor(
                mixT[hp:hp + 64, hc * 128: hc * 128 + 128],
                ptile[par][0:64, 0:128], ptile[par][0:64, 128:256], ALU.add),
                reads=(f"ptile{par}",), writes=(f"mixT_{hc}",))
        if _SUB < 7:
            continue
        tx = sdma(f32a[:, :], xtok_in[t * 128:(t + 1) * 128, :], writes=("f32a",))
        for half in range(2):
            pn = half
            for r in range(8):
                mm(ps[pn][:, 0:512],
                   mixT[:, r * 128:(r + 1) * 128],
                   woutT[:, r * D + half * 512: r * D + half * 512 + 512],
                   start=(r == 0), stop=(r == 7),
                   reads=("woutT",) + tuple(f"mixT_{rr}" for rr in range(8)),
                   writes=(f"ps{pn}",), deps=[towt])
            vo(lambda e, pn=pn, half=half: e.tensor_tensor(
                f32a[:, half * 512:(half + 1) * 512], ps[pn][:, 0:512],
                f32a[:, half * 512:(half + 1) * 512], ALU.add),
                reads=(f"ps{pn}", "f32a"), writes=("f32a",), deps=[tx])
            vo(lambda e, half=half: e.tensor_tensor(
                f32a[:, half * 512:(half + 1) * 512],
                f32a[:, half * 512:(half + 1) * 512],
                bcA[:, half * 512:(half + 1) * 512], ALU.add),
                reads=("f32a", "bcA"), writes=("f32a",))

        def fin_xmid(mean_ap, rstd_ap, t=t):
            vo(lambda e: e.tensor_scalar(
                xmid[:, t * D:(t + 1) * D], f32a[:, :], mean_ap, rstd_ap,
                ALU.subtract, ALU.mult),
                reads=("f32a", "c0", "c4"), writes=(f"xmid_{t}",))
            if _DEBUG:
                gdma(dbg["xmid"][t * 128:(t + 1) * 128, :],
                     xmid[:, t * D:(t + 1) * D], reads=(f"xmid_{t}",))
        plain_ln(f32a[:, :], "f32a", fin_xmid, f32b, "f32b")

    if _TRUNC == "P1e":
        return trunc_finish()
    P.barrier()
    for t in range(NT):
        def fin_nx1(mean_ap, rstd_ap, t=t):
            vo(lambda e: e.tensor_scalar(
                f32a[:, :], xmid[:, t * D:(t + 1) * D], mean_ap, rstd_ap,
                ALU.subtract, ALU.mult),
                reads=(f"xmid_{t}", "c0", "c4"), writes=("f32a",))
        plain_ln(xmid[:, t * D:(t + 1) * D], f"xmid_{t}", fin_nx1, f32b, "f32b")
        vo(lambda e: e.tensor_tensor(f32a[:, :], f32a[:, :], bcB[:, :], ALU.mult),
           reads=("f32a", "bcB"), writes=("f32a",))
        vo(lambda e: e.tensor_tensor(f32a[:, :], f32a[:, :], bcC[:, :], ALU.add),
           reads=("f32a", "bcC"), writes=("f32a",))
        if _DEBUG:
            sdma(dbg["nx1"][t * 128:(t + 1) * 128, :], f32a[:, :], reads=("f32a",))
        for half in range(2):
            tch = vo(lambda e, half=half: e.tensor_copy(
                ptile[0][:, 0:512], f32a[:, half * 512:(half + 1) * 512]),
                reads=("f32a",), writes=("ptile0",))
            for u in range(4):
                tpose(psT[:, (half * 4 + u) * 128:(half * 4 + u + 1) * 128],
                      ptile[0][:, u * 128:(u + 1) * 128], ident_bf[:, :],
                      reads=("ptile0", "ident_bf"), writes=("psT",), deps=[tch])
        vo(lambda e, t=t: e.tensor_copy(
            nx1T[:, :].rearrange("p (r w) -> p r w", r=8)
            [:, :, t * 128:(t + 1) * 128],
            psT[:, 0:1024].rearrange("p (r w) -> p r w", r=8)),
            reads=("psT",), writes=tuple(f"nx1T_{r}" for r in range(8)))

    ccst = [sal(f"ccst{i}", [128, TOK], BF16, SCR + 13344 + 6144 + i * 4096)
            for i in range(2)]
    kv_toks = []
    q_toks = []
    for ci in list(range(8, 24)) + list(range(8)):
        sl, tw = load_w(w_Wqkv, 3 * D, ci)
        qsl = ci % 2
        for gg in range(4):
            g0 = gg * 512
            pn = nps()
            for r in range(8):
                mm(ps[pn][:, 0:512],
                   wbuf[sl][:, r * 128:(r + 1) * 128],
                   nx1T[:, r * TOK + g0: r * TOK + g0 + 512],
                   start=(r == 0), stop=(r == 7),
                   reads=(f"wbuf{sl}", f"nx1T_{r}"), writes=(f"ps{pn}",), deps=[tw])
            if ci < 8:
                vo(lambda e, pn=pn, ci=ci, qsl=qsl, g0=g0: e.tensor_scalar(
                    ccst[qsl][:, g0:g0 + 512], ps[pn][:, 0:512],
                    sb["bcol_Wqkv"][:, ci:ci + 1],
                    0.125, ALU.add, ALU.mult),
                    reads=(f"ps{pn}", "bcol_Wqkv"), writes=(f"ccst{qsl}",))
            else:
                vo(lambda e, pn=pn, ci=ci, qsl=qsl, g0=g0: e.tensor_single_scalar(
                    ccst[qsl][:, g0:g0 + 512], ps[pn][:, 0:512],
                    sb["bcol_Wqkv"][:, ci:ci + 1], ALU.add),
                    reads=(f"ps{pn}", "bcol_Wqkv"), writes=(f"ccst{qsl}",))
        if ci < 8:
            q_toks.append(sdma(cc2q_in[ci * 128:(ci + 1) * 128, :],
                               ccst[qsl][:, :], reads=(f"ccst{qsl}",),
                               writes=("cc2q_in",)))
        else:
            kv_toks.append(sdma(cc2kv_in[(ci - 8) * 128:(ci - 7) * 128, :],
                                ccst[qsl][:, :], reads=(f"ccst{qsl}",),
                                writes=("cc2kv_in",)))

        if ci == 23:
            tcc2kv = P.op("gpsimd", lambda e: e.collective_compute(
                "AllGather", ALU.bypass,
                replica_groups=[[0, 1, 2, 3], [4, 5, 6, 7]],
                ins=[cc2kv_in.opt()], outs=[cc2kv_out.opt()]),
                reads=("cc2kv_in",), writes=("cc2kv_out",), sem="cc2", inc=1,
                deps=kv_toks)

    tcc2q = P.op("gpsimd", lambda e: e.collective_compute(
        "AllGather", ALU.bypass, replica_groups=[[0, 1, 2, 3], [4, 5, 6, 7]],
        ins=[cc2q_in.opt()], outs=[cc2q_out.opt()]),
        reads=("cc2q_in",), writes=("cc2q_out",), sem="cc2", inc=1, deps=q_toks)

    def km_copy(e):
        src = mkap(cc2kv_out, dynbase("kv", 4 * HD * TOK),
                   [[TOK, 4 * HD], [2 * D * TOK, 4], [1, TOK]])
        dd = mkap(km_d, 0, [[S, 4 * HD], [TOK, 4], [1, TOK]])
        return e.dma_start(out=dd, in_=src)

    def vm_copy(e):
        src = mkap(cc2kv_out, dynbase("kv", 4 * HD * TOK) + D * TOK,
                   [[TOK, 4 * HD], [2 * D * TOK, 4], [1, TOK]])
        dd = mkap(vm_d, 0, [[S, 4 * HD], [TOK, 4], [1, TOK]])
        return e.dma_start(out=dd, in_=src)

    def qm_copy(e):
        src = mkap(cc2q_out, dynbase("kv", 4 * HD * TOK),
                   [[TOK, 4 * HD], [D * TOK, 4], [1, TOK]])
        dd = mkap(qm_d, 0, [[S, 4 * HD], [TOK, 4], [1, TOK]])
        return e.dma_start(out=dd, in_=src)

    P.op("gpsimd", km_copy, reads=("cc2kv_out",), writes=("km_d",),
         sem="gmy", inc=16, deps=[tcc2kv])
    tkvm = P.op("gpsimd", vm_copy, reads=("cc2kv_out",), writes=("vm_d",),
                sem="gmy", inc=16, deps=[tcc2kv])
    P.wtok["km_d"] = tkvm
    P.wtok["vm_d"] = tkvm
    tkm = tvm = tkvm
    tqm = P.op("gpsimd", qm_copy, reads=("cc2q_out",), writes=("qm_d",),
               sem="gam", inc=16, deps=[tcc2q])

    tmcau = gdma(mcau[:, :].rearrange("p (m w) -> p m w", m=4),
                 mkap(mcau_in, 0, [[512, 128], [128 * 512, 4], [1, 512]]),
                 writes=("mcau",), dsem="c3")

    for ei in range(NE):
        gdma(mkap(w1bf, ei * D * FFN, [[FFN, D], [1, FFN]]),
             mkap(w1_in, ei * D * FFN, [[FFN, D], [1, FFN]]),
             writes=(f"w1bf_{ei}",), deps=[tvm], dsem="wc")
        wc_last = gdma(mkap(w2bf, ei * FFN * D, [[D, FFN], [1, D]]),
                       mkap(w2_in, ei * FFN * D, [[D, FFN], [1, D]]),
                       writes=(f"w2bf_{ei}",), deps=[tvm], dsem="wc")
    for ei in range(NE):
        P.wtok[f"w1bf_{ei}"] = wc_last
        P.wtok[f"w2bf_{ei}"] = wc_last

    if _TRUNC == "P1f":
        return trunc_finish()
    P.barrier(exclude=("wc",))
    ptile4 = {0: [ptile[0], pt_extra[0], pt_extra[1], pt_extra[2]],
              1: [ptile[1], pt_extra[3], pt_extra[4], pt_extra[5]]}
    ksq2b = sal("ksq2b", [64, 512], BF16, SCR + 13344 + 6144 + 8192)

    def p2_prep(hh):
        par = hh % 2
        kb = khat2[par]
        kbn = f"khat2_{par}"

        tkl = sdma(kb[0:64, :], km_d[hh * 64:(hh + 1) * 64, :],
                   reads=("km_d",), writes=(kbn,), deps=[tkm])
        if hh < 2:
            vo(lambda e, kb=kb: e.memset(kb[64:65, :], 1.0), writes=(f"{kbn}_ones",))
        vb = vhat2[par]
        vbn = f"vhat2_{par}"
        if hh < 2:
            vo(lambda e, vb=vb: e.memset(
                vb[:, :].rearrange("p (t j) -> p t j", j=65)[:, :, 64:65], 1.0),
               writes=(f"{vbn}_ones",))
        for s4 in range(4):
            vs = vth[s4 % 2]
            vsn = f"vth{s4 % 2}"

            tvl = sdma(vs[0:64, :], vm_d[hh * 64:(hh + 1) * 64,
                                         s4 * TOK:(s4 + 1) * TOK],
                       reads=("vm_d",), writes=(vsn,), deps=[tvm])
            for u in range(16):
                tpose(psT[:, u * 64:(u + 1) * 64],
                      vs[0:64, u * 128:(u + 1) * 128], ident_bf[0:64, 0:64],
                      reads=(vsn, "ident_bf"), writes=("psT",), deps=[tvl])
            vo(lambda e, vb=vb, s4=s4: e.tensor_copy(
                vb[:, :].rearrange("p (t j) -> p t j", j=65)
                [:, s4 * 16:(s4 + 1) * 16, 0:64],
                psT[:, 0:1024].rearrange("p (t j) -> p t j", j=64)),
                reads=("psT",), writes=(vbn,))
        for g0 in range(0, S, 512):
            t1 = so(lambda e, kb=kb, g0=g0: e.activation(
                ksq2b[0:64, 0:512], kb[0:64, g0:g0 + 512], AF.Square),
                reads=(kbn,), writes=("ksq2b",), deps=[tkl])
            t2 = mm(ps[5][0:1, 0:512], ones_b[0:64, 0:1], ksq2b[0:64, 0:512],
                    start=True, stop=True, reads=("ksq2b", "ones_b"),
                    writes=("ps5",), deps=[t1])
            vo(lambda e: e.reduce_max(cols[0:1, 7:8], ps[5][0:1, 0:512], axis=AX.X),
               reads=("ps5",), writes=("c7",))
            if g0 == 0:
                vo(lambda e, hh=hh: e.tensor_copy(mk2[:, hh:hh + 1], cols[0:1, 7:8]),
                   reads=("c7",), writes=(f"mk2_{hh}",))
            else:
                vo(lambda e, hh=hh: e.tensor_max(mk2[:, hh:hh + 1], mk2[:, hh:hh + 1],
                                                 cols[0:1, 7:8]),
                   reads=("c7", f"mk2_{hh}"), writes=(f"mk2_{hh}",))

    p2_prep(0)
    for hh in range(4):
        par = hh % 2
        kb = khat2[par]
        kbn = f"khat2_{par}"
        vb = vhat2[par]
        vbn = f"vhat2_{par}"
        if hh < 3:
            p2_prep(hh + 1)
        for tau in range(16):
            qpar = tau % 2
            qb = qhat2[qpar]
            qbn = f"qhat2_{qpar}"
            tql = sdma(qb[0:64, :], qm_d[hh * 64:(hh + 1) * 64,
                                         tau * 512:(tau + 1) * 512],
                       reads=("qm_d",), writes=(qbn,), deps=[tqm])
            t1 = so(lambda e, qb=qb: e.activation(ksq2[0:64, 0:512], qb[0:64, :],
                                                  AF.Square),
                    reads=(qbn,), writes=("ksq2",), deps=[tql])
            t2 = mm(ps[6][0:1, 0:512], ones_b[0:64, 0:1], ksq2[0:64, 0:512],
                    start=True, stop=True, reads=("ksq2", "ones_b"),
                    writes=("ps6",), deps=[t1])
            vo(lambda e, qpar=qpar: e.tensor_copy(l2[qpar][:, 0:512],
                                                  ps[6][0:1, 0:512]),
               reads=("ps6",), writes=(f"l2_{qpar}",))
            vo(lambda e, hh=hh, qpar=qpar: e.tensor_single_scalar(
                l2[qpar][:, 0:512], l2[qpar][:, 0:512], mk2[:, hh:hh + 1], ALU.mult),
                reads=(f"l2_{qpar}", f"mk2_{hh}"), writes=(f"l2_{qpar}",))
            so(lambda e, qpar=qpar: e.activation(
                l2[qpar][:, 0:512], l2[qpar][:, 0:512], AF.Ln,
                bias=np.float32(1e-30)),
                reads=(f"l2_{qpar}",), writes=(f"l2_{qpar}",))
            so(lambda e, qpar=qpar: e.activation(
                l2[qpar][:, 0:512], l2[qpar][:, 0:512], AF.Exp,
                scale=np.float32(0.5)),
                reads=(f"l2_{qpar}",), writes=(f"l2_{qpar}",))
            t7 = vo(lambda e, qb=qb, qpar=qpar: e.tensor_scalar_mul(
                qb[64:65, :], l2[qpar][:, 0:512], -1.0),
                reads=(f"l2_{qpar}",), writes=(qbn,))
            nch = 4 * tau + 4
            pE = 2 + qpar
            QKB = (0, 1, 6)
            exp_done = {}

            def emit_qk(kc):
                pA = QKB[kc % 3]
                mm(ps[pA][:, 0:512], kb[:, kc * 128:(kc + 1) * 128], qb[:, :],
                   start=True, stop=True,
                   reads=(kbn, qbn, f"khat2_{par}_ones"), writes=(f"ps{pA}",),
                   deps=[t7])
                dk = kc - 4 * tau
                if dk >= 0:
                    vo(lambda e, pA=pA, dk=dk: e.tensor_tensor(
                        ps[pA][:, 0:512], ps[pA][:, 0:512],
                        mcau[:, dk * 512:(dk + 1) * 512], ALU.add),
                        reads=(f"ps{pA}", "mcau"), writes=(f"ps{pA}",),
                        deps=[tmcau])
                pp = kc % 4
                so(lambda e, pA=pA, pp=pp, qpar=qpar: e.activation(
                    ptile4[qpar][pp][:, 0:512], ps[pA][:, 0:512], AF.Exp),
                    reads=(f"ps{pA}",), writes=(f"pt{qpar}_{pp}",))

            def emit_av(kc):
                pp = kc % 4
                mm(ps[pE][0:65, 0:512],
                   vb[:, kc * 65:(kc + 1) * 65], ptile4[qpar][pp][:, 0:512],
                   start=(kc == 0), stop=(kc == nch - 1),
                   reads=(f"pt{qpar}_{pp}", vbn, f"vhat2_{par}_ones"),
                   writes=(f"ps{pE}",))

            emit_qk(0)
            if nch > 1:
                emit_qk(1)
            for kc in range(2, nch):
                emit_qk(kc)
                emit_av(kc - 2)
            emit_av(nch - 2)
            emit_av(nch - 1)
            vo(lambda e, pE=pE, qpar=qpar: e.tensor_scalar_max(
                l2[qpar][:, 0:512], ps[pE][64:65, 0:512], 1e-30),
                reads=(f"ps{pE}",), writes=(f"l2_{qpar}",))
            vo(lambda e, qpar=qpar: e.reciprocal(l2[qpar][:, 0:512],
                                                 l2[qpar][:, 0:512]),
               reads=(f"l2_{qpar}",), writes=(f"l2_{qpar}",))
            mm(ps[6][0:64, 0:512], ones_f[0:1, 0:64], l2[qpar][:, 0:512],
               start=True, stop=True, reads=(f"l2_{qpar}", "ones_f"),
               writes=("ps6",))
            vo(lambda e, qpar=qpar: e.tensor_copy(bcs2[qpar][:, 0:512],
                                                  ps[6][0:64, 0:512]),
               reads=("ps6",), writes=(f"bcs2_{qpar}",))
            vo(lambda e, hh=hh, tau=tau, pE=pE, qpar=qpar: e.tensor_tensor(
                attnT2[(hh % 2) * 64:(hh % 2) * 64 + 64,
                       (hh // 2) * S + tau * 512:(hh // 2) * S + (tau + 1) * 512],
                ps[pE][0:64, 0:512], bcs2[qpar][:, 0:512], ALU.mult),
                reads=(f"ps{pE}", f"bcs2_{qpar}"), writes=(f"attnT2_{hh}",))

    tct1 = sdma(cc3_in[0:128, :], attnT2[:, 0:S],
                reads=("attnT2_0", "attnT2_1"), writes=("cc3_in",))
    tct2 = sdma(cc3_in[128:256, :], attnT2[:, S:2 * S],
                reads=("attnT2_2", "attnT2_3"), writes=("cc3_in",))
    tcc3 = P.op("gpsimd", lambda e: e.collective_compute(
        "AllGather", ALU.bypass, replica_groups=[[0, 1, 2, 3], [4, 5, 6, 7]],
        ins=[cc3_in.opt()], outs=[cc3_out.opt()]),
        reads=("cc3_in",), writes=("cc3_out",), sem="cc3", inc=1,
        deps=[tct1, tct2])

    def my_cols(e):
        src = mkap(cc3_out, dynbase("c3", TOK), [[S, D], [1, TOK]])
        dd = mkap(am_d, 0, [[TOK, D], [1, TOK]])
        return e.dma_start(out=dd, in_=src)
    tam = P.op("gpsimd", my_cols, reads=("cc3_out",), writes=("am_d",),
               sem="gam", inc=16, deps=[tcc3])

    if _TRUNC == "P2":
        return trunc_finish()
    P.barrier(exclude=("wc",))
    sdma(fc2b[:, :], fc2b_in[:, :], writes=("fc2b",))
    sdma(ident_f[:, :], ident_in[:, :], writes=("ident_f",))
    sdma(fc1b[:, :].rearrange("p (ee c) -> p ee c", ee=8),
         mkap(bcol_fc1_in, 0, [[32, 128], [128 * 32, 8], [1, 32]]),
         writes=("fc1b",))
    gdma(bcA[:, :], bo_in[:, :], writes=("bcA",), dsem="c3")
    gdma(bcB[:, :], ln2w_in[:, :], writes=("bcB",), dsem="c3")
    c3_last = gdma(bcC[:, :], ln2b_in[:, :], writes=("bcC",), dsem="c3")
    for nm in ("bcA", "bcB", "bcC"):
        P.wtok[nm] = c3_last
    twL = gdma(woutT[:, :].rearrange("p (r c) -> p r c", r=8),
               mkap(w_Wo, 0, [[D, 128], [128 * D, 8], [1, D]]),
               writes=("woutT",))

    for t in range(NT):
        p3 = t % 2
        tat = sdma(attnTo_t[p3][:, :].rearrange("p (r c) -> p r c", r=8),
                   mkap(am_d, t * 128, [[TOK, 128], [128 * TOK, 8], [1, 128]]),
                   reads=("am_d",), writes=(f"attnTo{p3}",), deps=[tam])
        for half in range(2):
            pn = half
            for r in range(8):
                mm(ps[pn][:, 0:512],
                   attnTo_t[p3][:, r * 128:(r + 1) * 128],
                   woutT[:, r * D + half * 512: r * D + half * 512 + 512],
                   start=(r == 0), stop=(r == 7),
                   reads=("woutT", f"attnTo{p3}"), writes=(f"ps{pn}",),
                   deps=[twL, tat])
            vo(lambda e, pn=pn, half=half: e.tensor_tensor(
                f3a[:, half * 512:(half + 1) * 512], ps[pn][:, 0:512],
                bcA[:, half * 512:(half + 1) * 512], ALU.add),
                reads=(f"ps{pn}", "bcA"), writes=("f3a",))
        if _DEBUG:
            sdma(dbg["attn"][t * 128:(t + 1) * 128, :], f3a[:, :], reads=("f3a",))
        vo(lambda e, t=t: e.tensor_tensor(xres_t[:, :], f3a[:, :],
                                          xmid[:, t * D:(t + 1) * D], ALU.add),
           reads=("f3a", f"xmid_{t}"), writes=("xres_t",))
        vo(lambda e, t=t: e.tensor_tensor(moe_acc[:, t * D:(t + 1) * D],
                                          xres_t[:, :],
                                          xmid[:, t * D:(t + 1) * D], ALU.add),
           reads=("xres_t", f"xmid_{t}"), writes=(f"moe_{t}",))

        def fin_nx2(mean_ap, rstd_ap):
            vo(lambda e: e.tensor_scalar(f3a[:, :], xres_t[:, :], mean_ap, rstd_ap,
                                         ALU.subtract, ALU.mult),
               reads=("xres_t", "c0", "c4"), writes=("f3a",))
        plain_ln(xres_t[:, :], "xres_t", fin_nx2, f3b, "f3b")
        vo(lambda e: e.tensor_tensor(f3a[:, :], f3a[:, :], bcB[:, :], ALU.mult),
           reads=("f3a", "bcB"), writes=("f3a",))
        vo(lambda e: e.tensor_tensor(nx2f[:, :], f3a[:, :], bcC[:, :], ALU.add),
           reads=("f3a", "bcC"), writes=("nx2f",))
        for half in range(2):
            tch = vo(lambda e, half=half: e.tensor_copy(
                rT3[:, half * 512:(half + 1) * 512],
                nx2f[:, half * 512:(half + 1) * 512]),
                reads=("nx2f",), writes=("rT3",))
            for u in range(4):
                tpose(psT[:, (half * 4 + u) * 128:(half * 4 + u + 1) * 128],
                      rT3[:, half * 512 + u * 128: half * 512 + (u + 1) * 128],
                      ident_bf[:, :],
                      reads=("rT3", "ident_bf"), writes=("psT",), deps=[tch])
        vo(lambda e, t=t: e.tensor_copy(
            nx2T[:, :].rearrange("p (r w) -> p r w", r=8)
            [:, :, t * 128:(t + 1) * 128],
            psT[:, 0:1024].rearrange("p (r w) -> p r w", r=8)),
            reads=("psT",), writes=tuple(f"nx2T_{r}" for r in range(8)))
        for half in range(2):
            tsub = vo(lambda e, half=half: e.tensor_tensor(
                f3b[:, 0:512], nx2f[:, half * 512:(half + 1) * 512],
                rT3[:, half * 512:(half + 1) * 512], ALU.subtract),
                reads=("nx2f", "rT3"), writes=("f3b",))
            tch = vo(lambda e, half=half: e.tensor_copy(
                rT3[:, half * 512:(half + 1) * 512], f3b[:, 0:512]),
                reads=("f3b",), writes=("rT3x",))
            for u in range(4):
                tpose(psT[:, (half * 4 + u) * 128:(half * 4 + u + 1) * 128],
                      rT3[:, half * 512 + u * 128: half * 512 + (u + 1) * 128],
                      ident_bf[:, :],
                      reads=("rT3x", "ident_bf"), writes=("psT",), deps=[tch])
        vo(lambda e: e.tensor_copy(rT3[:, :], psT[:, 0:1024]),
           reads=("psT", "rT3x"), writes=("rT3",))
        for r in range(8):
            mm(ps[2][:, 0:8],
               nx2T[:, r * TOK + t * 128: r * TOK + (t + 1) * 128],
               gateW[:, r * 8:(r + 1) * 8],
               start=(r == 0), stop=False,
               reads=(f"nx2T_{r}", "gateW"), writes=("ps2",))
        for r in range(8):
            mm(ps[2][:, 0:8],
               rT3[:, r * 128:(r + 1) * 128],
               gateW[:, r * 8:(r + 1) * 8],
               start=False, stop=(r == 7),
               reads=("rT3", "gateW"), writes=("ps2",))
        vo(lambda e: e.tensor_copy(lg8[:, :], ps[2][:, 0:8]),
           reads=("ps2",), writes=("lg8",))
        vo(lambda e: e.reduce_max(w12[:, 0:1], lg8[:, :], axis=AX.X),
           reads=("lg8",), writes=("w0",))
        vo(lambda e: e.tensor_single_scalar(is1[:, :], lg8[:, :], w12[:, 0:1],
                                            ALU.is_ge),
           reads=("lg8", "w0"), writes=("is1",))
        vo(lambda e: e.scalar_tensor_tensor(msk8[:, :], is1[:, :], NEG, lg8[:, :],
                                            ALU.mult, ALU.add),
           reads=("is1", "lg8"), writes=("msk8",))
        vo(lambda e: e.reduce_max(w12[:, 1:2], msk8[:, :], axis=AX.X),
           reads=("msk8",), writes=("w1c",))
        vo(lambda e: e.tensor_single_scalar(is2[:, :], msk8[:, :], w12[:, 1:2],
                                            ALU.is_ge),
           reads=("msk8", "w1c"), writes=("is2",))
        vo(lambda e: e.tensor_sub(w12[:, 2:3], w12[:, 1:2], w12[:, 0:1]),
           reads=("w0", "w1c"), writes=("w2c",))
        so(lambda e: e.activation(w12[:, 2:3], w12[:, 2:3], AF.Exp),
           reads=("w2c",), writes=("w2c",))
        vo(lambda e: e.tensor_scalar_add(w12[:, 3:4], w12[:, 2:3], 1.0),
           reads=("w2c",), writes=("w3c",))
        vo(lambda e: e.reciprocal(w12[:, 3:4], w12[:, 3:4]),
           reads=("w3c",), writes=("w3c",))
        vo(lambda e: e.tensor_mul(w12[:, 2:3], w12[:, 2:3], w12[:, 3:4]),
           reads=("w2c", "w3c"), writes=("w2c",))
        vo(lambda e: e.tensor_single_scalar(is2[:, :], is2[:, :], w12[:, 2:3],
                                            ALU.mult),
           reads=("is2", "w2c"), writes=("is2",))
        vo(lambda e, t=t: e.scalar_tensor_tensor(
            gates[:, t * 8:(t + 1) * 8], is1[:, :], w12[:, 3:4], is2[:, :],
            ALU.mult, ALU.add),
            reads=("is1", "is2", "w3c"), writes=(f"gates_{t}",))
        tgt = tpose(ps[3][0:8, 0:128], gates[:, t * 8:(t + 1) * 8], ident_f[:, :],
                    reads=(f"gates_{t}", "ident_f"), writes=("ps3",))
        tge = vo(lambda e: e.tensor_copy(g8T[:, :], ps[3][0:8, 0:128]),
                 reads=("ps3",), writes=("g8T",))
        for half in range(2):
            pn = half
            mm(ps[pn][:, 0:512], g8T[:, :], fc2b[:, half * 512:(half + 1) * 512],
               start=True, stop=True, reads=("g8T", "fc2b"), writes=(f"ps{pn}",),
               deps=[tge])
            vo(lambda e, pn=pn, t=t, half=half: e.tensor_tensor(
                moe_acc[:, t * D + half * 512: t * D + (half + 1) * 512],
                moe_acc[:, t * D + half * 512: t * D + (half + 1) * 512],
                ps[pn][:, 0:512], ALU.add),
                reads=(f"ps{pn}", f"moe_{t}"), writes=(f"moe_{t}",))

    if _TRUNC == "P3wo":
        return trunc_finish()
    P.barrier()
    w1rot = [0]
    w2rot = [0]
    for ei in range(NE):
        for qg in range(4):
            for fh in range(8):
                sl = w1rot[0] % 2
                w1rot[0] += 1
                t_w1 = gdma(w1buf[sl][:, :].rearrange("p (r c) -> p r c", r=8),
                            mkap(w1bf, ei * D * FFN + fh * 512,
                                 [[FFN, 128], [128 * FFN, 8], [1, 512]]),
                            writes=(f"w1buf{sl}",), deps=[P.wtok[f"w1bf_{ei}"]],
                            dsem=f"w1{sl}")
                for fs in range(4):
                    pn = nps()
                    for r in range(8):
                        mm(ps[pn][:, 0:512],
                           w1buf[sl][:, r * 512 + fs * 128: r * 512 + (fs + 1) * 128],
                           nx2T[:, r * TOK + qg * 512: r * TOK + (qg + 1) * 512],
                           start=(r == 0), stop=(r == 7),
                           reads=(f"w1buf{sl}", f"nx2T_{r}"), writes=(f"ps{pn}",),
                           deps=[t_w1])
                    fc = fh * 4 + fs
                    so(lambda e, pn=pn, ei=ei, fc=fc: e.activation(
                        hT[:, fc * 512:(fc + 1) * 512], ps[pn][:, 0:512], AF.Gelu,
                        bias=fc1b[:, ei * 32 + fc: ei * 32 + fc + 1]),
                        reads=(f"ps{pn}", "fc1b"), writes=(f"hT_{fc}",))
            for f2h in range(16):
                sl = w2rot[0] % 2
                w2rot[0] += 1
                t_w2 = gdma(w2buf[sl][:, :].rearrange("p (r c) -> p r c", r=2),
                            mkap(w2bf, ei * FFN * D + f2h * 256 * D,
                                 [[D, 128], [128 * D, 2], [1, D]]),
                            writes=(f"w2buf{sl}",), deps=[P.wtok[f"w2bf_{ei}"]],
                            dsem=f"w2{sl}")
                for qt in range(4):
                    for nh in range(2):
                        pn = qt * 2 + nh
                        pname = f"ps{pn}" if pn < 7 else "psT"
                        out_ap = (ps[pn][:, 0:512] if pn < 7
                                  else psT[:, 0:1024].bitcast(F32))
                        for fs in range(2):
                            fc = f2h * 2 + fs
                            mm(out_ap,
                               hT[:, fc * 512 + qt * 128: fc * 512 + (qt + 1) * 128],
                               w2buf[sl][:, fs * D + nh * 512: fs * D + nh * 512 + 512],
                               start=(f2h == 0 and fs == 0),
                               stop=(f2h == 15 and fs == 1),
                               reads=(f"w2buf{sl}", f"hT_{fc}"),
                               writes=(pname,), deps=[t_w2])
            for qt in range(4):
                for nh in range(2):
                    pn = qt * 2 + nh
                    pname = f"ps{pn}" if pn < 7 else "psT"
                    src = (ps[pn][:, 0:512] if pn < 7
                           else psT[:, 0:1024].bitcast(F32))
                    tt = qg * 4 + qt
                    vo(lambda e, src=src, tt=tt, nh=nh, ei=ei: e.scalar_tensor_tensor(
                        moe_acc[:, tt * D + nh * 512: tt * D + (nh + 1) * 512],
                        src, gates[:, tt * 8 + ei: tt * 8 + ei + 1],
                        moe_acc[:, tt * D + nh * 512: tt * D + (nh + 1) * 512],
                        ALU.mult, ALU.add),
                        reads=(pname, f"gates_{tt}", f"moe_{tt}"),
                        writes=(f"moe_{tt}",))

    P.barrier()
    out_toks = []
    for t in range(NT):
        def fin_out(mean_ap, rstd_ap, t=t):
            vo(lambda e: e.tensor_scalar(
                f3a[:, :], moe_acc[:, t * D:(t + 1) * D], mean_ap, rstd_ap,
                ALU.subtract, ALU.mult),
                reads=(f"moe_{t}", "c0", "c4"), writes=("f3a",))
        plain_ln(moe_acc[:, t * D:(t + 1) * D], f"moe_{t}", fin_out, f3b, "f3b")
        out_toks.append(sdma(out_dram[t * 128:(t + 1) * 128, :], f3a[:, :],
                             reads=("f3a",)))

    for tok in out_toks:
        P._wait("sync", tok[0], tok[1])

    P.finalize()
    import contextlib
    with contextlib.ExitStack() as st:
        for s in P.sem_names:
            P.sems[s] = st.enter_context(nc.semaphore(s))
        block = st.enter_context(nc.Block())
        P.replay(block)
    return nc


_NC_CACHE = {}


def kernel(**inputs):
    f32 = np.float32
    g = {k: np.asarray(v, dtype=f32) for k, v in inputs.items()}
    x = g["L0_attn_x"]

    def bc(v):
        return np.ascontiguousarray(np.broadcast_to(v.reshape(1, D), (128, D)))

    shared = {
        "w_qkv": g["L0_attn_qkv_w"],
        "bcol_qkv": np.ascontiguousarray(g["L0_attn_qkv_b"].reshape(24, 128).T),
        "w_rq": g["L0_attn_rq_w"],
        "bcol_rq": np.ascontiguousarray(g["L0_attn_rq_b"].reshape(2, 128).T),
        "w_rkv": g["L0_attn_rkv_w"],
        "bcol_rkv": np.ascontiguousarray(g["L0_attn_rkv_b"].reshape(4, 128).T),
        "w_rout": g["L0_attn_rout_w"],
        "bcol_rout": np.ascontiguousarray(g["L0_attn_rout_b"].reshape(8, 128).T),
        "w_out": g["L0_attn_out_w"],
        "outb_bc": bc(g["L0_attn_out_b"]),
        "gl": np.ascontiguousarray(np.concatenate([g["L0_attn_gl_w"]] * 2, 0)),
        "gr": np.ascontiguousarray(np.concatenate([g["L0_attn_gr_w"]] * 2, 0)),
        "w_Wqkv": g["L0_ffn_Wqkv"],
        "bcol_Wqkv": np.ascontiguousarray(g["L0_ffn_bqkv"].reshape(24, 128).T),
        "w_Wo": g["L0_ffn_Wo"],
        "bo_bc": bc(g["L0_ffn_bo"]),
        "ln1w_bc": bc(g["L0_ffn_ln1_w"]),
        "ln1b_bc": bc(g["L0_ffn_ln1_b"]),
        "ln2w_bc": bc(g["L0_ffn_ln2_w"]),
        "ln2b_bc": bc(g["L0_ffn_ln2_b"]),
        "w_gate": g["L0_ffn_gate_W"],
        "w1": np.ascontiguousarray(g["L0_ffn_fc1_w"].transpose(0, 2, 1)),
        "bcol_fc1": np.ascontiguousarray(
            g["L0_ffn_fc1_b"].reshape(NE, 32, 128).transpose(0, 2, 1)),
        "w2": np.ascontiguousarray(g["L0_ffn_fc2_w"].transpose(0, 2, 1)),
        "fc2b": g["L0_ffn_fc2_b"],
        "ident": np.eye(128, dtype=f32),
    }
    jj, ii = np.meshgrid(np.arange(128), np.arange(128), indexing="ij")
    shared["mlo"] = np.where(ii <= jj, 0.0, NEG).astype(f32)
    shared["mhi"] = np.where(jj < ii, 0.0, NEG).astype(f32)
    mcau = np.empty((4, 128, 512), f32)
    uu = np.arange(512)
    for m in range(4):
        mcau[m] = np.where(uu[None, :] >= (128 * m + np.arange(128))[:, None],
                           0.0, NEG)
    shared["mcau"] = mcau

    in_maps = []
    for c in range(8):
        b, qd = c // 4, c % 4
        t0 = qd * TOK
        xT = np.zeros((D, XW), f32)
        xb = np.ascontiguousarray(x[b].T)
        xT[:, HALO:] = xb[:, t0:t0 + TOK]
        if qd > 0:
            xT[:, :HALO] = xb[:, t0 - HALO:t0]
        lat = np.arange(64)[:, None]
        tt = np.arange(NT)[None, :]
        m = dict(shared)
        m["xT"] = xT
        m["xtok"] = np.ascontiguousarray(x[b, t0:t0 + TOK])
        m["mlo0"] = (np.full((128, 128), NEG, f32) if qd == 0 else shared["mlo"])
        m["mlat"] = np.where(lat < (qd * 16 + tt), 0.0, NEG).astype(f32)
        in_maps.append(m)

    if "nc" not in _NC_CACHE:
        _NC_CACHE["nc"] = build_nc()
    nc = _NC_CACHE["nc"]
    res = run_bass_kernel_spmd(nc, in_maps, core_ids=list(range(8)))
    out = np.empty((B, S, D), f32)
    for c in range(8):
        b, qd = c // 4, c % 4
        out[b, qd * TOK:(qd + 1) * TOK] = np.asarray(res.results[c]["out"])
    return out


# version 10
# speedup vs baseline: 1.2848x; 1.0059x over previous
import dataclasses
import numpy as np
import ml_dtypes
import concourse.bass as bass
import concourse.mybir as mybir
from concourse.bass_utils import run_bass_kernel_spmd

F32 = mybir.dt.float32
BF16 = mybir.dt.bfloat16
AF = mybir.ActivationFunctionType
ALU = mybir.AluOpType
AX = mybir.AxisListType

B, S, D = 2, 8192, 1024
NH, HD = 16, 64
LH = 4
NE, FFN = 8, 4096
NEG = -1.0e9
TOK = 2048
NT = 16
HALO = 128
XW = HALO + TOK
NTX = XW // 128

_DEBUG = False
_TRUNC = None
_SUB = 9


class Prog:
    ENGINES = ("sync", "gpsimd", "vector", "scalar", "tensor")

    def __init__(self, nc):
        self.nc = nc
        self.q = {e: [] for e in self.ENGINES}
        self.sem_names = (list(self.ENGINES) + ["sdma", "gdma", "cc1", "cc2", "cc3", "cst", "xt", "wb0", "wb1", "wc", "gmy", "gam", "c3", "w10", "w11", "w20", "w21", "kv0", "kv1", "qq0", "qq1"])
        self.cnt = {s: 0 for s in self.sem_names}
        self.hwm = {}
        self.wtok = {}
        self.rtok = {}
        self.sems = {}
        self.used = {s: set() for s in self.sem_names}
        self.incs = {s: {} for s in self.sem_names}
        self.remap = {s: {} for s in self.sem_names}
        self.final_inc = None
        self.btoks = []
        self.dma_owner = {}
        for sname in self.sem_names:
            if sname in ("sdma",):
                self.dma_owner[sname] = "sync"
            elif sname not in self.ENGINES and not sname.startswith("cc"):
                self.dma_owner[sname] = "gpsimd"
        self.owned = {}
        for sname, eng in self.dma_owner.items():
            self.owned.setdefault(eng, []).append(sname)
        self.boundaries = {sname: [0] for sname in self.dma_owner}

    def barrier(self, exclude=()):
        self.btoks = [(s, self.cnt[s]) for s in self.sem_names
                      if self.cnt[s] > 0 and s not in exclude]

    def snap(self, sem, val):
        if sem not in self.dma_owner:
            return val
        for b in self.boundaries[sem]:
            if b >= val:
                return b
        owner = self.dma_owner[sem]
        total = self.cnt[sem]
        self.boundaries[sem].append(total)
        self._wait_raw(owner, sem, total)
        return total

    def _wait_raw(self, eng, sem, val):
        if val <= 0 or self.hwm.get((eng, sem), 0) >= val:
            return
        self.hwm[(eng, sem)] = val
        self.used[sem].add(val)
        P = self

        def w(e, s=sem, v=val):
            e.wait_ge(P.sems[s], P.remap[s][v])
        self.q[eng].append(w)

    def _wait(self, eng, sem, val):
        self._wait_raw(eng, sem, self.snap(sem, val))

    def op(self, eng, fn, reads=(), writes=(), sem=None, inc=1, deps=()):
        dep = list(deps) + list(self.btoks)
        for b in reads:
            t = self.wtok.get(b)
            if t:
                dep.append(t)
        for b in writes:
            t = self.wtok.get(b)
            if t:
                dep.append(t)
            dep.extend(self.rtok.get(b, ()))
        sem = sem or eng
        best = {}
        for (s, v) in dep:
            if eng == "tensor" and s == "tensor":
                continue
            if v > best.get(s, 0):
                best[s] = v
        for s, v in sorted(best.items()):
            assert v <= self.cnt[s], (eng, s, v, self.cnt[s])
            self._wait(eng, s, v)
        self.cnt[sem] += inc
        raw = self.cnt[sem]
        self.incs[sem][raw] = inc
        P = self

        def run(e, f=fn, s=sem, r=raw):
            ins = f(e)
            amt = P.final_inc[s].get(r)
            if amt:
                ins.then_inc(P.sems[s], amt)
        self.q[eng].append(run)
        tok = (sem, raw)
        for b in writes:
            self.wtok[b] = tok
            self.rtok[b] = []
        for b in reads:
            self.rtok.setdefault(b, []).append(tok)
        return tok

    def finalize(self):
        self.final_inc = {}
        for s in self.sem_names:
            keep_all = s in self.dma_owner or s.startswith("cc")
            mx = max(self.used[s]) if self.used[s] else 0
            fi = {}
            run = 0
            for raw in sorted(self.incs[s]):
                inc = self.incs[s][raw]
                if keep_all or raw <= mx:
                    fi[raw] = inc
                    run += inc
                self.remap[s][raw] = run
            self.final_inc[s] = fi

    def replay(self, block):
        for ename in self.ENGINES:
            if not self.q[ename]:
                continue

            def body(e, lst=tuple(self.q[ename])):
                for f in lst:
                    f(e)
            getattr(block, ename)(body)


def mkap(base_ap, offset, dims):
    return dataclasses.replace(base_ap, offset=offset, ap=dims)


def build_nc():
    nc = bass.Bass("TRN2", target_bir_lowering=False)
    P = Prog(nc)
    fin_state = {}
    _dyn = {}

    def dynbase(key, mult):
        if key not in _dyn:
            _dyn[key] = (nc.partition_id() % 4) * mult
        return _dyn[key]

    def trunc_finish():
        out_toks = []
        for t2 in range(NT):
            out_toks.append(fin_state["sdma"](
                fin_state["out"][t2 * 128:(t2 + 1) * 128, :],
                fin_state["f32a"][:, :]))
        for tok in out_toks:
            P._wait("sync", tok[0], tok[1])
        P.finalize()
        import contextlib
        with contextlib.ExitStack() as st:
            for s2 in P.sem_names:
                P.sems[s2] = st.enter_context(nc.semaphore(s2))
            block = st.enter_context(nc.Block())
            P.replay(block)
        return nc

    def din(name, shape):
        return nc.dram_tensor(name, shape, F32, kind="ExternalInput").ap()

    xT_in = din("xT", [D, XW])
    xtok_in = din("xtok", [TOK, D])
    w_qkv = din("w_qkv", [D, 3 * D])
    bcol_qkv_in = din("bcol_qkv", [128, 24])
    w_rq = din("w_rq", [D, 256])
    bcol_rq_in = din("bcol_rq", [128, 2])
    w_rkv = din("w_rkv", [D, 512])
    bcol_rkv_in = din("bcol_rkv", [128, 4])
    w_rout = din("w_rout", [256, D])
    bcol_rout_in = din("bcol_rout", [128, 8])
    w_out = din("w_out", [D, D])
    outb_in = din("outb_bc", [128, D])
    gl_in = din("gl", [128, 1])
    gr_in = din("gr", [128, 1])
    w_Wqkv = din("w_Wqkv", [D, 3 * D])
    bcol_Wqkv_in = din("bcol_Wqkv", [128, 24])
    w_Wo = din("w_Wo", [D, D])
    bo_in = din("bo_bc", [128, D])
    ln1w_in = din("ln1w_bc", [128, D])
    ln1b_in = din("ln1b_bc", [128, D])
    ln2w_in = din("ln2w_bc", [128, D])
    ln2b_in = din("ln2b_bc", [128, D])
    w_gate = din("w_gate", [D, NE])
    w1_in = din("w1", [NE, D, FFN])
    bcol_fc1_in = din("bcol_fc1", [NE, 128, 32])
    w2_in = din("w2", [NE, FFN, D])
    fc2b_in = din("fc2b", [NE, D])
    mlo0_in = din("mlo0", [128, 128])
    mlo_in = din("mlo", [128, 128])
    mhi_in = din("mhi", [128, 128])
    mcau_in = din("mcau", [4, 128, 512])
    mlat_in = din("mlat", [64, NT])
    ident_in = din("ident", [128, 128])

    out_dram = nc.dram_tensor("out", [TOK, D], F32, kind="ExternalOutput").ap()
    dbg = {}
    if _DEBUG:
        for nm in ("xmid", "attn", "nx1"):
            dbg[nm] = nc.dram_tensor("dbg_" + nm, [TOK, D], F32,
                                     kind="ExternalOutput").ap()

    cc1_in = nc.dram_tensor("cc1_in", [D, 16], BF16).ap()
    cc1_out = nc.dram_tensor("cc1_out", [4 * D, 16], BF16).ap()
    cc2kv_in = nc.dram_tensor("cc2kv_in", [2 * D, TOK], BF16).ap()
    cc2kv_out = nc.dram_tensor("cc2kv_out", [8 * D, TOK], BF16).ap()
    cc2q_in = nc.dram_tensor("cc2q_in", [D, TOK], BF16).ap()
    cc2q_out = nc.dram_tensor("cc2q_out", [4 * D, TOK], BF16).ap()
    cc3_in = nc.dram_tensor("cc3_in", [4 * HD, S], BF16).ap()
    cc3_out = nc.dram_tensor("cc3_out", [D, S], BF16).ap()
    qm_d = nc.dram_tensor("qm_d", [4 * HD, S], BF16).ap()
    km_d = nc.dram_tensor("km_d", [4 * HD, S], BF16).ap()
    vm_d = nc.dram_tensor("vm_d", [4 * HD, S], BF16).ap()
    am_d = nc.dram_tensor("am_d", [D, TOK], BF16).ap()
    w1bf = nc.dram_tensor("w1bf", [NE, D, FFN], BF16).ap()
    w2bf = nc.dram_tensor("w2bf", [NE, FFN, D], BF16).ap()

    SB_LIMIT = 212800
    SB_BASE = 16512
    sb = {}

    def sal(name, shape, dt, at):
        per = int(np.prod(shape[1:])) * mybir.dt.size(dt)
        assert at % 32 == 0 and at + per <= SB_LIMIT, (name, at, per, at + per)
        sb[name] = nc.alloc_sbuf_tensor_at(name, list(shape), dt,
                                           offset=at + SB_BASE)
        return sb[name]

    ident_bf = sal("ident_bf", [128, 128], BF16, 0)
    sal("mlo0", [128, 128], BF16, 256)
    sal("mlo", [128, 128], BF16, 512)
    sal("mhi", [128, 128], BF16, 768)
    mlat = sal("mlat", [64, NT], F32, 1024)
    sal("bcol_qkv", [128, 24], F32, 1088)
    sal("bcol_rq", [128, 2], F32, 1184)
    sal("bcol_rkv", [128, 4], F32, 1216)
    sal("bcol_rout", [128, 8], F32, 1248)
    sal("bcol_Wqkv", [128, 24], F32, 1280)
    gl_sb = sal("gl", [128, 1], BF16, 1376)
    gr_sb = sal("gr", [128, 1], BF16, 1408)
    ones_b = sal("ones_b", [128, 1], BF16, 1440)
    ones_f = sal("ones_f", [1, 128], F32, 1472)
    bcA = sal("bcA", [128, D], BF16, 1984)
    bcB = sal("bcB", [128, D], BF16, 4032)
    bcC = sal("bcC", [128, D], BF16, 6080)
    cols = sal("cols", [128, 8], F32, 8128)
    mk1 = sal("mk1", [1, 16], F32, 8160)
    gateW = sal("gateW_bf", [128, NE * 8], BF16, 8224)
    PB = 8352

    SLOT_A = PB
    SLOT_K = PB + 16384
    SLOT_Q = PB + 51200
    SLOT_V = PB + 83968
    SLOT_R = PB + 118784
    SLOT_M = PB + 126976
    SCR = PB + 159744
    assert SCR == 168096

    xTq = sal("xTq", [128, 8 * 512], BF16, SLOT_A)
    woutT = sal("woutT", [128, 8 * D], BF16, SLOT_A)
    kT = sal("kT", [128, 8 * XW], BF16, SLOT_K)
    qT1 = sal("qT1", [128, 8 * TOK], BF16, SLOT_Q)
    vtok = sal("vtok", [128, NTX * D], BF16, SLOT_V)
    rqT = sal("rqT", [128, 2 * TOK], BF16, SLOT_R)
    xmid = sal("xmid", [128, NT * D], BF16, SLOT_M)

    _o = [SCR]

    def bump(n):
        a = _o[0]
        _o[0] = (a + n + 31) // 32 * 32
        return a

    wbuf = [sal(f"wbuf{i}", [128, 8 * 128], BF16, bump(2048)) for i in range(2)]
    pool_f = sal("pool_f", [128, 8 * 16], F32, bump(512))
    pooledT = sal("pooledT", [128, 8 * 64], BF16, bump(1024))
    rkT = sal("rkT", [128, 2 * 64], BF16, bump(256))
    rvT = sal("rvT", [128, 2 * 64], BF16, bump(256))
    rkhat = sal("rkhat", [65, LH * 64], BF16, bump(512))
    rvhat = sal("rvhat", [64, LH * 65], BF16, bump(544))
    rv_tok = sal("rv_tok", [64, 256], BF16, bump(512))
    cneg_lat = sal("cneg_lat", [LH, TOK], BF16, bump(4096))
    cneg_loc = sal("cneg_loc", [NH, TOK], BF16, bump(4096))
    sq64 = sal("sq64", [64, 512], BF16, bump(1024))
    n2row = sal("n2row", [1, 512], F32, bump(2048))
    khatb = [sal(f"khatb{i}", [65, 256], BF16, bump(512)) for i in range(2)]
    vhatb = [sal(f"vhatb{i}", [128, 130], BF16, bump(288)) for i in range(2)]
    qhatb = [sal(f"qhatb{i}", [65, 128], BF16, bump(256)) for i in range(2)]
    latqb = [sal(f"latqb{i}", [65, 128], BF16, bump(256)) for i in range(2)]
    ptile = [sal(f"ptile{i}", [128, 512], BF16, bump(1024)) for i in range(2)]
    tloc = [sal(f"tloc{i}", [65, 128], BF16, bump(256)) for i in range(2)]
    lr = [sal(f"lr{i}", [1, 128], F32, bump(512)) for i in range(2)]
    grow = [sal(f"grow{i}", [1, 384], F32, bump(1536)) for i in range(2)]
    bcsb = [sal(f"bcs{i}", [128, 256], BF16, bump(512)) for i in range(2)]
    remh = [sal(f"remh{i}", [64, 128], BF16, bump(256)) for i in range(2)]
    latnT = sal("latnT", [128, 2 * 128], BF16, bump(512))
    remT = sal("remT", [128, 8 * 128], BF16, bump(2048))
    mixT = sal("mixT", [128, 8 * 128], BF16, bump(2048))
    f32a = sal("f32a", [128, D], F32, bump(4096))
    f32b = sal("f32b", [128, 512], F32, bump(2048))
    rout_sb = sal("rout_sb", [128, 2 * D], BF16, bump(4096))
    assert _o[0] <= SB_LIMIT, _o[0]

    _oe = [SCR]

    def bumpe(n):
        a = _oe[0]
        _oe[0] = (a + n + 31) // 32 * 32
        return a

    for i in (2, 3):
        khatb.append(sal(f"khatb{i}", [65, 256], BF16, bumpe(512)))
        vhatb.append(sal(f"vhatb{i}", [128, 130], BF16, bumpe(288)))
        qhatb.append(sal(f"qhatb{i}", [65, 128], BF16, bumpe(256)))
        latqb.append(sal(f"latqb{i}", [65, 128], BF16, bumpe(256)))
        tloc.append(sal(f"tloc{i}", [65, 128], BF16, bumpe(256)))
    assert _oe[0] <= SCR + 4096, _oe[0]
    _oe[0] = SCR + 4096
    ptl4 = [ptile[0], ptile[1],
            sal("ptl2", [128, 256], BF16, bumpe(512)),
            sal("ptl3", [128, 256], BF16, bumpe(512))]
    assert _oe[0] <= SCR + 5632, _oe[0]
    _oe[0] = SCR + 15904
    for i in (2, 3):
        lr.append(sal(f"lr{i}", [1, 128], F32, bumpe(512)))
        bcsb.append(sal(f"bcs{i}", [128, 256], BF16, bumpe(512)))
    assert _oe[0] <= SCR + 18976, _oe[0]

    khat2 = [sal(f"khat2_{i}", [65, S], BF16, SLOT_K + i * 16384) for i in range(2)]
    vth = [sal(f"vth{i}", [64, TOK], BF16, SLOT_Q + i * 4096) for i in range(2)]
    vhat2 = [sal(f"vhat2_{i}", [128, 64 * 65], BF16, SLOT_Q + 8192 + i * 8320)
             for i in range(2)]
    attnT2 = sal("attnT2", [128, 2 * S], BF16, SLOT_V)
    nx1T = sal("nx1T", [128, 8 * TOK], BF16, SLOT_V)
    qhat2 = [sal(f"qhat2_{i}", [65, 512], BF16, SCR + i * 1024) for i in range(2)]
    pt_extra = [sal(f"ptx{i}", [128, 512], BF16, SCR + 13344 + i * 1024)
                for i in range(6)]
    ksq2 = sal("ksq2", [64, 512], BF16, SCR + 2048)
    l2 = [sal(f"l2_{i}", [1, 512], F32, SCR + 3072 + i * 2048) for i in range(2)]
    bcs2 = [sal(f"bcs2_{i}", [64, 512], BF16, SCR + 7168 + i * 1024) for i in range(2)]
    mk2 = sal("mk2", [1, 8], F32, SCR + 9216)
    mcau = sal("mcau", [128, 4 * 512], BF16, SCR + 9248)

    nx2T = sal("nx2T", [128, 8 * TOK], BF16, SLOT_K)
    moe_acc = sal("moe_acc", [128, NT * D], F32, SLOT_Q)
    hT = sal("hT", [128, 32 * 512], BF16, SLOT_M)
    w1buf = [sal(f"w1buf{i}", [128, 8 * 512], BF16, SCR + i * 8192) for i in range(2)]
    w2buf = [sal(f"w2buf{i}", [128, 2 * D], BF16, SCR + 16384 + i * 4096)
             for i in range(2)]
    _o3 = [SCR + 24576]

    def bump3(n):
        a = _o3[0]
        _o3[0] = (a + n + 31) // 32 * 32
        return a

    attnTo_t = [sal(f"attnTo{i}", [128, 8 * 128], BF16, bump3(2048))
                for i in range(2)]
    f3a = sal("f3a", [128, D], F32, bump3(4096))
    f3b = sal("f3b", [128, 512], F32, bump3(2048))
    rT3 = sal("rT3", [128, 8 * 128], BF16, bump3(2048))
    gates = sal("gates", [128, NT * 8], F32, bump3(512))
    g8T = sal("g8T", [8, 128], F32, bump3(512))
    ident_f = sal("ident_f", [128, 128], F32, bump3(512))
    lg8 = sal("lg8", [128, 8], F32, bump3(32))
    is1 = sal("is1", [128, 8], F32, bump3(32))
    is2 = sal("is2", [128, 8], F32, bump3(32))
    msk8 = sal("msk8", [128, 8], F32, bump3(32))
    w12 = sal("w12", [128, 4], F32, bump3(32))
    fc2b = sal("fc2b", [NE, D], F32, bump3(4096))
    fc1b = sal("fc1b", [128, NE * 32], F32, bump3(1024))
    assert _o3[0] <= SB_LIMIT, _o3[0]
    xres_t = sal("xres_t", [128, D], F32, SLOT_R)
    nx2f = sal("nx2f", [128, D], F32, SLOT_R + 4096)

    ps = {i: nc.alloc_psum_tensor(f"ps{i}", [128, 512], F32) for i in range(7)}
    psT = nc.alloc_psum_tensor("psT", [128, 1024], BF16)

    def sdma(dst, src, reads=(), writes=(), deps=()):
        return P.op("sync", lambda e: e.dma_start(out=dst, in_=src),
                    reads=reads, writes=writes, sem="sdma", inc=16, deps=deps)

    def gdma(dst, src, reads=(), writes=(), deps=(), dsem="gdma"):
        return P.op("gpsimd", lambda e: e.dma_start(out=dst, in_=src),
                    reads=reads, writes=writes, sem=dsem, inc=16, deps=deps)

    def mm(out, lhsT, rhs, start, stop, reads=(), writes=(), deps=()):
        return P.op("tensor",
                    lambda e: e.matmul(out, lhsT, rhs, start=start, stop=stop),
                    reads=reads, writes=writes, deps=deps)

    def tpose(out, in_, ident, reads=(), writes=(), deps=()):
        return P.op("tensor", lambda e: e.transpose(out, in_, ident),
                    reads=reads, writes=writes, deps=deps)

    def vo(fn, reads=(), writes=(), deps=()):
        return P.op("vector", fn, reads=reads, writes=writes, deps=deps)

    def so(fn, reads=(), writes=(), deps=()):
        return P.op("scalar", fn, reads=reads, writes=writes, deps=deps)

    def go(fn, reads=(), writes=(), deps=()):
        return P.op("gpsimd", fn, reads=reads, writes=writes, deps=deps)

    def plain_ln(src_ap, src_name, fin, sq_scr, sq_name):
        vo(lambda e: e.reduce_sum(cols[:, 0:1], src_ap, axis=AX.X),
           reads=(src_name,), writes=("c0",))
        so(lambda e: e.activation(sq_scr[:, 0:512], src_ap[:, 0:512], AF.Square,
                                  accum_out=cols[:, 5:6]),
           reads=(src_name,), writes=(sq_name, "c5"))
        so(lambda e: e.activation(sq_scr[:, 0:512], src_ap[:, 512:1024], AF.Square,
                                  accum_out=cols[:, 6:7]),
           reads=(src_name,), writes=(sq_name, "c6"))
        vo(lambda e: e.tensor_add(cols[:, 5:6], cols[:, 5:6], cols[:, 6:7]),
           reads=("c5", "c6"), writes=("c5",))
        vo(lambda e: e.tensor_scalar_mul(cols[:, 0:1], cols[:, 0:1], 1.0 / D),
           reads=("c0",), writes=("c0",))
        vo(lambda e: e.tensor_mul(cols[:, 1:2], cols[:, 0:1], cols[:, 0:1]),
           reads=("c0",), writes=("c1",))
        vo(lambda e: e.tensor_scalar(cols[:, 2:3], cols[:, 5:6], 1.0 / D, None,
                                     ALU.mult),
           reads=("c5",), writes=("c2",))
        vo(lambda e: e.tensor_sub(cols[:, 2:3], cols[:, 2:3], cols[:, 1:2]),
           reads=("c2", "c1"), writes=("c2",))
        so(lambda e: e.activation(cols[:, 3:4], cols[:, 2:3], AF.Ln,
                                  bias=np.float32(1e-5)),
           reads=("c2",), writes=("c3",))
        so(lambda e: e.activation(cols[:, 4:5], cols[:, 3:4], AF.Exp,
                                  scale=np.float32(-0.5)),
           reads=("c3",), writes=("c4",))
        fin(cols[:, 0:1], cols[:, 4:5])

    fin_state["sdma"] = sdma
    fin_state["out"] = out_dram
    fin_state["f32a"] = f32a

    gdma(ident_bf[:, :], ident_in[:, :], writes=("ident_bf",), dsem="cst")
    gdma(sb["mlo0"][:, :], mlo0_in[:, :], writes=("mlo0",), dsem="cst")
    gdma(sb["mlo"][:, :], mlo_in[:, :], writes=("mlo",), dsem="cst")
    gdma(sb["mhi"][:, :], mhi_in[:, :], writes=("mhi",), dsem="cst")
    sdma(mlat[:, :], mlat_in[:, :], writes=("mlat",))
    for nm, src in [("bcol_qkv", bcol_qkv_in), ("bcol_rq", bcol_rq_in),
                    ("bcol_rkv", bcol_rkv_in), ("bcol_rout", bcol_rout_in),
                    ("bcol_Wqkv", bcol_Wqkv_in)]:
        sdma(sb[nm][:, :], src[:, :], writes=(nm,))
    gdma(bcA[:, :], outb_in[:, :], writes=("bcA",), dsem="cst")
    gdma(bcB[:, :], ln1w_in[:, :], writes=("bcB",), dsem="cst")
    gdma(bcC[:, :], ln1b_in[:, :], writes=("bcC",), dsem="cst")
    gdma(gl_sb[:, :], gl_in[:, :], writes=("gl",), dsem="cst")
    gdma(gr_sb[:, :], gr_in[:, :], writes=("gr",), dsem="cst")
    cst_last = gdma(gateW[:, :].rearrange("p (r e) -> p r e", r=8),
                    mkap(w_gate, 0, [[NE, 128], [NE * 128, 8], [1, NE]]),
                    writes=("gateW",), dsem="cst")
    for nm in ("ident_bf", "mlo0", "mlo", "mhi", "bcA", "bcB", "bcC",
               "gl", "gr", "gateW"):
        P.wtok[nm] = cst_last
    vo(lambda e: e.memset(ones_b[:, :], 1.0), writes=("ones_b",))
    vo(lambda e: e.memset(ones_f[:, :], 1.0), writes=("ones_f",))
    for i in range(2):
        vo(lambda e, i=i: e.memset(khatb[i][64:65, :], 1.0), writes=(f"khatb{i}_ones",))
        vo(lambda e, i=i: e.memset(vhatb[i][:, 64:65], 1.0), writes=(f"vhatb{i}_o1",))
        vo(lambda e, i=i: e.memset(vhatb[i][:, 129:130], 1.0), writes=(f"vhatb{i}_o2",))
    def p1e_presets():
        for i in (2, 3):
            vo(lambda e, i=i: e.memset(khatb[i][64:65, :], 1.0),
               writes=(f"khatb{i}_ones",))
            vo(lambda e, i=i: e.memset(vhatb[i][:, 64:65], 1.0),
               writes=(f"vhatb{i}_o1",))
            vo(lambda e, i=i: e.memset(vhatb[i][:, 129:130], 1.0),
               writes=(f"vhatb{i}_o2",))
    vo(lambda e: e.memset(rkhat[64:65, :], 1.0), writes=("rkhat_ones",))
    for h in range(LH):
        vo(lambda e, h=h: e.memset(rvhat[:, h * 65 + 64:h * 65 + 65], 1.0),
           writes=(f"rvhat_o{h}",))

    psrot = [0]

    def nps():
        r = psrot[0] % 4
        psrot[0] += 1
        return r

    wrot = [0]

    def load_w(w_ap, wtot, ci, ncols=128):
        sl = wrot[0] % 2
        wrot[0] += 1
        t = gdma(wbuf[sl][:, 0:8 * ncols].rearrange("p (r c) -> p r c", r=8),
                 mkap(w_ap, ci * ncols,
                      [[wtot, 128], [128 * wtot, 8], [1, ncols]]),
                 writes=(f"wbuf{sl}",), dsem=f"wb{sl}")
        return sl, t

    XCH = [(0, 512), (512, 512), (1024, 512), (1536, 512), (2048, 128)]
    for (x0, xw) in XCH:
        txq = gdma(xTq[:, 0:8 * xw].rearrange("p (r w) -> p r w", r=8),
                   mkap(xT_in, x0, [[XW, 128], [128 * XW, 8], [1, xw]]),
                   writes=("xTq",), dsem="xt")
        for li0 in range(xw // 128):
            gtok = x0 + li0 * 128
            if gtok < HALO:
                continue
            li = (gtok - HALO) // 128
            for r in range(8):
                vo(lambda e, r=r, li0=li0, li=li, xw=xw: e.reduce_sum(
                    pool_f[:, r * 16 + li:r * 16 + li + 1],
                    xTq[:, r * xw + li0 * 128: r * xw + (li0 + 1) * 128],
                    axis=AX.X),
                    reads=("xTq",), writes=(f"pool_{r}_{li}",), deps=[txq])
        for ci in range(26):
            if ci < 24:
                w_ap, wtot, wci = w_qkv, 3 * D, ci
            else:
                w_ap, wtot, wci = w_rq, 256, ci - 24
            sl, tw = load_w(w_ap, wtot, wci)
            pn = nps()
            for r in range(8):
                mm(ps[pn][:, 0:xw],
                   wbuf[sl][:, r * 128:(r + 1) * 128],
                   xTq[:, r * xw:(r + 1) * xw],
                   start=(r == 0), stop=(r == 7),
                   reads=(f"wbuf{sl}", "xTq"), writes=(f"ps{pn}",), deps=[tw, txq])
            if ci < 8:
                lo = max(x0, HALO)
                if lo < x0 + xw:
                    off, wd = lo - x0, x0 + xw - lo
                    vo(lambda e, pn=pn, ci=ci, lo=lo, off=off, wd=wd: e.tensor_scalar(
                        qT1[:, ci * TOK + lo - HALO: ci * TOK + lo - HALO + wd],
                        ps[pn][:, off:off + wd], sb["bcol_qkv"][:, ci:ci + 1],
                        0.125, ALU.add, ALU.mult),
                        reads=(f"ps{pn}", "bcol_qkv"), writes=(f"qT1_{ci}",))
            elif ci < 16:
                kc = ci - 8
                vo(lambda e, pn=pn, ci=ci, kc=kc, x0=x0, xw=xw: e.tensor_single_scalar(
                    kT[:, kc * XW + x0: kc * XW + x0 + xw],
                    ps[pn][:, 0:xw], sb["bcol_qkv"][:, ci:ci + 1], ALU.add),
                    reads=(f"ps{pn}", "bcol_qkv"), writes=(f"kT_{kc}",))
            elif ci < 24:
                vc = ci - 16
                tv = vo(lambda e, pn=pn, ci=ci, xw=xw: e.tensor_single_scalar(
                    ptile[1][:, 0:xw], ps[pn][:, 0:xw],
                    sb["bcol_qkv"][:, ci:ci + 1], ALU.add),
                    reads=(f"ps{pn}", "bcol_qkv"), writes=("ptile1",))
                nt_ = xw // 128
                for u in range(nt_):
                    tpose(psT[:, u * 128:(u + 1) * 128],
                          ptile[1][:, u * 128:(u + 1) * 128], ident_bf[:, :],
                          reads=("ptile1", "ident_bf"), writes=("psT",), deps=[tv])
                tt0 = x0 // 128
                for u in range(nt_):
                    vo(lambda e, u=u, tt0=tt0, vc=vc: e.tensor_copy(
                        vtok[:, (tt0 + u) * D + vc * 128:(tt0 + u) * D + (vc + 1) * 128],
                        psT[:, u * 128:(u + 1) * 128]),
                        reads=("psT",), writes=(f"vtok_{tt0 + u}",))
            else:
                rc = ci - 24
                lo = max(x0, HALO)
                if lo < x0 + xw:
                    off, wd = lo - x0, x0 + xw - lo
                    vo(lambda e, pn=pn, rc=rc, lo=lo, off=off, wd=wd: e.tensor_scalar(
                        rqT[:, rc * TOK + lo - HALO: rc * TOK + lo - HALO + wd],
                        ps[pn][:, off:off + wd], sb["bcol_rq"][:, rc:rc + 1],
                        0.125, ALU.add, ALU.mult),
                        reads=(f"ps{pn}", "bcol_rq"), writes=(f"rqT_{rc}",))

    if _TRUNC == "P1a":
        return trunc_finish()
    tpool = vo(lambda e: e.tensor_scalar_mul(pooledT[:, 0:128], pool_f[:, :],
                                             1.0 / 128),
               reads=tuple(f"pool_{r}_{li}" for r in range(8) for li in range(16)),
               writes=("pool_own",))
    tc1 = sdma(mkap(cc1_in, 0, [[16, 128], [2048, 8], [1, 16]]),
               pooledT[:, 0:128].rearrange("p (r l) -> p r l", r=8),
               reads=("pool_own",), writes=("cc1_in",))
    tcc1 = P.op("gpsimd", lambda e: e.collective_compute(
        "AllGather", ALU.bypass, replica_groups=[[0, 1, 2, 3], [4, 5, 6, 7]],
        ins=[cc1_in.opt()], outs=[cc1_out.opt()]),
        reads=("cc1_in",), writes=("cc1_out",), sem="cc1", inc=1, deps=[tc1])
    for rr in range(4):
        tpoolg = sdma(
            pooledT[:, :].rearrange("p (d l) -> p d l", d=8)[:, :, rr * 16:(rr + 1) * 16],
            mkap(cc1_out, rr * 16384, [[16, 128], [2048, 8], [1, 16]]),
            reads=("cc1_out",), writes=("pooledT",), deps=[tcc1])

    towt = gdma(woutT[:, :].rearrange("p (r c) -> p r c", r=8),
                mkap(w_out, 0, [[D, 128], [128 * D, 8], [1, D]]),
                writes=("woutT",), dsem="cst",
                deps=[P.wtok["xTq"]] + list(P.rtok.get("xTq", [])))

    if _TRUNC == "P1b0":
        return trunc_finish()
    for ci in range(4):
        sl, tw = load_w(w_rkv, 512, ci)
        pn = nps()
        for r in range(8):
            mm(ps[pn][:, 0:64],
               wbuf[sl][:, r * 128:(r + 1) * 128],
               pooledT[:, r * 64:(r + 1) * 64],
               start=(r == 0), stop=(r == 7),
               reads=(f"wbuf{sl}", "pooledT"), writes=(f"ps{pn}",),
               deps=[tw, tpoolg])
        dst, dnm = (rkT, "rkT") if ci < 2 else (rvT, "rvT")
        cc = ci % 2
        vo(lambda e, pn=pn, ci=ci, dst=dst, cc=cc: e.tensor_single_scalar(
            dst[:, cc * 64:(cc + 1) * 64], ps[pn][:, 0:64],
            sb["bcol_rkv"][:, ci:ci + 1], ALU.add),
            reads=(f"ps{pn}", "bcol_rkv"), writes=(f"{dnm}_{cc}",))
    for h in range(LH):
        go(lambda e, h=h: e.tensor_copy(
            rkhat[0:64, h * 64:(h + 1) * 64],
            rkT[(h % 2) * 64:(h % 2) * 64 + 64, (h // 2) * 64:(h // 2) * 64 + 64]),
            reads=(f"rkT_{h // 2}",), writes=(f"rkhat_{h}",))
    for cchunk in range(2):
        tpose(psT[0:64, 0:128], rvT[:, cchunk * 64:(cchunk + 1) * 64], ident_bf[:, :],
              reads=(f"rvT_{cchunk}", "ident_bf"), writes=("psT",))
        vo(lambda e, cchunk=cchunk: e.tensor_copy(
            rv_tok[0:64, cchunk * 128:(cchunk + 1) * 128], psT[0:64, 0:128]),
            reads=("psT",), writes=(f"rv_tok_{cchunk}",))
    for h in range(LH):
        go(lambda e, h=h: e.tensor_copy(
            rvhat[0:64, h * 65:h * 65 + 64], rv_tok[0:64, h * 64:(h + 1) * 64]),
            reads=(f"rv_tok_{h // 2}",), writes=(f"rvhat_{h}",))

    def shift_rows(n_heads, kbuf, kname_fn, kw, qbuf, qname_fn, dst, dname_fn, mko):
        for h in range(n_heads):
            hp, hc = (h % 2) * 64, h // 2
            first = True
            for g0 in range(0, kw, 512):
                gw = min(512, kw - g0)
                t1 = so(lambda e, hp=hp, hc=hc, g0=g0, gw=gw: e.activation(
                    sq64[0:64, 0:gw],
                    kbuf[hp:hp + 64, hc * kw + g0: hc * kw + g0 + gw], AF.Square),
                    reads=(kname_fn(hc),), writes=("sq64",))
                t2 = mm(ps[6][0:1, 0:gw], ones_b[0:64, 0:1], sq64[0:64, 0:gw],
                        start=True, stop=True, reads=("sq64", "ones_b"),
                        writes=("ps6",), deps=[t1])
                t3 = vo(lambda e, gw=gw: e.reduce_max(cols[0:1, 7:8],
                                                      ps[6][0:1, 0:gw], axis=AX.X),
                        reads=("ps6",), writes=("c7",))
                if first:
                    vo(lambda e, h=h, mko=mko: e.tensor_copy(
                        mk1[:, mko + h:mko + h + 1], cols[0:1, 7:8]),
                        reads=("c7",), writes=(f"mk_{mko + h}",))
                    first = False
                else:
                    vo(lambda e, h=h, mko=mko: e.tensor_max(
                        mk1[:, mko + h:mko + h + 1], mk1[:, mko + h:mko + h + 1],
                        cols[0:1, 7:8]),
                        reads=("c7", f"mk_{mko + h}"), writes=(f"mk_{mko + h}",))
            for g0 in range(0, TOK, 512):
                t4 = so(lambda e, hp=hp, hc=hc, g0=g0: e.activation(
                    sq64[0:64, 0:512],
                    qbuf[hp:hp + 64, hc * TOK + g0: hc * TOK + g0 + 512], AF.Square),
                    reads=(qname_fn(hc),), writes=("sq64",))
                t5 = mm(ps[6][0:1, 0:512], ones_b[0:64, 0:1], sq64[0:64, 0:512],
                        start=True, stop=True, reads=("sq64", "ones_b"),
                        writes=("ps6",), deps=[t4])
                vo(lambda e: e.tensor_copy(n2row[:, 0:512], ps[6][0:1, 0:512]),
                   reads=("ps6",), writes=("n2row",))
                vo(lambda e, h=h, mko=mko: e.tensor_single_scalar(
                    n2row[:, 0:512], n2row[:, 0:512], mk1[:, mko + h:mko + h + 1],
                    ALU.mult),
                    reads=("n2row", f"mk_{mko + h}"), writes=("n2row",))
                so(lambda e: e.activation(n2row[:, 0:512], n2row[:, 0:512], AF.Ln,
                                          bias=np.float32(1e-30)),
                   reads=("n2row",), writes=("n2row",))
                so(lambda e: e.activation(n2row[:, 0:512], n2row[:, 0:512], AF.Exp,
                                          scale=np.float32(-0.5) if False else np.float32(0.5)),
                   reads=("n2row",), writes=("n2row",))
                vo(lambda e: e.tensor_scalar_mul(ptile[1][0:1, 0:512],
                                                 n2row[:, 0:512], -1.0),
                   reads=("n2row",), writes=("ptile1",))
                sdma(dst[h:h + 1, g0:g0 + 512], ptile[1][0:1, 0:512],
                     reads=("ptile1",), writes=(dname_fn(h),))

    shift_rows(LH, rkT, lambda hc: f"rkT_{hc}", 64,
               rqT, lambda hc: f"rqT_{hc}", cneg_lat, lambda h: f"cneg_lat{h}", 0)
    shift_rows(NH, kT, lambda hc: f"kT_{hc}", XW,
               qT1, lambda hc: f"qT1_{hc}", cneg_loc, lambda h: f"cneg_loc{h}", 0)

    trout = gdma(rout_sb[:, :].rearrange("p (r c) -> p r c", r=2),
                 mkap(w_rout, 0, [[D, 128], [128 * D, 2], [1, D]]),
                 writes=("rout_sb",), dsem="cst", deps=[towt])
    P.wtok["woutT"] = trout

    if _TRUNC == "P1b":
        return trunc_finish()
    for i in range(2):
        if f"ptile{i}" in P.wtok:
            P.wtok[f"ptl{i}"] = P.wtok[f"ptile{i}"]
        P.rtok[f"ptl{i}"] = list(P.rtok.get(f"ptile{i}", []))
    p1e_presets()
    for t in range(NT):
        for h in range(LH if _SUB >= 1 else 0):
            par = (t * LH + h) % 4
            bpar = par % 2
            lq = latqb[par]
            lqn = f"latqb{par}"
            hp, hc = (h % 2) * 64, h // 2
            ta = go(lambda e, hp=hp, hc=hc, t=t, lq=lq: e.tensor_copy(
                lq[0:64, :], rqT[hp:hp + 64, hc * TOK + t * 128: hc * TOK + (t + 1) * 128]),
                reads=(f"rqT_{hc}",), writes=(lqn,))
            tb = sdma(lq[64:65, :], cneg_lat[h:h + 1, t * 128:(t + 1) * 128],
                      reads=(f"cneg_lat{h}",), writes=(lqn,))
            pA = bpar
            mm(ps[pA][0:64, 0:128], rkhat[:, h * 64:(h + 1) * 64], lq[:, :],
               start=True, stop=True,
               reads=(lqn, f"rkhat_{h}", "rkhat_ones"), writes=(f"ps{pA}",),
               deps=[ta, tb])
            so(lambda e, t=t, pA=pA, par=par: e.activation(
                ptl4[par][0:64, 0:128], ps[pA][0:64, 0:128], AF.Exp,
                bias=mlat[:, t:t + 1]),
                reads=(f"ps{pA}", "mlat"), writes=(f"ptl{par}",))
            pE = 2 + bpar
            mm(ps[pE][0:65, 0:128], rvhat[0:64, h * 65:(h + 1) * 65],
               ptl4[par][0:64, 0:128], start=True, stop=True,
               reads=(f"ptl{par}", f"rvhat_{h}", f"rvhat_o{h}"),
               writes=(f"ps{pE}",))
            vo(lambda e, pE=pE, par=par: e.tensor_scalar_max(
                lr[par][:, 0:128], ps[pE][64:65, 0:128], 1e-30),
                reads=(f"ps{pE}",), writes=(f"lr{par}",))
            vo(lambda e, par=par: e.reciprocal(lr[par][:, 0:128], lr[par][:, 0:128]),
               reads=(f"lr{par}",), writes=(f"lr{par}",))
            pG = 4 + bpar
            mm(ps[pG][:, 0:128], ones_f[0:1, 0:128], lr[par][:, 0:128],
               start=True, stop=True, reads=(f"lr{par}", "ones_f"),
               writes=(f"ps{pG}",))
            vo(lambda e, pG=pG, par=par: e.tensor_copy(
                bcsb[par][:, 0:128], ps[pG][:, 0:128]),
                reads=(f"ps{pG}",), writes=(f"bcs{par}",))
            vo(lambda e, hp=hp, hc=hc, pE=pE, par=par: e.tensor_tensor(
                latnT[hp:hp + 64, hc * 128: hc * 128 + 128],
                ps[pE][0:64, 0:128], bcsb[par][0:64, 0:128], ALU.mult),
                reads=(f"ps{pE}", f"bcs{par}"), writes=(f"latnT_{h}",))
        for ci in range(8 if _SUB >= 2 else 0):
            pn = nps()
            for r in range(2):
                mm(ps[pn][:, 0:128],
                   rout_sb[:, r * D + ci * 128: r * D + (ci + 1) * 128],
                   latnT[:, r * 128:(r + 1) * 128],
                   start=(r == 0), stop=(r == 1),
                   reads=("rout_sb",) + tuple(f"latnT_{hh}" for hh in range(LH)),
                   writes=(f"ps{pn}",), deps=[trout])
            vo(lambda e, pn=pn, ci=ci: e.tensor_single_scalar(
                remT[:, ci * 128:(ci + 1) * 128], ps[pn][:, 0:128],
                sb["bcol_rout"][:, ci:ci + 1], ALU.add),
                reads=(f"ps{pn}", "bcol_rout"), writes=(f"remT_{ci}",))
        for h in range(NH if _SUB >= 3 else 0):
            par = (t * NH + h) % 4
            bpar = par % 2
            kb, vb, qb = khatb[par], vhatb[par], qhatb[par]
            kbn, vbn, qbn = f"khatb{par}", f"vhatb{par}", f"qhatb{par}"
            hp, hc = (h % 2) * 64, h // 2
            tk = go(lambda e, hp=hp, hc=hc, t=t, kb=kb: e.tensor_copy(
                kb[0:64, :], kT[hp:hp + 64, hc * XW + t * 128: hc * XW + t * 128 + 256]),
                reads=(f"kT_{hc}",), writes=(kbn,))
            tv1 = go(lambda e, h=h, t=t, vb=vb: e.tensor_copy(
                vb[:, 0:64], vtok[:, t * D + h * 64: t * D + h * 64 + 64]),
                reads=(f"vtok_{t}",), writes=(vbn,))
            tv2 = go(lambda e, h=h, t=t, vb=vb: e.tensor_copy(
                vb[:, 65:129], vtok[:, (t + 1) * D + h * 64:(t + 1) * D + h * 64 + 64]),
                reads=(f"vtok_{t + 1}",), writes=(vbn,))
            trh = go(lambda e, hp=hp, hc=hc, par=par: e.tensor_copy(
                remh[par % 2][0:64, :], remT[hp:hp + 64, hc * 128: hc * 128 + 128]),
                reads=(f"remT_{hc}",), writes=(f"remh{par % 2}",))
            tq1 = go(lambda e, hp=hp, hc=hc, t=t, qb=qb: e.tensor_copy(
                qb[0:64, :], qT1[hp:hp + 64, hc * TOK + t * 128: hc * TOK + (t + 1) * 128]),
                reads=(f"qT1_{hc}",), writes=(qbn,))
            tq2 = sdma(qb[64:65, :], cneg_loc[h:h + 1, t * 128:(t + 1) * 128],
                       reads=(f"cneg_loc{h}",), writes=(qbn,))
            pA = bpar
            mm(ps[pA][:, 0:128], kb[:, 0:128], qb[:, :], start=True, stop=True,
               reads=(kbn, qbn, f"khatb{par}_ones"), writes=(f"ps{pA}",),
               deps=[tk, tq1, tq2])
            mm(ps[pA][:, 128:256], kb[:, 128:256], qb[:, :], start=True, stop=True,
               reads=(kbn, qbn), writes=(f"ps{pA}",))
            mlon = "mlo0" if t == 0 else "mlo"
            vo(lambda e, pA=pA, mlon=mlon: e.tensor_tensor(
                ps[pA][:, 0:128], ps[pA][:, 0:128], sb[mlon][:, :], ALU.add),
                reads=(f"ps{pA}", mlon), writes=(f"ps{pA}",))
            vo(lambda e, pA=pA: e.tensor_tensor(
                ps[pA][:, 128:256], ps[pA][:, 128:256], sb["mhi"][:, :], ALU.add),
                reads=(f"ps{pA}", "mhi"), writes=(f"ps{pA}",))
            so(lambda e, pA=pA, par=par: e.activation(
                ptl4[par][:, 0:256], ps[pA][:, 0:256], AF.Exp),
                reads=(f"ps{pA}",), writes=(f"ptl{par}",))
            if _SUB < 4:
                continue
            pE = 2 + bpar
            mm(ps[pE][0:65, 0:128], vb[:, 0:65], ptl4[par][:, 0:128],
               start=True, stop=False,
               reads=(vbn, f"ptl{par}", f"vhatb{par}_o1"), writes=(f"ps{pE}",),
               deps=[tv1])
            mm(ps[pE][0:65, 0:128], vb[:, 65:130], ptl4[par][:, 128:256],
               start=False, stop=True,
               reads=(vbn, f"ptl{par}", f"vhatb{par}_o2"), writes=(f"ps{pE}",),
               deps=[tv2])
            tl1 = vo(lambda e, pE=pE, par=par: e.tensor_copy(
                tloc[par][0:64, :], ps[pE][0:64, 0:128]),
                reads=(f"ps{pE}",), writes=(f"tloc{par}",))
            vo(lambda e, pE=pE, par=par: e.tensor_scalar_max(
                lr[par][:, 0:128], ps[pE][64:65, 0:128], 1e-30),
                reads=(f"ps{pE}",), writes=(f"lr{par}",))
            if _SUB < 5:
                continue
            pG = 4 + bpar
            mm(ps[pG][0:1, 0:128], gl_sb[0:64, :], tloc[par][0:64, :],
               start=True, stop=True, reads=(f"tloc{par}", "gl"),
               writes=(f"ps{pG}",), deps=[tl1])
            mm(ps[pG][0:1, 128:256], gr_sb[0:64, :], remh[par % 2][0:64, :],
               start=True, stop=True,
               reads=(f"remh{par % 2}", "gr"), writes=(f"ps{pG}",))
            vo(lambda e, par=par: e.reciprocal(lr[par][:, 0:128], lr[par][:, 0:128]),
               reads=(f"lr{par}",), writes=(f"lr{par}",))
            vo(lambda e, pG=pG, par=par: e.tensor_tensor(
                grow[par % 2][:, 0:128], ps[pG][0:1, 0:128], lr[par][:, 0:128], ALU.mult),
                reads=(f"ps{pG}", f"lr{par}"), writes=(f"grow{par % 2}",))
            vo(lambda e, pG=pG, par=par: e.tensor_tensor(
                grow[par % 2][:, 0:128], grow[par % 2][:, 0:128], ps[pG][0:1, 128:256],
                ALU.add),
                reads=(f"grow{par % 2}", f"ps{pG}"), writes=(f"grow{par % 2}",))
            so(lambda e, par=par: e.activation(
                grow[par % 2][:, 128:256], grow[par % 2][:, 0:128], AF.Exp,
                scale=np.float32(-1.0)),
                reads=(f"grow{par % 2}",), writes=(f"grow{par % 2}",))
            vo(lambda e, par=par: e.tensor_scalar_add(
                grow[par % 2][:, 128:256], grow[par % 2][:, 128:256], 1.0),
                reads=(f"grow{par % 2}",), writes=(f"grow{par % 2}",))
            vo(lambda e, par=par: e.reciprocal(
                grow[par % 2][:, 128:256], grow[par % 2][:, 128:256]),
                reads=(f"grow{par % 2}",), writes=(f"grow{par % 2}",))
            vo(lambda e, par=par: e.tensor_tensor(
                grow[par % 2][:, 0:128], grow[par % 2][:, 128:256], lr[par][:, 0:128],
                ALU.mult),
                reads=(f"grow{par % 2}", f"lr{par}"), writes=(f"grow{par % 2}",))
            vo(lambda e, par=par: e.tensor_scalar(
                grow[par % 2][:, 256:384], grow[par % 2][:, 128:256], -1.0, 1.0,
                ALU.mult, ALU.add),
                reads=(f"grow{par % 2}",), writes=(f"grow{par % 2}",))
            if _SUB < 6:
                continue
            mm(ps[pG][:, 256:384], ones_f[0:1, 0:128], grow[par % 2][:, 0:128],
               start=True, stop=True, reads=(f"grow{par % 2}", "ones_f"),
               writes=(f"ps{pG}",))
            mm(ps[pG][:, 384:512], ones_f[0:1, 0:128], grow[par % 2][:, 256:384],
               start=True, stop=True, reads=(f"grow{par % 2}", "ones_f"),
               writes=(f"ps{pG}",))
            vo(lambda e, pG=pG, par=par: e.tensor_copy(
                bcsb[par][:, 0:256], ps[pG][:, 256:512]),
                reads=(f"ps{pG}",), writes=(f"bcs{par}",))
            vo(lambda e, par=par: e.tensor_tensor(
                ptl4[par][0:64, 0:128], tloc[par][0:64, :], bcsb[par][0:64, 0:128],
                ALU.mult),
                reads=(f"tloc{par}", f"bcs{par}"), writes=(f"ptl{par}",))
            vo(lambda e, par=par: e.tensor_tensor(
                ptl4[par][0:64, 128:256],
                remh[par % 2][0:64, :],
                bcsb[par][0:64, 128:256], ALU.mult),
                reads=(f"remh{par % 2}", f"bcs{par}"), writes=(f"ptl{par}",))
            vo(lambda e, hp=hp, hc=hc, par=par: e.tensor_tensor(
                mixT[hp:hp + 64, hc * 128: hc * 128 + 128],
                ptl4[par][0:64, 0:128], ptl4[par][0:64, 128:256], ALU.add),
                reads=(f"ptl{par}",), writes=(f"mixT_{hc}",))
        if _SUB < 7:
            continue
        tx = sdma(f32a[:, :], xtok_in[t * 128:(t + 1) * 128, :], writes=("f32a",))
        for half in range(2):
            pn = half
            for r in range(8):
                mm(ps[pn][:, 0:512],
                   mixT[:, r * 128:(r + 1) * 128],
                   woutT[:, r * D + half * 512: r * D + half * 512 + 512],
                   start=(r == 0), stop=(r == 7),
                   reads=("woutT",) + tuple(f"mixT_{rr}" for rr in range(8)),
                   writes=(f"ps{pn}",), deps=[towt])
            vo(lambda e, pn=pn, half=half: e.tensor_tensor(
                f32a[:, half * 512:(half + 1) * 512], ps[pn][:, 0:512],
                f32a[:, half * 512:(half + 1) * 512], ALU.add),
                reads=(f"ps{pn}", "f32a"), writes=("f32a",), deps=[tx])
            vo(lambda e, half=half: e.tensor_tensor(
                f32a[:, half * 512:(half + 1) * 512],
                f32a[:, half * 512:(half + 1) * 512],
                bcA[:, half * 512:(half + 1) * 512], ALU.add),
                reads=("f32a", "bcA"), writes=("f32a",))

        def fin_xmid(mean_ap, rstd_ap, t=t):
            vo(lambda e: e.tensor_scalar(
                xmid[:, t * D:(t + 1) * D], f32a[:, :], mean_ap, rstd_ap,
                ALU.subtract, ALU.mult),
                reads=("f32a", "c0", "c4"), writes=(f"xmid_{t}",))
            if _DEBUG:
                gdma(dbg["xmid"][t * 128:(t + 1) * 128, :],
                     xmid[:, t * D:(t + 1) * D], reads=(f"xmid_{t}",))
        plain_ln(f32a[:, :], "f32a", fin_xmid, f32b, "f32b")

    if _TRUNC == "P1e":
        return trunc_finish()
    P.barrier()
    for t in range(NT):
        def fin_nx1(mean_ap, rstd_ap, t=t):
            vo(lambda e: e.tensor_scalar(
                f32a[:, :], xmid[:, t * D:(t + 1) * D], mean_ap, rstd_ap,
                ALU.subtract, ALU.mult),
                reads=(f"xmid_{t}", "c0", "c4"), writes=("f32a",))
        plain_ln(xmid[:, t * D:(t + 1) * D], f"xmid_{t}", fin_nx1, f32b, "f32b")
        vo(lambda e: e.tensor_tensor(f32a[:, :], f32a[:, :], bcB[:, :], ALU.mult),
           reads=("f32a", "bcB"), writes=("f32a",))
        vo(lambda e: e.tensor_tensor(f32a[:, :], f32a[:, :], bcC[:, :], ALU.add),
           reads=("f32a", "bcC"), writes=("f32a",))
        if _DEBUG:
            sdma(dbg["nx1"][t * 128:(t + 1) * 128, :], f32a[:, :], reads=("f32a",))
        for half in range(2):
            tch = vo(lambda e, half=half: e.tensor_copy(
                ptile[0][:, 0:512], f32a[:, half * 512:(half + 1) * 512]),
                reads=("f32a",), writes=("ptile0",))
            for u in range(4):
                tpose(psT[:, (half * 4 + u) * 128:(half * 4 + u + 1) * 128],
                      ptile[0][:, u * 128:(u + 1) * 128], ident_bf[:, :],
                      reads=("ptile0", "ident_bf"), writes=("psT",), deps=[tch])
        vo(lambda e, t=t: e.tensor_copy(
            nx1T[:, :].rearrange("p (r w) -> p r w", r=8)
            [:, :, t * 128:(t + 1) * 128],
            psT[:, 0:1024].rearrange("p (r w) -> p r w", r=8)),
            reads=("psT",), writes=tuple(f"nx1T_{r}" for r in range(8)))

    ccst = [sal(f"ccst{i}", [128, TOK], BF16, SCR + 13344 + 6144 + i * 4096)
            for i in range(2)]
    kv_toks = []
    q_toks = []
    for ci in list(range(8, 24)) + list(range(8)):
        sl, tw = load_w(w_Wqkv, 3 * D, ci)
        qsl = ci % 2
        for gg in range(4):
            g0 = gg * 512
            pn = nps()
            for r in range(8):
                mm(ps[pn][:, 0:512],
                   wbuf[sl][:, r * 128:(r + 1) * 128],
                   nx1T[:, r * TOK + g0: r * TOK + g0 + 512],
                   start=(r == 0), stop=(r == 7),
                   reads=(f"wbuf{sl}", f"nx1T_{r}"), writes=(f"ps{pn}",), deps=[tw])
            if ci < 8:
                vo(lambda e, pn=pn, ci=ci, qsl=qsl, g0=g0: e.tensor_scalar(
                    ccst[qsl][:, g0:g0 + 512], ps[pn][:, 0:512],
                    sb["bcol_Wqkv"][:, ci:ci + 1],
                    0.125, ALU.add, ALU.mult),
                    reads=(f"ps{pn}", "bcol_Wqkv"), writes=(f"ccst{qsl}",))
            else:
                vo(lambda e, pn=pn, ci=ci, qsl=qsl, g0=g0: e.tensor_single_scalar(
                    ccst[qsl][:, g0:g0 + 512], ps[pn][:, 0:512],
                    sb["bcol_Wqkv"][:, ci:ci + 1], ALU.add),
                    reads=(f"ps{pn}", "bcol_Wqkv"), writes=(f"ccst{qsl}",))
        if ci < 8:
            q_toks.append(sdma(cc2q_in[ci * 128:(ci + 1) * 128, :],
                               ccst[qsl][:, :], reads=(f"ccst{qsl}",),
                               writes=("cc2q_in",)))
        else:
            kv_toks.append(sdma(cc2kv_in[(ci - 8) * 128:(ci - 7) * 128, :],
                                ccst[qsl][:, :], reads=(f"ccst{qsl}",),
                                writes=("cc2kv_in",)))

        if ci == 23:
            tcc2kv = P.op("gpsimd", lambda e: e.collective_compute(
                "AllGather", ALU.bypass,
                replica_groups=[[0, 1, 2, 3], [4, 5, 6, 7]],
                ins=[cc2kv_in.opt()], outs=[cc2kv_out.opt()]),
                reads=("cc2kv_in",), writes=("cc2kv_out",), sem="cc2", inc=1,
                deps=kv_toks)

    tcc2q = P.op("gpsimd", lambda e: e.collective_compute(
        "AllGather", ALU.bypass, replica_groups=[[0, 1, 2, 3], [4, 5, 6, 7]],
        ins=[cc2q_in.opt()], outs=[cc2q_out.opt()]),
        reads=("cc2q_in",), writes=("cc2q_out",), sem="cc2", inc=1, deps=q_toks)

    def km_copy(e):
        src = mkap(cc2kv_out, dynbase("kv", 4 * HD * TOK),
                   [[TOK, 4 * HD], [2 * D * TOK, 4], [1, TOK]])
        dd = mkap(km_d, 0, [[S, 4 * HD], [TOK, 4], [1, TOK]])
        return e.dma_start(out=dd, in_=src)

    def vm_copy(e):
        src = mkap(cc2kv_out, dynbase("kv", 4 * HD * TOK) + D * TOK,
                   [[TOK, 4 * HD], [2 * D * TOK, 4], [1, TOK]])
        dd = mkap(vm_d, 0, [[S, 4 * HD], [TOK, 4], [1, TOK]])
        return e.dma_start(out=dd, in_=src)

    def qm_copy(e):
        src = mkap(cc2q_out, dynbase("kv", 4 * HD * TOK),
                   [[TOK, 4 * HD], [D * TOK, 4], [1, TOK]])
        dd = mkap(qm_d, 0, [[S, 4 * HD], [TOK, 4], [1, TOK]])
        return e.dma_start(out=dd, in_=src)

    P.op("gpsimd", km_copy, reads=("cc2kv_out",), writes=("km_d",),
         sem="gmy", inc=16, deps=[tcc2kv])
    tkvm = P.op("gpsimd", vm_copy, reads=("cc2kv_out",), writes=("vm_d",),
                sem="gmy", inc=16, deps=[tcc2kv])
    P.wtok["km_d"] = tkvm
    P.wtok["vm_d"] = tkvm
    tkm = tvm = tkvm
    tqm = P.op("gpsimd", qm_copy, reads=("cc2q_out",), writes=("qm_d",),
               sem="gam", inc=16, deps=[tcc2q])

    tmcau = gdma(mcau[:, :].rearrange("p (m w) -> p m w", m=4),
                 mkap(mcau_in, 0, [[512, 128], [128 * 512, 4], [1, 512]]),
                 writes=("mcau",), dsem="c3")

    for ei in range(NE):
        gdma(mkap(w1bf, ei * D * FFN, [[FFN, D], [1, FFN]]),
             mkap(w1_in, ei * D * FFN, [[FFN, D], [1, FFN]]),
             writes=(f"w1bf_{ei}",), deps=[tvm], dsem="wc")
        wc_last = gdma(mkap(w2bf, ei * FFN * D, [[D, FFN], [1, D]]),
                       mkap(w2_in, ei * FFN * D, [[D, FFN], [1, D]]),
                       writes=(f"w2bf_{ei}",), deps=[tvm], dsem="wc")
    for ei in range(NE):
        P.wtok[f"w1bf_{ei}"] = wc_last
        P.wtok[f"w2bf_{ei}"] = wc_last

    if _TRUNC == "P1f":
        return trunc_finish()
    P.barrier(exclude=("wc",))
    ptile4 = {0: [ptile[0], pt_extra[0], pt_extra[1], pt_extra[2]],
              1: [ptile[1], pt_extra[3], pt_extra[4], pt_extra[5]]}
    ksq2b = sal("ksq2b", [64, 512], BF16, SCR + 13344 + 6144 + 8192)

    def p2_prep(hh):
        par = hh % 2
        kb = khat2[par]
        kbn = f"khat2_{par}"

        tkl = sdma(kb[0:64, :], km_d[hh * 64:(hh + 1) * 64, :],
                   reads=("km_d",), writes=(kbn,), deps=[tkm])
        if hh < 2:
            vo(lambda e, kb=kb: e.memset(kb[64:65, :], 1.0), writes=(f"{kbn}_ones",))
        vb = vhat2[par]
        vbn = f"vhat2_{par}"
        if hh < 2:
            vo(lambda e, vb=vb: e.memset(
                vb[:, :].rearrange("p (t j) -> p t j", j=65)[:, :, 64:65], 1.0),
               writes=(f"{vbn}_ones",))
        for s4 in range(4):
            vs = vth[s4 % 2]
            vsn = f"vth{s4 % 2}"

            tvl = sdma(vs[0:64, :], vm_d[hh * 64:(hh + 1) * 64,
                                         s4 * TOK:(s4 + 1) * TOK],
                       reads=("vm_d",), writes=(vsn,), deps=[tvm])
            for u in range(16):
                tpose(psT[:, u * 64:(u + 1) * 64],
                      vs[0:64, u * 128:(u + 1) * 128], ident_bf[0:64, 0:64],
                      reads=(vsn, "ident_bf"), writes=("psT",), deps=[tvl])
            vo(lambda e, vb=vb, s4=s4: e.tensor_copy(
                vb[:, :].rearrange("p (t j) -> p t j", j=65)
                [:, s4 * 16:(s4 + 1) * 16, 0:64],
                psT[:, 0:1024].rearrange("p (t j) -> p t j", j=64)),
                reads=("psT",), writes=(vbn,))
        for g0 in range(0, S, 512):
            t1 = so(lambda e, kb=kb, g0=g0: e.activation(
                ksq2b[0:64, 0:512], kb[0:64, g0:g0 + 512], AF.Square),
                reads=(kbn,), writes=("ksq2b",), deps=[tkl])
            t2 = mm(ps[5][0:1, 0:512], ones_b[0:64, 0:1], ksq2b[0:64, 0:512],
                    start=True, stop=True, reads=("ksq2b", "ones_b"),
                    writes=("ps5",), deps=[t1])
            vo(lambda e: e.reduce_max(cols[0:1, 7:8], ps[5][0:1, 0:512], axis=AX.X),
               reads=("ps5",), writes=("c7",))
            if g0 == 0:
                vo(lambda e, hh=hh: e.tensor_copy(mk2[:, hh:hh + 1], cols[0:1, 7:8]),
                   reads=("c7",), writes=(f"mk2_{hh}",))
            else:
                vo(lambda e, hh=hh: e.tensor_max(mk2[:, hh:hh + 1], mk2[:, hh:hh + 1],
                                                 cols[0:1, 7:8]),
                   reads=("c7", f"mk2_{hh}"), writes=(f"mk2_{hh}",))

    p2_prep(0)
    for hh in range(4):
        par = hh % 2
        kb = khat2[par]
        kbn = f"khat2_{par}"
        vb = vhat2[par]
        vbn = f"vhat2_{par}"
        if hh < 3:
            p2_prep(hh + 1)
        for tau in range(16):
            qpar = tau % 2
            qb = qhat2[qpar]
            qbn = f"qhat2_{qpar}"
            tql = sdma(qb[0:64, :], qm_d[hh * 64:(hh + 1) * 64,
                                         tau * 512:(tau + 1) * 512],
                       reads=("qm_d",), writes=(qbn,), deps=[tqm])
            t1 = so(lambda e, qb=qb: e.activation(ksq2[0:64, 0:512], qb[0:64, :],
                                                  AF.Square),
                    reads=(qbn,), writes=("ksq2",), deps=[tql])
            t2 = mm(ps[6][0:1, 0:512], ones_b[0:64, 0:1], ksq2[0:64, 0:512],
                    start=True, stop=True, reads=("ksq2", "ones_b"),
                    writes=("ps6",), deps=[t1])
            vo(lambda e, qpar=qpar: e.tensor_copy(l2[qpar][:, 0:512],
                                                  ps[6][0:1, 0:512]),
               reads=("ps6",), writes=(f"l2_{qpar}",))
            vo(lambda e, hh=hh, qpar=qpar: e.tensor_single_scalar(
                l2[qpar][:, 0:512], l2[qpar][:, 0:512], mk2[:, hh:hh + 1], ALU.mult),
                reads=(f"l2_{qpar}", f"mk2_{hh}"), writes=(f"l2_{qpar}",))
            so(lambda e, qpar=qpar: e.activation(
                l2[qpar][:, 0:512], l2[qpar][:, 0:512], AF.Ln,
                bias=np.float32(1e-30)),
                reads=(f"l2_{qpar}",), writes=(f"l2_{qpar}",))
            so(lambda e, qpar=qpar: e.activation(
                l2[qpar][:, 0:512], l2[qpar][:, 0:512], AF.Exp,
                scale=np.float32(0.5)),
                reads=(f"l2_{qpar}",), writes=(f"l2_{qpar}",))
            t7 = vo(lambda e, qb=qb, qpar=qpar: e.tensor_scalar_mul(
                qb[64:65, :], l2[qpar][:, 0:512], -1.0),
                reads=(f"l2_{qpar}",), writes=(qbn,))
            nch = 4 * tau + 4
            pE = 2 + qpar
            QKB = (0, 1, 6)
            exp_done = {}

            def emit_qk(kc):
                pA = QKB[kc % 3]
                mm(ps[pA][:, 0:512], kb[:, kc * 128:(kc + 1) * 128], qb[:, :],
                   start=True, stop=True,
                   reads=(kbn, qbn, f"khat2_{par}_ones"), writes=(f"ps{pA}",),
                   deps=[t7])
                dk = kc - 4 * tau
                if dk >= 0:
                    vo(lambda e, pA=pA, dk=dk: e.tensor_tensor(
                        ps[pA][:, 0:512], ps[pA][:, 0:512],
                        mcau[:, dk * 512:(dk + 1) * 512], ALU.add),
                        reads=(f"ps{pA}", "mcau"), writes=(f"ps{pA}",),
                        deps=[tmcau])
                pp = kc % 4
                so(lambda e, pA=pA, pp=pp, qpar=qpar: e.activation(
                    ptile4[qpar][pp][:, 0:512], ps[pA][:, 0:512], AF.Exp),
                    reads=(f"ps{pA}",), writes=(f"pt{qpar}_{pp}",))

            def emit_av(kc):
                pp = kc % 4
                mm(ps[pE][0:65, 0:512],
                   vb[:, kc * 65:(kc + 1) * 65], ptile4[qpar][pp][:, 0:512],
                   start=(kc == 0), stop=(kc == nch - 1),
                   reads=(f"pt{qpar}_{pp}", vbn, f"vhat2_{par}_ones"),
                   writes=(f"ps{pE}",))

            emit_qk(0)
            if nch > 1:
                emit_qk(1)
            for kc in range(2, nch):
                emit_qk(kc)
                emit_av(kc - 2)
            emit_av(nch - 2)
            emit_av(nch - 1)
            vo(lambda e, pE=pE, qpar=qpar: e.tensor_scalar_max(
                l2[qpar][:, 0:512], ps[pE][64:65, 0:512], 1e-30),
                reads=(f"ps{pE}",), writes=(f"l2_{qpar}",))
            vo(lambda e, qpar=qpar: e.reciprocal(l2[qpar][:, 0:512],
                                                 l2[qpar][:, 0:512]),
               reads=(f"l2_{qpar}",), writes=(f"l2_{qpar}",))
            mm(ps[6][0:64, 0:512], ones_f[0:1, 0:64], l2[qpar][:, 0:512],
               start=True, stop=True, reads=(f"l2_{qpar}", "ones_f"),
               writes=("ps6",))
            vo(lambda e, qpar=qpar: e.tensor_copy(bcs2[qpar][:, 0:512],
                                                  ps[6][0:64, 0:512]),
               reads=("ps6",), writes=(f"bcs2_{qpar}",))
            vo(lambda e, hh=hh, tau=tau, pE=pE, qpar=qpar: e.tensor_tensor(
                attnT2[(hh % 2) * 64:(hh % 2) * 64 + 64,
                       (hh // 2) * S + tau * 512:(hh // 2) * S + (tau + 1) * 512],
                ps[pE][0:64, 0:512], bcs2[qpar][:, 0:512], ALU.mult),
                reads=(f"ps{pE}", f"bcs2_{qpar}"), writes=(f"attnT2_{hh}",))

    tct1 = sdma(cc3_in[0:128, :], attnT2[:, 0:S],
                reads=("attnT2_0", "attnT2_1"), writes=("cc3_in",))
    tct2 = sdma(cc3_in[128:256, :], attnT2[:, S:2 * S],
                reads=("attnT2_2", "attnT2_3"), writes=("cc3_in",))
    tcc3 = P.op("gpsimd", lambda e: e.collective_compute(
        "AllGather", ALU.bypass, replica_groups=[[0, 1, 2, 3], [4, 5, 6, 7]],
        ins=[cc3_in.opt()], outs=[cc3_out.opt()]),
        reads=("cc3_in",), writes=("cc3_out",), sem="cc3", inc=1,
        deps=[tct1, tct2])

    def my_cols(e):
        src = mkap(cc3_out, dynbase("c3", TOK), [[S, D], [1, TOK]])
        dd = mkap(am_d, 0, [[TOK, D], [1, TOK]])
        return e.dma_start(out=dd, in_=src)
    tam = P.op("gpsimd", my_cols, reads=("cc3_out",), writes=("am_d",),
               sem="gam", inc=16, deps=[tcc3])

    if _TRUNC == "P2":
        return trunc_finish()
    P.barrier(exclude=("wc",))
    sdma(fc2b[:, :], fc2b_in[:, :], writes=("fc2b",))
    sdma(ident_f[:, :], ident_in[:, :], writes=("ident_f",))
    sdma(fc1b[:, :].rearrange("p (ee c) -> p ee c", ee=8),
         mkap(bcol_fc1_in, 0, [[32, 128], [128 * 32, 8], [1, 32]]),
         writes=("fc1b",))
    gdma(bcA[:, :], bo_in[:, :], writes=("bcA",), dsem="c3")
    gdma(bcB[:, :], ln2w_in[:, :], writes=("bcB",), dsem="c3")
    c3_last = gdma(bcC[:, :], ln2b_in[:, :], writes=("bcC",), dsem="c3")
    for nm in ("bcA", "bcB", "bcC"):
        P.wtok[nm] = c3_last
    twL = gdma(woutT[:, :].rearrange("p (r c) -> p r c", r=8),
               mkap(w_Wo, 0, [[D, 128], [128 * D, 8], [1, D]]),
               writes=("woutT",))

    for t in range(NT):
        p3 = t % 2
        tat = sdma(attnTo_t[p3][:, :].rearrange("p (r c) -> p r c", r=8),
                   mkap(am_d, t * 128, [[TOK, 128], [128 * TOK, 8], [1, 128]]),
                   reads=("am_d",), writes=(f"attnTo{p3}",), deps=[tam])
        for half in range(2):
            pn = half
            for r in range(8):
                mm(ps[pn][:, 0:512],
                   attnTo_t[p3][:, r * 128:(r + 1) * 128],
                   woutT[:, r * D + half * 512: r * D + half * 512 + 512],
                   start=(r == 0), stop=(r == 7),
                   reads=("woutT", f"attnTo{p3}"), writes=(f"ps{pn}",),
                   deps=[twL, tat])
            vo(lambda e, pn=pn, half=half: e.tensor_tensor(
                f3a[:, half * 512:(half + 1) * 512], ps[pn][:, 0:512],
                bcA[:, half * 512:(half + 1) * 512], ALU.add),
                reads=(f"ps{pn}", "bcA"), writes=("f3a",))
        if _DEBUG:
            sdma(dbg["attn"][t * 128:(t + 1) * 128, :], f3a[:, :], reads=("f3a",))
        vo(lambda e, t=t: e.tensor_tensor(xres_t[:, :], f3a[:, :],
                                          xmid[:, t * D:(t + 1) * D], ALU.add),
           reads=("f3a", f"xmid_{t}"), writes=("xres_t",))
        vo(lambda e, t=t: e.tensor_tensor(moe_acc[:, t * D:(t + 1) * D],
                                          xres_t[:, :],
                                          xmid[:, t * D:(t + 1) * D], ALU.add),
           reads=("xres_t", f"xmid_{t}"), writes=(f"moe_{t}",))

        def fin_nx2(mean_ap, rstd_ap):
            vo(lambda e: e.tensor_scalar(f3a[:, :], xres_t[:, :], mean_ap, rstd_ap,
                                         ALU.subtract, ALU.mult),
               reads=("xres_t", "c0", "c4"), writes=("f3a",))
        plain_ln(xres_t[:, :], "xres_t", fin_nx2, f3b, "f3b")
        vo(lambda e: e.tensor_tensor(f3a[:, :], f3a[:, :], bcB[:, :], ALU.mult),
           reads=("f3a", "bcB"), writes=("f3a",))
        vo(lambda e: e.tensor_tensor(nx2f[:, :], f3a[:, :], bcC[:, :], ALU.add),
           reads=("f3a", "bcC"), writes=("nx2f",))
        for half in range(2):
            tch = vo(lambda e, half=half: e.tensor_copy(
                rT3[:, half * 512:(half + 1) * 512],
                nx2f[:, half * 512:(half + 1) * 512]),
                reads=("nx2f",), writes=("rT3",))
            for u in range(4):
                tpose(psT[:, (half * 4 + u) * 128:(half * 4 + u + 1) * 128],
                      rT3[:, half * 512 + u * 128: half * 512 + (u + 1) * 128],
                      ident_bf[:, :],
                      reads=("rT3", "ident_bf"), writes=("psT",), deps=[tch])
        vo(lambda e, t=t: e.tensor_copy(
            nx2T[:, :].rearrange("p (r w) -> p r w", r=8)
            [:, :, t * 128:(t + 1) * 128],
            psT[:, 0:1024].rearrange("p (r w) -> p r w", r=8)),
            reads=("psT",), writes=tuple(f"nx2T_{r}" for r in range(8)))
        for half in range(2):
            tsub = vo(lambda e, half=half: e.tensor_tensor(
                f3b[:, 0:512], nx2f[:, half * 512:(half + 1) * 512],
                rT3[:, half * 512:(half + 1) * 512], ALU.subtract),
                reads=("nx2f", "rT3"), writes=("f3b",))
            tch = vo(lambda e, half=half: e.tensor_copy(
                rT3[:, half * 512:(half + 1) * 512], f3b[:, 0:512]),
                reads=("f3b",), writes=("rT3x",))
            for u in range(4):
                tpose(psT[:, (half * 4 + u) * 128:(half * 4 + u + 1) * 128],
                      rT3[:, half * 512 + u * 128: half * 512 + (u + 1) * 128],
                      ident_bf[:, :],
                      reads=("rT3x", "ident_bf"), writes=("psT",), deps=[tch])
        vo(lambda e: e.tensor_copy(rT3[:, :], psT[:, 0:1024]),
           reads=("psT", "rT3x"), writes=("rT3",))
        for r in range(8):
            mm(ps[2][:, 0:8],
               nx2T[:, r * TOK + t * 128: r * TOK + (t + 1) * 128],
               gateW[:, r * 8:(r + 1) * 8],
               start=(r == 0), stop=False,
               reads=(f"nx2T_{r}", "gateW"), writes=("ps2",))
        for r in range(8):
            mm(ps[2][:, 0:8],
               rT3[:, r * 128:(r + 1) * 128],
               gateW[:, r * 8:(r + 1) * 8],
               start=False, stop=(r == 7),
               reads=("rT3", "gateW"), writes=("ps2",))
        vo(lambda e: e.tensor_copy(lg8[:, :], ps[2][:, 0:8]),
           reads=("ps2",), writes=("lg8",))
        vo(lambda e: e.reduce_max(w12[:, 0:1], lg8[:, :], axis=AX.X),
           reads=("lg8",), writes=("w0",))
        vo(lambda e: e.tensor_single_scalar(is1[:, :], lg8[:, :], w12[:, 0:1],
                                            ALU.is_ge),
           reads=("lg8", "w0"), writes=("is1",))
        vo(lambda e: e.scalar_tensor_tensor(msk8[:, :], is1[:, :], NEG, lg8[:, :],
                                            ALU.mult, ALU.add),
           reads=("is1", "lg8"), writes=("msk8",))
        vo(lambda e: e.reduce_max(w12[:, 1:2], msk8[:, :], axis=AX.X),
           reads=("msk8",), writes=("w1c",))
        vo(lambda e: e.tensor_single_scalar(is2[:, :], msk8[:, :], w12[:, 1:2],
                                            ALU.is_ge),
           reads=("msk8", "w1c"), writes=("is2",))
        vo(lambda e: e.tensor_sub(w12[:, 2:3], w12[:, 1:2], w12[:, 0:1]),
           reads=("w0", "w1c"), writes=("w2c",))
        so(lambda e: e.activation(w12[:, 2:3], w12[:, 2:3], AF.Exp),
           reads=("w2c",), writes=("w2c",))
        vo(lambda e: e.tensor_scalar_add(w12[:, 3:4], w12[:, 2:3], 1.0),
           reads=("w2c",), writes=("w3c",))
        vo(lambda e: e.reciprocal(w12[:, 3:4], w12[:, 3:4]),
           reads=("w3c",), writes=("w3c",))
        vo(lambda e: e.tensor_mul(w12[:, 2:3], w12[:, 2:3], w12[:, 3:4]),
           reads=("w2c", "w3c"), writes=("w2c",))
        vo(lambda e: e.tensor_single_scalar(is2[:, :], is2[:, :], w12[:, 2:3],
                                            ALU.mult),
           reads=("is2", "w2c"), writes=("is2",))
        vo(lambda e, t=t: e.scalar_tensor_tensor(
            gates[:, t * 8:(t + 1) * 8], is1[:, :], w12[:, 3:4], is2[:, :],
            ALU.mult, ALU.add),
            reads=("is1", "is2", "w3c"), writes=(f"gates_{t}",))
        tgt = tpose(ps[3][0:8, 0:128], gates[:, t * 8:(t + 1) * 8], ident_f[:, :],
                    reads=(f"gates_{t}", "ident_f"), writes=("ps3",))
        tge = vo(lambda e: e.tensor_copy(g8T[:, :], ps[3][0:8, 0:128]),
                 reads=("ps3",), writes=("g8T",))
        for half in range(2):
            pn = half
            mm(ps[pn][:, 0:512], g8T[:, :], fc2b[:, half * 512:(half + 1) * 512],
               start=True, stop=True, reads=("g8T", "fc2b"), writes=(f"ps{pn}",),
               deps=[tge])
            vo(lambda e, pn=pn, t=t, half=half: e.tensor_tensor(
                moe_acc[:, t * D + half * 512: t * D + (half + 1) * 512],
                moe_acc[:, t * D + half * 512: t * D + (half + 1) * 512],
                ps[pn][:, 0:512], ALU.add),
                reads=(f"ps{pn}", f"moe_{t}"), writes=(f"moe_{t}",))

    if _TRUNC == "P3wo":
        return trunc_finish()
    P.barrier()
    w1rot = [0]
    w2rot = [0]
    for ei in range(NE):
        for qg in range(4):
            for fh in range(8):
                sl = w1rot[0] % 2
                w1rot[0] += 1
                t_w1 = gdma(w1buf[sl][:, :].rearrange("p (r c) -> p r c", r=8),
                            mkap(w1bf, ei * D * FFN + fh * 512,
                                 [[FFN, 128], [128 * FFN, 8], [1, 512]]),
                            writes=(f"w1buf{sl}",), deps=[P.wtok[f"w1bf_{ei}"]],
                            dsem=f"w1{sl}")
                for fs in range(4):
                    pn = nps()
                    for r in range(8):
                        mm(ps[pn][:, 0:512],
                           w1buf[sl][:, r * 512 + fs * 128: r * 512 + (fs + 1) * 128],
                           nx2T[:, r * TOK + qg * 512: r * TOK + (qg + 1) * 512],
                           start=(r == 0), stop=(r == 7),
                           reads=(f"w1buf{sl}", f"nx2T_{r}"), writes=(f"ps{pn}",),
                           deps=[t_w1])
                    fc = fh * 4 + fs
                    so(lambda e, pn=pn, ei=ei, fc=fc: e.activation(
                        hT[:, fc * 512:(fc + 1) * 512], ps[pn][:, 0:512], AF.Gelu,
                        bias=fc1b[:, ei * 32 + fc: ei * 32 + fc + 1]),
                        reads=(f"ps{pn}", "fc1b"), writes=(f"hT_{fc}",))
            for f2h in range(16):
                sl = w2rot[0] % 2
                w2rot[0] += 1
                t_w2 = gdma(w2buf[sl][:, :].rearrange("p (r c) -> p r c", r=2),
                            mkap(w2bf, ei * FFN * D + f2h * 256 * D,
                                 [[D, 128], [128 * D, 2], [1, D]]),
                            writes=(f"w2buf{sl}",), deps=[P.wtok[f"w2bf_{ei}"]],
                            dsem=f"w2{sl}")
                for qt in range(4):
                    for nh in range(2):
                        pn = qt * 2 + nh
                        pname = f"ps{pn}" if pn < 7 else "psT"
                        out_ap = (ps[pn][:, 0:512] if pn < 7
                                  else psT[:, 0:1024].bitcast(F32))
                        for fs in range(2):
                            fc = f2h * 2 + fs
                            mm(out_ap,
                               hT[:, fc * 512 + qt * 128: fc * 512 + (qt + 1) * 128],
                               w2buf[sl][:, fs * D + nh * 512: fs * D + nh * 512 + 512],
                               start=(f2h == 0 and fs == 0),
                               stop=(f2h == 15 and fs == 1),
                               reads=(f"w2buf{sl}", f"hT_{fc}"),
                               writes=(pname,), deps=[t_w2])
            for qt in range(4):
                for nh in range(2):
                    pn = qt * 2 + nh
                    pname = f"ps{pn}" if pn < 7 else "psT"
                    src = (ps[pn][:, 0:512] if pn < 7
                           else psT[:, 0:1024].bitcast(F32))
                    tt = qg * 4 + qt
                    vo(lambda e, src=src, tt=tt, nh=nh, ei=ei: e.scalar_tensor_tensor(
                        moe_acc[:, tt * D + nh * 512: tt * D + (nh + 1) * 512],
                        src, gates[:, tt * 8 + ei: tt * 8 + ei + 1],
                        moe_acc[:, tt * D + nh * 512: tt * D + (nh + 1) * 512],
                        ALU.mult, ALU.add),
                        reads=(pname, f"gates_{tt}", f"moe_{tt}"),
                        writes=(f"moe_{tt}",))

    P.barrier()
    out_toks = []
    for t in range(NT):
        def fin_out(mean_ap, rstd_ap, t=t):
            vo(lambda e: e.tensor_scalar(
                f3a[:, :], moe_acc[:, t * D:(t + 1) * D], mean_ap, rstd_ap,
                ALU.subtract, ALU.mult),
                reads=(f"moe_{t}", "c0", "c4"), writes=("f3a",))
        plain_ln(moe_acc[:, t * D:(t + 1) * D], f"moe_{t}", fin_out, f3b, "f3b")
        out_toks.append(sdma(out_dram[t * 128:(t + 1) * 128, :], f3a[:, :],
                             reads=("f3a",)))

    for tok in out_toks:
        P._wait("sync", tok[0], tok[1])

    P.finalize()
    import contextlib
    with contextlib.ExitStack() as st:
        for s in P.sem_names:
            P.sems[s] = st.enter_context(nc.semaphore(s))
        block = st.enter_context(nc.Block())
        P.replay(block)
    return nc


_NC_CACHE = {}


def kernel(**inputs):
    f32 = np.float32
    g = {k: np.asarray(v, dtype=f32) for k, v in inputs.items()}
    x = g["L0_attn_x"]

    def bc(v):
        return np.ascontiguousarray(np.broadcast_to(v.reshape(1, D), (128, D)))

    shared = {
        "w_qkv": g["L0_attn_qkv_w"],
        "bcol_qkv": np.ascontiguousarray(g["L0_attn_qkv_b"].reshape(24, 128).T),
        "w_rq": g["L0_attn_rq_w"],
        "bcol_rq": np.ascontiguousarray(g["L0_attn_rq_b"].reshape(2, 128).T),
        "w_rkv": g["L0_attn_rkv_w"],
        "bcol_rkv": np.ascontiguousarray(g["L0_attn_rkv_b"].reshape(4, 128).T),
        "w_rout": g["L0_attn_rout_w"],
        "bcol_rout": np.ascontiguousarray(g["L0_attn_rout_b"].reshape(8, 128).T),
        "w_out": g["L0_attn_out_w"],
        "outb_bc": bc(g["L0_attn_out_b"]),
        "gl": np.ascontiguousarray(np.concatenate([g["L0_attn_gl_w"]] * 2, 0)),
        "gr": np.ascontiguousarray(np.concatenate([g["L0_attn_gr_w"]] * 2, 0)),
        "w_Wqkv": g["L0_ffn_Wqkv"],
        "bcol_Wqkv": np.ascontiguousarray(g["L0_ffn_bqkv"].reshape(24, 128).T),
        "w_Wo": g["L0_ffn_Wo"],
        "bo_bc": bc(g["L0_ffn_bo"]),
        "ln1w_bc": bc(g["L0_ffn_ln1_w"]),
        "ln1b_bc": bc(g["L0_ffn_ln1_b"]),
        "ln2w_bc": bc(g["L0_ffn_ln2_w"]),
        "ln2b_bc": bc(g["L0_ffn_ln2_b"]),
        "w_gate": g["L0_ffn_gate_W"],
        "w1": np.ascontiguousarray(g["L0_ffn_fc1_w"].transpose(0, 2, 1)),
        "bcol_fc1": np.ascontiguousarray(
            g["L0_ffn_fc1_b"].reshape(NE, 32, 128).transpose(0, 2, 1)),
        "w2": np.ascontiguousarray(g["L0_ffn_fc2_w"].transpose(0, 2, 1)),
        "fc2b": g["L0_ffn_fc2_b"],
        "ident": np.eye(128, dtype=f32),
    }
    jj, ii = np.meshgrid(np.arange(128), np.arange(128), indexing="ij")
    shared["mlo"] = np.where(ii <= jj, 0.0, NEG).astype(f32)
    shared["mhi"] = np.where(jj < ii, 0.0, NEG).astype(f32)
    mcau = np.empty((4, 128, 512), f32)
    uu = np.arange(512)
    for m in range(4):
        mcau[m] = np.where(uu[None, :] >= (128 * m + np.arange(128))[:, None],
                           0.0, NEG)
    shared["mcau"] = mcau

    in_maps = []
    for c in range(8):
        b, qd = c // 4, c % 4
        t0 = qd * TOK
        xT = np.zeros((D, XW), f32)
        xb = np.ascontiguousarray(x[b].T)
        xT[:, HALO:] = xb[:, t0:t0 + TOK]
        if qd > 0:
            xT[:, :HALO] = xb[:, t0 - HALO:t0]
        lat = np.arange(64)[:, None]
        tt = np.arange(NT)[None, :]
        m = dict(shared)
        m["xT"] = xT
        m["xtok"] = np.ascontiguousarray(x[b, t0:t0 + TOK])
        m["mlo0"] = (np.full((128, 128), NEG, f32) if qd == 0 else shared["mlo"])
        m["mlat"] = np.where(lat < (qd * 16 + tt), 0.0, NEG).astype(f32)
        in_maps.append(m)

    if "nc" not in _NC_CACHE:
        _NC_CACHE["nc"] = build_nc()
    nc = _NC_CACHE["nc"]
    res = run_bass_kernel_spmd(nc, in_maps, core_ids=list(range(8)))
    out = np.empty((B, S, D), f32)
    for c in range(8):
        b, qd = c // 4, c % 4
        out[b, qd * TOK:(qd + 1) * TOK] = np.asarray(res.results[c]["out"])
    return out


# version 11 (final)
# speedup vs baseline: 1.2910x; 1.0048x over previous
import dataclasses
import numpy as np
import ml_dtypes
import concourse.bass as bass
import concourse.mybir as mybir
from concourse.bass_utils import run_bass_kernel_spmd

F32 = mybir.dt.float32
BF16 = mybir.dt.bfloat16
AF = mybir.ActivationFunctionType
ALU = mybir.AluOpType
AX = mybir.AxisListType

B, S, D = 2, 8192, 1024
NH, HD = 16, 64
LH = 4
NE, FFN = 8, 4096
NEG = -1.0e9
TOK = 2048
NT = 16
HALO = 128
XW = HALO + TOK
NTX = XW // 128

_DEBUG = False
_TRUNC = None
_SUB = 9


class Prog:
    ENGINES = ("sync", "gpsimd", "vector", "scalar", "tensor")

    def __init__(self, nc):
        self.nc = nc
        self.q = {e: [] for e in self.ENGINES}
        self.sem_names = (list(self.ENGINES) + ["sdma", "gdma", "cc1", "cc2", "cc3", "cst", "xt", "wb0", "wb1", "wc", "gmy", "gam", "c3", "w10", "w11", "w20", "w21", "kv0", "kv1", "qq0", "qq1"])
        self.cnt = {s: 0 for s in self.sem_names}
        self.hwm = {}
        self.wtok = {}
        self.rtok = {}
        self.sems = {}
        self.used = {s: set() for s in self.sem_names}
        self.incs = {s: {} for s in self.sem_names}
        self.remap = {s: {} for s in self.sem_names}
        self.final_inc = None
        self.btoks = []
        self.dma_owner = {}
        for sname in self.sem_names:
            if sname in ("sdma",):
                self.dma_owner[sname] = "sync"
            elif sname not in self.ENGINES and not sname.startswith("cc"):
                self.dma_owner[sname] = "gpsimd"
        self.owned = {}
        for sname, eng in self.dma_owner.items():
            self.owned.setdefault(eng, []).append(sname)
        self.boundaries = {sname: [0] for sname in self.dma_owner}

    def barrier(self, exclude=()):
        self.btoks = [(s, self.cnt[s]) for s in self.sem_names
                      if self.cnt[s] > 0 and s not in exclude]

    def snap(self, sem, val):
        if sem not in self.dma_owner:
            return val
        for b in self.boundaries[sem]:
            if b >= val:
                return b
        owner = self.dma_owner[sem]
        total = self.cnt[sem]
        self.boundaries[sem].append(total)
        self._wait_raw(owner, sem, total)
        return total

    def _wait_raw(self, eng, sem, val):
        if val <= 0 or self.hwm.get((eng, sem), 0) >= val:
            return
        self.hwm[(eng, sem)] = val
        self.used[sem].add(val)
        P = self

        def w(e, s=sem, v=val):
            e.wait_ge(P.sems[s], P.remap[s][v])
        self.q[eng].append(w)

    def _wait(self, eng, sem, val):
        self._wait_raw(eng, sem, self.snap(sem, val))

    def op(self, eng, fn, reads=(), writes=(), sem=None, inc=1, deps=()):
        dep = list(deps) + list(self.btoks)
        for b in reads:
            t = self.wtok.get(b)
            if t:
                dep.append(t)
        for b in writes:
            t = self.wtok.get(b)
            if t:
                dep.append(t)
            dep.extend(self.rtok.get(b, ()))
        sem = sem or eng
        best = {}
        for (s, v) in dep:
            if eng == "tensor" and s == "tensor":
                continue
            if v > best.get(s, 0):
                best[s] = v
        for s, v in sorted(best.items()):
            assert v <= self.cnt[s], (eng, s, v, self.cnt[s])
            self._wait(eng, s, v)
        self.cnt[sem] += inc
        raw = self.cnt[sem]
        self.incs[sem][raw] = inc
        P = self

        def run(e, f=fn, s=sem, r=raw):
            ins = f(e)
            amt = P.final_inc[s].get(r)
            if amt:
                ins.then_inc(P.sems[s], amt)
        self.q[eng].append(run)
        tok = (sem, raw)
        for b in writes:
            self.wtok[b] = tok
            self.rtok[b] = []
        for b in reads:
            self.rtok.setdefault(b, []).append(tok)
        return tok

    def finalize(self):
        self.final_inc = {}
        for s in self.sem_names:
            keep_all = s in self.dma_owner or s.startswith("cc")
            mx = max(self.used[s]) if self.used[s] else 0
            fi = {}
            run = 0
            for raw in sorted(self.incs[s]):
                inc = self.incs[s][raw]
                if keep_all or raw <= mx:
                    fi[raw] = inc
                    run += inc
                self.remap[s][raw] = run
            self.final_inc[s] = fi

    def replay(self, block):
        for ename in self.ENGINES:
            if not self.q[ename]:
                continue

            def body(e, lst=tuple(self.q[ename])):
                for f in lst:
                    f(e)
            getattr(block, ename)(body)


def mkap(base_ap, offset, dims):
    return dataclasses.replace(base_ap, offset=offset, ap=dims)


def build_nc():
    nc = bass.Bass("TRN2", target_bir_lowering=False)
    P = Prog(nc)
    fin_state = {}
    _dyn = {}

    def dynbase(key, mult):
        if key not in _dyn:
            _dyn[key] = (nc.partition_id() % 4) * mult
        return _dyn[key]

    def trunc_finish():
        out_toks = []
        for t2 in range(NT):
            out_toks.append(fin_state["sdma"](
                fin_state["out"][t2 * 128:(t2 + 1) * 128, :],
                fin_state["f32a"][:, :]))
        for tok in out_toks:
            P._wait("sync", tok[0], tok[1])
        P.finalize()
        import contextlib
        with contextlib.ExitStack() as st:
            for s2 in P.sem_names:
                P.sems[s2] = st.enter_context(nc.semaphore(s2))
            block = st.enter_context(nc.Block())
            P.replay(block)
        return nc

    def din(name, shape):
        return nc.dram_tensor(name, shape, F32, kind="ExternalInput").ap()

    xT_in = din("xT", [D, XW])
    xtok_in = din("xtok", [TOK, D])
    w_qkv = din("w_qkv", [D, 3 * D])
    bcol_qkv_in = din("bcol_qkv", [128, 24])
    w_rq = din("w_rq", [D, 256])
    bcol_rq_in = din("bcol_rq", [128, 2])
    w_rkv = din("w_rkv", [D, 512])
    bcol_rkv_in = din("bcol_rkv", [128, 4])
    w_rout = din("w_rout", [256, D])
    bcol_rout_in = din("bcol_rout", [128, 8])
    w_out = din("w_out", [D, D])
    outb_in = din("outb_bc", [128, D])
    gl_in = din("gl", [128, 1])
    gr_in = din("gr", [128, 1])
    w_Wqkv = din("w_Wqkv", [D, 3 * D])
    bcol_Wqkv_in = din("bcol_Wqkv", [128, 24])
    w_Wo = din("w_Wo", [D, D])
    bo_in = din("bo_bc", [128, D])
    ln1w_in = din("ln1w_bc", [128, D])
    ln1b_in = din("ln1b_bc", [128, D])
    ln2w_in = din("ln2w_bc", [128, D])
    ln2b_in = din("ln2b_bc", [128, D])
    w_gate = din("w_gate", [D, NE])
    w1_in = din("w1", [NE, D, FFN])
    bcol_fc1_in = din("bcol_fc1", [NE, 128, 32])
    w2_in = din("w2", [NE, FFN, D])
    fc2b_in = din("fc2b", [NE, D])
    mlo0_in = din("mlo0", [128, 128])
    mlo_in = din("mlo", [128, 128])
    mhi_in = din("mhi", [128, 128])
    mcau_in = din("mcau", [4, 128, 512])
    mlat_in = din("mlat", [64, NT])
    ident_in = din("ident", [128, 128])

    out_dram = nc.dram_tensor("out", [TOK, D], F32, kind="ExternalOutput").ap()
    dbg = {}
    if _DEBUG:
        for nm in ("xmid", "attn", "nx1"):
            dbg[nm] = nc.dram_tensor("dbg_" + nm, [TOK, D], F32,
                                     kind="ExternalOutput").ap()

    cc1_in = nc.dram_tensor("cc1_in", [D, 16], BF16).ap()
    cc1_out = nc.dram_tensor("cc1_out", [4 * D, 16], BF16).ap()
    cc2kv_in = nc.dram_tensor("cc2kv_in", [2 * D, TOK], BF16).ap()
    cc2kv_out = nc.dram_tensor("cc2kv_out", [8 * D, TOK], BF16).ap()
    cc2q_in = nc.dram_tensor("cc2q_in", [D, TOK], BF16).ap()
    cc2q_out = nc.dram_tensor("cc2q_out", [4 * D, TOK], BF16).ap()
    cc3_in = nc.dram_tensor("cc3_in", [4 * HD, S], BF16).ap()
    cc3_out = nc.dram_tensor("cc3_out", [D, S], BF16).ap()
    qm_d = nc.dram_tensor("qm_d", [4 * HD, S], BF16).ap()
    km_d = nc.dram_tensor("km_d", [4 * HD, S], BF16).ap()
    vm_d = nc.dram_tensor("vm_d", [4 * HD, S], BF16).ap()
    am_d = nc.dram_tensor("am_d", [D, TOK], BF16).ap()
    w1bf = nc.dram_tensor("w1bf", [NE, D, FFN], BF16).ap()
    w2bf = nc.dram_tensor("w2bf", [NE, FFN, D], BF16).ap()

    SB_LIMIT = 212800
    SB_BASE = 16512
    sb = {}

    def sal(name, shape, dt, at):
        per = int(np.prod(shape[1:])) * mybir.dt.size(dt)
        assert at % 32 == 0 and at + per <= SB_LIMIT, (name, at, per, at + per)
        sb[name] = nc.alloc_sbuf_tensor_at(name, list(shape), dt,
                                           offset=at + SB_BASE)
        return sb[name]

    ident_bf = sal("ident_bf", [128, 128], BF16, 0)
    sal("mlo0", [128, 128], BF16, 256)
    sal("mlo", [128, 128], BF16, 512)
    sal("mhi", [128, 128], BF16, 768)
    mlat = sal("mlat", [64, NT], F32, 1024)
    sal("bcol_qkv", [128, 24], F32, 1088)
    sal("bcol_rq", [128, 2], F32, 1184)
    sal("bcol_rkv", [128, 4], F32, 1216)
    sal("bcol_rout", [128, 8], F32, 1248)
    sal("bcol_Wqkv", [128, 24], F32, 1280)
    gl_sb = sal("gl", [128, 1], BF16, 1376)
    gr_sb = sal("gr", [128, 1], BF16, 1408)
    ones_b = sal("ones_b", [128, 1], BF16, 1440)
    ones_f = sal("ones_f", [1, 128], F32, 1472)
    bcA = sal("bcA", [128, D], BF16, 1984)
    bcB = sal("bcB", [128, D], BF16, 4032)
    bcC = sal("bcC", [128, D], BF16, 6080)
    cols = sal("cols", [128, 8], F32, 8128)
    mk1 = sal("mk1", [1, 16], F32, 8160)
    gateW = sal("gateW_bf", [128, NE * 8], BF16, 8224)
    PB = 8352

    SLOT_A = PB
    SLOT_K = PB + 16384
    SLOT_Q = PB + 51200
    SLOT_V = PB + 83968
    SLOT_R = PB + 118784
    SLOT_M = PB + 126976
    SCR = PB + 159744
    assert SCR == 168096

    xTq = sal("xTq", [128, 8 * 512], BF16, SLOT_A)
    woutT = sal("woutT", [128, 8 * D], BF16, SLOT_A)
    kT = sal("kT", [128, 8 * XW], BF16, SLOT_K)
    qT1 = sal("qT1", [128, 8 * TOK], BF16, SLOT_Q)
    vtok = sal("vtok", [128, NTX * D], BF16, SLOT_V)
    rqT = sal("rqT", [128, 2 * TOK], BF16, SLOT_R)
    xmid = sal("xmid", [128, NT * D], BF16, SLOT_M)

    _o = [SCR]

    def bump(n):
        a = _o[0]
        _o[0] = (a + n + 31) // 32 * 32
        return a

    wbuf = [sal(f"wbuf{i}", [128, 8 * 128], BF16, bump(2048)) for i in range(2)]
    pool_f = sal("pool_f", [128, 8 * 16], F32, bump(512))
    pooledT = sal("pooledT", [128, 8 * 64], BF16, bump(1024))
    rkT = sal("rkT", [128, 2 * 64], BF16, bump(256))
    rvT = sal("rvT", [128, 2 * 64], BF16, bump(256))
    rkhat = sal("rkhat", [65, LH * 64], BF16, bump(512))
    rvhat = sal("rvhat", [64, LH * 65], BF16, bump(544))
    rv_tok = sal("rv_tok", [64, 256], BF16, bump(512))
    cneg_lat = sal("cneg_lat", [LH, TOK], BF16, bump(4096))
    cneg_loc = sal("cneg_loc", [NH, TOK], BF16, bump(4096))
    sq64 = sal("sq64", [64, 512], BF16, bump(1024))
    n2row = sal("n2row", [1, 512], F32, bump(2048))
    khatb = [sal(f"khatb{i}", [65, 256], BF16, bump(512)) for i in range(2)]
    vhatb = [sal(f"vhatb{i}", [128, 130], BF16, bump(288)) for i in range(2)]
    qhatb = [sal(f"qhatb{i}", [65, 128], BF16, bump(256)) for i in range(2)]
    latqb = [sal(f"latqb{i}", [65, 128], BF16, bump(256)) for i in range(2)]
    ptile = [sal(f"ptile{i}", [128, 512], BF16, bump(1024)) for i in range(2)]
    tloc = [sal(f"tloc{i}", [65, 128], BF16, bump(256)) for i in range(2)]
    lr = [sal(f"lr{i}", [1, 128], F32, bump(512)) for i in range(2)]
    grow = [sal(f"grow{i}", [1, 384], F32, bump(1536)) for i in range(2)]
    bcsb = [sal(f"bcs{i}", [128, 256], BF16, bump(512)) for i in range(2)]
    remh = [sal(f"remh{i}", [64, 128], BF16, bump(256)) for i in range(2)]
    latnT = sal("latnT", [128, 2 * 128], BF16, bump(512))
    remT = sal("remT", [128, 8 * 128], BF16, bump(2048))
    mixT = sal("mixT", [128, 8 * 128], BF16, bump(2048))
    f32a = sal("f32a", [128, D], F32, bump(4096))
    f32b = sal("f32b", [128, 512], F32, bump(2048))
    rout_sb = sal("rout_sb", [128, 2 * D], BF16, bump(4096))
    assert _o[0] <= SB_LIMIT, _o[0]

    _oe = [SCR]

    def bumpe(n):
        a = _oe[0]
        _oe[0] = (a + n + 31) // 32 * 32
        return a

    for i in (2, 3):
        khatb.append(sal(f"khatb{i}", [65, 256], BF16, bumpe(512)))
        vhatb.append(sal(f"vhatb{i}", [128, 130], BF16, bumpe(288)))
        qhatb.append(sal(f"qhatb{i}", [65, 128], BF16, bumpe(256)))
        latqb.append(sal(f"latqb{i}", [65, 128], BF16, bumpe(256)))
        tloc.append(sal(f"tloc{i}", [65, 128], BF16, bumpe(256)))
    assert _oe[0] <= SCR + 4096, _oe[0]
    _oe[0] = SCR + 4096
    ptl4 = [ptile[0], ptile[1],
            sal("ptl2", [128, 256], BF16, bumpe(512)),
            sal("ptl3", [128, 256], BF16, bumpe(512))]
    assert _oe[0] <= SCR + 5632, _oe[0]
    _oe[0] = SCR + 15904
    for i in (2, 3):
        lr.append(sal(f"lr{i}", [1, 128], F32, bumpe(512)))
        bcsb.append(sal(f"bcs{i}", [128, 256], BF16, bumpe(512)))
    assert _oe[0] <= SCR + 18976, _oe[0]

    khat2 = [sal(f"khat2_{i}", [65, S], BF16, SLOT_K + i * 16384) for i in range(2)]
    vth = [sal(f"vth{i}", [64, TOK], BF16, SLOT_Q + i * 4096) for i in range(2)]
    vhat2 = [sal(f"vhat2_{i}", [128, 64 * 65], BF16, SLOT_Q + 8192 + i * 8320)
             for i in range(2)]
    attnT2 = sal("attnT2", [128, 2 * S], BF16, SLOT_V)
    nx1T = sal("nx1T", [128, 8 * TOK], BF16, SLOT_V)
    qhat2 = [sal(f"qhat2_{i}", [65, 512], BF16, SCR + i * 1024) for i in range(2)]
    pt_extra = [sal(f"ptx{i}", [128, 512], BF16, SCR + 13344 + i * 1024)
                for i in range(6)]
    ksq2 = sal("ksq2", [64, 512], BF16, SCR + 2048)
    l2 = [sal(f"l2_{i}", [1, 512], F32, SCR + 3072 + i * 2048) for i in range(2)]
    bcs2 = [sal(f"bcs2_{i}", [64, 512], BF16, SCR + 7168 + i * 1024) for i in range(2)]
    mk2 = sal("mk2", [1, 8], F32, SCR + 9216)
    mcau = sal("mcau", [128, 4 * 512], BF16, SCR + 9248)

    nx2T = sal("nx2T", [128, 8 * TOK], BF16, SLOT_K)
    moe_acc = sal("moe_acc", [128, NT * D], F32, SLOT_Q)
    hT = sal("hT", [128, 32 * 512], BF16, SLOT_M)
    w1buf = [sal(f"w1buf{i}", [128, 8 * 512], BF16, SCR + i * 8192) for i in range(2)]
    w2buf = [sal(f"w2buf{i}", [128, 2 * D], BF16, SCR + 16384 + i * 4096)
             for i in range(2)]
    _o3 = [SCR + 24576]

    def bump3(n):
        a = _o3[0]
        _o3[0] = (a + n + 31) // 32 * 32
        return a

    attnTo_t = [sal(f"attnTo{i}", [128, 8 * 128], BF16, bump3(2048))
                for i in range(2)]
    f3a = sal("f3a", [128, D], F32, bump3(4096))
    f3b = sal("f3b", [128, 512], F32, bump3(2048))
    rT3 = sal("rT3", [128, 8 * 128], BF16, bump3(2048))
    gates = sal("gates", [128, NT * 8], F32, bump3(512))
    g8T = sal("g8T", [8, 128], F32, bump3(512))
    ident_f = sal("ident_f", [128, 128], F32, bump3(512))
    lg8 = sal("lg8", [128, 8], F32, bump3(32))
    is1 = sal("is1", [128, 8], F32, bump3(32))
    is2 = sal("is2", [128, 8], F32, bump3(32))
    msk8 = sal("msk8", [128, 8], F32, bump3(32))
    w12 = sal("w12", [128, 4], F32, bump3(32))
    fc2b = sal("fc2b", [NE, D], F32, bump3(4096))
    fc1b = sal("fc1b", [128, NE * 32], F32, bump3(1024))
    assert _o3[0] <= SB_LIMIT, _o3[0]
    xres_t = sal("xres_t", [128, D], F32, SLOT_R)
    nx2f = sal("nx2f", [128, D], F32, SLOT_R + 4096)

    ps = {i: nc.alloc_psum_tensor(f"ps{i}", [128, 512], F32) for i in range(7)}
    psT = nc.alloc_psum_tensor("psT", [128, 1024], BF16)

    def sdma(dst, src, reads=(), writes=(), deps=()):
        return P.op("sync", lambda e: e.dma_start(out=dst, in_=src),
                    reads=reads, writes=writes, sem="sdma", inc=16, deps=deps)

    def gdma(dst, src, reads=(), writes=(), deps=(), dsem="gdma"):
        return P.op("gpsimd", lambda e: e.dma_start(out=dst, in_=src),
                    reads=reads, writes=writes, sem=dsem, inc=16, deps=deps)

    def mm(out, lhsT, rhs, start, stop, reads=(), writes=(), deps=()):
        return P.op("tensor",
                    lambda e: e.matmul(out, lhsT, rhs, start=start, stop=stop),
                    reads=reads, writes=writes, deps=deps)

    def tpose(out, in_, ident, reads=(), writes=(), deps=()):
        return P.op("tensor", lambda e: e.transpose(out, in_, ident),
                    reads=reads, writes=writes, deps=deps)

    def vo(fn, reads=(), writes=(), deps=()):
        return P.op("vector", fn, reads=reads, writes=writes, deps=deps)

    def so(fn, reads=(), writes=(), deps=()):
        return P.op("scalar", fn, reads=reads, writes=writes, deps=deps)

    def go(fn, reads=(), writes=(), deps=()):
        return P.op("gpsimd", fn, reads=reads, writes=writes, deps=deps)

    def plain_ln(src_ap, src_name, fin, sq_scr, sq_name):
        vo(lambda e: e.reduce_sum(cols[:, 0:1], src_ap, axis=AX.X),
           reads=(src_name,), writes=("c0",))
        so(lambda e: e.activation(sq_scr[:, 0:512], src_ap[:, 0:512], AF.Square,
                                  accum_out=cols[:, 5:6]),
           reads=(src_name,), writes=(sq_name, "c5"))
        so(lambda e: e.activation(sq_scr[:, 0:512], src_ap[:, 512:1024], AF.Square,
                                  accum_out=cols[:, 6:7]),
           reads=(src_name,), writes=(sq_name, "c6"))
        vo(lambda e: e.tensor_add(cols[:, 5:6], cols[:, 5:6], cols[:, 6:7]),
           reads=("c5", "c6"), writes=("c5",))
        vo(lambda e: e.tensor_scalar_mul(cols[:, 0:1], cols[:, 0:1], 1.0 / D),
           reads=("c0",), writes=("c0",))
        vo(lambda e: e.tensor_mul(cols[:, 1:2], cols[:, 0:1], cols[:, 0:1]),
           reads=("c0",), writes=("c1",))
        vo(lambda e: e.tensor_scalar(cols[:, 2:3], cols[:, 5:6], 1.0 / D, None,
                                     ALU.mult),
           reads=("c5",), writes=("c2",))
        vo(lambda e: e.tensor_sub(cols[:, 2:3], cols[:, 2:3], cols[:, 1:2]),
           reads=("c2", "c1"), writes=("c2",))
        so(lambda e: e.activation(cols[:, 3:4], cols[:, 2:3], AF.Ln,
                                  bias=np.float32(1e-5)),
           reads=("c2",), writes=("c3",))
        so(lambda e: e.activation(cols[:, 4:5], cols[:, 3:4], AF.Exp,
                                  scale=np.float32(-0.5)),
           reads=("c3",), writes=("c4",))
        fin(cols[:, 0:1], cols[:, 4:5])

    fin_state["sdma"] = sdma
    fin_state["out"] = out_dram
    fin_state["f32a"] = f32a

    gdma(ident_bf[:, :], ident_in[:, :], writes=("ident_bf",), dsem="cst")
    gdma(sb["mlo0"][:, :], mlo0_in[:, :], writes=("mlo0",), dsem="cst")
    gdma(sb["mlo"][:, :], mlo_in[:, :], writes=("mlo",), dsem="cst")
    gdma(sb["mhi"][:, :], mhi_in[:, :], writes=("mhi",), dsem="cst")
    sdma(mlat[:, :], mlat_in[:, :], writes=("mlat",))
    for nm, src in [("bcol_qkv", bcol_qkv_in), ("bcol_rq", bcol_rq_in),
                    ("bcol_rkv", bcol_rkv_in), ("bcol_rout", bcol_rout_in),
                    ("bcol_Wqkv", bcol_Wqkv_in)]:
        sdma(sb[nm][:, :], src[:, :], writes=(nm,))
    gdma(bcA[:, :], outb_in[:, :], writes=("bcA",), dsem="cst")
    gdma(bcB[:, :], ln1w_in[:, :], writes=("bcB",), dsem="cst")
    gdma(bcC[:, :], ln1b_in[:, :], writes=("bcC",), dsem="cst")
    gdma(gl_sb[:, :], gl_in[:, :], writes=("gl",), dsem="cst")
    gdma(gr_sb[:, :], gr_in[:, :], writes=("gr",), dsem="cst")
    cst_last = gdma(gateW[:, :].rearrange("p (r e) -> p r e", r=8),
                    mkap(w_gate, 0, [[NE, 128], [NE * 128, 8], [1, NE]]),
                    writes=("gateW",), dsem="cst")
    for nm in ("ident_bf", "mlo0", "mlo", "mhi", "bcA", "bcB", "bcC",
               "gl", "gr", "gateW"):
        P.wtok[nm] = cst_last
    vo(lambda e: e.memset(ones_b[:, :], 1.0), writes=("ones_b",))
    vo(lambda e: e.memset(ones_f[:, :], 1.0), writes=("ones_f",))
    for i in range(2):
        vo(lambda e, i=i: e.memset(khatb[i][64:65, :], 1.0), writes=(f"khatb{i}_ones",))
        vo(lambda e, i=i: e.memset(vhatb[i][:, 64:65], 1.0), writes=(f"vhatb{i}_o1",))
        vo(lambda e, i=i: e.memset(vhatb[i][:, 129:130], 1.0), writes=(f"vhatb{i}_o2",))
    def p1e_presets():
        for i in (2, 3):
            vo(lambda e, i=i: e.memset(khatb[i][64:65, :], 1.0),
               writes=(f"khatb{i}_ones",))
            vo(lambda e, i=i: e.memset(vhatb[i][:, 64:65], 1.0),
               writes=(f"vhatb{i}_o1",))
            vo(lambda e, i=i: e.memset(vhatb[i][:, 129:130], 1.0),
               writes=(f"vhatb{i}_o2",))
    vo(lambda e: e.memset(rkhat[64:65, :], 1.0), writes=("rkhat_ones",))
    for h in range(LH):
        vo(lambda e, h=h: e.memset(rvhat[:, h * 65 + 64:h * 65 + 65], 1.0),
           writes=(f"rvhat_o{h}",))

    psrot = [0]

    def nps():
        r = psrot[0] % 4
        psrot[0] += 1
        return r

    wrot = [0]

    def load_w(w_ap, wtot, ci, ncols=128):
        sl = wrot[0] % 2
        wrot[0] += 1
        t = gdma(wbuf[sl][:, 0:8 * ncols].rearrange("p (r c) -> p r c", r=8),
                 mkap(w_ap, ci * ncols,
                      [[wtot, 128], [128 * wtot, 8], [1, ncols]]),
                 writes=(f"wbuf{sl}",), dsem=f"wb{sl}")
        return sl, t

    XCH = [(0, 512), (512, 512), (1024, 512), (1536, 512), (2048, 128)]
    for (x0, xw) in XCH:
        txq = gdma(xTq[:, 0:8 * xw].rearrange("p (r w) -> p r w", r=8),
                   mkap(xT_in, x0, [[XW, 128], [128 * XW, 8], [1, xw]]),
                   writes=("xTq",), dsem="xt")
        for li0 in range(xw // 128):
            gtok = x0 + li0 * 128
            if gtok < HALO:
                continue
            li = (gtok - HALO) // 128
            for r in range(8):
                vo(lambda e, r=r, li0=li0, li=li, xw=xw: e.reduce_sum(
                    pool_f[:, r * 16 + li:r * 16 + li + 1],
                    xTq[:, r * xw + li0 * 128: r * xw + (li0 + 1) * 128],
                    axis=AX.X),
                    reads=("xTq",), writes=(f"pool_{r}_{li}",), deps=[txq])
        for ci in range(26):
            if ci < 24:
                w_ap, wtot, wci = w_qkv, 3 * D, ci
            else:
                w_ap, wtot, wci = w_rq, 256, ci - 24
            sl, tw = load_w(w_ap, wtot, wci)
            pn = nps()
            for r in range(8):
                mm(ps[pn][:, 0:xw],
                   wbuf[sl][:, r * 128:(r + 1) * 128],
                   xTq[:, r * xw:(r + 1) * xw],
                   start=(r == 0), stop=(r == 7),
                   reads=(f"wbuf{sl}", "xTq"), writes=(f"ps{pn}",), deps=[tw, txq])
            if ci < 8:
                lo = max(x0, HALO)
                if lo < x0 + xw:
                    off, wd = lo - x0, x0 + xw - lo
                    vo(lambda e, pn=pn, ci=ci, lo=lo, off=off, wd=wd: e.tensor_scalar(
                        qT1[:, ci * TOK + lo - HALO: ci * TOK + lo - HALO + wd],
                        ps[pn][:, off:off + wd], sb["bcol_qkv"][:, ci:ci + 1],
                        0.125, ALU.add, ALU.mult),
                        reads=(f"ps{pn}", "bcol_qkv"), writes=(f"qT1_{ci}",))
            elif ci < 16:
                kc = ci - 8
                vo(lambda e, pn=pn, ci=ci, kc=kc, x0=x0, xw=xw: e.tensor_single_scalar(
                    kT[:, kc * XW + x0: kc * XW + x0 + xw],
                    ps[pn][:, 0:xw], sb["bcol_qkv"][:, ci:ci + 1], ALU.add),
                    reads=(f"ps{pn}", "bcol_qkv"), writes=(f"kT_{kc}",))
            elif ci < 24:
                vc = ci - 16
                tv = vo(lambda e, pn=pn, ci=ci, xw=xw: e.tensor_single_scalar(
                    ptile[1][:, 0:xw], ps[pn][:, 0:xw],
                    sb["bcol_qkv"][:, ci:ci + 1], ALU.add),
                    reads=(f"ps{pn}", "bcol_qkv"), writes=("ptile1",))
                nt_ = xw // 128
                for u in range(nt_):
                    tpose(psT[:, u * 128:(u + 1) * 128],
                          ptile[1][:, u * 128:(u + 1) * 128], ident_bf[:, :],
                          reads=("ptile1", "ident_bf"), writes=("psT",), deps=[tv])
                tt0 = x0 // 128
                for u in range(nt_):
                    vo(lambda e, u=u, tt0=tt0, vc=vc: e.tensor_copy(
                        vtok[:, (tt0 + u) * D + vc * 128:(tt0 + u) * D + (vc + 1) * 128],
                        psT[:, u * 128:(u + 1) * 128]),
                        reads=("psT",), writes=(f"vtok_{tt0 + u}",))
            else:
                rc = ci - 24
                lo = max(x0, HALO)
                if lo < x0 + xw:
                    off, wd = lo - x0, x0 + xw - lo
                    vo(lambda e, pn=pn, rc=rc, lo=lo, off=off, wd=wd: e.tensor_scalar(
                        rqT[:, rc * TOK + lo - HALO: rc * TOK + lo - HALO + wd],
                        ps[pn][:, off:off + wd], sb["bcol_rq"][:, rc:rc + 1],
                        0.125, ALU.add, ALU.mult),
                        reads=(f"ps{pn}", "bcol_rq"), writes=(f"rqT_{rc}",))

    if _TRUNC == "P1a":
        return trunc_finish()
    tpool = vo(lambda e: e.tensor_scalar_mul(pooledT[:, 0:128], pool_f[:, :],
                                             1.0 / 128),
               reads=tuple(f"pool_{r}_{li}" for r in range(8) for li in range(16)),
               writes=("pool_own",))
    tc1 = sdma(mkap(cc1_in, 0, [[16, 128], [2048, 8], [1, 16]]),
               pooledT[:, 0:128].rearrange("p (r l) -> p r l", r=8),
               reads=("pool_own",), writes=("cc1_in",))
    tcc1 = P.op("gpsimd", lambda e: e.collective_compute(
        "AllGather", ALU.bypass, replica_groups=[[0, 1, 2, 3], [4, 5, 6, 7]],
        ins=[cc1_in.opt()], outs=[cc1_out.opt()]),
        reads=("cc1_in",), writes=("cc1_out",), sem="cc1", inc=1, deps=[tc1])
    for rr in range(4):
        tpoolg = sdma(
            pooledT[:, :].rearrange("p (d l) -> p d l", d=8)[:, :, rr * 16:(rr + 1) * 16],
            mkap(cc1_out, rr * 16384, [[16, 128], [2048, 8], [1, 16]]),
            reads=("cc1_out",), writes=("pooledT",), deps=[tcc1])

    towt = gdma(woutT[:, :].rearrange("p (r c) -> p r c", r=8),
                mkap(w_out, 0, [[D, 128], [128 * D, 8], [1, D]]),
                writes=("woutT",), dsem="cst",
                deps=[P.wtok["xTq"]] + list(P.rtok.get("xTq", [])))

    if _TRUNC == "P1b0":
        return trunc_finish()
    for ci in range(4):
        sl, tw = load_w(w_rkv, 512, ci)
        pn = nps()
        for r in range(8):
            mm(ps[pn][:, 0:64],
               wbuf[sl][:, r * 128:(r + 1) * 128],
               pooledT[:, r * 64:(r + 1) * 64],
               start=(r == 0), stop=(r == 7),
               reads=(f"wbuf{sl}", "pooledT"), writes=(f"ps{pn}",),
               deps=[tw, tpoolg])
        dst, dnm = (rkT, "rkT") if ci < 2 else (rvT, "rvT")
        cc = ci % 2
        vo(lambda e, pn=pn, ci=ci, dst=dst, cc=cc: e.tensor_single_scalar(
            dst[:, cc * 64:(cc + 1) * 64], ps[pn][:, 0:64],
            sb["bcol_rkv"][:, ci:ci + 1], ALU.add),
            reads=(f"ps{pn}", "bcol_rkv"), writes=(f"{dnm}_{cc}",))
    for h in range(LH):
        go(lambda e, h=h: e.tensor_copy(
            rkhat[0:64, h * 64:(h + 1) * 64],
            rkT[(h % 2) * 64:(h % 2) * 64 + 64, (h // 2) * 64:(h // 2) * 64 + 64]),
            reads=(f"rkT_{h // 2}",), writes=(f"rkhat_{h}",))
    for cchunk in range(2):
        tpose(psT[0:64, 0:128], rvT[:, cchunk * 64:(cchunk + 1) * 64], ident_bf[:, :],
              reads=(f"rvT_{cchunk}", "ident_bf"), writes=("psT",))
        vo(lambda e, cchunk=cchunk: e.tensor_copy(
            rv_tok[0:64, cchunk * 128:(cchunk + 1) * 128], psT[0:64, 0:128]),
            reads=("psT",), writes=(f"rv_tok_{cchunk}",))
    for h in range(LH):
        go(lambda e, h=h: e.tensor_copy(
            rvhat[0:64, h * 65:h * 65 + 64], rv_tok[0:64, h * 64:(h + 1) * 64]),
            reads=(f"rv_tok_{h // 2}",), writes=(f"rvhat_{h}",))

    def shift_rows(n_heads, kbuf, kname_fn, kw, qbuf, qname_fn, dst, dname_fn, mko):
        for h in range(n_heads):
            hp, hc = (h % 2) * 64, h // 2
            first = True
            for g0 in range(0, kw, 512):
                gw = min(512, kw - g0)
                t1 = so(lambda e, hp=hp, hc=hc, g0=g0, gw=gw: e.activation(
                    sq64[0:64, 0:gw],
                    kbuf[hp:hp + 64, hc * kw + g0: hc * kw + g0 + gw], AF.Square),
                    reads=(kname_fn(hc),), writes=("sq64",))
                t2 = mm(ps[6][0:1, 0:gw], ones_b[0:64, 0:1], sq64[0:64, 0:gw],
                        start=True, stop=True, reads=("sq64", "ones_b"),
                        writes=("ps6",), deps=[t1])
                t3 = vo(lambda e, gw=gw: e.reduce_max(cols[0:1, 7:8],
                                                      ps[6][0:1, 0:gw], axis=AX.X),
                        reads=("ps6",), writes=("c7",))
                if first:
                    vo(lambda e, h=h, mko=mko: e.tensor_copy(
                        mk1[:, mko + h:mko + h + 1], cols[0:1, 7:8]),
                        reads=("c7",), writes=(f"mk_{mko + h}",))
                    first = False
                else:
                    vo(lambda e, h=h, mko=mko: e.tensor_max(
                        mk1[:, mko + h:mko + h + 1], mk1[:, mko + h:mko + h + 1],
                        cols[0:1, 7:8]),
                        reads=("c7", f"mk_{mko + h}"), writes=(f"mk_{mko + h}",))
            for g0 in range(0, TOK, 512):
                t4 = so(lambda e, hp=hp, hc=hc, g0=g0: e.activation(
                    sq64[0:64, 0:512],
                    qbuf[hp:hp + 64, hc * TOK + g0: hc * TOK + g0 + 512], AF.Square),
                    reads=(qname_fn(hc),), writes=("sq64",))
                t5 = mm(ps[6][0:1, 0:512], ones_b[0:64, 0:1], sq64[0:64, 0:512],
                        start=True, stop=True, reads=("sq64", "ones_b"),
                        writes=("ps6",), deps=[t4])
                vo(lambda e: e.tensor_copy(n2row[:, 0:512], ps[6][0:1, 0:512]),
                   reads=("ps6",), writes=("n2row",))
                vo(lambda e, h=h, mko=mko: e.tensor_single_scalar(
                    n2row[:, 0:512], n2row[:, 0:512], mk1[:, mko + h:mko + h + 1],
                    ALU.mult),
                    reads=("n2row", f"mk_{mko + h}"), writes=("n2row",))
                so(lambda e: e.activation(n2row[:, 0:512], n2row[:, 0:512], AF.Ln,
                                          bias=np.float32(1e-30)),
                   reads=("n2row",), writes=("n2row",))
                so(lambda e: e.activation(n2row[:, 0:512], n2row[:, 0:512], AF.Exp,
                                          scale=np.float32(-0.5) if False else np.float32(0.5)),
                   reads=("n2row",), writes=("n2row",))
                vo(lambda e: e.tensor_scalar_mul(ptile[1][0:1, 0:512],
                                                 n2row[:, 0:512], -1.0),
                   reads=("n2row",), writes=("ptile1",))
                sdma(dst[h:h + 1, g0:g0 + 512], ptile[1][0:1, 0:512],
                     reads=("ptile1",), writes=(dname_fn(h),))

    shift_rows(LH, rkT, lambda hc: f"rkT_{hc}", 64,
               rqT, lambda hc: f"rqT_{hc}", cneg_lat, lambda h: f"cneg_lat{h}", 0)
    shift_rows(NH, kT, lambda hc: f"kT_{hc}", XW,
               qT1, lambda hc: f"qT1_{hc}", cneg_loc, lambda h: f"cneg_loc{h}", 0)

    trout = gdma(rout_sb[:, :].rearrange("p (r c) -> p r c", r=2),
                 mkap(w_rout, 0, [[D, 128], [128 * D, 2], [1, D]]),
                 writes=("rout_sb",), dsem="cst", deps=[towt])
    P.wtok["woutT"] = trout

    if _TRUNC == "P1b":
        return trunc_finish()
    for i in range(2):
        if f"ptile{i}" in P.wtok:
            P.wtok[f"ptl{i}"] = P.wtok[f"ptile{i}"]
        P.rtok[f"ptl{i}"] = list(P.rtok.get(f"ptile{i}", []))
    p1e_presets()
    for t in range(NT):
        for h in range(LH if _SUB >= 1 else 0):
            par = (t * LH + h) % 4
            bpar = par % 2
            lq = latqb[par]
            lqn = f"latqb{par}"
            hp, hc = (h % 2) * 64, h // 2
            ta = go(lambda e, hp=hp, hc=hc, t=t, lq=lq: e.tensor_copy(
                lq[0:64, :], rqT[hp:hp + 64, hc * TOK + t * 128: hc * TOK + (t + 1) * 128]),
                reads=(f"rqT_{hc}",), writes=(lqn,))
            tb = sdma(lq[64:65, :], cneg_lat[h:h + 1, t * 128:(t + 1) * 128],
                      reads=(f"cneg_lat{h}",), writes=(lqn,))
            pA = bpar
            mm(ps[pA][0:64, 0:128], rkhat[:, h * 64:(h + 1) * 64], lq[:, :],
               start=True, stop=True,
               reads=(lqn, f"rkhat_{h}", "rkhat_ones"), writes=(f"ps{pA}",),
               deps=[ta, tb])
            so(lambda e, t=t, pA=pA, par=par: e.activation(
                ptl4[par][0:64, 0:128], ps[pA][0:64, 0:128], AF.Exp,
                bias=mlat[:, t:t + 1]),
                reads=(f"ps{pA}", "mlat"), writes=(f"ptl{par}",))
            pE = 2 + bpar
            mm(ps[pE][0:65, 0:128], rvhat[0:64, h * 65:(h + 1) * 65],
               ptl4[par][0:64, 0:128], start=True, stop=True,
               reads=(f"ptl{par}", f"rvhat_{h}", f"rvhat_o{h}"),
               writes=(f"ps{pE}",))
            vo(lambda e, pE=pE, par=par: e.tensor_scalar_max(
                lr[par][:, 0:128], ps[pE][64:65, 0:128], 1e-30),
                reads=(f"ps{pE}",), writes=(f"lr{par}",))
            vo(lambda e, par=par: e.reciprocal(lr[par][:, 0:128], lr[par][:, 0:128]),
               reads=(f"lr{par}",), writes=(f"lr{par}",))
            pG = 4 + bpar
            mm(ps[pG][:, 0:128], ones_f[0:1, 0:128], lr[par][:, 0:128],
               start=True, stop=True, reads=(f"lr{par}", "ones_f"),
               writes=(f"ps{pG}",))
            so(lambda e, pG=pG, par=par: e.copy(
                bcsb[par][:, 0:128], ps[pG][:, 0:128]),
                reads=(f"ps{pG}",), writes=(f"bcs{par}",))
            vo(lambda e, hp=hp, hc=hc, pE=pE, par=par: e.tensor_tensor(
                latnT[hp:hp + 64, hc * 128: hc * 128 + 128],
                ps[pE][0:64, 0:128], bcsb[par][0:64, 0:128], ALU.mult),
                reads=(f"ps{pE}", f"bcs{par}"), writes=(f"latnT_{h}",))
        for ci in range(8 if _SUB >= 2 else 0):
            pn = nps()
            for r in range(2):
                mm(ps[pn][:, 0:128],
                   rout_sb[:, r * D + ci * 128: r * D + (ci + 1) * 128],
                   latnT[:, r * 128:(r + 1) * 128],
                   start=(r == 0), stop=(r == 1),
                   reads=("rout_sb",) + tuple(f"latnT_{hh}" for hh in range(LH)),
                   writes=(f"ps{pn}",), deps=[trout])
            vo(lambda e, pn=pn, ci=ci: e.tensor_single_scalar(
                remT[:, ci * 128:(ci + 1) * 128], ps[pn][:, 0:128],
                sb["bcol_rout"][:, ci:ci + 1], ALU.add),
                reads=(f"ps{pn}", "bcol_rout"), writes=(f"remT_{ci}",))
        for h in range(NH if _SUB >= 3 else 0):
            par = (t * NH + h) % 4
            bpar = par % 2
            kb, vb, qb = khatb[par], vhatb[par], qhatb[par]
            kbn, vbn, qbn = f"khatb{par}", f"vhatb{par}", f"qhatb{par}"
            hp, hc = (h % 2) * 64, h // 2
            tk = go(lambda e, hp=hp, hc=hc, t=t, kb=kb: e.tensor_copy(
                kb[0:64, :], kT[hp:hp + 64, hc * XW + t * 128: hc * XW + t * 128 + 256]),
                reads=(f"kT_{hc}",), writes=(kbn,))
            tv1 = go(lambda e, h=h, t=t, vb=vb: e.tensor_copy(
                vb[:, 0:64], vtok[:, t * D + h * 64: t * D + h * 64 + 64]),
                reads=(f"vtok_{t}",), writes=(vbn,))
            tv2 = go(lambda e, h=h, t=t, vb=vb: e.tensor_copy(
                vb[:, 65:129], vtok[:, (t + 1) * D + h * 64:(t + 1) * D + h * 64 + 64]),
                reads=(f"vtok_{t + 1}",), writes=(vbn,))
            trh = go(lambda e, hp=hp, hc=hc, par=par: e.tensor_copy(
                remh[par % 2][0:64, :], remT[hp:hp + 64, hc * 128: hc * 128 + 128]),
                reads=(f"remT_{hc}",), writes=(f"remh{par % 2}",))
            tq1 = go(lambda e, hp=hp, hc=hc, t=t, qb=qb: e.tensor_copy(
                qb[0:64, :], qT1[hp:hp + 64, hc * TOK + t * 128: hc * TOK + (t + 1) * 128]),
                reads=(f"qT1_{hc}",), writes=(qbn,))
            tq2 = sdma(qb[64:65, :], cneg_loc[h:h + 1, t * 128:(t + 1) * 128],
                       reads=(f"cneg_loc{h}",), writes=(qbn,))
            pA = bpar
            mm(ps[pA][:, 0:128], kb[:, 0:128], qb[:, :], start=True, stop=True,
               reads=(kbn, qbn, f"khatb{par}_ones"), writes=(f"ps{pA}",),
               deps=[tk, tq1, tq2])
            mm(ps[pA][:, 128:256], kb[:, 128:256], qb[:, :], start=True, stop=True,
               reads=(kbn, qbn), writes=(f"ps{pA}",))
            mlon = "mlo0" if t == 0 else "mlo"
            vo(lambda e, pA=pA, mlon=mlon: e.tensor_tensor(
                ps[pA][:, 0:128], ps[pA][:, 0:128], sb[mlon][:, :], ALU.add),
                reads=(f"ps{pA}", mlon), writes=(f"ps{pA}",))
            vo(lambda e, pA=pA: e.tensor_tensor(
                ps[pA][:, 128:256], ps[pA][:, 128:256], sb["mhi"][:, :], ALU.add),
                reads=(f"ps{pA}", "mhi"), writes=(f"ps{pA}",))
            so(lambda e, pA=pA, par=par: e.activation(
                ptl4[par][:, 0:256], ps[pA][:, 0:256], AF.Exp),
                reads=(f"ps{pA}",), writes=(f"ptl{par}",))
            if _SUB < 4:
                continue
            pE = 2 + bpar
            mm(ps[pE][0:65, 0:128], vb[:, 0:65], ptl4[par][:, 0:128],
               start=True, stop=False,
               reads=(vbn, f"ptl{par}", f"vhatb{par}_o1"), writes=(f"ps{pE}",),
               deps=[tv1])
            mm(ps[pE][0:65, 0:128], vb[:, 65:130], ptl4[par][:, 128:256],
               start=False, stop=True,
               reads=(vbn, f"ptl{par}", f"vhatb{par}_o2"), writes=(f"ps{pE}",),
               deps=[tv2])
            tl1 = so(lambda e, pE=pE, par=par: e.copy(
                tloc[par][0:64, :], ps[pE][0:64, 0:128]),
                reads=(f"ps{pE}",), writes=(f"tloc{par}",))
            vo(lambda e, pE=pE, par=par: e.tensor_scalar_max(
                lr[par][:, 0:128], ps[pE][64:65, 0:128], 1e-30),
                reads=(f"ps{pE}",), writes=(f"lr{par}",))
            if _SUB < 5:
                continue
            pG = 4 + bpar
            mm(ps[pG][0:1, 0:128], gl_sb[0:64, :], tloc[par][0:64, :],
               start=True, stop=True, reads=(f"tloc{par}", "gl"),
               writes=(f"ps{pG}",), deps=[tl1])
            mm(ps[pG][0:1, 128:256], gr_sb[0:64, :], remh[par % 2][0:64, :],
               start=True, stop=True,
               reads=(f"remh{par % 2}", "gr"), writes=(f"ps{pG}",))
            vo(lambda e, par=par: e.reciprocal(lr[par][:, 0:128], lr[par][:, 0:128]),
               reads=(f"lr{par}",), writes=(f"lr{par}",))
            vo(lambda e, pG=pG, par=par: e.tensor_tensor(
                grow[par % 2][:, 0:128], ps[pG][0:1, 0:128], lr[par][:, 0:128], ALU.mult),
                reads=(f"ps{pG}", f"lr{par}"), writes=(f"grow{par % 2}",))
            vo(lambda e, pG=pG, par=par: e.tensor_tensor(
                grow[par % 2][:, 0:128], grow[par % 2][:, 0:128], ps[pG][0:1, 128:256],
                ALU.add),
                reads=(f"grow{par % 2}", f"ps{pG}"), writes=(f"grow{par % 2}",))
            so(lambda e, par=par: e.activation(
                grow[par % 2][:, 128:256], grow[par % 2][:, 0:128], AF.Exp,
                scale=np.float32(-1.0)),
                reads=(f"grow{par % 2}",), writes=(f"grow{par % 2}",))
            vo(lambda e, par=par: e.tensor_scalar_add(
                grow[par % 2][:, 128:256], grow[par % 2][:, 128:256], 1.0),
                reads=(f"grow{par % 2}",), writes=(f"grow{par % 2}",))
            vo(lambda e, par=par: e.reciprocal(
                grow[par % 2][:, 128:256], grow[par % 2][:, 128:256]),
                reads=(f"grow{par % 2}",), writes=(f"grow{par % 2}",))
            vo(lambda e, par=par: e.tensor_tensor(
                grow[par % 2][:, 0:128], grow[par % 2][:, 128:256], lr[par][:, 0:128],
                ALU.mult),
                reads=(f"grow{par % 2}", f"lr{par}"), writes=(f"grow{par % 2}",))
            vo(lambda e, par=par: e.tensor_scalar(
                grow[par % 2][:, 256:384], grow[par % 2][:, 128:256], -1.0, 1.0,
                ALU.mult, ALU.add),
                reads=(f"grow{par % 2}",), writes=(f"grow{par % 2}",))
            if _SUB < 6:
                continue
            mm(ps[pG][:, 256:384], ones_f[0:1, 0:128], grow[par % 2][:, 0:128],
               start=True, stop=True, reads=(f"grow{par % 2}", "ones_f"),
               writes=(f"ps{pG}",))
            mm(ps[pG][:, 384:512], ones_f[0:1, 0:128], grow[par % 2][:, 256:384],
               start=True, stop=True, reads=(f"grow{par % 2}", "ones_f"),
               writes=(f"ps{pG}",))
            so(lambda e, pG=pG, par=par: e.copy(
                bcsb[par][:, 0:256], ps[pG][:, 256:512]),
                reads=(f"ps{pG}",), writes=(f"bcs{par}",))
            vo(lambda e, par=par: e.tensor_tensor(
                ptl4[par][0:64, 0:128], tloc[par][0:64, :], bcsb[par][0:64, 0:128],
                ALU.mult),
                reads=(f"tloc{par}", f"bcs{par}"), writes=(f"ptl{par}",))
            vo(lambda e, par=par: e.tensor_tensor(
                ptl4[par][0:64, 128:256],
                remh[par % 2][0:64, :],
                bcsb[par][0:64, 128:256], ALU.mult),
                reads=(f"remh{par % 2}", f"bcs{par}"), writes=(f"ptl{par}",))
            vo(lambda e, hp=hp, hc=hc, par=par: e.tensor_tensor(
                mixT[hp:hp + 64, hc * 128: hc * 128 + 128],
                ptl4[par][0:64, 0:128], ptl4[par][0:64, 128:256], ALU.add),
                reads=(f"ptl{par}",), writes=(f"mixT_{hc}",))
        if _SUB < 7:
            continue
        tx = sdma(f32a[:, :], xtok_in[t * 128:(t + 1) * 128, :], writes=("f32a",))
        for half in range(2):
            pn = half
            for r in range(8):
                mm(ps[pn][:, 0:512],
                   mixT[:, r * 128:(r + 1) * 128],
                   woutT[:, r * D + half * 512: r * D + half * 512 + 512],
                   start=(r == 0), stop=(r == 7),
                   reads=("woutT",) + tuple(f"mixT_{rr}" for rr in range(8)),
                   writes=(f"ps{pn}",), deps=[towt])
            vo(lambda e, pn=pn, half=half: e.tensor_tensor(
                f32a[:, half * 512:(half + 1) * 512], ps[pn][:, 0:512],
                f32a[:, half * 512:(half + 1) * 512], ALU.add),
                reads=(f"ps{pn}", "f32a"), writes=("f32a",), deps=[tx])
            vo(lambda e, half=half: e.tensor_tensor(
                f32a[:, half * 512:(half + 1) * 512],
                f32a[:, half * 512:(half + 1) * 512],
                bcA[:, half * 512:(half + 1) * 512], ALU.add),
                reads=("f32a", "bcA"), writes=("f32a",))

        def fin_xmid(mean_ap, rstd_ap, t=t):
            vo(lambda e: e.tensor_scalar(
                xmid[:, t * D:(t + 1) * D], f32a[:, :], mean_ap, rstd_ap,
                ALU.subtract, ALU.mult),
                reads=("f32a", "c0", "c4"), writes=(f"xmid_{t}",))
            if _DEBUG:
                gdma(dbg["xmid"][t * 128:(t + 1) * 128, :],
                     xmid[:, t * D:(t + 1) * D], reads=(f"xmid_{t}",))
        plain_ln(f32a[:, :], "f32a", fin_xmid, f32b, "f32b")

    if _TRUNC == "P1e":
        return trunc_finish()
    P.barrier()
    for t in range(NT):
        def fin_nx1(mean_ap, rstd_ap, t=t):
            vo(lambda e: e.tensor_scalar(
                f32a[:, :], xmid[:, t * D:(t + 1) * D], mean_ap, rstd_ap,
                ALU.subtract, ALU.mult),
                reads=(f"xmid_{t}", "c0", "c4"), writes=("f32a",))
        plain_ln(xmid[:, t * D:(t + 1) * D], f"xmid_{t}", fin_nx1, f32b, "f32b")
        vo(lambda e: e.tensor_tensor(f32a[:, :], f32a[:, :], bcB[:, :], ALU.mult),
           reads=("f32a", "bcB"), writes=("f32a",))
        vo(lambda e: e.tensor_tensor(f32a[:, :], f32a[:, :], bcC[:, :], ALU.add),
           reads=("f32a", "bcC"), writes=("f32a",))
        if _DEBUG:
            sdma(dbg["nx1"][t * 128:(t + 1) * 128, :], f32a[:, :], reads=("f32a",))
        for half in range(2):
            tch = vo(lambda e, half=half: e.tensor_copy(
                ptile[0][:, 0:512], f32a[:, half * 512:(half + 1) * 512]),
                reads=("f32a",), writes=("ptile0",))
            for u in range(4):
                tpose(psT[:, (half * 4 + u) * 128:(half * 4 + u + 1) * 128],
                      ptile[0][:, u * 128:(u + 1) * 128], ident_bf[:, :],
                      reads=("ptile0", "ident_bf"), writes=("psT",), deps=[tch])
        vo(lambda e, t=t: e.tensor_copy(
            nx1T[:, :].rearrange("p (r w) -> p r w", r=8)
            [:, :, t * 128:(t + 1) * 128],
            psT[:, 0:1024].rearrange("p (r w) -> p r w", r=8)),
            reads=("psT",), writes=tuple(f"nx1T_{r}" for r in range(8)))

    ccst = [sal(f"ccst{i}", [128, TOK], BF16, SCR + 13344 + 6144 + i * 4096)
            for i in range(2)]
    kv_toks = []
    q_toks = []
    for ci in list(range(8, 24)) + list(range(8)):
        sl, tw = load_w(w_Wqkv, 3 * D, ci)
        qsl = ci % 2
        for gg in range(4):
            g0 = gg * 512
            pn = nps()
            for r in range(8):
                mm(ps[pn][:, 0:512],
                   wbuf[sl][:, r * 128:(r + 1) * 128],
                   nx1T[:, r * TOK + g0: r * TOK + g0 + 512],
                   start=(r == 0), stop=(r == 7),
                   reads=(f"wbuf{sl}", f"nx1T_{r}"), writes=(f"ps{pn}",), deps=[tw])
            if ci < 8:
                vo(lambda e, pn=pn, ci=ci, qsl=qsl, g0=g0: e.tensor_scalar(
                    ccst[qsl][:, g0:g0 + 512], ps[pn][:, 0:512],
                    sb["bcol_Wqkv"][:, ci:ci + 1],
                    0.125, ALU.add, ALU.mult),
                    reads=(f"ps{pn}", "bcol_Wqkv"), writes=(f"ccst{qsl}",))
            else:
                vo(lambda e, pn=pn, ci=ci, qsl=qsl, g0=g0: e.tensor_single_scalar(
                    ccst[qsl][:, g0:g0 + 512], ps[pn][:, 0:512],
                    sb["bcol_Wqkv"][:, ci:ci + 1], ALU.add),
                    reads=(f"ps{pn}", "bcol_Wqkv"), writes=(f"ccst{qsl}",))
        if ci < 8:
            q_toks.append(sdma(cc2q_in[ci * 128:(ci + 1) * 128, :],
                               ccst[qsl][:, :], reads=(f"ccst{qsl}",),
                               writes=("cc2q_in",)))
        else:
            kv_toks.append(sdma(cc2kv_in[(ci - 8) * 128:(ci - 7) * 128, :],
                                ccst[qsl][:, :], reads=(f"ccst{qsl}",),
                                writes=("cc2kv_in",)))

        if ci == 23:
            tcc2kv = P.op("gpsimd", lambda e: e.collective_compute(
                "AllGather", ALU.bypass,
                replica_groups=[[0, 1, 2, 3], [4, 5, 6, 7]],
                ins=[cc2kv_in.opt()], outs=[cc2kv_out.opt()]),
                reads=("cc2kv_in",), writes=("cc2kv_out",), sem="cc2", inc=1,
                deps=kv_toks)

    tcc2q = P.op("gpsimd", lambda e: e.collective_compute(
        "AllGather", ALU.bypass, replica_groups=[[0, 1, 2, 3], [4, 5, 6, 7]],
        ins=[cc2q_in.opt()], outs=[cc2q_out.opt()]),
        reads=("cc2q_in",), writes=("cc2q_out",), sem="cc2", inc=1, deps=q_toks)

    def km_copy(e):
        src = mkap(cc2kv_out, dynbase("kv", 4 * HD * TOK),
                   [[TOK, 4 * HD], [2 * D * TOK, 4], [1, TOK]])
        dd = mkap(km_d, 0, [[S, 4 * HD], [TOK, 4], [1, TOK]])
        return e.dma_start(out=dd, in_=src)

    def vm_copy(e):
        src = mkap(cc2kv_out, dynbase("kv", 4 * HD * TOK) + D * TOK,
                   [[TOK, 4 * HD], [2 * D * TOK, 4], [1, TOK]])
        dd = mkap(vm_d, 0, [[S, 4 * HD], [TOK, 4], [1, TOK]])
        return e.dma_start(out=dd, in_=src)

    def qm_copy(e):
        src = mkap(cc2q_out, dynbase("kv", 4 * HD * TOK),
                   [[TOK, 4 * HD], [D * TOK, 4], [1, TOK]])
        dd = mkap(qm_d, 0, [[S, 4 * HD], [TOK, 4], [1, TOK]])
        return e.dma_start(out=dd, in_=src)

    P.op("gpsimd", km_copy, reads=("cc2kv_out",), writes=("km_d",),
         sem="gmy", inc=16, deps=[tcc2kv])
    tkvm = P.op("gpsimd", vm_copy, reads=("cc2kv_out",), writes=("vm_d",),
                sem="gmy", inc=16, deps=[tcc2kv])
    P.wtok["km_d"] = tkvm
    P.wtok["vm_d"] = tkvm
    tkm = tvm = tkvm
    tqm = P.op("gpsimd", qm_copy, reads=("cc2q_out",), writes=("qm_d",),
               sem="gam", inc=16, deps=[tcc2q])

    tmcau = gdma(mcau[:, :].rearrange("p (m w) -> p m w", m=4),
                 mkap(mcau_in, 0, [[512, 128], [128 * 512, 4], [1, 512]]),
                 writes=("mcau",), dsem="c3")

    for ei in range(NE):
        gdma(mkap(w1bf, ei * D * FFN, [[FFN, D], [1, FFN]]),
             mkap(w1_in, ei * D * FFN, [[FFN, D], [1, FFN]]),
             writes=(f"w1bf_{ei}",), deps=[tvm], dsem="wc")
        wc_last = gdma(mkap(w2bf, ei * FFN * D, [[D, FFN], [1, D]]),
                       mkap(w2_in, ei * FFN * D, [[D, FFN], [1, D]]),
                       writes=(f"w2bf_{ei}",), deps=[tvm], dsem="wc")
    for ei in range(NE):
        P.wtok[f"w1bf_{ei}"] = wc_last
        P.wtok[f"w2bf_{ei}"] = wc_last

    if _TRUNC == "P1f":
        return trunc_finish()
    P.barrier(exclude=("wc",))
    ptile4 = {0: [ptile[0], pt_extra[0], pt_extra[1], pt_extra[2]],
              1: [ptile[1], pt_extra[3], pt_extra[4], pt_extra[5]]}
    ksq2b = sal("ksq2b", [64, 512], BF16, SCR + 13344 + 6144 + 8192)

    def p2_prep(hh):
        par = hh % 2
        kb = khat2[par]
        kbn = f"khat2_{par}"

        tkl = sdma(kb[0:64, :], km_d[hh * 64:(hh + 1) * 64, :],
                   reads=("km_d",), writes=(kbn,), deps=[tkm])
        if hh < 2:
            vo(lambda e, kb=kb: e.memset(kb[64:65, :], 1.0), writes=(f"{kbn}_ones",))
        vb = vhat2[par]
        vbn = f"vhat2_{par}"
        if hh < 2:
            vo(lambda e, vb=vb: e.memset(
                vb[:, :].rearrange("p (t j) -> p t j", j=65)[:, :, 64:65], 1.0),
               writes=(f"{vbn}_ones",))
        for s4 in range(4):
            vs = vth[s4 % 2]
            vsn = f"vth{s4 % 2}"

            tvl = sdma(vs[0:64, :], vm_d[hh * 64:(hh + 1) * 64,
                                         s4 * TOK:(s4 + 1) * TOK],
                       reads=("vm_d",), writes=(vsn,), deps=[tvm])
            for u in range(16):
                tpose(psT[:, u * 64:(u + 1) * 64],
                      vs[0:64, u * 128:(u + 1) * 128], ident_bf[0:64, 0:64],
                      reads=(vsn, "ident_bf"), writes=("psT",), deps=[tvl])
            vo(lambda e, vb=vb, s4=s4: e.tensor_copy(
                vb[:, :].rearrange("p (t j) -> p t j", j=65)
                [:, s4 * 16:(s4 + 1) * 16, 0:64],
                psT[:, 0:1024].rearrange("p (t j) -> p t j", j=64)),
                reads=("psT",), writes=(vbn,))
        for g0 in range(0, S, 512):
            t1 = so(lambda e, kb=kb, g0=g0: e.activation(
                ksq2b[0:64, 0:512], kb[0:64, g0:g0 + 512], AF.Square),
                reads=(kbn,), writes=("ksq2b",), deps=[tkl])
            t2 = mm(ps[5][0:1, 0:512], ones_b[0:64, 0:1], ksq2b[0:64, 0:512],
                    start=True, stop=True, reads=("ksq2b", "ones_b"),
                    writes=("ps5",), deps=[t1])
            vo(lambda e: e.reduce_max(cols[0:1, 7:8], ps[5][0:1, 0:512], axis=AX.X),
               reads=("ps5",), writes=("c7",))
            if g0 == 0:
                vo(lambda e, hh=hh: e.tensor_copy(mk2[:, hh:hh + 1], cols[0:1, 7:8]),
                   reads=("c7",), writes=(f"mk2_{hh}",))
            else:
                vo(lambda e, hh=hh: e.tensor_max(mk2[:, hh:hh + 1], mk2[:, hh:hh + 1],
                                                 cols[0:1, 7:8]),
                   reads=("c7", f"mk2_{hh}"), writes=(f"mk2_{hh}",))

    p2_prep(0)
    for hh in range(4):
        par = hh % 2
        kb = khat2[par]
        kbn = f"khat2_{par}"
        vb = vhat2[par]
        vbn = f"vhat2_{par}"
        if hh < 3:
            p2_prep(hh + 1)
        for tau in range(16):
            qpar = tau % 2
            qb = qhat2[qpar]
            qbn = f"qhat2_{qpar}"
            tql = sdma(qb[0:64, :], qm_d[hh * 64:(hh + 1) * 64,
                                         tau * 512:(tau + 1) * 512],
                       reads=("qm_d",), writes=(qbn,), deps=[tqm])
            t1 = so(lambda e, qb=qb: e.activation(ksq2[0:64, 0:512], qb[0:64, :],
                                                  AF.Square),
                    reads=(qbn,), writes=("ksq2",), deps=[tql])
            t2 = mm(ps[6][0:1, 0:512], ones_b[0:64, 0:1], ksq2[0:64, 0:512],
                    start=True, stop=True, reads=("ksq2", "ones_b"),
                    writes=("ps6",), deps=[t1])
            vo(lambda e, qpar=qpar: e.tensor_copy(l2[qpar][:, 0:512],
                                                  ps[6][0:1, 0:512]),
               reads=("ps6",), writes=(f"l2_{qpar}",))
            vo(lambda e, hh=hh, qpar=qpar: e.tensor_single_scalar(
                l2[qpar][:, 0:512], l2[qpar][:, 0:512], mk2[:, hh:hh + 1], ALU.mult),
                reads=(f"l2_{qpar}", f"mk2_{hh}"), writes=(f"l2_{qpar}",))
            so(lambda e, qpar=qpar: e.activation(
                l2[qpar][:, 0:512], l2[qpar][:, 0:512], AF.Ln,
                bias=np.float32(1e-30)),
                reads=(f"l2_{qpar}",), writes=(f"l2_{qpar}",))
            so(lambda e, qpar=qpar: e.activation(
                l2[qpar][:, 0:512], l2[qpar][:, 0:512], AF.Exp,
                scale=np.float32(0.5)),
                reads=(f"l2_{qpar}",), writes=(f"l2_{qpar}",))
            t7 = vo(lambda e, qb=qb, qpar=qpar: e.tensor_scalar_mul(
                qb[64:65, :], l2[qpar][:, 0:512], -1.0),
                reads=(f"l2_{qpar}",), writes=(qbn,))
            nch = 4 * tau + 4
            pE = 2 + qpar
            QKB = (0, 1, 6)
            exp_done = {}

            def emit_qk(kc):
                pA = QKB[kc % 3]
                mm(ps[pA][:, 0:512], kb[:, kc * 128:(kc + 1) * 128], qb[:, :],
                   start=True, stop=True,
                   reads=(kbn, qbn, f"khat2_{par}_ones"), writes=(f"ps{pA}",),
                   deps=[t7])
                dk = kc - 4 * tau
                if dk >= 0:
                    vo(lambda e, pA=pA, dk=dk: e.tensor_tensor(
                        ps[pA][:, 0:512], ps[pA][:, 0:512],
                        mcau[:, dk * 512:(dk + 1) * 512], ALU.add),
                        reads=(f"ps{pA}", "mcau"), writes=(f"ps{pA}",),
                        deps=[tmcau])
                pp = kc % 4
                so(lambda e, pA=pA, pp=pp, qpar=qpar: e.activation(
                    ptile4[qpar][pp][:, 0:512], ps[pA][:, 0:512], AF.Exp),
                    reads=(f"ps{pA}",), writes=(f"pt{qpar}_{pp}",))

            def emit_av(kc):
                pp = kc % 4
                mm(ps[pE][0:65, 0:512],
                   vb[:, kc * 65:(kc + 1) * 65], ptile4[qpar][pp][:, 0:512],
                   start=(kc == 0), stop=(kc == nch - 1),
                   reads=(f"pt{qpar}_{pp}", vbn, f"vhat2_{par}_ones"),
                   writes=(f"ps{pE}",))

            lag = 3
            for kc in range(min(lag, nch)):
                emit_qk(kc)
            for kc in range(lag, nch):
                emit_qk(kc)
                emit_av(kc - lag)
            for kc in range(max(0, nch - lag), nch):
                emit_av(kc)
            vo(lambda e, pE=pE, qpar=qpar: e.tensor_scalar_max(
                l2[qpar][:, 0:512], ps[pE][64:65, 0:512], 1e-30),
                reads=(f"ps{pE}",), writes=(f"l2_{qpar}",))
            vo(lambda e, qpar=qpar: e.reciprocal(l2[qpar][:, 0:512],
                                                 l2[qpar][:, 0:512]),
               reads=(f"l2_{qpar}",), writes=(f"l2_{qpar}",))
            mm(ps[6][0:64, 0:512], ones_f[0:1, 0:64], l2[qpar][:, 0:512],
               start=True, stop=True, reads=(f"l2_{qpar}", "ones_f"),
               writes=("ps6",))
            vo(lambda e, qpar=qpar: e.tensor_copy(bcs2[qpar][:, 0:512],
                                                  ps[6][0:64, 0:512]),
               reads=("ps6",), writes=(f"bcs2_{qpar}",))
            vo(lambda e, hh=hh, tau=tau, pE=pE, qpar=qpar: e.tensor_tensor(
                attnT2[(hh % 2) * 64:(hh % 2) * 64 + 64,
                       (hh // 2) * S + tau * 512:(hh // 2) * S + (tau + 1) * 512],
                ps[pE][0:64, 0:512], bcs2[qpar][:, 0:512], ALU.mult),
                reads=(f"ps{pE}", f"bcs2_{qpar}"), writes=(f"attnT2_{hh}",))

    tct1 = sdma(cc3_in[0:128, :], attnT2[:, 0:S],
                reads=("attnT2_0", "attnT2_1"), writes=("cc3_in",))
    tct2 = sdma(cc3_in[128:256, :], attnT2[:, S:2 * S],
                reads=("attnT2_2", "attnT2_3"), writes=("cc3_in",))
    tcc3 = P.op("gpsimd", lambda e: e.collective_compute(
        "AllGather", ALU.bypass, replica_groups=[[0, 1, 2, 3], [4, 5, 6, 7]],
        ins=[cc3_in.opt()], outs=[cc3_out.opt()]),
        reads=("cc3_in",), writes=("cc3_out",), sem="cc3", inc=1,
        deps=[tct1, tct2])

    def my_cols(e):
        src = mkap(cc3_out, dynbase("c3", TOK), [[S, D], [1, TOK]])
        dd = mkap(am_d, 0, [[TOK, D], [1, TOK]])
        return e.dma_start(out=dd, in_=src)
    tam = P.op("gpsimd", my_cols, reads=("cc3_out",), writes=("am_d",),
               sem="gam", inc=16, deps=[tcc3])

    if _TRUNC == "P2":
        return trunc_finish()
    P.barrier(exclude=("wc",))
    sdma(fc2b[:, :], fc2b_in[:, :], writes=("fc2b",))
    sdma(ident_f[:, :], ident_in[:, :], writes=("ident_f",))
    sdma(fc1b[:, :].rearrange("p (ee c) -> p ee c", ee=8),
         mkap(bcol_fc1_in, 0, [[32, 128], [128 * 32, 8], [1, 32]]),
         writes=("fc1b",))
    gdma(bcA[:, :], bo_in[:, :], writes=("bcA",), dsem="c3")
    gdma(bcB[:, :], ln2w_in[:, :], writes=("bcB",), dsem="c3")
    c3_last = gdma(bcC[:, :], ln2b_in[:, :], writes=("bcC",), dsem="c3")
    for nm in ("bcA", "bcB", "bcC"):
        P.wtok[nm] = c3_last
    twL = gdma(woutT[:, :].rearrange("p (r c) -> p r c", r=8),
               mkap(w_Wo, 0, [[D, 128], [128 * D, 8], [1, D]]),
               writes=("woutT",))

    for t in range(NT):
        p3 = t % 2
        tat = sdma(attnTo_t[p3][:, :].rearrange("p (r c) -> p r c", r=8),
                   mkap(am_d, t * 128, [[TOK, 128], [128 * TOK, 8], [1, 128]]),
                   reads=("am_d",), writes=(f"attnTo{p3}",), deps=[tam])
        for half in range(2):
            pn = half
            for r in range(8):
                mm(ps[pn][:, 0:512],
                   attnTo_t[p3][:, r * 128:(r + 1) * 128],
                   woutT[:, r * D + half * 512: r * D + half * 512 + 512],
                   start=(r == 0), stop=(r == 7),
                   reads=("woutT", f"attnTo{p3}"), writes=(f"ps{pn}",),
                   deps=[twL, tat])
            vo(lambda e, pn=pn, half=half: e.tensor_tensor(
                f3a[:, half * 512:(half + 1) * 512], ps[pn][:, 0:512],
                bcA[:, half * 512:(half + 1) * 512], ALU.add),
                reads=(f"ps{pn}", "bcA"), writes=("f3a",))
        if _DEBUG:
            sdma(dbg["attn"][t * 128:(t + 1) * 128, :], f3a[:, :], reads=("f3a",))
        vo(lambda e, t=t: e.tensor_tensor(xres_t[:, :], f3a[:, :],
                                          xmid[:, t * D:(t + 1) * D], ALU.add),
           reads=("f3a", f"xmid_{t}"), writes=("xres_t",))
        vo(lambda e, t=t: e.tensor_tensor(moe_acc[:, t * D:(t + 1) * D],
                                          xres_t[:, :],
                                          xmid[:, t * D:(t + 1) * D], ALU.add),
           reads=("xres_t", f"xmid_{t}"), writes=(f"moe_{t}",))

        def fin_nx2(mean_ap, rstd_ap):
            vo(lambda e: e.tensor_scalar(f3a[:, :], xres_t[:, :], mean_ap, rstd_ap,
                                         ALU.subtract, ALU.mult),
               reads=("xres_t", "c0", "c4"), writes=("f3a",))
        plain_ln(xres_t[:, :], "xres_t", fin_nx2, f3b, "f3b")
        vo(lambda e: e.tensor_tensor(f3a[:, :], f3a[:, :], bcB[:, :], ALU.mult),
           reads=("f3a", "bcB"), writes=("f3a",))
        vo(lambda e: e.tensor_tensor(nx2f[:, :], f3a[:, :], bcC[:, :], ALU.add),
           reads=("f3a", "bcC"), writes=("nx2f",))
        for half in range(2):
            tch = vo(lambda e, half=half: e.tensor_copy(
                rT3[:, half * 512:(half + 1) * 512],
                nx2f[:, half * 512:(half + 1) * 512]),
                reads=("nx2f",), writes=("rT3",))
            for u in range(4):
                tpose(psT[:, (half * 4 + u) * 128:(half * 4 + u + 1) * 128],
                      rT3[:, half * 512 + u * 128: half * 512 + (u + 1) * 128],
                      ident_bf[:, :],
                      reads=("rT3", "ident_bf"), writes=("psT",), deps=[tch])
        vo(lambda e, t=t: e.tensor_copy(
            nx2T[:, :].rearrange("p (r w) -> p r w", r=8)
            [:, :, t * 128:(t + 1) * 128],
            psT[:, 0:1024].rearrange("p (r w) -> p r w", r=8)),
            reads=("psT",), writes=tuple(f"nx2T_{r}" for r in range(8)))
        for half in range(2):
            tsub = vo(lambda e, half=half: e.tensor_tensor(
                f3b[:, 0:512], nx2f[:, half * 512:(half + 1) * 512],
                rT3[:, half * 512:(half + 1) * 512], ALU.subtract),
                reads=("nx2f", "rT3"), writes=("f3b",))
            tch = vo(lambda e, half=half: e.tensor_copy(
                rT3[:, half * 512:(half + 1) * 512], f3b[:, 0:512]),
                reads=("f3b",), writes=("rT3x",))
            for u in range(4):
                tpose(psT[:, (half * 4 + u) * 128:(half * 4 + u + 1) * 128],
                      rT3[:, half * 512 + u * 128: half * 512 + (u + 1) * 128],
                      ident_bf[:, :],
                      reads=("rT3x", "ident_bf"), writes=("psT",), deps=[tch])
        vo(lambda e: e.tensor_copy(rT3[:, :], psT[:, 0:1024]),
           reads=("psT", "rT3x"), writes=("rT3",))
        for r in range(8):
            mm(ps[2][:, 0:8],
               nx2T[:, r * TOK + t * 128: r * TOK + (t + 1) * 128],
               gateW[:, r * 8:(r + 1) * 8],
               start=(r == 0), stop=False,
               reads=(f"nx2T_{r}", "gateW"), writes=("ps2",))
        for r in range(8):
            mm(ps[2][:, 0:8],
               rT3[:, r * 128:(r + 1) * 128],
               gateW[:, r * 8:(r + 1) * 8],
               start=False, stop=(r == 7),
               reads=("rT3", "gateW"), writes=("ps2",))
        vo(lambda e: e.tensor_copy(lg8[:, :], ps[2][:, 0:8]),
           reads=("ps2",), writes=("lg8",))
        vo(lambda e: e.reduce_max(w12[:, 0:1], lg8[:, :], axis=AX.X),
           reads=("lg8",), writes=("w0",))
        vo(lambda e: e.tensor_single_scalar(is1[:, :], lg8[:, :], w12[:, 0:1],
                                            ALU.is_ge),
           reads=("lg8", "w0"), writes=("is1",))
        vo(lambda e: e.scalar_tensor_tensor(msk8[:, :], is1[:, :], NEG, lg8[:, :],
                                            ALU.mult, ALU.add),
           reads=("is1", "lg8"), writes=("msk8",))
        vo(lambda e: e.reduce_max(w12[:, 1:2], msk8[:, :], axis=AX.X),
           reads=("msk8",), writes=("w1c",))
        vo(lambda e: e.tensor_single_scalar(is2[:, :], msk8[:, :], w12[:, 1:2],
                                            ALU.is_ge),
           reads=("msk8", "w1c"), writes=("is2",))
        vo(lambda e: e.tensor_sub(w12[:, 2:3], w12[:, 1:2], w12[:, 0:1]),
           reads=("w0", "w1c"), writes=("w2c",))
        so(lambda e: e.activation(w12[:, 2:3], w12[:, 2:3], AF.Exp),
           reads=("w2c",), writes=("w2c",))
        vo(lambda e: e.tensor_scalar_add(w12[:, 3:4], w12[:, 2:3], 1.0),
           reads=("w2c",), writes=("w3c",))
        vo(lambda e: e.reciprocal(w12[:, 3:4], w12[:, 3:4]),
           reads=("w3c",), writes=("w3c",))
        vo(lambda e: e.tensor_mul(w12[:, 2:3], w12[:, 2:3], w12[:, 3:4]),
           reads=("w2c", "w3c"), writes=("w2c",))
        vo(lambda e: e.tensor_single_scalar(is2[:, :], is2[:, :], w12[:, 2:3],
                                            ALU.mult),
           reads=("is2", "w2c"), writes=("is2",))
        vo(lambda e, t=t: e.scalar_tensor_tensor(
            gates[:, t * 8:(t + 1) * 8], is1[:, :], w12[:, 3:4], is2[:, :],
            ALU.mult, ALU.add),
            reads=("is1", "is2", "w3c"), writes=(f"gates_{t}",))
        tgt = tpose(ps[3][0:8, 0:128], gates[:, t * 8:(t + 1) * 8], ident_f[:, :],
                    reads=(f"gates_{t}", "ident_f"), writes=("ps3",))
        tge = vo(lambda e: e.tensor_copy(g8T[:, :], ps[3][0:8, 0:128]),
                 reads=("ps3",), writes=("g8T",))
        for half in range(2):
            pn = half
            mm(ps[pn][:, 0:512], g8T[:, :], fc2b[:, half * 512:(half + 1) * 512],
               start=True, stop=True, reads=("g8T", "fc2b"), writes=(f"ps{pn}",),
               deps=[tge])
            vo(lambda e, pn=pn, t=t, half=half: e.tensor_tensor(
                moe_acc[:, t * D + half * 512: t * D + (half + 1) * 512],
                moe_acc[:, t * D + half * 512: t * D + (half + 1) * 512],
                ps[pn][:, 0:512], ALU.add),
                reads=(f"ps{pn}", f"moe_{t}"), writes=(f"moe_{t}",))

    if _TRUNC == "P3wo":
        return trunc_finish()
    P.barrier()
    w1rot = [0]
    w2rot = [0]
    for ei in range(NE):
        for qg in range(4):
            for fh in range(8):
                sl = w1rot[0] % 2
                w1rot[0] += 1
                t_w1 = gdma(w1buf[sl][:, :].rearrange("p (r c) -> p r c", r=8),
                            mkap(w1bf, ei * D * FFN + fh * 512,
                                 [[FFN, 128], [128 * FFN, 8], [1, 512]]),
                            writes=(f"w1buf{sl}",), deps=[P.wtok[f"w1bf_{ei}"]],
                            dsem=f"w1{sl}")
                for fs in range(4):
                    pn = nps()
                    for r in range(8):
                        mm(ps[pn][:, 0:512],
                           w1buf[sl][:, r * 512 + fs * 128: r * 512 + (fs + 1) * 128],
                           nx2T[:, r * TOK + qg * 512: r * TOK + (qg + 1) * 512],
                           start=(r == 0), stop=(r == 7),
                           reads=(f"w1buf{sl}", f"nx2T_{r}"), writes=(f"ps{pn}",),
                           deps=[t_w1])
                    fc = fh * 4 + fs
                    so(lambda e, pn=pn, ei=ei, fc=fc: e.activation(
                        hT[:, fc * 512:(fc + 1) * 512], ps[pn][:, 0:512], AF.Gelu,
                        bias=fc1b[:, ei * 32 + fc: ei * 32 + fc + 1]),
                        reads=(f"ps{pn}", "fc1b"), writes=(f"hT_{fc}",))
            for f2h in range(16):
                sl = w2rot[0] % 2
                w2rot[0] += 1
                t_w2 = gdma(w2buf[sl][:, :].rearrange("p (r c) -> p r c", r=2),
                            mkap(w2bf, ei * FFN * D + f2h * 256 * D,
                                 [[D, 128], [128 * D, 2], [1, D]]),
                            writes=(f"w2buf{sl}",), deps=[P.wtok[f"w2bf_{ei}"]],
                            dsem=f"w2{sl}")
                for qt in range(4):
                    for nh in range(2):
                        pn = qt * 2 + nh
                        pname = f"ps{pn}" if pn < 7 else "psT"
                        out_ap = (ps[pn][:, 0:512] if pn < 7
                                  else psT[:, 0:1024].bitcast(F32))
                        for fs in range(2):
                            fc = f2h * 2 + fs
                            mm(out_ap,
                               hT[:, fc * 512 + qt * 128: fc * 512 + (qt + 1) * 128],
                               w2buf[sl][:, fs * D + nh * 512: fs * D + nh * 512 + 512],
                               start=(f2h == 0 and fs == 0),
                               stop=(f2h == 15 and fs == 1),
                               reads=(f"w2buf{sl}", f"hT_{fc}"),
                               writes=(pname,), deps=[t_w2])
            for qt in range(4):
                for nh in range(2):
                    pn = qt * 2 + nh
                    pname = f"ps{pn}" if pn < 7 else "psT"
                    src = (ps[pn][:, 0:512] if pn < 7
                           else psT[:, 0:1024].bitcast(F32))
                    tt = qg * 4 + qt
                    vo(lambda e, src=src, tt=tt, nh=nh, ei=ei: e.scalar_tensor_tensor(
                        moe_acc[:, tt * D + nh * 512: tt * D + (nh + 1) * 512],
                        src, gates[:, tt * 8 + ei: tt * 8 + ei + 1],
                        moe_acc[:, tt * D + nh * 512: tt * D + (nh + 1) * 512],
                        ALU.mult, ALU.add),
                        reads=(pname, f"gates_{tt}", f"moe_{tt}"),
                        writes=(f"moe_{tt}",))

    P.barrier()
    out_toks = []
    for t in range(NT):
        def fin_out(mean_ap, rstd_ap, t=t):
            vo(lambda e: e.tensor_scalar(
                f3a[:, :], moe_acc[:, t * D:(t + 1) * D], mean_ap, rstd_ap,
                ALU.subtract, ALU.mult),
                reads=(f"moe_{t}", "c0", "c4"), writes=("f3a",))
        plain_ln(moe_acc[:, t * D:(t + 1) * D], f"moe_{t}", fin_out, f3b, "f3b")
        out_toks.append(sdma(out_dram[t * 128:(t + 1) * 128, :], f3a[:, :],
                             reads=("f3a",)))

    for tok in out_toks:
        P._wait("sync", tok[0], tok[1])

    P.finalize()
    import contextlib
    with contextlib.ExitStack() as st:
        for s in P.sem_names:
            P.sems[s] = st.enter_context(nc.semaphore(s))
        block = st.enter_context(nc.Block())
        P.replay(block)
    return nc


_NC_CACHE = {}


def kernel(**inputs):
    f32 = np.float32
    g = {k: np.asarray(v, dtype=f32) for k, v in inputs.items()}
    x = g["L0_attn_x"]

    def bc(v):
        return np.ascontiguousarray(np.broadcast_to(v.reshape(1, D), (128, D)))

    shared = {
        "w_qkv": g["L0_attn_qkv_w"],
        "bcol_qkv": np.ascontiguousarray(g["L0_attn_qkv_b"].reshape(24, 128).T),
        "w_rq": g["L0_attn_rq_w"],
        "bcol_rq": np.ascontiguousarray(g["L0_attn_rq_b"].reshape(2, 128).T),
        "w_rkv": g["L0_attn_rkv_w"],
        "bcol_rkv": np.ascontiguousarray(g["L0_attn_rkv_b"].reshape(4, 128).T),
        "w_rout": g["L0_attn_rout_w"],
        "bcol_rout": np.ascontiguousarray(g["L0_attn_rout_b"].reshape(8, 128).T),
        "w_out": g["L0_attn_out_w"],
        "outb_bc": bc(g["L0_attn_out_b"]),
        "gl": np.ascontiguousarray(np.concatenate([g["L0_attn_gl_w"]] * 2, 0)),
        "gr": np.ascontiguousarray(np.concatenate([g["L0_attn_gr_w"]] * 2, 0)),
        "w_Wqkv": g["L0_ffn_Wqkv"],
        "bcol_Wqkv": np.ascontiguousarray(g["L0_ffn_bqkv"].reshape(24, 128).T),
        "w_Wo": g["L0_ffn_Wo"],
        "bo_bc": bc(g["L0_ffn_bo"]),
        "ln1w_bc": bc(g["L0_ffn_ln1_w"]),
        "ln1b_bc": bc(g["L0_ffn_ln1_b"]),
        "ln2w_bc": bc(g["L0_ffn_ln2_w"]),
        "ln2b_bc": bc(g["L0_ffn_ln2_b"]),
        "w_gate": g["L0_ffn_gate_W"],
        "w1": np.ascontiguousarray(g["L0_ffn_fc1_w"].transpose(0, 2, 1)),
        "bcol_fc1": np.ascontiguousarray(
            g["L0_ffn_fc1_b"].reshape(NE, 32, 128).transpose(0, 2, 1)),
        "w2": np.ascontiguousarray(g["L0_ffn_fc2_w"].transpose(0, 2, 1)),
        "fc2b": g["L0_ffn_fc2_b"],
        "ident": np.eye(128, dtype=f32),
    }
    jj, ii = np.meshgrid(np.arange(128), np.arange(128), indexing="ij")
    shared["mlo"] = np.where(ii <= jj, 0.0, NEG).astype(f32)
    shared["mhi"] = np.where(jj < ii, 0.0, NEG).astype(f32)
    mcau = np.empty((4, 128, 512), f32)
    uu = np.arange(512)
    for m in range(4):
        mcau[m] = np.where(uu[None, :] >= (128 * m + np.arange(128))[:, None],
                           0.0, NEG)
    shared["mcau"] = mcau

    in_maps = []
    for c in range(8):
        b, qd = c // 4, c % 4
        t0 = qd * TOK
        xT = np.zeros((D, XW), f32)
        xb = np.ascontiguousarray(x[b].T)
        xT[:, HALO:] = xb[:, t0:t0 + TOK]
        if qd > 0:
            xT[:, :HALO] = xb[:, t0 - HALO:t0]
        lat = np.arange(64)[:, None]
        tt = np.arange(NT)[None, :]
        m = dict(shared)
        m["xT"] = xT
        m["xtok"] = np.ascontiguousarray(x[b, t0:t0 + TOK])
        m["mlo0"] = (np.full((128, 128), NEG, f32) if qd == 0 else shared["mlo"])
        m["mlat"] = np.where(lat < (qd * 16 + tt), 0.0, NEG).astype(f32)
        in_maps.append(m)

    if "nc" not in _NC_CACHE:
        _NC_CACHE["nc"] = build_nc()
    nc = _NC_CACHE["nc"]
    res = run_bass_kernel_spmd(nc, in_maps, core_ids=list(range(8)))
    out = np.empty((B, S, D), f32)
    for c in range(8):
        b, qd = c // 4, c % 4
        out[b, qd * TOK:(qd + 1) * TOK] = np.asarray(res.results[c]["out"])
    return out
